# version 07
# baseline (speedup 1.0000x reference)
import sys
sys.path.insert(0, "/opt/trn_rl_repo")
import contextlib
import numpy as np
import concourse.bass as bass
import concourse.mybir as mybir

F32 = mybir.dt.float32
BF16 = mybir.dt.bfloat16
F16 = mybir.dt.float16
I32 = mybir.dt.int32
I16 = mybir.dt.int16
AF = mybir.ActivationFunctionType
AX = mybir.AxisListType
OP = mybir.AluOpType

SECTION_OF = {"pe": "tensor", "dve": "vector", "act": "scalar", "pool": "gpsimd",
              "qsp": "sync", "qact": "scalar", "qpool": "gpsimd"}
for _q in ("qsp", "qact", "qpool"):
    for _i in range(8):
        SECTION_OF[f"{_q}{_i}"] = SECTION_OF[_q]


class Tok:
    __slots__ = ("name", "writer", "readers")

    def __init__(self, name=""):
        self.name = name
        self.writer = None
        self.readers = []


class _Eng:
    __slots__ = ("name", "sem", "seq", "cells", "step", "last_real", "prefix")

    def __init__(self, name, sem, step):
        self.name = name
        self.sem = sem
        self.seq = 0
        self.cells = []
        self.step = step
        self.last_real = -1
        self.prefix = None


class P:
    def __init__(self, nc, ctx):
        self.nc = nc
        self.ctx = ctx
        self._scopes = []
        self.thunks = {s: [] for s in ("tensor", "vector", "scalar", "gpsimd", "sync")}
        self.eng = {}
        self.sem_tok = {}
        self.qslots = {}
        for name in ("pe", "dve", "act", "pool"):
            sem = ctx.enter_context(nc.semaphore(f"s_{name}"))
            self.eng[name] = _Eng(name, sem, 1)
        for q in ("qsp", "qact", "qpool"):
            self.qslots[q] = []
            for i in range(8):
                name = f"{q}{i}"
                sem = ctx.enter_context(nc.semaphore(f"s_{name}"))
                self.eng[name] = _Eng(name, sem, 16)
                self.sem_tok[name] = Tok(f"sem_{name}")
                self.qslots[q].append(name)
        self._qrr = {q: 0 for q in ("qsp", "qact", "qpool")}
        self.waited = {}
        self._sbuf_bytes = 0

    def tok(self, name=""):
        return Tok(name)

    def sb(self, name, shape, dtype, root=False):
        ctx = self.ctx if (root or not self._scopes) else self._scopes[-1]
        h = ctx.enter_context(self.nc.sbuf_tensor(name, shape, dtype))
        nb = int(np.prod(shape[1:])) * mybir.dt.size(dtype)
        self._sbuf_bytes += nb
        return h, Tok(name)

    def ps(self, name, shape):
        h = self.ctx.enter_context(self.nc.psum_tensor(name, shape, F32))
        return h, Tok(name)

    def push_scope(self):
        self._scopes.append(contextlib.ExitStack())

    def pop_scope(self):
        self.global_barrier()
        self._scopes.pop().close()

    def _close(self, e, seq):
        cell = e.cells[seq]
        if cell["inc"] == 0:
            assert not cell.get("noinst"), f"close({e.name},{seq}) targets wait-only op"
            cell["inc"] = e.step
        return seq

    def emit(self, eng_name, fn, reads=(), writes=(), close=False, tag=""):
        e = self.eng[eng_name]
        seq_self = e.seq
        section = SECTION_OF[eng_name]
        need = {}
        for t in reads:
            if t.writer is not None:
                pe_, sq = t.writer
                need[pe_] = max(need.get(pe_, -1), sq)
        for t in writes:
            if t.writer is not None:
                pe_, sq = t.writer
                need[pe_] = max(need.get(pe_, -1), sq)
            for pe_, sq in t.readers:
                need[pe_] = max(need.get(pe_, -1), sq)
        waits = []
        for pe_, sq in need.items():
            key = (section, pe_)
            if self.waited.get(key, -1) >= sq:
                continue
            self.waited[key] = sq
            self._close(self.eng[pe_], sq)
            waits.append((self.eng[pe_], sq))
            st = self.sem_tok.get(pe_)
            if st is not None:
                st.readers.append((eng_name, seq_self))
        e.seq += 1
        cell = {"inc": 0}
        if fn is None:
            cell["noinst"] = True
        e.cells.append(cell)
        if fn is not None:
            e.last_real = seq_self

        def thunk(nc=self.nc, waits=waits, fn=fn, cell=cell, sem=e.sem,
                  eng_obj_name=section):
            eng = getattr(nc, eng_obj_name)
            for pe_obj, sq in waits:
                eng.wait_ge(pe_obj.sem, pe_obj.prefix[sq])
            if fn is not None:
                instr = fn()
                if cell["inc"]:
                    instr.then_inc(sem, cell["inc"])

        self.thunks[section].append(thunk)
        if fn is not None:
            for t in reads:
                t.readers.append((eng_name, seq_self))
            for t in writes:
                t.writer = (eng_name, seq_self)
                t.readers = []
        if close:
            self._close(e, seq_self)
        return seq_self

    def global_barrier(self):
        names = ["pe", "dve", "act", "pool"] + [s for q in self.qslots.values() for s in q]
        vals = {}
        for n in names:
            e = self.eng[n]
            if e.last_real >= 0:
                vals[n] = self._close(e, e.last_real)
        for sec in ("tensor", "vector", "scalar", "gpsimd", "sync"):
            waits = []
            for n, sq in vals.items():
                key = (sec, n)
                if self.waited.get(key, -1) >= sq:
                    continue
                self.waited[key] = sq
                waits.append((self.eng[n], sq))
            if not waits:
                continue

            def thunk(nc=self.nc, waits=tuple(waits), obj=sec):
                eng = getattr(nc, obj)
                for pe_obj, sq in waits:
                    eng.wait_ge(pe_obj.sem, pe_obj.prefix[sq])
            self.thunks[sec].append(thunk)

    def barrier_wait(self, eng_name, toks):
        self.emit(eng_name, None, reads=toks)

    def finish(self):
        for q in ("qsp", "qact", "qpool"):
            obj = SECTION_OF[q]
            for name in self.qslots[q]:
                e = self.eng[name]
                if e.last_real < 0:
                    continue
                self._close(e, e.last_real)

                def thunk(nc=self.nc, e=e, obj=obj):
                    if e.prefix and e.prefix[-1] > 0:
                        getattr(nc, obj).wait_ge(e.sem, e.prefix[-1])
                self.thunks[obj].append(thunk)

    def finalize_into_block(self, block):
        for e in self.eng.values():
            run = 0
            pref = []
            for cell in e.cells:
                run += cell["inc"]
                pref.append(run)
            e.prefix = pref
        for section in ("sync", "gpsimd", "tensor", "vector", "scalar"):
            ths = self.thunks[section]
            if not ths:
                continue
            deco = getattr(block, section)

            def body(engine, ths=ths):
                for t in ths:
                    t()
            deco(body)

    def dma(self, queue, out_ap, in_ap, reads=(), writes=(), **kw):
        slot = self.qslots[queue][self._qrr[queue] % len(self.qslots[queue])]
        self._qrr[queue] += 1
        eng_obj = SECTION_OF[queue]
        return self.emit(slot, lambda: getattr(self.nc, eng_obj).dma_start(out=out_ap, in_=in_ap, **kw),
                         reads=list(reads),
                         writes=list(writes) + [self.sem_tok[slot]], close=True)


def make_nc():
    return bass.Bass("TRN2", target_bir_lowering=False, debug=False)


"""Stage builders for the mk_recurrent_d2 megakernel. All per-core (one batch elem).

Layouts:
 - activations token-major [tok x feat] row-major in HBM
 - xnT in HBM: plain [E, T] bf16 row-major (feature-major)
 - s/g outputs token-major [T, E] bf16
"""
import numpy as np

USE_DMA_TRANSPOSE = False
T = 2048
E = 2048
NT = T // 128
NC = E // 128


class C:

    def __init__(self, p: P):
        self.p = p
        self.nc = p.nc
        self.psum_h, _ = p.ps("psumall", [128, 4096])
        self.banks = [(self.psum_h[:, i * 512:(i + 1) * 512], Tok(f"bank{i}")) for i in range(8)]
        self.scores = (self.psum_h[:, 2048:4096], [self.banks[i][1] for i in range(4, 8)])
        self.ident = self.nc.dram_tensor("c_ident", [128, 128], F32, kind="ExternalInput").ap()
        self.ident_sb, self.t_ident = p.sb("identsb", [128, 128], BF16)
        p.dma("qpool", self.ident_sb[:], self.ident[:, :], writes=[self.t_ident])
        self.ident32_sb, self.t_ident32 = p.sb("ident32sb", [128, 128], F32)
        p.dma("qsp", self.ident32_sb[:], self.ident[:, :], writes=[self.t_ident32])
        self.maskc = self.nc.dram_tensor("c_mask", [128, 128], F32, kind="ExternalInput").ap()
        self.maskc_sb, self.t_maskc = p.sb("maskcsb", [128, 128], F32)
        p.dma("qsp", self.maskc_sb[:], self.maskc[:, :], writes=[self.t_maskc])
        self._consts = {}
        for v in (0.0, 1.0, -0.5, 1e-5, 1e-6, 1.0 / 2048, 1.0 / 512, 1.0 / 1536,
                  1.0 / np.sqrt(192.0), 1.0 / np.sqrt(128.0), -2.0, -6.0, 2.0):
            self.const(v)

    def const(self, val):
        key = float(val)
        if key not in self._consts:
            assert not self.p._scopes, f"const {key} requested inside a scope; pre-register in C.__init__"
            h, t = self.p.sb(f"c{len(self._consts)}", [128, 1], F32, root=True)
            self.p.emit("dve", lambda h=h, v=key: self.nc.vector.memset(h[:], v), writes=[t])
            try:
                self.nc.const_aps.register(mybir.dt.float32, key, h[:])
            except Exception:
                pass
            self._consts[key] = (h, t)
        return self._consts[key]

    def host_consts(self):
        return {"c_ident": np.eye(128, dtype=np.float32),
                "c_mask": np.where(np.triu(np.ones((128, 128), bool), k=1), -1e30, 0.0).astype(np.float32)}

    def transpose(self, bank_i, col_off, src_ap, reads, n_rows=128):
        bh, bt = self.banks[bank_i]
        self.p.emit("pe", lambda: self.nc.tensor.matmul(
            bh[0:src_ap.shape[-1], col_off:col_off + n_rows],
            src_ap, self.ident_sb[0:n_rows, 0:n_rows], start=True, stop=True),
            reads=list(reads) + [self.t_ident], writes=[bt])
        return bh, bt

    def mm(self, bank_i, cols, lhsT_ap, rhs_ap, reads, start, stop, close=False):
        bh, bt = self.banks[bank_i]
        self.p.emit("pe", lambda: self.nc.tensor.matmul(
            bh[0:lhsT_ap.shape[-1], cols[0]:cols[1]], lhsT_ap, rhs_ap, start=start, stop=stop),
            reads=list(reads), writes=[bt], close=close)
        return bh, bt


def s_ln(c: C, x_ap, xnt_ap, pfx, eps=1e-5):
    p, nc = c.p, c.nc
    p.push_scope()
    xs = [p.sb(f"{pfx}x{i}", [128, E], F32) for i in range(2)]
    sq, t_sq = p.sb(f"{pfx}sq", [128, E], F32)
    xns = [p.sb(f"{pfx}xn{i}", [128, E], BF16) for i in range(2)]
    st, t_st = p.sb(f"{pfx}st", [128, 8], F32)
    evs = [p.sb(f"{pfx}ev{i}", [128, 512], BF16) for i in range(4)]
    eb, ebt = c.const(eps)
    one, onet = c.const(1.0)
    nh, nht = c.const(-0.5)
    zz, zzt = c.const(0.0)
    for t in range(NT):
        xh, xt = xs[t % 2]
        xn, t_xn = xns[t % 2]
        p.dma("qsp", xh[:], x_ap[t * 128:(t + 1) * 128, :], writes=[xt])
        p.emit("dve", lambda xh=xh: nc.vector.tensor_reduce(st[:, 0:1], xh[:], AX.X, OP.add),
               reads=[xt], writes=[t_st])
        p.emit("dve", lambda xh=xh: nc.vector.tensor_tensor(sq[:], xh[:], xh[:], OP.mult),
               reads=[xt], writes=[t_sq])
        p.emit("dve", lambda: nc.vector.tensor_reduce(st[:, 1:2], sq[:], AX.X, OP.add),
               reads=[t_sq], writes=[t_st])
        p.emit("dve", lambda: nc.vector.tensor_scalar(st[:, 2:3], st[:, 0:1], 1.0 / E, None, OP.mult),
               reads=[t_st], writes=[t_st])
        p.emit("dve", lambda: nc.vector.tensor_tensor(st[:, 3:4], st[:, 2:3], st[:, 2:3], OP.mult),
               reads=[t_st], writes=[t_st])
        p.emit("dve", lambda: nc.vector.scalar_tensor_tensor(
            st[:, 4:5], st[:, 1:2], 1.0 / E, st[:, 3:4], OP.mult, OP.subtract),
            reads=[t_st], writes=[t_st])
        p.emit("act", lambda: nc.scalar.activation(st[:, 5:6], st[:, 4:5], AF.Ln, bias=eb[:], scale=one[:]),
               reads=[t_st, ebt, onet], writes=[t_st])
        p.emit("act", lambda: nc.scalar.activation(st[:, 6:7], st[:, 5:6], AF.Exp, bias=zz[:], scale=nh[:]),
               reads=[t_st, nht, zzt], writes=[t_st])
        p.emit("dve", lambda xh=xh, xn=xn: nc.vector.tensor_scalar(
            xn[:], xh[:], st[:, 2:3], st[:, 6:7], OP.subtract, OP.mult),
            reads=[xt, t_st], writes=[t_xn])
        for quad in range(4):
            bi = (t % 2) * 4 + quad
            for j in range(4):
                cc = quad * 4 + j
                c.transpose(bi, j * 128, xn[:, cc * 128:(cc + 1) * 128], [t_xn])
            evh, evt = evs[quad]
            bh, bt = c.banks[bi]
            p.emit("act", lambda bh=bh, evh=evh: nc.scalar.copy(evh[:], bh[:]),
                   reads=[bt], writes=[evt])
            dst = xnt_ap[quad * 512:(quad + 1) * 512, t * 128:(t + 1) * 128]
            dst = dst.rearrange("(c f) k -> f c k", c=4)
            p.dma("qact", dst, evh[:].rearrange("f (c k) -> f c k", c=4), reads=[evt])
    p.pop_scope()


def ref_ln(x):
    m = x.mean(-1, keepdims=True)
    v = x.var(-1, keepdims=True)
    return (x - m) / np.sqrt(v + 1e-5)



def decode_fp4_l0(c: C, pk_ap, sct_pair, ds, n_words, reads_pk):
    p, nc = c.p, c.nc
    K = 8 * n_words
    G = n_words // 16
    (ki, t_ki), (sgn, t_sgn), (ssc, t_ssc) = ds["ki"], ds["sgn"], ds["ssc"]
    (va, t_va), (t1h, t_t1), (t2h, t_t2), (t3h, t_t3), (vv, t_vv) = \
        ds["va"], ds["t1"], ds["t2"], ds["t3"], ds["vv"]
    wd, t_wd = ds["wd"]
    sch, t_sch = sct_pair
    for j in range(8):
        sl = slice(j * n_words, (j + 1) * n_words)
        if j == 7:
            p.emit("dve", lambda sl=sl: nc.vector.tensor_scalar(
                ki[:, sl], pk_ap, 0x70000000, None, OP.bitwise_and), reads=reads_pk, writes=[t_ki])
        else:
            p.emit("dve", lambda sl=sl, j=j: nc.vector.tensor_scalar(
                ki[:, sl], pk_ap, 28 - 4 * j, 0x70000000, OP.logical_shift_left, OP.bitwise_and),
                reads=reads_pk, writes=[t_ki])
        if j == 7:
            p.emit("dve", lambda sl=sl: nc.vector.tensor_scalar(
                sgn[:, sl], pk_ap, -0x80000000, None, OP.bitwise_and),
                reads=reads_pk, writes=[t_sgn])
        else:
            p.emit("dve", lambda sl=sl, j=j: nc.vector.tensor_scalar(
                sgn[:, sl], pk_ap, 28 - 4 * j, -0x80000000, OP.logical_shift_left, OP.bitwise_and),
                reads=reads_pk, writes=[t_sgn])
    sb = sch[:, 0:G].bitcast(I32).unsqueeze(1).unsqueeze(3).broadcast_to([128, 8, G, 16])
    p.emit("dve", lambda: nc.vector.tensor_tensor(
        ssc[:, 0:K].bitcast(I32).rearrange("p (j g r) -> p j g r", j=8, g=G),
        sb, sgn[:, 0:K].rearrange("p (j g r) -> p j g r", j=8, g=G), OP.bitwise_or),
        reads=[t_sch, t_sgn], writes=[t_ssc])
    p.emit("dve", lambda: nc.vector.tensor_scalar(va[:, 0:K], ki[:, 0:K], 0.5 / 268435456.0, None, OP.mult),
           reads=[t_ki], writes=[t_va])
    p.emit("dve", lambda: nc.vector.tensor_scalar(t1h[:, 0:K], va[:, 0:K], 2.0, -2.0, OP.mult, OP.add),
           reads=[t_va], writes=[t_t1])
    p.emit("dve", lambda: nc.vector.tensor_scalar(t3h[:, 0:K], t1h[:, 0:K], 2.0, -4.0, OP.mult, OP.add),
           reads=[t_t1], writes=[t_t3])
    p.emit("dve", lambda: nc.vector.tensor_tensor(t2h[:, 0:K], va[:, 0:K], t1h[:, 0:K], OP.max),
           reads=[t_va, t_t1], writes=[t_t2])
    p.emit("dve", lambda: nc.vector.tensor_tensor(vv[:, 0:K], t2h[:, 0:K], t3h[:, 0:K], OP.max),
           reads=[t_t2, t_t3], writes=[t_vv])
    p.emit("pool", lambda: nc.gpsimd.tensor_tensor(
        wd[:, 0:K].rearrange("p (w j) -> p j w", j=8),
        vv[:, 0:K].rearrange("p (j w) -> p j w", j=8),
        ssc[:, 0:K].rearrange("p (j w) -> p j w", j=8), OP.mult),
        reads=[t_vv, t_ssc], writes=[t_wd])
    return wd, t_wd


def decode_bufs(p, pfx, n_words, nset=3):
    K = 8 * n_words
    shared = {}
    for nm, dt in (("ki", I32), ("sgn", I32), ("va", BF16), ("t1", BF16),
                   ("t2", BF16), ("t3", BF16)):
        shared[nm] = p.sb(f"{pfx}{nm}", [128, K], dt)
    sets = []
    for i in range(max(3, nset)):
        d = dict(shared)
        for nm, dt in (("ssc", F32), ("vv", BF16), ("wd", BF16)):
            d[nm] = p.sb(f"{pfx}{nm}{i}", [128, K], dt)
        sets.append(d)
    return sets


def s_l0_ffn(c: C, xnt_ap, pk_ap, sc_ap, dpk_ap, dsc_ap, ht_ap, y_ap, g_ap, pfx="f0"):
    p, nc = c.p, c.nc
    zz, zzt = c.const(0.0)
    one, onet = c.const(1.0)
    eb, ebt = c.const(1e-6)
    nh, nht = c.const(-0.5)
    inve, invet = c.const(1.0 / E)
    t_ht = [p.tok(f"ht{i}") for i in range(44)]
    t_y = [p.tok(f"y{i}") for i in range(NT)]
    p.push_scope()
    gscs = [p.sb(f"{pfx}gsc{i}", [44, 176], F32) for i in range(2)]
    dec = decode_bufs(p, f"{pfx}d", 256, 3)
    pks = [p.sb(f"{pfx}pk{i}", [128, 256], I32) for i in range(2)]
    scts = [p.sb(f"{pfx}sct{i}", [128, 16], F32) for i in range(2)]
    p.push_scope()
    xnt, t_xnt = p.sb(f"{pfx}xnt", [128, NC, T], BF16)
    for cc in range(NC):
        p.dma("qsp", xnt[:, cc, :], xnt_ap[cc * 128:(cc + 1) * 128, :], writes=[t_xnt])
    wtiles = [p.sb(f"{pfx}wt{i}", [128, 2048], BF16) for i in range(2)]
    gsbs = [p.sb(f"{pfx}gsb{i}", [128, 2048], BF16) for i in range(3)]
    hsbs = [p.sb(f"{pfx}hsb{i}", [128, 2048], BF16) for i in range(3)]
    usbs = [p.sb(f"{pfx}usb{i}", [128, 2048], BF16) for i in range(3)]
    pend = []

    def _flush_hmult(item):
        (j, gsb, t_gsb, usb, t_usb, hsb, t_hsb) = item
        p.emit("dve", lambda gsb=gsb, usb=usb, hsb=hsb: nc.vector.tensor_tensor(
            hsb[:], gsb[:], usb[:], OP.mult), reads=[t_usb, t_gsb], writes=[t_hsb])
        p.dma("qact", ht_ap[j * 128:(j + 1) * 128, :], hsb[:], reads=[t_hsb],
              writes=[t_ht[j]])

    def sc_transpose(src_ap, src_tok, gcount, sct, t_sct):
        bh6, bt6 = c.banks[6]
        p.emit("pe", lambda: nc.tensor.matmul(
            bh6[0:128, 0:gcount], src_ap, c.ident32_sb[0:gcount, 0:gcount],
            start=True, stop=True), reads=[src_tok, c.t_ident32], writes=[bt6])
        p.emit("act", lambda: nc.scalar.copy(sct[:, 0:gcount], bh6[0:128, 0:gcount]),
               reads=[bt6], writes=[t_sct])

    def wd_transposes(wd, t_wd, kchunks, store_fn):
        for g4 in range((kchunks + 3) // 4):
            bi = 4 + (g4 % 2)
            bh, bt = c.banks[bi]
            js = list(range(g4 * 4, min(g4 * 4 + 4, kchunks)))
            for idx, kc in enumerate(js):
                c.transpose(bi, idx * 128, wd[:, kc * 128:(kc + 1) * 128], [t_wd])
            store_fn(js, bh, bt)

    for i in range(44):
        for gu in range(2):
            n = i + gu * 44
            it = 2 * i + gu
            pkh, t_pk = pks[it % 2]
            sct, t_sct = scts[it % 2]
            ds = dec[it % 3]
            wtile, t_wtile = wtiles[it % 2]
            p.dma("qsp", pkh[:], pk_ap[n * 128:(n + 1) * 128, :], writes=[t_pk])
            gsch, t_gsch = gscs[it % 2]
            p.dma("qpool", gsch[0:16, 0:128], sc_ap[:, n * 128:(n + 1) * 128], writes=[t_gsch])
            sc_transpose(gsch[0:16, 0:128], t_gsch, 16, sct, t_sct)
            decode_fp4_l0(c, pkh[:], (sct, t_sct), ds, 256, [t_pk])
            wd, t_wd = ds["wd"]

            def store(js, bh, bt, wtile=wtile, t_wtile=t_wtile):
                w0, n_ = js[0], len(js)
                p.emit("act", lambda w0=w0, n_=n_, bh=bh: nc.scalar.copy(
                    wtile[:, w0 * 128:(w0 + n_) * 128], bh[:, 0:n_ * 128]),
                    reads=[bt], writes=[t_wtile])
            wd_transposes(wd, t_wd, 16, store)
            for kc in range(NC):
                for tc in range(4):
                    c.mm(tc, (0, 512), wtile[:, kc * 128:(kc + 1) * 128],
                         xnt[:, kc, tc * 512:(tc + 1) * 512], [t_wtile, t_xnt],
                         start=(kc == 0), stop=(kc == NC - 1))
            gsb, t_gsb = gsbs[i % 3]
            usb, t_usb = usbs[i % 3]
            hsb, t_hsb = hsbs[i % 3]
            if gu == 0:
                for tc in range(4):
                    bh, bt = c.banks[tc]
                    p.emit("act", lambda bh=bh, gsb=gsb, tc=tc: nc.scalar.activation(
                        gsb[:, tc * 512:(tc + 1) * 512], bh[:], AF.Silu, bias=zz[:], scale=one[:]),
                        reads=[bt, zzt, onet], writes=[t_gsb])
            else:
                for tc in range(4):
                    bh, bt = c.banks[tc]
                    p.emit("act", lambda bh=bh, usb=usb, tc=tc: nc.scalar.copy(
                        usb[:, tc * 512:(tc + 1) * 512], bh[:]), reads=[bt], writes=[t_usb])
                pend.append((i, gsb, t_gsb, usb, t_usb, hsb, t_hsb))
                if len(pend) >= 3:
                    _flush_hmult(pend.pop(0))
    while pend:
        _flush_hmult(pend.pop(0))

    p.pop_scope()
    p.push_scope()
    wdt, t_wdt = p.sb(f"{pfx}wdt", [128, 44, 256], BF16)
    ypsb, t_ypsb = p.sb(f"{pfx}ypsb", [128, 512], BF16)
    hch = [p.sb(f"{pfx}hch{i}", [128, 2048], BF16) for i in range(2)]
    ytev, t_ytev = p.sb(f"{pfx}ytev", [128, 4096], BF16)
    for ps in range(8):
        for sub2 in range(2):
            hrow = ps * 2 + sub2
            pkh, t_pk = pks[sub2 % 2]
            sct, t_sct = scts[sub2 % 2]
            for sl4 in range(4):
                ds = dec[(sub2 * 4 + sl4) % 3]
                w0 = sl4 * 176
                p.dma("qsp", pkh[:, 0:176], dpk_ap[hrow * 128:(hrow + 1) * 128, w0:w0 + 176],
                      writes=[t_pk])
                gsch, t_gsch = gscs[sub2 % 2]
                p.dma("qpool", gsch[0:11, 0:128],
                      dsc_ap[sl4 * 11:sl4 * 11 + 11, hrow * 128:(hrow + 1) * 128], writes=[t_gsch])
                sc_transpose(gsch[0:11, 0:128], t_gsch, 11, sct, t_sct)
                decode_fp4_l0(c, pkh[:, 0:176], (sct, t_sct), ds, 176, [t_pk])
                wd, t_wd = ds["wd"]

                def store(js, bh, bt, sl4=sl4, sub2=sub2):
                    for idx, kc in enumerate(js):
                        ic = sl4 * 11 + kc
                        p.emit("act", lambda bh=bh, idx=idx, ic=ic, sub2=sub2: nc.scalar.copy(
                            wdt[:, ic, sub2 * 128:(sub2 + 1) * 128],
                            bh[:, idx * 128:(idx + 1) * 128]),
                            reads=[bt], writes=[t_wdt])
                wd_transposes(wd, t_wd, 11, store)
        for ic in range(44):
            hh, t_hh = hch[ic % 2]
            p.dma("qsp", hh[:], ht_ap[ic * 128:(ic + 1) * 128, :], reads=[t_ht[ic]], writes=[t_hh])
            for hj in range(2):
                for tc in range(4):
                    c.mm(hj * 4 + tc, (0, 512), wdt[:, ic, hj * 128:(hj + 1) * 128],
                         hh[:, tc * 512:(tc + 1) * 512], [t_wdt, t_hh],
                         start=(ic == 0), stop=(ic == 43))
        for bi in range(8):
            bh, bt = c.banks[bi]
            p.emit("act", lambda bh=bh, bi=bi: nc.scalar.copy(
                ytev[:, bi * 512:(bi + 1) * 512], bh[:]), reads=[bt], writes=[t_ytev])
        for hj in range(2):
            for q in range(4):
                bi = q % 2
                bh, bt = c.banks[bi]
                for idx in range(4):
                    tt = q * 4 + idx
                    c.transpose(bi, idx * 128,
                                ytev[:, hj * 2048 + tt * 128: hj * 2048 + (tt + 1) * 128],
                                [t_ytev])
                tt0 = q * 4
                p.emit("act", lambda bh=bh: nc.scalar.copy(ypsb[:], bh[:]),
                       reads=[bt], writes=[t_ypsb])
                dst = y_ap[tt0 * 128:(tt0 + 4) * 128, ps * 256 + hj * 128: ps * 256 + (hj + 1) * 128]
                dst = dst.rearrange("(a p) h -> p a h", a=4)
                p.dma("qact", dst, ypsb[:].rearrange("p (a h) -> p a h", a=4), reads=[t_ypsb],
                      writes=[t_y[tt0 + _i] for _i in range(4)])
    p.pop_scope()
    p.pop_scope()
    p.push_scope()
    sq2, t_sq2 = p.sb(f"{pfx}sq2", [128, E], F32)
    st2, t_st2 = p.sb(f"{pfx}st2", [128, 4], F32)
    gout, t_gout = p.sb(f"{pfx}gout", [128, E], BF16)
    yts = [p.sb(f"{pfx}yt{i}", [128, E], BF16) for i in range(2)]
    for tt in range(NT):
        yth, t_yth = yts[tt % 2]
        p.dma("qsp", yth[:], y_ap[tt * 128:(tt + 1) * 128, :], reads=[t_y[tt]], writes=[t_yth])
        p.emit("dve", lambda yth=yth: nc.vector.tensor_tensor(
            sq2[:], yth[:], yth[:], OP.mult), reads=[t_yth], writes=[t_sq2])
        p.emit("dve", lambda: nc.vector.tensor_reduce(st2[:, 0:1], sq2[:], AX.X, OP.add),
               reads=[t_sq2], writes=[t_st2])
        p.emit("act", lambda: nc.scalar.activation(st2[:, 1:2], st2[:, 0:1], AF.Ln, bias=eb[:], scale=inve[:]),
               reads=[t_st2, ebt, invet], writes=[t_st2])
        p.emit("act", lambda: nc.scalar.activation(st2[:, 2:3], st2[:, 1:2], AF.Exp, bias=zz[:], scale=nh[:]),
               reads=[t_st2, nht, zzt], writes=[t_st2])
        p.emit("act", lambda yth=yth: nc.scalar.activation(
            gout[:], yth[:], AF.Silu, bias=zz[:], scale=st2[:, 2:3]),
            reads=[t_yth, t_st2, zzt], writes=[t_gout])
        p.dma("qact", g_ap[tt * 128:(tt + 1) * 128, :], gout[:], reads=[t_gout])
    p.pop_scope()



def decode_fp4_l1(c: C, pk_ap, sc_pair, ds, n_words, reads_pk):
    p, nc = c.p, c.nc
    K = 2 * n_words
    G = n_words // 16
    (ki, t_ki), (sgn, t_sgn), (ssc, t_ssc) = ds["ki"], ds["sgn"], ds["ssc"]
    (va, t_va), (t1h, t_t1), (t2h, t_t2), (t3h, t_t3), (vv, t_vv) = \
        ds["va"], ds["t1"], ds["t2"], ds["t3"], ds["vv"]
    wd, t_wd = ds["wd"]
    sch, t_sch = sc_pair
    p.emit("dve", lambda: nc.vector.tensor_scalar(
        ki[:, 0:n_words], pk_ap, 28, 0x70000000, OP.logical_shift_left, OP.bitwise_and),
        reads=reads_pk, writes=[t_ki])
    p.emit("dve", lambda: nc.vector.tensor_scalar(
        ki[:, n_words:K], pk_ap, 24, 0x70000000, OP.logical_shift_left, OP.bitwise_and),
        reads=reads_pk, writes=[t_ki])
    p.emit("dve", lambda: nc.vector.tensor_scalar(
        sgn[:, 0:n_words], pk_ap, 28, -0x80000000, OP.logical_shift_left, OP.bitwise_and),
        reads=reads_pk, writes=[t_sgn])
    p.emit("dve", lambda: nc.vector.tensor_scalar(
        sgn[:, n_words:K], pk_ap, 24, -0x80000000, OP.logical_shift_left, OP.bitwise_and),
        reads=reads_pk, writes=[t_sgn])
    sb = sch[:, 0:G].bitcast(I32).unsqueeze(1).unsqueeze(3).broadcast_to([128, 2, G, 16])
    p.emit("dve", lambda: nc.vector.tensor_tensor(
        ssc[:, 0:K].bitcast(I32).rearrange("p (j g r) -> p j g r", j=2, g=G),
        sb, sgn[:, 0:K].rearrange("p (j g r) -> p j g r", j=2, g=G), OP.bitwise_or),
        reads=[t_sch, t_sgn], writes=[t_ssc])
    p.emit("dve", lambda: nc.vector.tensor_scalar(va[:, 0:K], ki[:, 0:K], 0.5 / 268435456.0, None, OP.mult),
           reads=[t_ki], writes=[t_va])
    p.emit("dve", lambda: nc.vector.tensor_scalar(t1h[:, 0:K], va[:, 0:K], 2.0, -2.0, OP.mult, OP.add),
           reads=[t_va], writes=[t_t1])
    p.emit("dve", lambda: nc.vector.tensor_scalar(t3h[:, 0:K], t1h[:, 0:K], 2.0, -4.0, OP.mult, OP.add),
           reads=[t_t1], writes=[t_t3])
    p.emit("dve", lambda: nc.vector.tensor_tensor(t2h[:, 0:K], va[:, 0:K], t1h[:, 0:K], OP.max),
           reads=[t_va, t_t1], writes=[t_t2])
    p.emit("dve", lambda: nc.vector.tensor_tensor(vv[:, 0:K], t2h[:, 0:K], t3h[:, 0:K], OP.max),
           reads=[t_t2, t_t3], writes=[t_vv])
    p.emit("pool", lambda: nc.gpsimd.tensor_tensor(
        wd[:, 0:K].rearrange("p (w j) -> p j w", j=2),
        vv[:, 0:K].rearrange("p (j w) -> p j w", j=2),
        ssc[:, 0:K].rearrange("p (j w) -> p j w", j=2), OP.mult),
        reads=[t_vv, t_ssc], writes=[t_wd])
    return wd, t_wd


def decode_fp4_l1t(c: C, pk_ap, reads_pk, scpl_pair, ds, pkT_pair, wd_ap=None, wd_tok=None):
    p, nc = c.p, c.nc
    (ki, t_ki), (sgn, t_sgn), (ssc, t_ssc) = ds["ki"], ds["sgn"], ds["ssc"]
    (va, t_va), (t1h, t_t1), (t2h, t_t2), (t3h, t_t3), (vv, t_vv) = \
        ds["va"], ds["t1"], ds["t2"], ds["t3"], ds["vv"]
    scpl, t_scpl = scpl_pair
    if wd_ap is None:
        wdh, wd_tok = ds["wd"]
        wd_ap = wdh[:].rearrange("p (s j n) -> p j s n", s=8, j=2)
    pkT, t_pkT = pkT_pair
    pkv = pk_ap.rearrange("p (s n) -> p s n", s=8)
    pTv = pkT[:].rearrange("p (s n) -> p s n", s=8)
    for i in range(4):
        for j in range(4):
            p.emit("dve", lambda i=i, j=j, pkv=pkv, pTv=pTv: nc.vector.transpose(
                pTv[32 * i:32 * (i + 1), :, 32 * j:32 * (j + 1)],
                pkv[32 * j:32 * (j + 1), :, 32 * i:32 * (i + 1)]),
                reads=list(reads_pk), writes=[t_pkT])
    for jn in range(2):
        p.emit("dve", lambda jn=jn, pkT=pkT: nc.vector.tensor_scalar(
            ki[:, jn * 1024:(jn + 1) * 1024], pkT[:], 28 - 4 * jn, 0x70000000,
            OP.logical_shift_left, OP.bitwise_and), reads=[t_pkT], writes=[t_ki])
        p.emit("dve", lambda jn=jn, pkT=pkT: nc.vector.tensor_scalar(
            sgn[:, jn * 1024:(jn + 1) * 1024], pkT[:], 28 - 4 * jn, -0x80000000,
            OP.logical_shift_left, OP.bitwise_and), reads=[t_pkT], writes=[t_sgn])
    sb = scpl[:].bitcast(I32).unsqueeze(1).broadcast_to([128, 2, 8, 128])
    p.emit("dve", lambda sb=sb: nc.vector.tensor_tensor(
        ssc[:, 0:2048].bitcast(I32).rearrange("p (j s n) -> p j s n", j=2, s=8),
        sb, sgn[:, 0:2048].rearrange("p (j s n) -> p j s n", j=2, s=8), OP.bitwise_or),
        reads=[t_scpl, t_sgn], writes=[t_ssc])
    p.emit("dve", lambda: nc.vector.tensor_scalar(va[:, 0:2048], ki[:, 0:2048],
           0.5 / 268435456.0, None, OP.mult), reads=[t_ki], writes=[t_va])
    p.emit("dve", lambda: nc.vector.tensor_scalar(t1h[:, 0:2048], va[:, 0:2048], 2.0, -2.0,
           OP.mult, OP.add), reads=[t_va], writes=[t_t1])
    p.emit("dve", lambda: nc.vector.tensor_scalar(t3h[:, 0:2048], t1h[:, 0:2048], 2.0, -4.0,
           OP.mult, OP.add), reads=[t_t1], writes=[t_t3])
    p.emit("dve", lambda: nc.vector.tensor_tensor(t2h[:, 0:2048], va[:, 0:2048],
           t1h[:, 0:2048], OP.max), reads=[t_va, t_t1], writes=[t_t2])
    p.emit("dve", lambda: nc.vector.tensor_tensor(vv[:, 0:2048], t2h[:, 0:2048],
           t3h[:, 0:2048], OP.max), reads=[t_t2, t_t3], writes=[t_vv])
    p.emit("pool", lambda wd_ap=wd_ap: nc.gpsimd.tensor_tensor(
        wd_ap, vv[:, 0:2048].rearrange("p (j s n) -> p j s n", j=2, s=8),
        ssc[:, 0:2048].rearrange("p (j s n) -> p j s n", j=2, s=8), OP.mult),
        reads=[t_vv, t_ssc], writes=[wd_tok])
    return wd_tok


def s_l1_ffn(c: C, xnt_ap, gpk_ap, gsc_ap, upk_ap, usc_ap, dpk_ap, dsc_ap,
             it_ap, y_ap, g_ap, pfx="f1", n_tiles=128, phases="all"):
    p, nc = c.p, c.nc
    N1 = 16384
    zz, zzt = c.const(0.0)
    one, onet = c.const(1.0)
    t_it = [p.tok(f"it{i}") for i in range(128)]
    t_y = [p.tok(f"yy{i}") for i in range(NT)]

    p.push_scope()
    xnt, t_xnt = p.sb(f"{pfx}xnt", [128, NC, T], BF16)
    for cc in range(NC):
        p.dma("qsp", xnt[:, cc, :], xnt_ap[cc * 128:(cc + 1) * 128, :], writes=[t_xnt])
    dec = decode_bufs(p, f"{pfx}d", 256, 3)
    pks = [p.sb(f"{pfx}pk{i}", [128, 1024], I32) for i in range(2)]
    wtiles = [p.sb(f"{pfx}wt{i}", [128, 2048], BF16) for i in range(2)]
    scs = [p.sb(f"{pfx}sc{i}", [128, 64], F32) for i in range(2)]
    gsbs = [p.sb(f"{pfx}gsb{i}", [128, 2048], BF16) for i in range(3)]
    hsbs = [p.sb(f"{pfx}hsb{i}", [128, 2048], BF16) for i in range(3)]
    usbs = [p.sb(f"{pfx}usb{i}", [128, 2048], BF16) for i in range(3)]
    pend = []

    def _flush_hmult(item):
        (j, gsb, t_gsb, usb, t_usb, hsb, t_hsb) = item
        p.emit("dve", lambda gsb=gsb, usb=usb, hsb=hsb: nc.vector.tensor_tensor(
            hsb[:], gsb[:], usb[:], OP.mult), reads=[t_usb, t_gsb], writes=[t_hsb])
        p.dma("qact", it_ap[j * 128:(j + 1) * 128, :], hsb[:], reads=[t_hsb],
              writes=[t_it[j]])

    def wd_transposes(wd, t_wd, kchunks, store_fn):
        for g4 in range((kchunks + 3) // 4):
            bi = 4 + (g4 % 2)
            bh, bt = c.banks[bi]
            js = list(range(g4 * 4, min(g4 * 4 + 4, kchunks)))
            for idx, kc in enumerate(js):
                c.transpose(bi, idx * 128, wd[:, kc * 128:(kc + 1) * 128], [t_wd])
            store_fn(js, bh, bt)

    for i in range(n_tiles):
        for gu in range(2):
            it = 2 * i + gu
            pk_src = gpk_ap if gu == 0 else upk_ap
            sc_src = gsc_ap if gu == 0 else usc_ap
            pkh, t_pk = pks[it % 2]
            sch, t_sch = scs[it % 2]
            ds = dec[it % 3]
            wtile, t_wtile = wtiles[it % 2]
            p.dma("qsp", sch[:], sc_src[i * 128:(i + 1) * 128, :], writes=[t_sch])
            p.dma("qsp", pkh[:], pk_src[i * 128:(i + 1) * 128, :], writes=[t_pk])
            decode_fp4_l1(c, pkh[:], (sch, t_sch), ds, 1024, [t_pk])
            wd, t_wd = ds["wd"]

            def store(js, bh, bt, wtile=wtile, t_wtile=t_wtile):
                w0, n_ = js[0], len(js)
                p.emit("act", lambda w0=w0, n_=n_, bh=bh: nc.scalar.copy(
                    wtile[:, w0 * 128:(w0 + n_) * 128], bh[:, 0:n_ * 128]),
                    reads=[bt], writes=[t_wtile])
            wd_transposes(wd, t_wd, 16, store)
            for kc in range(NC):
                for tc in range(4):
                    c.mm(tc, (0, 512), wtile[:, kc * 128:(kc + 1) * 128],
                         xnt[:, kc, tc * 512:(tc + 1) * 512], [t_wtile, t_xnt],
                         start=(kc == 0), stop=(kc == NC - 1))
            gsb, t_gsb = gsbs[i % 3]
            usb, t_usb = usbs[i % 3]
            hsb, t_hsb = hsbs[i % 3]
            if gu == 0:
                for tc in range(4):
                    bh, bt = c.banks[tc]
                    p.emit("act", lambda bh=bh, gsb=gsb, tc=tc: nc.scalar.activation(
                        gsb[:, tc * 512:(tc + 1) * 512], bh[:], AF.Gelu_apprx_tanh,
                        bias=zz[:], scale=one[:]),
                        reads=[bt, zzt, onet], writes=[t_gsb])
            else:
                for tc in range(4):
                    bh, bt = c.banks[tc]
                    p.emit("act", lambda bh=bh, usb=usb, tc=tc: nc.scalar.copy(
                        usb[:, tc * 512:(tc + 1) * 512], bh[:]), reads=[bt], writes=[t_usb])
                pend.append((i, gsb, t_gsb, usb, t_usb, hsb, t_hsb))
                if len(pend) >= 3:
                    _flush_hmult(pend.pop(0))
    while pend:
        _flush_hmult(pend.pop(0))
    p.pop_scope()
    if phases == "gu":
        return

    p.push_scope()
    dec2 = decode_bufs(p, f"{pfx}e", 256, 3)
    pks2 = [p.sb(f"{pfx}qk{i}", [128, 1024], I32) for i in range(2)]
    scs2 = [p.sb(f"{pfx}sd{i}", [128, 512], F32) for i in range(2)]
    wdt, _twdt_unused = p.sb(f"{pfx}wdt", [128, 128, 256], BF16)
    t_wdtq = [p.tok(f"{pfx}wdtq{i}") for i in range(4)]
    hch = [p.sb(f"{pfx}hch{i}", [128, 2048], BF16) for i in range(4)]
    ytev, t_ytev = p.sb(f"{pfx}ytev", [128, 4096], BF16)
    ypsb, t_ypsb = p.sb(f"{pfx}ypsb", [128, 512], BF16)

    def emit_dec(ps, q):
        def dve_T(item):
            wd, t_wd, sl8, sub2 = item
            wdv = wd[:].rearrange("p (kc x) -> p kc x", kc=16)
            dst = wdt[:, sl8 * 16:(sl8 + 1) * 16, sub2 * 128:(sub2 + 1) * 128]
            for i in range(4):
                for j in range(4):
                    p.emit("dve", lambda i=i, j=j, wdv=wdv, dst=dst: nc.vector.transpose(
                        dst[32 * i:32 * (i + 1), :, 32 * j:32 * (j + 1)],
                        wdv[32 * j:32 * (j + 1), :, 32 * i:32 * (i + 1)]),
                        reads=[t_wd], writes=[t_wdtq[q]])

        def pe_T(item):
            wd, t_wd, sl8, sub2 = item

            def store(js, bh, bt, sl8=sl8, sub2=sub2):
                ic0, n_ = sl8 * 16 + js[0], len(js)
                p.emit("act", lambda ic0=ic0, n_=n_, bh=bh, sub2=sub2: nc.scalar.copy(
                    wdt[:, ic0:ic0 + n_, sub2 * 128:(sub2 + 1) * 128],
                    bh[:, 0:n_ * 128].rearrange("p (a x) -> p a x", a=n_)),
                    reads=[bt], writes=[t_wdtq[q]])
            wd_transposes(wd, t_wd, 16, store)

        dpend = []
        flush_at = 2 if q < 3 else 3
        do_T = dve_T if q < 3 else pe_T
        for sub2 in range(2):
            hrow = ps * 2 + sub2
            sch, t_sch = scs2[sub2 % 2]
            pkq = "qsp" if ps == 0 else "qpool"
            if q == 0:
                p.dma(pkq, sch[:], dsc_ap[hrow * 128:(hrow + 1) * 128, :], writes=[t_sch])
            for sl8 in range(q * 2, q * 2 + 2):
                gidx = hrow * 8 + sl8
                pkh, t_pk = pks2[gidx % 2]
                ds = dec2[gidx % 3]
                p.dma(pkq, pkh[:],
                      dpk_ap[hrow * 128:(hrow + 1) * 128, sl8 * 1024:(sl8 + 1) * 1024],
                      writes=[t_pk])
                decode_fp4_l1(c, pkh[:], (sch[:, sl8 * 64:(sl8 + 1) * 64], t_sch),
                              ds, 1024, [t_pk])
                wd, t_wd = ds["wd"]
                dpend.append((wd, t_wd, sl8, sub2))
                if len(dpend) >= flush_at:
                    do_T(dpend.pop(0))
        while dpend:
            do_T(dpend.pop(0))

    for q in range(4):
        emit_dec(0, q)
    for ps in range(8):
        for ic in range(128):
            if ic in (32, 64, 96) and ps + 1 < 8:
                emit_dec(ps + 1, ic // 32 - 1)
            hh, t_hh = hch[ic % 4]
            p.dma("qsp" if ic % 2 == 0 else "qact", hh[:], it_ap[ic * 128:(ic + 1) * 128, :],
                  reads=[t_it[ic]], writes=[t_hh])
            for hj in range(2):
                for tc in range(4):
                    c.mm(hj * 4 + tc, (0, 512), wdt[:, ic, hj * 128:(hj + 1) * 128],
                         hh[:, tc * 512:(tc + 1) * 512], [t_wdtq[ic // 32], t_hh],
                         start=(ic == 0), stop=(ic == 127))
        for bi in range(8):
            bh, bt = c.banks[bi]
            p.emit("act", lambda bh=bh, bi=bi: nc.scalar.copy(
                ytev[:, bi * 512:(bi + 1) * 512], bh[:]), reads=[bt], writes=[t_ytev])
        for hj in range(2):
            for q in range(4):
                bi = q % 2
                bh, bt = c.banks[bi]
                for idx in range(4):
                    tt = q * 4 + idx
                    c.transpose(bi, idx * 128,
                                ytev[:, hj * 2048 + tt * 128: hj * 2048 + (tt + 1) * 128],
                                [t_ytev])
                tt0 = q * 4
                p.emit("act", lambda bh=bh: nc.scalar.copy(ypsb[:], bh[:]),
                       reads=[bt], writes=[t_ypsb])
                dst = y_ap[tt0 * 128:(tt0 + 4) * 128, ps * 256 + hj * 128: ps * 256 + (hj + 1) * 128]
                dst = dst.rearrange("(a p) h -> p a h", a=4)
                p.dma("qact", dst, ypsb[:].rearrange("p (a h) -> p a h", a=4), reads=[t_ypsb],
                      writes=[t_y[tt0 + _i] for _i in range(4)])
        if ps + 1 < 8:
            emit_dec(ps + 1, 3)
    p.pop_scope()
    p.push_scope()
    gout, t_gout = p.sb(f"{pfx}gout", [128, E], BF16)
    yts = [p.sb(f"{pfx}yt{i}", [128, E], BF16) for i in range(2)]
    for tt in range(NT):
        yth, t_yth = yts[tt % 2]
        p.dma("qsp", yth[:], y_ap[tt * 128:(tt + 1) * 128, :], reads=[t_y[tt]], writes=[t_yth])
        p.emit("act", lambda yth=yth: nc.scalar.activation(
            gout[:], yth[:], AF.Silu, bias=zz[:], scale=one[:]),
            reads=[t_yth, t_st2 if False else zzt, onet], writes=[t_gout])
        p.dma("qact", g_ap[tt * 128:(tt + 1) * 128, :], gout[:], reads=[t_gout])
    p.pop_scope()


def s_l1_attn(c: C, xnt_ap, wp_ap, wo_ap, s_ap, pfx="a1", dbg=None):
    p, nc = c.p, c.nc
    H, D = 16, 128
    sm = 1.0 / np.sqrt(D)
    zz, zzt = c.const(0.0)
    one, onet = c.const(1.0)
    smc, smct = c.const(sm)
    t_s = [p.tok(f"{pfx}s{i}") for i in range(NT)]

    p.push_scope()
    xnt, t_xnt = p.sb(f"{pfx}xnt", [128, NC, T], BF16)
    for cc in range(NC):
        p.dma("qsp", xnt[:, cc, :], xnt_ap[cc * 128:(cc + 1) * 128, :], writes=[t_xnt])
    oT, t_oT = p.sb(f"{pfx}oT", [128, H, T], BF16)
    p.push_scope()
    wrow = [p.sb(f"{pfx}wr{i}", [128, E], BF16) for i in range(2)]
    wqT, t_wqT = p.sb(f"{pfx}wqT", [128, NC, 128], BF16)
    wkT, t_wkT = p.sb(f"{pfx}wkT", [128, NC, 128], BF16)
    wvT, t_wvT = p.sb(f"{pfx}wvT", [128, NC, 128], BF16)
    qT, t_qT = p.sb(f"{pfx}qT", [128, T], BF16)
    kT, t_kT = p.sb(f"{pfx}kT", [128, T], BF16)
    vtm, t_vtm = p.sb(f"{pfx}vtm", [128, NT, 128], BF16)
    vT, t_vT = p.sb(f"{pfx}vT", [128, T], BF16)
    psls = [(p.sb(f"{pfx}psl{i}", [128, T], BF16)[0],
             [p.tok(f"{pfx}pt{i}_{j}") for j in range(4)]) for i in range(3)]
    lsums = [p.sb(f"{pfx}ls{i}", [128, 12], F32) for i in range(3)]
    diags = [p.sb(f"{pfx}dg{i}", [128, 128], BF16) for i in range(3)]
    pT, t_pT = p.sb(f"{pfx}pT", [128, NT, 512], BF16)

    def load_wT(h, dst, t_dst, row0):
        wr, t_wr = wrow[h % 2]
        p.dma("qpool", wr[:], wp_ap[row0:row0 + 128, :], writes=[t_wr])
        for g4 in range(4):
            bi = g4 % 2
            bh, bt = c.banks[bi]
            for j in range(4):
                cc = g4 * 4 + j
                c.transpose(bi, j * 128, wr[:, cc * 128:(cc + 1) * 128], [t_wr])
            p.emit("act", lambda bh=bh, g4=g4, dst=dst: nc.scalar.copy(
                dst[:, g4 * 4:(g4 + 1) * 4, :].rearrange("p a b -> p (a b)"), bh[:]),
                reads=[bt], writes=[t_dst])

    for h in range(H):
        load_wT(h, wqT, t_wqT, h * 128)
        load_wT(h, wkT, t_wkT, 2048 + h * 128)
        load_wT(h, wvT, t_wvT, 4096 + h * 128)
        for tc in range(4):
            for cc in range(NC):
                c.mm(tc, (0, 512), wqT[:, cc, :], xnt[:, cc, tc * 512:(tc + 1) * 512],
                     [t_wqT, t_xnt], start=(cc == 0), stop=(cc == NC - 1))
            bh, bt = c.banks[tc]
            p.emit("act", lambda bh=bh, tc=tc: nc.scalar.activation(
                qT[:, tc * 512:(tc + 1) * 512], bh[:], AF.Copy, bias=0.0, scale=smc[:]),
                reads=[bt, zzt, smct], writes=[t_qT])
        for tc in range(4):
            for cc in range(NC):
                c.mm(tc, (0, 512), wkT[:, cc, :], xnt[:, cc, tc * 512:(tc + 1) * 512],
                     [t_wkT, t_xnt], start=(cc == 0), stop=(cc == NC - 1))
            bh, bt = c.banks[tc]
            p.emit("act", lambda bh=bh, tc=tc: nc.scalar.copy(
                kT[:, tc * 512:(tc + 1) * 512], bh[:]), reads=[bt], writes=[t_kT])
        for tc in range(4):
            for cc in range(NC):
                c.mm(tc, (0, 512), wvT[:, cc, :], xnt[:, cc, tc * 512:(tc + 1) * 512],
                     [t_wvT, t_xnt], start=(cc == 0), stop=(cc == NC - 1))
            bh, bt = c.banks[tc]
            p.emit("act", lambda bh=bh, tc=tc: nc.scalar.copy(
                vT[:, tc * 512:(tc + 1) * 512], bh[:]), reads=[bt], writes=[t_vT])
        for tt0 in range(0, NT, 4):
            bi = (tt0 // 4) % 2
            bh, bt = c.banks[bi]
            for j in range(4):
                c.transpose(bi, j * 128, vT[:, (tt0 + j) * 128:(tt0 + j + 1) * 128], [t_vT])
            p.emit("act", lambda bh=bh, tt0=tt0: nc.scalar.copy(
                vtm[:, tt0:tt0 + 4, :].rearrange("p a b -> p (a b)"), bh[:, 0:512]),
                reads=[bt], writes=[t_vtm])

        def emit_T(item):
            qt, pslh, ptoks, dgh, t_dgh = item
            qi = qt % 4
            for kt0 in range(0, qt + 1, 4):
                ntl = min(4, qt + 1 - kt0)
                bi = (kt0 // 4) % 2
                bh, bt = c.banks[bi]
                for j in range(ntl):
                    kt = kt0 + j
                    p.emit("pe", lambda bh=bh, j=j, kt=kt, pslh=pslh, dgh=dgh:
                        nc.tensor.matmul(bh[0:128, j * 128:(j + 1) * 128],
                                         pslh[:, kt * 128:(kt + 1) * 128], dgh[:],
                                         start=True, stop=True),
                        reads=[ptoks[kt // 4], t_dgh], writes=[bt])
                if (kt0 // 4) % 2 == 0:
                    p.emit("dve", lambda bh=bh, kt0=kt0, ntl=ntl, qi=qi: nc.vector.tensor_copy(
                        pT[:, kt0:kt0 + ntl, qi * 128:(qi + 1) * 128],
                        bh[:, 0:ntl * 128].rearrange("p (a x) -> p a x", a=ntl)),
                        reads=[bt], writes=[t_pT])
                else:
                    p.emit("act", lambda bh=bh, kt0=kt0, ntl=ntl, qi=qi: nc.scalar.copy(
                        pT[:, kt0:kt0 + ntl, qi * 128:(qi + 1) * 128],
                        bh[:, 0:ntl * 128].rearrange("p (a x) -> p a x", a=ntl)),
                        reads=[bt], writes=[t_pT])

        def emit_O(qg, h):
            qmax = qg * 4 + 3
            bo, bo_t = c.banks[2]
            for kt in range(qmax + 1):
                off = max(0, (kt - qg * 4)) * 128 if kt > qg * 4 else 0
                p.emit("pe", lambda kt=kt, off=off, bo=bo, qmax=qmax: nc.tensor.matmul(
                    bo[0:128, off:512], vtm[:, kt, :], pT[:, kt, off:512],
                    start=(kt == 0), stop=(kt == qmax)),
                    reads=[t_vtm, t_pT], writes=[bo_t])
            p.emit("act", lambda h=h, qg=qg, bo=bo: nc.scalar.copy(
                oT[:, h, qg * 512:(qg + 1) * 512], bo[0:128, 0:512]),
                reads=[bo_t], writes=[t_oT])

        pend_T = []
        for qt in range(NT):
            klen = (qt + 1) * 128
            pslh, ptoks = psls[qt % 3]
            dgh, t_dgh = diags[qt % 3]
            lsh, t_lsh = lsums[qt % 3]
            sc_v, sc_toks = c.scores
            nchunk = (klen + 511) // 512
            mck = qt // 4
            for ck in range(nchunk):
                n = min(512, klen - ck * 512)
                tok = [sc_toks[ck]]
                p.emit("pe", lambda ck=ck, n=n, qt=qt: nc.tensor.matmul(
                    sc_v[:, ck * 512:ck * 512 + n], qT[:, qt * 128:(qt + 1) * 128],
                    kT[:, ck * 512:ck * 512 + n], start=True, stop=True),
                    reads=[t_qT, t_kT], writes=tok)
                if ck == mck:
                    p.emit("dve", lambda qt=qt: nc.vector.tensor_tensor(
                        sc_v[:, qt * 128:(qt + 1) * 128], sc_v[:, qt * 128:(qt + 1) * 128],
                        c.maskc_sb[:], OP.add), reads=tok + [c.t_maskc], writes=tok)
                p.emit("dve", lambda ck=ck, n=n, lsh=lsh: nc.vector.tensor_reduce(
                    lsh[:, 4 + ck:5 + ck], sc_v[:, ck * 512:ck * 512 + n], AX.X, OP.max),
                    reads=tok, writes=[t_lsh])
            p.emit("dve", lambda lsh=lsh, nchunk=nchunk: nc.vector.tensor_reduce(
                lsh[:, 0:1], lsh[:, 4:4 + nchunk], AX.X, OP.max, negate=True),
                reads=[t_lsh], writes=[t_lsh])
            if len(pend_T) >= 2:
                emit_T(pend_T.pop(0))
            for ck in range(nchunk):
                n = min(512, klen - ck * 512)
                tok = [sc_toks[ck]]
                p.emit("act", lambda ck=ck, n=n, pslh=pslh, lsh=lsh: nc.scalar.activation(
                    pslh[:, ck * 512:ck * 512 + n], sc_v[:, ck * 512:ck * 512 + n],
                    AF.Exp, bias=lsh[:, 0:1], scale=one[:], accum_out=lsh[:, 8 + ck:9 + ck]),
                    reads=tok + [t_lsh, onet], writes=[ptoks[ck], t_lsh])
            p.emit("dve", lambda lsh=lsh, nchunk=nchunk: nc.vector.tensor_reduce(
                lsh[:, 1:2], lsh[:, 8:8 + nchunk], AX.X, OP.add), reads=[t_lsh], writes=[t_lsh])
            p.emit("dve", lambda lsh=lsh: nc.vector.reciprocal(lsh[:, 2:3], lsh[:, 1:2]),
                   reads=[t_lsh], writes=[t_lsh])
            p.emit("dve", lambda dgh=dgh, lsh=lsh: nc.vector.tensor_tensor(
                dgh[:], c.ident_sb[:], lsh[:, 2:3].broadcast_to([128, 128]), OP.mult),
                reads=[c.t_ident, t_lsh], writes=[t_dgh])
            pend_T.append((qt, pslh, ptoks, dgh, t_dgh))
            if qt % 4 == 1 and qt >= 5:
                emit_O((qt - 5) // 4, h)
        while pend_T:
            emit_T(pend_T.pop(0))
        emit_O(3, h)
    if dbg is not None:
        p.dma("qsp", dbg["qT"][:, :], qT[:], reads=[t_qT])
        p.dma("qsp", dbg["kT"][:, :], kT[:], reads=[t_kT])
        p.dma("qsp", dbg["v"][:, :], vtm[:].rearrange("p a b -> p (a b)"), reads=[t_vtm])
        p.dma("qsp", dbg["oT"][:, :], oT[:, H - 1, :], reads=[t_oT])
    p.pop_scope()
    wosb = [p.sb(f"{pfx}wo{i}", [128, E], BF16) for i in range(2)]
    woT, t_woT = p.sb(f"{pfx}woT", [128, NC, 512], BF16)
    souts = [p.sb(f"{pfx}so{i}", [128, 512], BF16) for i in range(2)]
    for ec in range(4):
        for ot4 in range(4):
            wr, t_wr = wosb[ot4 % 2]
            p.dma("qpool", wr[:], wo_ap[ec * 512 + ot4 * 128: ec * 512 + (ot4 + 1) * 128, :],
                  writes=[t_wr])
            for g4 in range(4):
                bi = g4 % 2
                bh, bt = c.banks[bi]
                for j in range(4):
                    cc = g4 * 4 + j
                    c.transpose(bi, j * 128, wr[:, cc * 128:(cc + 1) * 128], [t_wr])
                for j in range(4):
                    cc = g4 * 4 + j
                    p.emit("act", lambda bh=bh, j=j, cc=cc, ot4=ot4: nc.scalar.copy(
                        woT[:, cc, ot4 * 128:(ot4 + 1) * 128], bh[:, j * 128:(j + 1) * 128]),
                        reads=[bt], writes=[t_woT])
        for tt in range(NT):
            bi = 4 + tt % 4
            sout, t_sout = souts[tt % 2]
            for hd in range(NC):
                c.mm(bi, (0, 512), oT[:, hd, tt * 128:(tt + 1) * 128], woT[:, hd, :],
                     [t_oT, t_woT], start=(hd == 0), stop=(hd == NC - 1))
            bh, bt = c.banks[bi]
            p.emit("act", lambda bh=bh, sout=sout: nc.scalar.copy(sout[:], bh[:]),
                   reads=[bt], writes=[t_sout])
            p.dma("qact", s_ap[tt * 128:(tt + 1) * 128, ec * 512:(ec + 1) * 512], sout[:],
                  reads=[t_sout], writes=[t_s[tt]])
    p.pop_scope()
    return t_s


def s_l0_attn(c: C, xnt_ap, wqa_ap, qlnw_ap, wqb_ap, wkva_ap, kvlnw_ap, wkvb_ap, wo_ap,
              qn_ap, ot_ap, s_ap, pfx="a0", dbg=None):
    p, nc = c.p, c.nc
    H, DN, DR, DV, R = 16, 128, 64, 128, 512
    sm = 1.0 / np.sqrt(DN + DR)
    zz, zzt = c.const(0.0)
    one, onet = c.const(1.0)
    eb, ebt = c.const(1e-6)
    nh, nht = c.const(-0.5)
    smc, smct = c.const(sm)
    i512, i512t = c.const(1.0 / 512)
    i1536, i1536t = c.const(1.0 / 1536)
    cos_d = nc.dram_tensor("c_cos", [T, 64], F32, kind="ExternalInput").ap()
    sin_d = nc.dram_tensor("c_sin", [T, 64], F32, kind="ExternalInput").ap()
    t_qn = [p.tok(f"{pfx}qn{h}") for h in range(H)]
    t_ot = [p.tok(f"{pfx}ot{h}") for h in range(H)]
    t_s = [p.tok(f"{pfx}s{i}") for i in range(NT)]

    p.push_scope()
    coss, t_cos = p.sb(f"{pfx}cos", [128, NT, 64], F32)
    sins, t_sin = p.sb(f"{pfx}sin", [128, NT, 64], F32)
    for tt in range(NT):
        p.dma("qsp", coss[:, tt, :], cos_d[tt * 128:(tt + 1) * 128, :], writes=[t_cos])
        p.dma("qsp", sins[:, tt, :], sin_d[tt * 128:(tt + 1) * 128, :], writes=[t_sin])
    ckvT, t_ckvT = p.sb(f"{pfx}ckvT", [128, 4, T], BF16)
    kpeT, t_kpeT = p.sb(f"{pfx}kpeT", [128, T], BF16)
    qpeT, t_qpeT = p.sb(f"{pfx}qpeT", [128, 8, T], BF16)

    p.push_scope()
    qanT, t_qanT = p.sb(f"{pfx}qanT", [128, 12, T], BF16)
    p.push_scope()
    xsl = [p.sb(f"{pfx}xsl{i}", [128, NC, 128], BF16) for i in range(2)]
    wkva, t_wkva = p.sb(f"{pfx}wkva", [128, NC, 576], BF16)
    for cc in range(NC):
        p.dma("qpool", wkva[:, cc, :], wkva_ap[cc * 128:(cc + 1) * 128, :], writes=[t_wkva])
    wqa, t_wqa = p.sb(f"{pfx}wqa", [128, NC, 1536], BF16)
    for cc in range(NC):
        p.dma("qpool", wqa[:, cc, :], wqa_ap[cc * 128:(cc + 1) * 128, :], writes=[t_wqa])
    cf32, t_cf32 = p.sb(f"{pfx}cf32", [128, 512], F32)
    qaf, t_qaf = p.sb(f"{pfx}qaf", [128, 1536], F32)
    sqs, t_sqs = p.sb(f"{pfx}sqs", [128, 1536], F32)
    st, t_st = p.sb(f"{pfx}st", [128, 8], F32)
    ckn, t_ckn = p.sb(f"{pfx}ckn", [128, 512], BF16)
    qan, t_qan = p.sb(f"{pfx}qan", [128, 1536], BF16)
    kpe, t_kpe = p.sb(f"{pfx}kpe", [128, 64], F32)
    kper, t_kper = p.sb(f"{pfx}kper", [128, 64], BF16)
    rr1, t_rr1 = p.sb(f"{pfx}rr1", [128, 64], F32)
    rr2, t_rr2 = p.sb(f"{pfx}rr2", [128, 64], F32)
    for tt in range(NT):
        xh, t_xh = xsl[tt % 2]
        src = xnt_ap[:, tt * 128:(tt + 1) * 128].rearrange("(c p) k -> p c k", p=128)
        p.dma("qsp", xh[:], src, writes=[t_xh])
        for cc in range(NC):
            c.mm(0, (0, 512), xh[:, cc, :], wkva[:, cc, 0:512], [t_xh, t_wkva],
                 start=(cc == 0), stop=(cc == NC - 1))
        for cc in range(NC):
            c.mm(1, (0, 64), xh[:, cc, :], wkva[:, cc, 512:576], [t_xh, t_wkva],
                 start=(cc == 0), stop=(cc == NC - 1))
        bh0, bt0 = c.banks[0]
        p.emit("act", lambda bh0=bh0: nc.scalar.copy(cf32[:], bh0[:]), reads=[bt0], writes=[t_cf32])
        p.emit("dve", lambda: nc.vector.tensor_tensor(sqs[:, 0:512], cf32[:], cf32[:], OP.mult),
               reads=[t_cf32], writes=[t_sqs])
        p.emit("dve", lambda: nc.vector.tensor_reduce(st[:, 0:1], sqs[:, 0:512], AX.X, OP.add),
               reads=[t_sqs], writes=[t_st])
        p.emit("act", lambda: nc.scalar.activation(st[:, 1:2], st[:, 0:1], AF.Ln, bias=eb[:], scale=i512[:]),
               reads=[t_st, ebt, i512t], writes=[t_st])
        p.emit("act", lambda: nc.scalar.activation(st[:, 2:3], st[:, 1:2], AF.Exp, bias=zz[:], scale=nh[:]),
               reads=[t_st, nht, zzt], writes=[t_st])
        p.emit("dve", lambda: nc.vector.tensor_scalar(ckn[:], cf32[:], st[:, 2:3], None, OP.mult),
               reads=[t_cf32, t_st], writes=[t_ckn])
        for rc in range(4):
            c.transpose(2, rc * 128, ckn[:, rc * 128:(rc + 1) * 128], [t_ckn])
        bh2, bt2 = c.banks[2]
        for rc in range(4):
            p.emit("act", lambda bh2=bh2, tt=tt, rc=rc: nc.scalar.copy(
                ckvT[:, rc, tt * 128:(tt + 1) * 128], bh2[:, rc * 128:(rc + 1) * 128]),
                reads=[bt2], writes=[t_ckvT])
        bh1, bt1 = c.banks[1]
        p.emit("act", lambda bh1=bh1: nc.scalar.copy(kpe[:], bh1[0:128, 0:64]), reads=[bt1], writes=[t_kpe])
        p.emit("dve", lambda tt=tt: nc.vector.tensor_tensor(
            rr1[:, 0:32], kpe[:, 0:32], coss[:, tt, 0:32], OP.mult), reads=[t_kpe, t_cos], writes=[t_rr1])
        p.emit("dve", lambda tt=tt: nc.vector.tensor_tensor(
            rr2[:, 0:32], kpe[:, 32:64], sins[:, tt, 0:32], OP.mult), reads=[t_kpe, t_sin], writes=[t_rr2])
        p.emit("dve", lambda: nc.vector.tensor_tensor(
            kper[:, 0:32], rr1[:, 0:32], rr2[:, 0:32], OP.subtract), reads=[t_rr1, t_rr2], writes=[t_kper])
        p.emit("dve", lambda tt=tt: nc.vector.tensor_tensor(
            rr1[:, 0:32], kpe[:, 32:64], coss[:, tt, 32:64], OP.mult), reads=[t_kpe, t_cos], writes=[t_rr1])
        p.emit("dve", lambda tt=tt: nc.vector.tensor_tensor(
            rr2[:, 0:32], kpe[:, 0:32], sins[:, tt, 32:64], OP.mult), reads=[t_kpe, t_sin], writes=[t_rr2])
        p.emit("dve", lambda: nc.vector.tensor_tensor(
            kper[:, 32:64], rr1[:, 0:32], rr2[:, 0:32], OP.add), reads=[t_rr1, t_rr2], writes=[t_kper])
        c.transpose(3, 0, kper[:], [t_kper])
        bh3, bt3 = c.banks[3]
        p.emit("act", lambda bh3=bh3, tt=tt: nc.scalar.copy(
            kpeT[0:64, tt * 128:(tt + 1) * 128], bh3[0:64, 0:128]), reads=[bt3], writes=[t_kpeT])
        for oc in range(3):
            for cc in range(NC):
                c.mm(5 + oc, (0, 512), xh[:, cc, :], wqa[:, cc, oc * 512:(oc + 1) * 512],
                     [t_xh, t_wqa], start=(cc == 0), stop=(cc == NC - 1))
            bho, bto = c.banks[5 + oc]
            p.emit("act", lambda bho=bho, oc=oc: nc.scalar.copy(
                qaf[:, oc * 512:(oc + 1) * 512], bho[:]), reads=[bto], writes=[t_qaf])
        p.emit("dve", lambda: nc.vector.tensor_tensor(sqs[:], qaf[:], qaf[:], OP.mult),
               reads=[t_qaf], writes=[t_sqs])
        p.emit("dve", lambda: nc.vector.tensor_reduce(st[:, 4:5], sqs[:], AX.X, OP.add),
               reads=[t_sqs], writes=[t_st])
        p.emit("act", lambda: nc.scalar.activation(st[:, 5:6], st[:, 4:5], AF.Ln, bias=eb[:], scale=i1536[:]),
               reads=[t_st, ebt, i1536t], writes=[t_st])
        p.emit("act", lambda: nc.scalar.activation(st[:, 6:7], st[:, 5:6], AF.Exp, bias=zz[:], scale=nh[:]),
               reads=[t_st, nht, zzt], writes=[t_st])
        p.emit("dve", lambda: nc.vector.tensor_scalar(qan[:], qaf[:], st[:, 6:7], None, OP.mult),
               reads=[t_qaf, t_st], writes=[t_qan])
        for g4 in range(3):
            bh4, bt4 = c.banks[4]
            for j in range(4):
                cc = g4 * 4 + j
                c.transpose(4, j * 128, qan[:, cc * 128:(cc + 1) * 128], [t_qan])
            for j in range(4):
                cc = g4 * 4 + j
                p.emit("act", lambda bh4=bh4, j=j, cc=cc, tt=tt: nc.scalar.copy(
                    qanT[:, cc, tt * 128:(tt + 1) * 128], bh4[:, j * 128:(j + 1) * 128]),
                    reads=[bt4], writes=[t_qanT])
    if dbg is not None:
        p.dma("qsp", dbg["qanT"][:, :], qanT[:].rearrange("p a b -> p (a b)"), reads=[t_qanT])
        p.dma("qsp", dbg["ckvT"][:, :], ckvT[:].rearrange("p a b -> p (a b)"), reads=[t_ckvT])
        p.dma("qsp", dbg["kpeT"][:, :], kpeT[:], reads=[t_kpeT])
    p.pop_scope()

    p.global_barrier()
    p.push_scope()
    lnq, t_lnq = p.sb(f"{pfx}lnq", [128, 12], F32)
    p.dma("qsp", lnq[:], qlnw_ap.rearrange("(cq pq) -> pq cq", pq=128), writes=[t_lnq],
          allow_slow_non_contiguous=True)
    wqbn, t_wqbn = p.sb(f"{pfx}wqbn", [128, 12, 2048], BF16)
    for cq in range(12):
        srcq = wqb_ap[cq * 128:(cq + 1) * 128, :].rearrange("pq (hh dd) -> pq hh dd", hh=16)[:, :, 0:128]
        p.dma("qpool", wqbn[:, cq, :].rearrange("pq (hh dd) -> pq hh dd", hh=16), srcq,
              writes=[t_wqbn])
    for cc in range(12):
        p.emit("dve", lambda cc=cc: nc.vector.tensor_scalar(
            wqbn[:, cc, :], wqbn[:, cc, :], lnq[:, cc:cc + 1], None, OP.mult),
            reads=[t_lnq, t_wqbn], writes=[t_wqbn])
    qnh, t_qnh = p.sb(f"{pfx}qnh", [128, T], BF16)
    if dbg is not None and "wqbn" in dbg:
        p.dma("qsp", dbg["wqbn"][:, :], wqbn[:].rearrange("p a b -> p (a b)"), reads=[t_wqbn])
    for h in range(H):
        for tc in range(T // 512):
            for cc in range(12):
                c.mm(tc, (0, 512), wqbn[:, cc, h * 128:(h + 1) * 128],
                     qanT[:, cc, tc * 512:(tc + 1) * 512], [t_wqbn, t_qanT],
                     start=(cc == 0), stop=(cc == 11))
            bh, bt = c.banks[tc]
            p.emit("act", lambda bh=bh, tc=tc, qnh=qnh: nc.scalar.activation(
                qnh[:, tc * 512:(tc + 1) * 512], bh[:], AF.Copy, bias=0.0, scale=smc[:]),
                reads=[bt, zzt, smct], writes=[t_qnh])
        if dbg is not None and "qnh0" in dbg and h == 0:
            p.dma("qsp", dbg["qnh0"][:, :], qnh[:], reads=[t_qnh])
        p.dma("qact", qn_ap[h * 128:(h + 1) * 128, :], qnh[:], reads=[t_qnh], writes=[t_qn[h]])
    p.pop_scope()
    if dbg is not None and dbg.get("stop_after_b3"):
        p.pop_scope()
        p.pop_scope()
        return t_s

    p.push_scope()
    wqbp, t_wqbp = p.sb(f"{pfx}wqbp", [128, 12, 1024], BF16)
    lnq2, t_lnq2 = p.sb(f"{pfx}lnq2", [128, 12], F32)
    p.dma("qsp", lnq2[:], qlnw_ap.rearrange("(cq pq) -> pq cq", pq=128), writes=[t_lnq2],
          allow_slow_non_contiguous=True)
    for cq in range(12):
        srcq = wqb_ap[cq * 128:(cq + 1) * 128, :].rearrange("pq (hh dd) -> pq hh dd", hh=16)[:, :, 128:192]
        p.dma("qpool", wqbp[:, cq, :].rearrange("pq (hh dd) -> pq hh dd", hh=16), srcq,
              writes=[t_wqbp])
    for cc in range(12):
        p.emit("dve", lambda cc=cc: nc.vector.tensor_scalar(
            wqbp[:, cc, :], wqbp[:, cc, :], lnq2[:, cc:cc + 1], None, OP.mult),
            reads=[t_lnq2, t_wqbp], writes=[t_wqbp])
    qpf, t_qpf = p.sb(f"{pfx}qpf", [128, 1024], BF16)
    qpr, t_qpr = p.sb(f"{pfx}qpr", [128, 1024], BF16)
    qr1, t_qr1 = p.sb(f"{pfx}qr1", [128, 512], F32)
    qr2, t_qr2 = p.sb(f"{pfx}qr2", [128, 512], F32)
    for tt in range(NT):
        for b2 in range(2):
            for cc in range(12):
                c.mm(b2, (0, 512), qanT[:, cc, tt * 128:(tt + 1) * 128],
                     wqbp[:, cc, b2 * 512:(b2 + 1) * 512], [t_qanT, t_wqbp],
                     start=(cc == 0), stop=(cc == 11))
            bh, bt = c.banks[b2]
            p.emit("act", lambda bh=bh, b2=b2: nc.scalar.activation(
                qpf[:, b2 * 512:(b2 + 1) * 512], bh[:], AF.Copy, bias=0.0, scale=smc[:]),
                reads=[bt, zzt, smct], writes=[t_qpf])
        qv = qpf[:].rearrange("p (hh d) -> p hh d", hh=16)
        qo = qpr[:].rearrange("p (hh d) -> p hh d", hh=16)
        cv = coss[:, tt, :].unsqueeze(1).broadcast_to([128, 16, 64])
        sv = sins[:, tt, :].unsqueeze(1).broadcast_to([128, 16, 64])
        r1v = qr1[:].rearrange("p (hh d) -> p hh d", hh=16)
        r2v = qr2[:].rearrange("p (hh d) -> p hh d", hh=16)
        p.emit("dve", lambda r1v=r1v, qv=qv, cv=cv: nc.vector.tensor_tensor(
            r1v[:, :, 0:32], qv[:, :, 0:32], cv[:, :, 0:32], OP.mult), reads=[t_qpf, t_cos], writes=[t_qr1])
        p.emit("dve", lambda r2v=r2v, qv=qv, sv=sv: nc.vector.tensor_tensor(
            r2v[:, :, 0:32], qv[:, :, 32:64], sv[:, :, 0:32], OP.mult), reads=[t_qpf, t_sin], writes=[t_qr2])
        p.emit("dve", lambda qo=qo, r1v=r1v, r2v=r2v: nc.vector.tensor_tensor(
            qo[:, :, 0:32], r1v[:, :, 0:32], r2v[:, :, 0:32], OP.subtract),
            reads=[t_qr1, t_qr2], writes=[t_qpr])
        p.emit("dve", lambda r1v=r1v, qv=qv, cv=cv: nc.vector.tensor_tensor(
            r1v[:, :, 0:32], qv[:, :, 32:64], cv[:, :, 32:64], OP.mult), reads=[t_qpf, t_cos], writes=[t_qr1])
        p.emit("dve", lambda r2v=r2v, qv=qv, sv=sv: nc.vector.tensor_tensor(
            r2v[:, :, 0:32], qv[:, :, 0:32], sv[:, :, 32:64], OP.mult), reads=[t_qpf, t_sin], writes=[t_qr2])
        p.emit("dve", lambda qo=qo, r1v=r1v, r2v=r2v: nc.vector.tensor_tensor(
            qo[:, :, 32:64], r1v[:, :, 0:32], r2v[:, :, 0:32], OP.add),
            reads=[t_qr1, t_qr2], writes=[t_qpr])
        for h in range(H):
            bi = 2 + h % 2
            bh, bt = c.banks[bi]
            c.transpose(bi, 0, qpr[:, h * 64:(h + 1) * 64], [t_qpr])
            b0 = (h % 2) * 64
            p.emit("act", lambda bh=bh, h=h, tt=tt, b0=b0: nc.scalar.copy(
                qpeT[b0:b0 + 64, h // 2, tt * 128:(tt + 1) * 128],
                bh[0:64, 0:128]), reads=[bt], writes=[t_qpeT])
    p.pop_scope()
    p.pop_scope()
    if dbg is not None and dbg.get("stop_after_b2"):
        p.pop_scope()
        return t_s

    p.push_scope()
    lnkv, t_lnkv = p.sb(f"{pfx}lnkv", [128, 4], F32)
    p.dma("qsp", lnkv[:], kvlnw_ap.rearrange("(cq pq) -> pq cq", pq=128), writes=[t_lnkv],
          allow_slow_non_contiguous=True)
    wkvb, t_wkvb = p.sb(f"{pfx}wkvb", [128, 4, 4096], BF16)
    for rc in range(4):
        p.dma("qpool", wkvb[:, rc, :], wkvb_ap[rc * 128:(rc + 1) * 128, :], writes=[t_wkvb])
    for rc in range(4):
        p.emit("dve", lambda rc=rc: nc.vector.tensor_scalar(
            wkvb[:, rc, :], wkvb[:, rc, :], lnkv[:, rc:rc + 1], None, OP.mult),
            reads=[t_lnkv, t_wkvb], writes=[t_wkvb])
    kabsT, t_kabsT = p.sb(f"{pfx}kabsT", [128, T], BF16)
    vT, t_vT = p.sb(f"{pfx}vT", [128, T], BF16)
    vtm, t_vtm = p.sb(f"{pfx}vtm", [128, NT, 128], BF16)
    qnhs = [p.sb(f"{pfx}qnh{i}", [128, T], BF16) for i in range(2)]
    psls = [(p.sb(f"{pfx}psl{i}", [128, T], BF16)[0],
             [p.tok(f"{pfx}pt{i}_{j}") for j in range(4)]) for i in range(3)]
    lsums = [p.sb(f"{pfx}ls{i}", [128, 12], F32) for i in range(3)]
    diags = [p.sb(f"{pfx}dg{i}", [128, 128], BF16) for i in range(3)]
    pT, t_pT = p.sb(f"{pfx}pT", [128, NT, 512], BF16)
    oThs = [p.sb(f"{pfx}oTh{i}", [128, 512], BF16) for i in range(2)]
    qpes, t_qpes = p.sb(f"{pfx}qpes", [128, T], BF16)
    for h in range(H):
        qnhd, t_qnhd = qnhs[h % 2]
        p.dma("qsp", qnhd[:], qn_ap[h * 128:(h + 1) * 128, :], reads=[t_qn[h]], writes=[t_qnhd])
        b0h = (h % 2) * 64
        if dbg is None or not dbg.get("no_qpes"):
            p.dma("qsp", qpes[0:64, :], qpeT[b0h:b0h + 64, h // 2, :], reads=[t_qpeT], writes=[t_qpes])
        for tc in range(T // 512):
            for rc in range(4):
                c.mm(tc, (0, 512), wkvb[:, rc, h * 256:h * 256 + 128],
                     ckvT[:, rc, tc * 512:(tc + 1) * 512], [t_wkvb, t_ckvT],
                     start=(rc == 0), stop=(rc == 3))
            bh, bt = c.banks[tc]
            p.emit("act", lambda bh=bh, tc=tc: nc.scalar.copy(
                kabsT[:, tc * 512:(tc + 1) * 512], bh[:]), reads=[bt], writes=[t_kabsT])
        for tc in range(T // 512):
            for rc in range(4):
                c.mm(tc, (0, 512), wkvb[:, rc, h * 256 + 128:h * 256 + 256],
                     ckvT[:, rc, tc * 512:(tc + 1) * 512], [t_wkvb, t_ckvT],
                     start=(rc == 0), stop=(rc == 3))
            bh, bt = c.banks[tc]
            p.emit("act", lambda bh=bh, tc=tc: nc.scalar.copy(
                vT[:, tc * 512:(tc + 1) * 512], bh[:]), reads=[bt], writes=[t_vT])
        for tt0 in range(0, NT, 4):
            bi = (tt0 // 4) % 2
            bh, bt = c.banks[bi]
            for j in range(4):
                c.transpose(bi, j * 128, vT[:, (tt0 + j) * 128:(tt0 + j + 1) * 128], [t_vT])
            p.emit("act", lambda bh=bh, tt0=tt0: nc.scalar.copy(
                vtm[:, tt0:tt0 + 4, :].rearrange("p a b -> p (a b)"), bh[:, 0:512]),
                reads=[bt], writes=[t_vtm])

        def emit_T(item):
            qt, pslh, ptoks, dgh, t_dgh = item
            qi = qt % 4
            for kt0 in range(0, qt + 1, 4):
                ntl = min(4, qt + 1 - kt0)
                bi = (kt0 // 4) % 2
                bh, bt = c.banks[bi]
                for j in range(ntl):
                    kt = kt0 + j
                    p.emit("pe", lambda bh=bh, j=j, kt=kt, pslh=pslh, dgh=dgh:
                        nc.tensor.matmul(bh[0:128, j * 128:(j + 1) * 128],
                                         pslh[:, kt * 128:(kt + 1) * 128], dgh[:],
                                         start=True, stop=True),
                        reads=[ptoks[kt // 4], t_dgh], writes=[bt])
                if (kt0 // 4) % 2 == 0:
                    p.emit("dve", lambda bh=bh, kt0=kt0, ntl=ntl, qi=qi: nc.vector.tensor_copy(
                        pT[:, kt0:kt0 + ntl, qi * 128:(qi + 1) * 128],
                        bh[:, 0:ntl * 128].rearrange("p (a x) -> p a x", a=ntl)),
                        reads=[bt], writes=[t_pT])
                else:
                    p.emit("act", lambda bh=bh, kt0=kt0, ntl=ntl, qi=qi: nc.scalar.copy(
                        pT[:, kt0:kt0 + ntl, qi * 128:(qi + 1) * 128],
                        bh[:, 0:ntl * 128].rearrange("p (a x) -> p a x", a=ntl)),
                        reads=[bt], writes=[t_pT])

        def emit_O(qg, h):
            qmax = qg * 4 + 3
            bo, bo_t = c.banks[2]
            oTh, t_oTh = oThs[qg % 2]
            for kt in range(qmax + 1):
                off = max(0, kt - qg * 4) * 128
                p.emit("pe", lambda kt=kt, off=off, bo=bo, qmax=qmax: nc.tensor.matmul(
                    bo[0:128, off:512], vtm[:, kt, :], pT[:, kt, off:512],
                    start=(kt == 0), stop=(kt == qmax)), reads=[t_vtm, t_pT], writes=[bo_t])
            p.emit("act", lambda bo=bo, oTh=oTh: nc.scalar.copy(oTh[:], bo[0:128, 0:512]),
                   reads=[bo_t], writes=[t_oTh])
            if dbg is None or not dbg.get("no_ot_write"):
                p.dma("qact", ot_ap[h * 128:(h + 1) * 128, qg * 512:(qg + 1) * 512], oTh[:],
                      reads=[t_oTh], writes=[t_ot[h]])

        pend_T = []
        for qt in range(NT):
            klen = (qt + 1) * 128
            pslh, ptoks = psls[qt % 3]
            dgh, t_dgh = diags[qt % 3]
            lsh, t_lsh = lsums[qt % 3]
            sc_v, sc_toks = c.scores
            nchunk = (klen + 511) // 512
            mck = qt // 4
            for ck in range(nchunk):
                n = min(512, klen - ck * 512)
                tok = [sc_toks[ck]]
                p.emit("pe", lambda ck=ck, n=n, qt=qt, qnhd=qnhd: nc.tensor.matmul(
                    sc_v[:, ck * 512:ck * 512 + n], qnhd[:, qt * 128:(qt + 1) * 128],
                    kabsT[:, ck * 512:ck * 512 + n], start=True, stop=False),
                    reads=[t_qnhd, t_kabsT], writes=tok)
                p.emit("pe", lambda ck=ck, n=n, qt=qt: nc.tensor.matmul(
                    sc_v[:, ck * 512:ck * 512 + n],
                    qpes[0:64, qt * 128:(qt + 1) * 128],
                    kpeT[0:64, ck * 512:ck * 512 + n], start=False, stop=True),
                    reads=[t_qpes, t_kpeT], writes=tok)
                if ck == mck:
                    p.emit("dve", lambda qt=qt: nc.vector.tensor_tensor(
                        sc_v[:, qt * 128:(qt + 1) * 128], sc_v[:, qt * 128:(qt + 1) * 128],
                        c.maskc_sb[:], OP.add), reads=tok + [c.t_maskc], writes=tok)
                p.emit("dve", lambda ck=ck, n=n, lsh=lsh: nc.vector.tensor_reduce(
                    lsh[:, 4 + ck:5 + ck], sc_v[:, ck * 512:ck * 512 + n], AX.X, OP.max),
                    reads=tok, writes=[t_lsh])
            p.emit("dve", lambda lsh=lsh, nchunk=nchunk: nc.vector.tensor_reduce(
                lsh[:, 0:1], lsh[:, 4:4 + nchunk], AX.X, OP.max, negate=True),
                reads=[t_lsh], writes=[t_lsh])
            if len(pend_T) >= 2:
                emit_T(pend_T.pop(0))
            for ck in range(nchunk):
                n = min(512, klen - ck * 512)
                tok = [sc_toks[ck]]
                p.emit("act", lambda ck=ck, n=n, pslh=pslh, lsh=lsh: nc.scalar.activation(
                    pslh[:, ck * 512:ck * 512 + n], sc_v[:, ck * 512:ck * 512 + n],
                    AF.Exp, bias=lsh[:, 0:1], scale=one[:], accum_out=lsh[:, 8 + ck:9 + ck]),
                    reads=tok + [t_lsh, onet], writes=[ptoks[ck], t_lsh])
            p.emit("dve", lambda lsh=lsh, nchunk=nchunk: nc.vector.tensor_reduce(
                lsh[:, 1:2], lsh[:, 8:8 + nchunk], AX.X, OP.add), reads=[t_lsh], writes=[t_lsh])
            p.emit("dve", lambda lsh=lsh: nc.vector.reciprocal(lsh[:, 2:3], lsh[:, 1:2]),
                   reads=[t_lsh], writes=[t_lsh])
            p.emit("dve", lambda dgh=dgh, lsh=lsh: nc.vector.tensor_tensor(
                dgh[:], c.ident_sb[:], lsh[:, 2:3].broadcast_to([128, 128]), OP.mult),
                reads=[c.t_ident, t_lsh], writes=[t_dgh])
            pend_T.append((qt, pslh, ptoks, dgh, t_dgh))
            if qt % 4 == 1 and qt >= 5:
                emit_O((qt - 5) // 4, h)
        while pend_T:
            emit_T(pend_T.pop(0))
        emit_O(3, h)
    p.pop_scope()
    if dbg is not None and dbg.get("stop_after_d"):
        p.pop_scope()
        return t_s

    p.push_scope()
    wos = [p.sb(f"{pfx}wo{i}", [128, NC, 512], BF16) for i in range(1)]
    oslab = [p.sb(f"{pfx}osl{i}", [128, NC, 128], BF16) for i in range(2)]
    sout, t_sout = p.sb(f"{pfx}so", [128, 512], BF16)
    wo_h, t_wo = wos[0]
    for ec in range(4):
        for cc in range(NC):
            p.dma("qpool", wo_h[:, cc, :], wo_ap[cc * 128:(cc + 1) * 128, ec * 512:(ec + 1) * 512],
                  writes=[t_wo])
        for tt in range(NT):
            osl, t_osl = oslab[tt % 2]
            if ec == 0 or True:
                srco = ot_ap[:, tt * 128:(tt + 1) * 128].rearrange("(hh pp) k -> pp hh k", pp=128)
                p.dma("qsp", osl[:], srco, reads=t_ot, writes=[t_osl])
            bi = 4 + tt % 4
            for hd in range(NC):
                c.mm(bi, (0, 512), osl[:, hd, :], wo_h[:, hd, :], [t_osl, t_wo],
                     start=(hd == 0), stop=(hd == NC - 1))
            bh, bt = c.banks[bi]
            p.emit("act", lambda bh=bh: nc.scalar.copy(sout[:], bh[:]), reads=[bt], writes=[t_sout])
            p.dma("qact", s_ap[tt * 128:(tt + 1) * 128, ec * 512:(ec + 1) * 512], sout[:],
                  reads=[t_sout], writes=[t_s[tt]])
    p.pop_scope()
    p.pop_scope()
    return t_s


def s_combine(c: C, x_ap, s_ap, g_ap, xout_ap, pfx, out_f32=True):
    p, nc = c.p, c.nc
    p.push_scope()
    xs = [p.sb(f"{pfx}x{i}", [128, E], F32) for i in range(2)]
    ss = [p.sb(f"{pfx}s{i}", [128, E], BF16) for i in range(2)]
    gs = [p.sb(f"{pfx}g{i}", [128, E], BF16) for i in range(2)]
    sg, t_sg = p.sb(f"{pfx}sg", [128, E], F32)
    xo = [p.sb(f"{pfx}o{i}", [128, E], F32) for i in range(2)]
    for t in range(NT):
        xh, xt = xs[t % 2]
        sh, st_ = ss[t % 2]
        gh, gt = gs[t % 2]
        oh, ot_ = xo[t % 2]
        p.dma("qsp", xh[:], x_ap[t * 128:(t + 1) * 128, :], writes=[xt])
        p.dma("qsp", sh[:], s_ap[t * 128:(t + 1) * 128, :], writes=[st_])
        p.dma("qsp", gh[:], g_ap[t * 128:(t + 1) * 128, :], writes=[gt])
        p.emit("dve", lambda sh=sh, gh=gh: nc.vector.tensor_tensor(sg[:], sh[:], gh[:], OP.mult),
               reads=[st_, gt], writes=[t_sg])
        p.emit("dve", lambda xh=xh, oh=oh: nc.vector.tensor_tensor(oh[:], xh[:], sg[:], OP.add),
               reads=[xt, t_sg], writes=[ot_])
        p.dma("qsp", xout_ap[t * 128:(t + 1) * 128, :], oh[:], reads=[ot_])
    p.pop_scope()



import contextlib as _ctxlib
import numpy as _np
import ml_dtypes as _mld
from concourse.bass_utils import run_bass_kernel_spmd as _run_spmd

_BF = _mld.bfloat16
_T = 2048
_E = 2048
_NCORES = 8
MODE = "single"


def _host_consts():
    DR = 64
    inv = 1.0 / 10000.0 ** (_np.arange(0, DR, 2, dtype=_np.float32) / DR)
    ang = _np.arange(_T, dtype=_np.float32)[:, None] * inv[None, :]
    cos = _np.concatenate([_np.cos(ang), _np.cos(ang)], -1).astype(_np.float32)
    sin = _np.concatenate([_np.sin(ang), _np.sin(ang)], -1).astype(_np.float32)
    return {
        "c_ident": _np.eye(128, dtype=_np.float32),
        "c_mask": _np.where(_np.triu(_np.ones((128, 128), bool), k=1), -1e30, 0.0).astype(_np.float32),
        "c_cos": cos, "c_sin": sin,
    }


def _tof(a):
    a = _np.asarray(a)
    return a.view(_BF) if a.dtype.kind == "V" else a


def _decl(nc, name, shape, dt, kind):
    return nc.dram_tensor(name, list(shape), dt, kind=kind).ap()


class _Prog:

    def __init__(self, name, ins, outs, internals, build_fn):
        self.name = name
        self.ins = ins
        self.outs = outs
        self.internals = internals
        self.build_fn = build_fn

    def build(self):
        nc = make_nc()
        ctx = _ctxlib.ExitStack()
        aps = {}
        with ctx:
            p = P(nc, ctx)
            c = C(p)
            for key, shape, dt in self.ins:
                if key in ("c_cos", "c_sin"):
                    continue
                aps[key] = _decl(nc, key, shape, dt, "ExternalInput")
            for key, shape, dt in self.outs:
                aps[key] = _decl(nc, key, shape, dt, "ExternalOutput")
            for key, shape, dt in self.internals:
                aps[key] = _decl(nc, key, shape, dt, "Internal")
            self.build_fn(c, aps)
            p.finish()
            block = ctx.enter_context(nc.Block())
            p.finalize_into_block(block)
        return nc


def _b_ln0(c, a):
    s_ln(c, a["x"], a["xnt0"], "n0_")


def _b_l0attn(c, a):
    s_l0_attn(c, a["xnt0"], a["L0_attn_w_q_a"], a["L0_attn_q_a_ln_w"], a["L0_attn_w_q_b"],
              a["L0_attn_w_kv_a"], a["L0_attn_kv_a_ln_w"], a["L0_attn_w_kv_b"],
              a["L0_attn_w_o"], a["qn0"], a["ot0"], a["s0"], pfx="a0")


def _b_l0ffn(c, a):
    s_l0_ffn(c, a["xnt0"], a["L0_ffn_gate_up_packed"], a["L0_ffn_gate_up_scales"],
             a["L0_ffn_down_packed"], a["L0_ffn_down_scales"], a["ht0"], a["y0"], a["g0"], pfx="f0")


def _b_comb0(c, a):
    s_combine(c, a["x"], a["s0"], a["g0"], a["x1"], "c0_")
    c.p.global_barrier()
    s_ln(c, a["x1"], a["xnt1"], "n1_")


def _b_l1attn(c, a):
    s_l1_attn(c, a["xnt1"], a["L1_attn_w_project"], a["L1_attn_w_out"], a["s1"], pfx="a1")


def _b_l1ffn(c, a):
    s_l1_ffn(c, a["xnt1"], a["L1_ffn_gate_W_packed"], a["L1_ffn_gate_scales"],
             a["L1_ffn_up_W_packed"], a["L1_ffn_up_scales"], a["L1_ffn_down_W_packed"],
             a["L1_ffn_down_scales"], a["it1"], a["y1"], a["g1"], pfx="f1")


def _b_comb1(c, a):
    s_combine(c, a["x1"], a["s1"], a["g1"], a["out"], "c1_")


_W0A = [("L0_attn_w_q_a", (2048, 1536), F32), ("L0_attn_q_a_ln_w", (1536,), F32),
        ("L0_attn_w_q_b", (1536, 3072), F32), ("L0_attn_w_kv_a", (2048, 576), F32),
        ("L0_attn_kv_a_ln_w", (512,), F32), ("L0_attn_w_kv_b", (512, 4096), F32),
        ("L0_attn_w_o", (2048, 2048), F32)]
_W0F = [("L0_ffn_gate_up_packed", (11264, 256), I32), ("L0_ffn_gate_up_scales", (16, 11264), F16),
        ("L0_ffn_down_packed", (2048, 704), I32), ("L0_ffn_down_scales", (44, 2048), F16)]
_W1A = [("L1_attn_w_project", (6144, 2048), F32), ("L1_attn_w_out", (2048, 2048), F32)]
_W1F = [("L1_ffn_gate_W_packed", (16384, 1024), I32), ("L1_ffn_gate_scales", (16384, 64), F32),
        ("L1_ffn_up_W_packed", (16384, 1024), I32), ("L1_ffn_up_scales", (16384, 64), F32),
        ("L1_ffn_down_W_packed", (2048, 8192), I32), ("L1_ffn_down_scales", (2048, 512), F32)]

_XNT = lambda n: (n, (_E, _T), BF16)
_TME = lambda n, dt=BF16: (n, (_T, _E), dt)

_LAUNCHES = [
    _Prog("ln0", [("x", (_T, _E), F32)], [_XNT("xnt0")], [], _b_ln0),
    _Prog("l0attn", [_XNT("xnt0")] + _W0A + [("c_cos", (_T, 64), F32), ("c_sin", (_T, 64), F32)],
          [_TME("s0")], [("qn0", (2048, _T), BF16), ("ot0", (2048, _T), BF16)], _b_l0attn),
    _Prog("l0ffn", [_XNT("xnt0")] + _W0F, [_TME("g0")],
          [("ht0", (5632, _T), BF16), ("y0", (_T, _E), BF16)], _b_l0ffn),
    _Prog("comb0", [("x", (_T, _E), F32), _TME("s0"), _TME("g0")],
          [("x1", (_T, _E), F32), _XNT("xnt1")], [], _b_comb0),
    _Prog("l1attn", [_XNT("xnt1")] + _W1A, [_TME("s1")], [], _b_l1attn),
    _Prog("l1ffn", [_XNT("xnt1")] + _W1F, [_TME("g1")],
          [("it1", (16384, _T), BF16), ("y1", (_T, _E), BF16)], _b_l1ffn),
    _Prog("comb1", [("x1", (_T, _E), F32), _TME("s1"), _TME("g1")],
          [("out", (_T, _E), F32)], [], _b_comb1),
]


def _build_single():
    nc = make_nc()
    ctx = _ctxlib.ExitStack()
    a = {}
    with ctx:
        p = P(nc, ctx)
        c = C(p)
        a["x"] = _decl(nc, "x", (_T, _E), F32, "ExternalInput")
        for key, shape, dt in _W0A + _W0F + _W1A + _W1F:
            a[key] = _decl(nc, key, shape, dt, "ExternalInput")
        a["out"] = _decl(nc, "out", (_T, _E), F32, "ExternalOutput")
        for key, shape, dt in [_XNT("xnt0"), _XNT("xnt1"), _TME("s0"), _TME("g0"),
                               ("x1", (_T, _E), F32), _TME("s1"), _TME("g1"),
                               ("qn0", (2048, _T), BF16), ("ot0", (2048, _T), BF16),
                               ("ht0", (5632, _T), BF16), ("y0", (_T, _E), BF16),
                               ("it1", (16384, _T), BF16), ("y1", (_T, _E), BF16)]:
            a[key] = _decl(nc, key, shape, dt, "Internal")
        _b_ln0(c, a)
        p.global_barrier()
        _b_l0attn(c, a)
        p.global_barrier()
        _b_l0ffn(c, a)
        p.global_barrier()
        _b_comb0(c, a)
        p.global_barrier()
        _b_l1attn(c, a)
        p.global_barrier()
        _b_l1ffn(c, a)
        p.global_barrier()
        _b_comb1(c, a)
        p.finish()
        block = ctx.enter_context(nc.Block())
        p.finalize_into_block(block)
    return nc


def _kernel_single(inputs):
    consts = _host_consts()
    x_full = _np.ascontiguousarray(inputs["L0_attn_x"].astype(_np.float32))
    nc = _build_single()
    in_maps = []
    for i in range(_NCORES):
        m = {"c_ident": consts["c_ident"], "c_mask": consts["c_mask"],
             "c_cos": consts["c_cos"], "c_sin": consts["c_sin"],
             "x": _np.ascontiguousarray(x_full[i])}
        for key, shape, dt in _W0A + _W0F + _W1A + _W1F:
            m[key] = _np.ascontiguousarray(inputs[key])
        in_maps.append(m)
    res = _run_spmd(nc, in_maps, core_ids=list(range(_NCORES)))
    return _np.stack([_np.asarray(res.results[i]["out"]).astype(_np.float32)
                      for i in range(_NCORES)], axis=0)


def kernel(**inputs):
    if MODE == "single":
        return _kernel_single(inputs)
    consts = _host_consts()
    x_full = _np.ascontiguousarray(inputs["L0_attn_x"].astype(_np.float32))
    percore = {i: {"x": _np.ascontiguousarray(x_full[i])} for i in range(_NCORES)}
    for lp in _LAUNCHES:
        nc = lp.build()
        in_maps = []
        for i in range(_NCORES):
            m = {"c_ident": consts["c_ident"], "c_mask": consts["c_mask"]}
            for key, shape, dt in lp.ins:
                if key in ("c_cos", "c_sin"):
                    m[key] = consts[key]
                elif key in inputs:
                    m[key] = _np.ascontiguousarray(inputs[key])
                else:
                    m[key] = _np.ascontiguousarray(percore[i][key])
            in_maps.append(m)
        res = _run_spmd(nc, in_maps, core_ids=list(range(_NCORES)))
        for i in range(_NCORES):
            for key, shape, dt in lp.outs:
                percore[i][key] = _tof(res.results[i][key])
    out = _np.stack([percore[i]["out"].astype(_np.float32) for i in range(_NCORES)], axis=0)
    return out



# version 08
# speedup vs baseline: 1.0885x; 1.0885x over previous
import sys
sys.path.insert(0, "/opt/trn_rl_repo")
import contextlib
import numpy as np
import concourse.bass as bass
import concourse.mybir as mybir

F32 = mybir.dt.float32
BF16 = mybir.dt.bfloat16
F16 = mybir.dt.float16
I32 = mybir.dt.int32
I16 = mybir.dt.int16
AF = mybir.ActivationFunctionType
AX = mybir.AxisListType
OP = mybir.AluOpType

SECTION_OF = {"pe": "tensor", "dve": "vector", "act": "scalar", "pool": "gpsimd",
              "qsp": "sync", "qact": "scalar", "qpool": "gpsimd"}
for _q in ("qsp", "qact", "qpool"):
    for _i in range(8):
        SECTION_OF[f"{_q}{_i}"] = SECTION_OF[_q]


class Tok:
    __slots__ = ("name", "writer", "readers")

    def __init__(self, name=""):
        self.name = name
        self.writer = None
        self.readers = []


class _Eng:
    __slots__ = ("name", "sem", "seq", "cells", "step", "last_real", "prefix")

    def __init__(self, name, sem, step):
        self.name = name
        self.sem = sem
        self.seq = 0
        self.cells = []
        self.step = step
        self.last_real = -1
        self.prefix = None


class P:
    def __init__(self, nc, ctx):
        self.nc = nc
        self.ctx = ctx
        self._scopes = []
        self.thunks = {s: [] for s in ("tensor", "vector", "scalar", "gpsimd", "sync")}
        self.eng = {}
        self.sem_tok = {}
        self.qslots = {}
        for name in ("pe", "dve", "act", "pool"):
            sem = ctx.enter_context(nc.semaphore(f"s_{name}"))
            self.eng[name] = _Eng(name, sem, 1)
        for q in ("qsp", "qact", "qpool"):
            self.qslots[q] = []
            for i in range(8):
                name = f"{q}{i}"
                sem = ctx.enter_context(nc.semaphore(f"s_{name}"))
                self.eng[name] = _Eng(name, sem, 16)
                self.sem_tok[name] = Tok(f"sem_{name}")
                self.qslots[q].append(name)
        self._qrr = {q: 0 for q in ("qsp", "qact", "qpool")}
        self.waited = {}
        self._sbuf_bytes = 0

    def tok(self, name=""):
        return Tok(name)

    def sb(self, name, shape, dtype, root=False):
        ctx = self.ctx if (root or not self._scopes) else self._scopes[-1]
        h = ctx.enter_context(self.nc.sbuf_tensor(name, shape, dtype))
        nb = int(np.prod(shape[1:])) * mybir.dt.size(dtype)
        self._sbuf_bytes += nb
        return h, Tok(name)

    def ps(self, name, shape):
        h = self.ctx.enter_context(self.nc.psum_tensor(name, shape, F32))
        return h, Tok(name)

    def push_scope(self):
        self._scopes.append(contextlib.ExitStack())

    def pop_scope(self):
        self.global_barrier()
        self._scopes.pop().close()

    def _close(self, e, seq):
        cell = e.cells[seq]
        if cell["inc"] == 0:
            assert not cell.get("noinst"), f"close({e.name},{seq}) targets wait-only op"
            cell["inc"] = e.step
        return seq

    def emit(self, eng_name, fn, reads=(), writes=(), close=False, tag=""):
        e = self.eng[eng_name]
        seq_self = e.seq
        section = SECTION_OF[eng_name]
        need = {}
        for t in reads:
            if t.writer is not None:
                pe_, sq = t.writer
                need[pe_] = max(need.get(pe_, -1), sq)
        for t in writes:
            if t.writer is not None:
                pe_, sq = t.writer
                need[pe_] = max(need.get(pe_, -1), sq)
            for pe_, sq in t.readers:
                need[pe_] = max(need.get(pe_, -1), sq)
        waits = []
        for pe_, sq in need.items():
            key = (section, pe_)
            if self.waited.get(key, -1) >= sq:
                continue
            self.waited[key] = sq
            self._close(self.eng[pe_], sq)
            waits.append((self.eng[pe_], sq))
            st = self.sem_tok.get(pe_)
            if st is not None:
                st.readers.append((eng_name, seq_self))
        e.seq += 1
        cell = {"inc": 0}
        if fn is None:
            cell["noinst"] = True
        e.cells.append(cell)
        if fn is not None:
            e.last_real = seq_self

        def thunk(nc=self.nc, waits=waits, fn=fn, cell=cell, sem=e.sem,
                  eng_obj_name=section):
            eng = getattr(nc, eng_obj_name)
            for pe_obj, sq in waits:
                eng.wait_ge(pe_obj.sem, pe_obj.prefix[sq])
            if fn is not None:
                instr = fn()
                if cell["inc"]:
                    instr.then_inc(sem, cell["inc"])

        self.thunks[section].append(thunk)
        if fn is not None:
            for t in reads:
                t.readers.append((eng_name, seq_self))
            for t in writes:
                t.writer = (eng_name, seq_self)
                t.readers = []
        if close:
            self._close(e, seq_self)
        return seq_self

    def global_barrier(self):
        names = ["pe", "dve", "act", "pool"] + [s for q in self.qslots.values() for s in q]
        vals = {}
        for n in names:
            e = self.eng[n]
            if e.last_real >= 0:
                vals[n] = self._close(e, e.last_real)
        for sec in ("tensor", "vector", "scalar", "gpsimd", "sync"):
            waits = []
            for n, sq in vals.items():
                key = (sec, n)
                if self.waited.get(key, -1) >= sq:
                    continue
                self.waited[key] = sq
                waits.append((self.eng[n], sq))
            if not waits:
                continue

            def thunk(nc=self.nc, waits=tuple(waits), obj=sec):
                eng = getattr(nc, obj)
                for pe_obj, sq in waits:
                    eng.wait_ge(pe_obj.sem, pe_obj.prefix[sq])
            self.thunks[sec].append(thunk)

    def barrier_wait(self, eng_name, toks):
        self.emit(eng_name, None, reads=toks)

    def finish(self):
        for q in ("qsp", "qact", "qpool"):
            obj = SECTION_OF[q]
            for name in self.qslots[q]:
                e = self.eng[name]
                if e.last_real < 0:
                    continue
                self._close(e, e.last_real)

                def thunk(nc=self.nc, e=e, obj=obj):
                    if e.prefix and e.prefix[-1] > 0:
                        getattr(nc, obj).wait_ge(e.sem, e.prefix[-1])
                self.thunks[obj].append(thunk)

    def finalize_into_block(self, block):
        for e in self.eng.values():
            run = 0
            pref = []
            for cell in e.cells:
                run += cell["inc"]
                pref.append(run)
            e.prefix = pref
        for section in ("sync", "gpsimd", "tensor", "vector", "scalar"):
            ths = self.thunks[section]
            if not ths:
                continue
            deco = getattr(block, section)

            def body(engine, ths=ths):
                for t in ths:
                    t()
            deco(body)

    def dma(self, queue, out_ap, in_ap, reads=(), writes=(), **kw):
        slot = self.qslots[queue][self._qrr[queue] % len(self.qslots[queue])]
        self._qrr[queue] += 1
        eng_obj = SECTION_OF[queue]
        return self.emit(slot, lambda: getattr(self.nc, eng_obj).dma_start(out=out_ap, in_=in_ap, **kw),
                         reads=list(reads),
                         writes=list(writes) + [self.sem_tok[slot]], close=True)


def make_nc():
    return bass.Bass("TRN2", target_bir_lowering=False, debug=False)


"""Stage builders for the mk_recurrent_d2 megakernel. All per-core (one batch elem).

Layouts:
 - activations token-major [tok x feat] row-major in HBM
 - xnT in HBM: plain [E, T] bf16 row-major (feature-major)
 - s/g outputs token-major [T, E] bf16
"""
import numpy as np

USE_DMA_TRANSPOSE = False
T = 2048
E = 2048
NT = T // 128
NC = E // 128


class C:

    def __init__(self, p: P):
        self.p = p
        self.nc = p.nc
        self.psum_h, _ = p.ps("psumall", [128, 4096])
        self.banks = [(self.psum_h[:, i * 512:(i + 1) * 512], Tok(f"bank{i}")) for i in range(8)]
        self.scores = (self.psum_h[:, 2048:4096], [self.banks[i][1] for i in range(4, 8)])
        self.ident = self.nc.dram_tensor("c_ident", [128, 128], F32, kind="ExternalInput").ap()
        self.ident_sb, self.t_ident = p.sb("identsb", [128, 128], BF16)
        p.dma("qpool", self.ident_sb[:], self.ident[:, :], writes=[self.t_ident])
        self.ident32_sb, self.t_ident32 = p.sb("ident32sb", [128, 128], F32)
        p.dma("qsp", self.ident32_sb[:], self.ident[:, :], writes=[self.t_ident32])
        self.maskc = self.nc.dram_tensor("c_mask", [128, 128], F32, kind="ExternalInput").ap()
        self.maskc_sb, self.t_maskc = p.sb("maskcsb", [128, 128], F32)
        p.dma("qsp", self.maskc_sb[:], self.maskc[:, :], writes=[self.t_maskc])
        self._consts = {}
        for v in (0.0, 1.0, -0.5, 1e-5, 1e-6, 1.0 / 2048, 1.0 / 512, 1.0 / 1536,
                  1.0 / np.sqrt(192.0), 1.0 / np.sqrt(128.0), -2.0, -6.0, 2.0):
            self.const(v)

    def const(self, val):
        key = float(val)
        if key not in self._consts:
            assert not self.p._scopes, f"const {key} requested inside a scope; pre-register in C.__init__"
            h, t = self.p.sb(f"c{len(self._consts)}", [128, 1], F32, root=True)
            self.p.emit("dve", lambda h=h, v=key: self.nc.vector.memset(h[:], v), writes=[t])
            try:
                self.nc.const_aps.register(mybir.dt.float32, key, h[:])
            except Exception:
                pass
            self._consts[key] = (h, t)
        return self._consts[key]

    def host_consts(self):
        return {"c_ident": np.eye(128, dtype=np.float32),
                "c_mask": np.where(np.triu(np.ones((128, 128), bool), k=1), -1e30, 0.0).astype(np.float32)}

    def transpose(self, bank_i, col_off, src_ap, reads, n_rows=128):
        bh, bt = self.banks[bank_i]
        self.p.emit("pe", lambda: self.nc.tensor.matmul(
            bh[0:src_ap.shape[-1], col_off:col_off + n_rows],
            src_ap, self.ident_sb[0:n_rows, 0:n_rows], start=True, stop=True),
            reads=list(reads) + [self.t_ident], writes=[bt])
        return bh, bt

    def mm(self, bank_i, cols, lhsT_ap, rhs_ap, reads, start, stop, close=False):
        bh, bt = self.banks[bank_i]
        self.p.emit("pe", lambda: self.nc.tensor.matmul(
            bh[0:lhsT_ap.shape[-1], cols[0]:cols[1]], lhsT_ap, rhs_ap, start=start, stop=stop),
            reads=list(reads), writes=[bt], close=close)
        return bh, bt


def s_ln(c: C, x_ap, xnt_ap, pfx, eps=1e-5):
    p, nc = c.p, c.nc
    p.push_scope()
    xs = [p.sb(f"{pfx}x{i}", [128, E], F32) for i in range(2)]
    sq, t_sq = p.sb(f"{pfx}sq", [128, E], F32)
    xns = [p.sb(f"{pfx}xn{i}", [128, E], BF16) for i in range(2)]
    st, t_st = p.sb(f"{pfx}st", [128, 8], F32)
    evs = [p.sb(f"{pfx}ev{i}", [128, 512], BF16) for i in range(4)]
    eb, ebt = c.const(eps)
    one, onet = c.const(1.0)
    nh, nht = c.const(-0.5)
    zz, zzt = c.const(0.0)
    for t in range(NT):
        xh, xt = xs[t % 2]
        xn, t_xn = xns[t % 2]
        p.dma("qsp", xh[:], x_ap[t * 128:(t + 1) * 128, :], writes=[xt])
        p.emit("dve", lambda xh=xh: nc.vector.tensor_reduce(st[:, 0:1], xh[:], AX.X, OP.add),
               reads=[xt], writes=[t_st])
        p.emit("dve", lambda xh=xh: nc.vector.tensor_tensor(sq[:], xh[:], xh[:], OP.mult),
               reads=[xt], writes=[t_sq])
        p.emit("dve", lambda: nc.vector.tensor_reduce(st[:, 1:2], sq[:], AX.X, OP.add),
               reads=[t_sq], writes=[t_st])
        p.emit("dve", lambda: nc.vector.tensor_scalar(st[:, 2:3], st[:, 0:1], 1.0 / E, None, OP.mult),
               reads=[t_st], writes=[t_st])
        p.emit("dve", lambda: nc.vector.tensor_tensor(st[:, 3:4], st[:, 2:3], st[:, 2:3], OP.mult),
               reads=[t_st], writes=[t_st])
        p.emit("dve", lambda: nc.vector.scalar_tensor_tensor(
            st[:, 4:5], st[:, 1:2], 1.0 / E, st[:, 3:4], OP.mult, OP.subtract),
            reads=[t_st], writes=[t_st])
        p.emit("act", lambda: nc.scalar.activation(st[:, 5:6], st[:, 4:5], AF.Ln, bias=eb[:], scale=one[:]),
               reads=[t_st, ebt, onet], writes=[t_st])
        p.emit("act", lambda: nc.scalar.activation(st[:, 6:7], st[:, 5:6], AF.Exp, bias=zz[:], scale=nh[:]),
               reads=[t_st, nht, zzt], writes=[t_st])
        p.emit("dve", lambda xh=xh, xn=xn: nc.vector.tensor_scalar(
            xn[:], xh[:], st[:, 2:3], st[:, 6:7], OP.subtract, OP.mult),
            reads=[xt, t_st], writes=[t_xn])
        for quad in range(4):
            bi = (t % 2) * 4 + quad
            for j in range(4):
                cc = quad * 4 + j
                c.transpose(bi, j * 128, xn[:, cc * 128:(cc + 1) * 128], [t_xn])
            evh, evt = evs[quad]
            bh, bt = c.banks[bi]
            p.emit("act", lambda bh=bh, evh=evh: nc.scalar.copy(evh[:], bh[:]),
                   reads=[bt], writes=[evt])
            dst = xnt_ap[quad * 512:(quad + 1) * 512, t * 128:(t + 1) * 128]
            dst = dst.rearrange("(c f) k -> f c k", c=4)
            p.dma("qact", dst, evh[:].rearrange("f (c k) -> f c k", c=4), reads=[evt])
    p.pop_scope()


def ref_ln(x):
    m = x.mean(-1, keepdims=True)
    v = x.var(-1, keepdims=True)
    return (x - m) / np.sqrt(v + 1e-5)



def decode_fp4_l0(c: C, pk_ap, sct_pair, ds, n_words, reads_pk):
    p, nc = c.p, c.nc
    K = 8 * n_words
    G = n_words // 16
    (ki, t_ki), (sgn, t_sgn), (ssc, t_ssc) = ds["ki"], ds["sgn"], ds["ssc"]
    (va, t_va), (t1h, t_t1), (t2h, t_t2), (t3h, t_t3), (vv, t_vv) = \
        ds["va"], ds["t1"], ds["t2"], ds["t3"], ds["vv"]
    wd, t_wd = ds["wd"]
    sch, t_sch = sct_pair
    for j in range(8):
        sl = slice(j * n_words, (j + 1) * n_words)
        if j == 7:
            p.emit("dve", lambda sl=sl: nc.vector.tensor_scalar(
                ki[:, sl], pk_ap, 0x70000000, None, OP.bitwise_and), reads=reads_pk, writes=[t_ki])
        else:
            p.emit("dve", lambda sl=sl, j=j: nc.vector.tensor_scalar(
                ki[:, sl], pk_ap, 28 - 4 * j, 0x70000000, OP.logical_shift_left, OP.bitwise_and),
                reads=reads_pk, writes=[t_ki])
        if j == 7:
            p.emit("dve", lambda sl=sl: nc.vector.tensor_scalar(
                sgn[:, sl], pk_ap, -0x80000000, None, OP.bitwise_and),
                reads=reads_pk, writes=[t_sgn])
        else:
            p.emit("dve", lambda sl=sl, j=j: nc.vector.tensor_scalar(
                sgn[:, sl], pk_ap, 28 - 4 * j, -0x80000000, OP.logical_shift_left, OP.bitwise_and),
                reads=reads_pk, writes=[t_sgn])
    sb = sch[:, 0:G].bitcast(I32).unsqueeze(1).unsqueeze(3).broadcast_to([128, 8, G, 16])
    p.emit("dve", lambda: nc.vector.tensor_tensor(
        ssc[:, 0:K].bitcast(I32).rearrange("p (j g r) -> p j g r", j=8, g=G),
        sb, sgn[:, 0:K].rearrange("p (j g r) -> p j g r", j=8, g=G), OP.bitwise_or),
        reads=[t_sch, t_sgn], writes=[t_ssc])
    p.emit("dve", lambda: nc.vector.tensor_scalar(va[:, 0:K], ki[:, 0:K], 0.5 / 268435456.0, None, OP.mult),
           reads=[t_ki], writes=[t_va])
    p.emit("dve", lambda: nc.vector.tensor_scalar(t1h[:, 0:K], va[:, 0:K], 2.0, -2.0, OP.mult, OP.add),
           reads=[t_va], writes=[t_t1])
    p.emit("dve", lambda: nc.vector.tensor_scalar(t3h[:, 0:K], t1h[:, 0:K], 2.0, -4.0, OP.mult, OP.add),
           reads=[t_t1], writes=[t_t3])
    p.emit("dve", lambda: nc.vector.tensor_tensor(t2h[:, 0:K], va[:, 0:K], t1h[:, 0:K], OP.max),
           reads=[t_va, t_t1], writes=[t_t2])
    p.emit("dve", lambda: nc.vector.tensor_tensor(vv[:, 0:K], t2h[:, 0:K], t3h[:, 0:K], OP.max),
           reads=[t_t2, t_t3], writes=[t_vv])
    p.emit("pool", lambda: nc.gpsimd.tensor_tensor(
        wd[:, 0:K].rearrange("p (w j) -> p j w", j=8),
        vv[:, 0:K].rearrange("p (j w) -> p j w", j=8),
        ssc[:, 0:K].rearrange("p (j w) -> p j w", j=8), OP.mult),
        reads=[t_vv, t_ssc], writes=[t_wd])
    return wd, t_wd


def decode_bufs(p, pfx, n_words, nset=3):
    K = 8 * n_words
    shared = {}
    for nm, dt in (("ki", I32), ("sgn", I32), ("va", BF16), ("t1", BF16),
                   ("t2", BF16), ("t3", BF16)):
        shared[nm] = p.sb(f"{pfx}{nm}", [128, K], dt)
    sets = []
    for i in range(max(3, nset)):
        d = dict(shared)
        for nm, dt in (("ssc", F32), ("vv", BF16), ("wd", BF16)):
            d[nm] = p.sb(f"{pfx}{nm}{i}", [128, K], dt)
        sets.append(d)
    return sets


def s_l0_ffn(c: C, xnt_ap, pk_ap, sc_ap, dpk_ap, dsc_ap, ht_ap, y_ap, g_ap, pfx="f0"):
    p, nc = c.p, c.nc
    zz, zzt = c.const(0.0)
    one, onet = c.const(1.0)
    eb, ebt = c.const(1e-6)
    nh, nht = c.const(-0.5)
    inve, invet = c.const(1.0 / E)
    t_ht = [p.tok(f"ht{i}") for i in range(44)]
    t_y = [p.tok(f"y{i}") for i in range(NT)]
    p.push_scope()
    gscs = [p.sb(f"{pfx}gsc{i}", [32, 176], F32) for i in range(2)]
    dec = decode_bufs(p, f"{pfx}d", 256, 3)
    pks = [p.sb(f"{pfx}pk{i}", [128, 256], I32) for i in range(2)]
    scts = [p.sb(f"{pfx}sct{i}", [128, 32], F32) for i in range(2)]
    p.push_scope()
    xnt, t_xnt = p.sb(f"{pfx}xnt", [128, NC, T], BF16)
    for cc in range(NC):
        p.dma("qsp", xnt[:, cc, :], xnt_ap[cc * 128:(cc + 1) * 128, :], writes=[t_xnt])
    wtiles = [p.sb(f"{pfx}wt{i}", [128, 2048], BF16) for i in range(2)]
    gsbs = [p.sb(f"{pfx}gsb{i}", [128, 2048], BF16) for i in range(3)]
    hsbs = [p.sb(f"{pfx}hsb{i}", [128, 2048], BF16) for i in range(3)]
    usbs = [p.sb(f"{pfx}usb{i}", [128, 2048], BF16) for i in range(3)]
    pend = []

    def _flush_hmult(item):
        (j, gsb, t_gsb, usb, t_usb, hsb, t_hsb) = item
        p.emit("dve", lambda gsb=gsb, usb=usb, hsb=hsb: nc.vector.tensor_tensor(
            hsb[:], gsb[:], usb[:], OP.mult), reads=[t_usb, t_gsb], writes=[t_hsb])
        p.dma("qact", ht_ap[j * 128:(j + 1) * 128, :], hsb[:], reads=[t_hsb],
              writes=[t_ht[j]])

    def sc_transpose(src_ap, src_tok, gcount, sct, t_sct):
        for i in range(4):
            p.emit("dve", lambda i=i, src_ap=src_ap, sct=sct: nc.vector.transpose(
                sct[32 * i:32 * (i + 1), 0:32], src_ap[:, 32 * i:32 * (i + 1)]),
                reads=[src_tok], writes=[t_sct])

    def wd_transposes(wd, t_wd, kchunks, store_fn):
        for g4 in range((kchunks + 3) // 4):
            bi = 4 + (g4 % 2)
            bh, bt = c.banks[bi]
            js = list(range(g4 * 4, min(g4 * 4 + 4, kchunks)))
            for idx, kc in enumerate(js):
                c.transpose(bi, idx * 128, wd[:, kc * 128:(kc + 1) * 128], [t_wd])
            store_fn(js, bh, bt)

    for i in range(44):
        for gu in range(2):
            n = i + gu * 44
            it = 2 * i + gu
            pkh, t_pk = pks[it % 2]
            sct, t_sct = scts[it % 2]
            ds = dec[it % 3]
            wtile, t_wtile = wtiles[it % 2]
            p.dma("qsp", pkh[:], pk_ap[n * 128:(n + 1) * 128, :], writes=[t_pk])
            gsch, t_gsch = gscs[it % 2]
            p.dma("qpool", gsch[0:16, 0:128], sc_ap[:, n * 128:(n + 1) * 128], writes=[t_gsch])
            sc_transpose(gsch[0:32, 0:128], t_gsch, 16, sct, t_sct)
            decode_fp4_l0(c, pkh[:], (sct, t_sct), ds, 256, [t_pk])
            wd, t_wd = ds["wd"]

            def store(js, bh, bt, wtile=wtile, t_wtile=t_wtile):
                w0, n_ = js[0], len(js)
                p.emit("act", lambda w0=w0, n_=n_, bh=bh: nc.scalar.copy(
                    wtile[:, w0 * 128:(w0 + n_) * 128], bh[:, 0:n_ * 128]),
                    reads=[bt], writes=[t_wtile])
            wd_transposes(wd, t_wd, 16, store)
            for kc in range(NC):
                for tc in range(4):
                    c.mm(tc, (0, 512), wtile[:, kc * 128:(kc + 1) * 128],
                         xnt[:, kc, tc * 512:(tc + 1) * 512], [t_wtile, t_xnt],
                         start=(kc == 0), stop=(kc == NC - 1))
            gsb, t_gsb = gsbs[i % 3]
            usb, t_usb = usbs[i % 3]
            hsb, t_hsb = hsbs[i % 3]
            if gu == 0:
                for tc in range(4):
                    bh, bt = c.banks[tc]
                    p.emit("act", lambda bh=bh, gsb=gsb, tc=tc: nc.scalar.activation(
                        gsb[:, tc * 512:(tc + 1) * 512], bh[:], AF.Silu, bias=zz[:], scale=one[:]),
                        reads=[bt, zzt, onet], writes=[t_gsb])
            else:
                for tc in range(4):
                    bh, bt = c.banks[tc]
                    p.emit("act", lambda bh=bh, usb=usb, tc=tc: nc.scalar.copy(
                        usb[:, tc * 512:(tc + 1) * 512], bh[:]), reads=[bt], writes=[t_usb])
                pend.append((i, gsb, t_gsb, usb, t_usb, hsb, t_hsb))
                if len(pend) >= 3:
                    _flush_hmult(pend.pop(0))
    while pend:
        _flush_hmult(pend.pop(0))

    p.pop_scope()
    p.push_scope()
    wdt, _twdt0 = p.sb(f"{pfx}wdt", [128, 44, 256], BF16)
    t_wdq = [p.tok(f"{pfx}wdq{i}") for i in range(4)]
    ypsb, t_ypsb = p.sb(f"{pfx}ypsb", [128, 512], BF16)
    hch = [p.sb(f"{pfx}hch{i}", [128, 2048], BF16) for i in range(4)]
    ytev, t_ytev = p.sb(f"{pfx}ytev", [128, 4096], BF16)

    def emit_dec(ps, q):
        for sub2 in range(2):
            hrow = ps * 2 + sub2
            sl4 = q
            pkh, t_pk = pks[sub2 % 2]
            sct, t_sct = scts[sub2 % 2]
            ds = dec[(hrow * 4 + sl4) % 3]
            w0 = sl4 * 176
            pkq = "qsp" if ps == 0 else "qpool"
            p.dma(pkq, pkh[:, 0:176], dpk_ap[hrow * 128:(hrow + 1) * 128, w0:w0 + 176],
                  writes=[t_pk])
            gsch, t_gsch = gscs[sub2 % 2]
            p.dma("qpool", gsch[0:11, 0:128],
                  dsc_ap[sl4 * 11:sl4 * 11 + 11, hrow * 128:(hrow + 1) * 128], writes=[t_gsch])
            sc_transpose(gsch[0:32, 0:128], t_gsch, 11, sct, t_sct)
            decode_fp4_l0(c, pkh[:, 0:176], (sct, t_sct), ds, 176, [t_pk])
            wd, t_wd = ds["wd"]
            if q < 3:
                wdv = wd[:, 0:1408].rearrange("p (kc x) -> p kc x", kc=11)
                dst = wdt[:, sl4 * 11:(sl4 + 1) * 11, sub2 * 128:(sub2 + 1) * 128]
                for i in range(4):
                    for j in range(4):
                        p.emit("dve", lambda i=i, j=j, wdv=wdv, dst=dst: nc.vector.transpose(
                            dst[32 * i:32 * (i + 1), :, 32 * j:32 * (j + 1)],
                            wdv[32 * j:32 * (j + 1), :, 32 * i:32 * (i + 1)]),
                            reads=[t_wd], writes=[t_wdq[q]])
            else:
                def store(js, bh, bt, sl4=sl4, sub2=sub2):
                    for idx, kc in enumerate(js):
                        ic = sl4 * 11 + kc
                        p.emit("act", lambda bh=bh, idx=idx, ic=ic, sub2=sub2: nc.scalar.copy(
                            wdt[:, ic, sub2 * 128:(sub2 + 1) * 128],
                            bh[:, idx * 128:(idx + 1) * 128]),
                            reads=[bt], writes=[t_wdq[3]])
                wd_transposes(wd, t_wd, 11, store)

    for q in range(4):
        emit_dec(0, q)
    for ps in range(8):
        for ic in range(44):
            if ic in (11, 22, 33) and ps + 1 < 8:
                emit_dec(ps + 1, ic // 11 - 1)
            hh, t_hh = hch[ic % 4]
            p.dma("qsp" if ic % 2 == 0 else "qact", hh[:], ht_ap[ic * 128:(ic + 1) * 128, :],
                  reads=[t_ht[ic]], writes=[t_hh])
            for hj in range(2):
                for tc in range(4):
                    c.mm(hj * 4 + tc, (0, 512), wdt[:, ic, hj * 128:(hj + 1) * 128],
                         hh[:, tc * 512:(tc + 1) * 512], [t_wdq[min(ic // 11, 3)], t_hh],
                         start=(ic == 0), stop=(ic == 43))
        for bi in range(8):
            bh, bt = c.banks[bi]
            p.emit("act", lambda bh=bh, bi=bi: nc.scalar.copy(
                ytev[:, bi * 512:(bi + 1) * 512], bh[:]), reads=[bt], writes=[t_ytev])
        for hj in range(2):
            for q in range(4):
                bi = q % 2
                bh, bt = c.banks[bi]
                for idx in range(4):
                    tt = q * 4 + idx
                    c.transpose(bi, idx * 128,
                                ytev[:, hj * 2048 + tt * 128: hj * 2048 + (tt + 1) * 128],
                                [t_ytev])
                tt0 = q * 4
                p.emit("act", lambda bh=bh: nc.scalar.copy(ypsb[:], bh[:]),
                       reads=[bt], writes=[t_ypsb])
                dst = y_ap[tt0 * 128:(tt0 + 4) * 128, ps * 256 + hj * 128: ps * 256 + (hj + 1) * 128]
                dst = dst.rearrange("(a p) h -> p a h", a=4)
                p.dma("qact", dst, ypsb[:].rearrange("p (a h) -> p a h", a=4), reads=[t_ypsb],
                      writes=[t_y[tt0 + _i] for _i in range(4)])
        if ps + 1 < 8:
            emit_dec(ps + 1, 3)
    p.pop_scope()
    p.pop_scope()
    p.push_scope()
    sq2, t_sq2 = p.sb(f"{pfx}sq2", [128, E], F32)
    st2, t_st2 = p.sb(f"{pfx}st2", [128, 4], F32)
    gout, t_gout = p.sb(f"{pfx}gout", [128, E], BF16)
    yts = [p.sb(f"{pfx}yt{i}", [128, E], BF16) for i in range(2)]
    for tt in range(NT):
        yth, t_yth = yts[tt % 2]
        p.dma("qsp", yth[:], y_ap[tt * 128:(tt + 1) * 128, :], reads=[t_y[tt]], writes=[t_yth])
        p.emit("dve", lambda yth=yth: nc.vector.tensor_tensor(
            sq2[:], yth[:], yth[:], OP.mult), reads=[t_yth], writes=[t_sq2])
        p.emit("dve", lambda: nc.vector.tensor_reduce(st2[:, 0:1], sq2[:], AX.X, OP.add),
               reads=[t_sq2], writes=[t_st2])
        p.emit("act", lambda: nc.scalar.activation(st2[:, 1:2], st2[:, 0:1], AF.Ln, bias=eb[:], scale=inve[:]),
               reads=[t_st2, ebt, invet], writes=[t_st2])
        p.emit("act", lambda: nc.scalar.activation(st2[:, 2:3], st2[:, 1:2], AF.Exp, bias=zz[:], scale=nh[:]),
               reads=[t_st2, nht, zzt], writes=[t_st2])
        p.emit("act", lambda yth=yth: nc.scalar.activation(
            gout[:], yth[:], AF.Silu, bias=zz[:], scale=st2[:, 2:3]),
            reads=[t_yth, t_st2, zzt], writes=[t_gout])
        p.dma("qact", g_ap[tt * 128:(tt + 1) * 128, :], gout[:], reads=[t_gout])
    p.pop_scope()



def decode_fp4_l1(c: C, pk_ap, sc_pair, ds, n_words, reads_pk):
    p, nc = c.p, c.nc
    K = 2 * n_words
    G = n_words // 16
    (ki, t_ki), (sgn, t_sgn), (ssc, t_ssc) = ds["ki"], ds["sgn"], ds["ssc"]
    (va, t_va), (t1h, t_t1), (t2h, t_t2), (t3h, t_t3), (vv, t_vv) = \
        ds["va"], ds["t1"], ds["t2"], ds["t3"], ds["vv"]
    wd, t_wd = ds["wd"]
    sch, t_sch = sc_pair
    p.emit("dve", lambda: nc.vector.tensor_scalar(
        ki[:, 0:n_words], pk_ap, 28, 0x70000000, OP.logical_shift_left, OP.bitwise_and),
        reads=reads_pk, writes=[t_ki])
    p.emit("dve", lambda: nc.vector.tensor_scalar(
        ki[:, n_words:K], pk_ap, 24, 0x70000000, OP.logical_shift_left, OP.bitwise_and),
        reads=reads_pk, writes=[t_ki])
    p.emit("dve", lambda: nc.vector.tensor_scalar(
        sgn[:, 0:n_words], pk_ap, 28, -0x80000000, OP.logical_shift_left, OP.bitwise_and),
        reads=reads_pk, writes=[t_sgn])
    p.emit("dve", lambda: nc.vector.tensor_scalar(
        sgn[:, n_words:K], pk_ap, 24, -0x80000000, OP.logical_shift_left, OP.bitwise_and),
        reads=reads_pk, writes=[t_sgn])
    sb = sch[:, 0:G].bitcast(I32).unsqueeze(1).unsqueeze(3).broadcast_to([128, 2, G, 16])
    p.emit("dve", lambda: nc.vector.tensor_tensor(
        ssc[:, 0:K].bitcast(I32).rearrange("p (j g r) -> p j g r", j=2, g=G),
        sb, sgn[:, 0:K].rearrange("p (j g r) -> p j g r", j=2, g=G), OP.bitwise_or),
        reads=[t_sch, t_sgn], writes=[t_ssc])
    p.emit("dve", lambda: nc.vector.tensor_scalar(va[:, 0:K], ki[:, 0:K], 0.5 / 268435456.0, None, OP.mult),
           reads=[t_ki], writes=[t_va])
    p.emit("dve", lambda: nc.vector.tensor_scalar(t1h[:, 0:K], va[:, 0:K], 2.0, -2.0, OP.mult, OP.add),
           reads=[t_va], writes=[t_t1])
    p.emit("dve", lambda: nc.vector.tensor_scalar(t3h[:, 0:K], t1h[:, 0:K], 2.0, -4.0, OP.mult, OP.add),
           reads=[t_t1], writes=[t_t3])
    p.emit("dve", lambda: nc.vector.tensor_tensor(t2h[:, 0:K], va[:, 0:K], t1h[:, 0:K], OP.max),
           reads=[t_va, t_t1], writes=[t_t2])
    p.emit("dve", lambda: nc.vector.tensor_tensor(vv[:, 0:K], t2h[:, 0:K], t3h[:, 0:K], OP.max),
           reads=[t_t2, t_t3], writes=[t_vv])
    p.emit("pool", lambda: nc.gpsimd.tensor_tensor(
        wd[:, 0:K].rearrange("p (w j) -> p j w", j=2),
        vv[:, 0:K].rearrange("p (j w) -> p j w", j=2),
        ssc[:, 0:K].rearrange("p (j w) -> p j w", j=2), OP.mult),
        reads=[t_vv, t_ssc], writes=[t_wd])
    return wd, t_wd


def decode_fp4_l1t(c: C, pk_ap, reads_pk, scpl_pair, ds, pkT_pair, wd_ap=None, wd_tok=None):
    p, nc = c.p, c.nc
    (ki, t_ki), (sgn, t_sgn), (ssc, t_ssc) = ds["ki"], ds["sgn"], ds["ssc"]
    (va, t_va), (t1h, t_t1), (t2h, t_t2), (t3h, t_t3), (vv, t_vv) = \
        ds["va"], ds["t1"], ds["t2"], ds["t3"], ds["vv"]
    scpl, t_scpl = scpl_pair
    if wd_ap is None:
        wdh, wd_tok = ds["wd"]
        wd_ap = wdh[:].rearrange("p (s j n) -> p j s n", s=8, j=2)
    pkT, t_pkT = pkT_pair
    pkv = pk_ap.rearrange("p (s n) -> p s n", s=8)
    pTv = pkT[:].rearrange("p (s n) -> p s n", s=8)
    for i in range(4):
        for j in range(4):
            p.emit("dve", lambda i=i, j=j, pkv=pkv, pTv=pTv: nc.vector.transpose(
                pTv[32 * i:32 * (i + 1), :, 32 * j:32 * (j + 1)],
                pkv[32 * j:32 * (j + 1), :, 32 * i:32 * (i + 1)]),
                reads=list(reads_pk), writes=[t_pkT])
    for jn in range(2):
        p.emit("dve", lambda jn=jn, pkT=pkT: nc.vector.tensor_scalar(
            ki[:, jn * 1024:(jn + 1) * 1024], pkT[:], 28 - 4 * jn, 0x70000000,
            OP.logical_shift_left, OP.bitwise_and), reads=[t_pkT], writes=[t_ki])
        p.emit("dve", lambda jn=jn, pkT=pkT: nc.vector.tensor_scalar(
            sgn[:, jn * 1024:(jn + 1) * 1024], pkT[:], 28 - 4 * jn, -0x80000000,
            OP.logical_shift_left, OP.bitwise_and), reads=[t_pkT], writes=[t_sgn])
    sb = scpl[:].bitcast(I32).unsqueeze(1).broadcast_to([128, 2, 8, 128])
    p.emit("dve", lambda sb=sb: nc.vector.tensor_tensor(
        ssc[:, 0:2048].bitcast(I32).rearrange("p (j s n) -> p j s n", j=2, s=8),
        sb, sgn[:, 0:2048].rearrange("p (j s n) -> p j s n", j=2, s=8), OP.bitwise_or),
        reads=[t_scpl, t_sgn], writes=[t_ssc])
    p.emit("dve", lambda: nc.vector.tensor_scalar(va[:, 0:2048], ki[:, 0:2048],
           0.5 / 268435456.0, None, OP.mult), reads=[t_ki], writes=[t_va])
    p.emit("dve", lambda: nc.vector.tensor_scalar(t1h[:, 0:2048], va[:, 0:2048], 2.0, -2.0,
           OP.mult, OP.add), reads=[t_va], writes=[t_t1])
    p.emit("dve", lambda: nc.vector.tensor_scalar(t3h[:, 0:2048], t1h[:, 0:2048], 2.0, -4.0,
           OP.mult, OP.add), reads=[t_t1], writes=[t_t3])
    p.emit("dve", lambda: nc.vector.tensor_tensor(t2h[:, 0:2048], va[:, 0:2048],
           t1h[:, 0:2048], OP.max), reads=[t_va, t_t1], writes=[t_t2])
    p.emit("dve", lambda: nc.vector.tensor_tensor(vv[:, 0:2048], t2h[:, 0:2048],
           t3h[:, 0:2048], OP.max), reads=[t_t2, t_t3], writes=[t_vv])
    p.emit("pool", lambda wd_ap=wd_ap: nc.gpsimd.tensor_tensor(
        wd_ap, vv[:, 0:2048].rearrange("p (j s n) -> p j s n", j=2, s=8),
        ssc[:, 0:2048].rearrange("p (j s n) -> p j s n", j=2, s=8), OP.mult),
        reads=[t_vv, t_ssc], writes=[wd_tok])
    return wd_tok


def s_l1_ffn(c: C, xnt_ap, gpk_ap, gsc_ap, upk_ap, usc_ap, dpk_ap, dsc_ap,
             it_ap, y_ap, g_ap, pfx="f1", n_tiles=128, phases="all"):
    p, nc = c.p, c.nc
    N1 = 16384
    zz, zzt = c.const(0.0)
    one, onet = c.const(1.0)
    t_it = [p.tok(f"it{i}") for i in range(128)]
    t_y = [p.tok(f"yy{i}") for i in range(NT)]

    p.push_scope()
    xnt, t_xnt = p.sb(f"{pfx}xnt", [128, NC, T], BF16)
    for cc in range(NC):
        p.dma("qsp", xnt[:, cc, :], xnt_ap[cc * 128:(cc + 1) * 128, :], writes=[t_xnt])
    dec = decode_bufs(p, f"{pfx}d", 256, 3)
    pks = [p.sb(f"{pfx}pk{i}", [128, 1024], I32) for i in range(2)]
    wtiles = [p.sb(f"{pfx}wt{i}", [128, 2048], BF16) for i in range(2)]
    scs = [p.sb(f"{pfx}sc{i}", [128, 64], F32) for i in range(2)]
    gsbs = [p.sb(f"{pfx}gsb{i}", [128, 2048], BF16) for i in range(3)]
    hsbs = [p.sb(f"{pfx}hsb{i}", [128, 2048], BF16) for i in range(3)]
    usbs = [p.sb(f"{pfx}usb{i}", [128, 2048], BF16) for i in range(3)]
    pend = []

    def _flush_hmult(item):
        (j, gsb, t_gsb, usb, t_usb, hsb, t_hsb) = item
        p.emit("dve", lambda gsb=gsb, usb=usb, hsb=hsb: nc.vector.tensor_tensor(
            hsb[:], gsb[:], usb[:], OP.mult), reads=[t_usb, t_gsb], writes=[t_hsb])
        p.dma("qact", it_ap[j * 128:(j + 1) * 128, :], hsb[:], reads=[t_hsb],
              writes=[t_it[j]])

    def wd_transposes(wd, t_wd, kchunks, store_fn):
        for g4 in range((kchunks + 3) // 4):
            bi = 4 + (g4 % 2)
            bh, bt = c.banks[bi]
            js = list(range(g4 * 4, min(g4 * 4 + 4, kchunks)))
            for idx, kc in enumerate(js):
                c.transpose(bi, idx * 128, wd[:, kc * 128:(kc + 1) * 128], [t_wd])
            store_fn(js, bh, bt)

    for i in range(n_tiles):
        for gu in range(2):
            it = 2 * i + gu
            pk_src = gpk_ap if gu == 0 else upk_ap
            sc_src = gsc_ap if gu == 0 else usc_ap
            pkh, t_pk = pks[it % 2]
            sch, t_sch = scs[it % 2]
            ds = dec[it % 3]
            wtile, t_wtile = wtiles[it % 2]
            p.dma("qsp", sch[:], sc_src[i * 128:(i + 1) * 128, :], writes=[t_sch])
            p.dma("qsp", pkh[:], pk_src[i * 128:(i + 1) * 128, :], writes=[t_pk])
            decode_fp4_l1(c, pkh[:], (sch, t_sch), ds, 1024, [t_pk])
            wd, t_wd = ds["wd"]

            def store(js, bh, bt, wtile=wtile, t_wtile=t_wtile):
                w0, n_ = js[0], len(js)
                p.emit("act", lambda w0=w0, n_=n_, bh=bh: nc.scalar.copy(
                    wtile[:, w0 * 128:(w0 + n_) * 128], bh[:, 0:n_ * 128]),
                    reads=[bt], writes=[t_wtile])
            wd_transposes(wd, t_wd, 16, store)
            for kc in range(NC):
                for tc in range(4):
                    c.mm(tc, (0, 512), wtile[:, kc * 128:(kc + 1) * 128],
                         xnt[:, kc, tc * 512:(tc + 1) * 512], [t_wtile, t_xnt],
                         start=(kc == 0), stop=(kc == NC - 1))
            gsb, t_gsb = gsbs[i % 3]
            usb, t_usb = usbs[i % 3]
            hsb, t_hsb = hsbs[i % 3]
            if gu == 0:
                for tc in range(4):
                    bh, bt = c.banks[tc]
                    p.emit("act", lambda bh=bh, gsb=gsb, tc=tc: nc.scalar.activation(
                        gsb[:, tc * 512:(tc + 1) * 512], bh[:], AF.Gelu_apprx_tanh,
                        bias=zz[:], scale=one[:]),
                        reads=[bt, zzt, onet], writes=[t_gsb])
            else:
                for tc in range(4):
                    bh, bt = c.banks[tc]
                    p.emit("act", lambda bh=bh, usb=usb, tc=tc: nc.scalar.copy(
                        usb[:, tc * 512:(tc + 1) * 512], bh[:]), reads=[bt], writes=[t_usb])
                pend.append((i, gsb, t_gsb, usb, t_usb, hsb, t_hsb))
                if len(pend) >= 3:
                    _flush_hmult(pend.pop(0))
    while pend:
        _flush_hmult(pend.pop(0))
    p.pop_scope()
    if phases == "gu":
        return

    p.push_scope()
    dec2 = decode_bufs(p, f"{pfx}e", 256, 3)
    pks2 = [p.sb(f"{pfx}qk{i}", [128, 1024], I32) for i in range(2)]
    scs2 = [p.sb(f"{pfx}sd{i}", [128, 512], F32) for i in range(2)]
    wdt, _twdt_unused = p.sb(f"{pfx}wdt", [128, 128, 256], BF16)
    t_wdtq = [p.tok(f"{pfx}wdtq{i}") for i in range(4)]
    hch = [p.sb(f"{pfx}hch{i}", [128, 2048], BF16) for i in range(4)]
    ytev, t_ytev = p.sb(f"{pfx}ytev", [128, 4096], BF16)
    ypsb, t_ypsb = p.sb(f"{pfx}ypsb", [128, 512], BF16)

    def emit_dec(ps, q):
        def dve_T(item):
            wd, t_wd, sl8, sub2 = item
            wdv = wd[:].rearrange("p (kc x) -> p kc x", kc=16)
            dst = wdt[:, sl8 * 16:(sl8 + 1) * 16, sub2 * 128:(sub2 + 1) * 128]
            for i in range(4):
                for j in range(4):
                    p.emit("dve", lambda i=i, j=j, wdv=wdv, dst=dst: nc.vector.transpose(
                        dst[32 * i:32 * (i + 1), :, 32 * j:32 * (j + 1)],
                        wdv[32 * j:32 * (j + 1), :, 32 * i:32 * (i + 1)]),
                        reads=[t_wd], writes=[t_wdtq[q]])

        def pe_T(item):
            wd, t_wd, sl8, sub2 = item

            def store(js, bh, bt, sl8=sl8, sub2=sub2):
                ic0, n_ = sl8 * 16 + js[0], len(js)
                p.emit("act", lambda ic0=ic0, n_=n_, bh=bh, sub2=sub2: nc.scalar.copy(
                    wdt[:, ic0:ic0 + n_, sub2 * 128:(sub2 + 1) * 128],
                    bh[:, 0:n_ * 128].rearrange("p (a x) -> p a x", a=n_)),
                    reads=[bt], writes=[t_wdtq[q]])
            wd_transposes(wd, t_wd, 16, store)

        dpend = []
        flush_at = 2 if q < 3 else 3
        do_T = dve_T if q < 3 else pe_T
        for sub2 in range(2):
            hrow = ps * 2 + sub2
            sch, t_sch = scs2[sub2 % 2]
            pkq = "qsp" if ps == 0 else "qpool"
            if q == 0:
                p.dma(pkq, sch[:], dsc_ap[hrow * 128:(hrow + 1) * 128, :], writes=[t_sch])
            for sl8 in range(q * 2, q * 2 + 2):
                gidx = hrow * 8 + sl8
                pkh, t_pk = pks2[gidx % 2]
                ds = dec2[gidx % 3]
                p.dma(pkq, pkh[:],
                      dpk_ap[hrow * 128:(hrow + 1) * 128, sl8 * 1024:(sl8 + 1) * 1024],
                      writes=[t_pk])
                decode_fp4_l1(c, pkh[:], (sch[:, sl8 * 64:(sl8 + 1) * 64], t_sch),
                              ds, 1024, [t_pk])
                wd, t_wd = ds["wd"]
                dpend.append((wd, t_wd, sl8, sub2))
                if len(dpend) >= flush_at:
                    do_T(dpend.pop(0))
        while dpend:
            do_T(dpend.pop(0))

    for q in range(4):
        emit_dec(0, q)
    for ps in range(8):
        for ic in range(128):
            if ic in (32, 64, 96) and ps + 1 < 8:
                emit_dec(ps + 1, ic // 32 - 1)
            hh, t_hh = hch[ic % 4]
            p.dma("qsp" if ic % 2 == 0 else "qact", hh[:], it_ap[ic * 128:(ic + 1) * 128, :],
                  reads=[t_it[ic]], writes=[t_hh])
            for hj in range(2):
                for tc in range(4):
                    c.mm(hj * 4 + tc, (0, 512), wdt[:, ic, hj * 128:(hj + 1) * 128],
                         hh[:, tc * 512:(tc + 1) * 512], [t_wdtq[ic // 32], t_hh],
                         start=(ic == 0), stop=(ic == 127))
        for bi in range(8):
            bh, bt = c.banks[bi]
            p.emit("act", lambda bh=bh, bi=bi: nc.scalar.copy(
                ytev[:, bi * 512:(bi + 1) * 512], bh[:]), reads=[bt], writes=[t_ytev])
        for hj in range(2):
            for q in range(4):
                bi = q % 2
                bh, bt = c.banks[bi]
                for idx in range(4):
                    tt = q * 4 + idx
                    c.transpose(bi, idx * 128,
                                ytev[:, hj * 2048 + tt * 128: hj * 2048 + (tt + 1) * 128],
                                [t_ytev])
                tt0 = q * 4
                p.emit("act", lambda bh=bh: nc.scalar.copy(ypsb[:], bh[:]),
                       reads=[bt], writes=[t_ypsb])
                dst = y_ap[tt0 * 128:(tt0 + 4) * 128, ps * 256 + hj * 128: ps * 256 + (hj + 1) * 128]
                dst = dst.rearrange("(a p) h -> p a h", a=4)
                p.dma("qact", dst, ypsb[:].rearrange("p (a h) -> p a h", a=4), reads=[t_ypsb],
                      writes=[t_y[tt0 + _i] for _i in range(4)])
        if ps + 1 < 8:
            emit_dec(ps + 1, 3)
    p.pop_scope()
    p.push_scope()
    gout, t_gout = p.sb(f"{pfx}gout", [128, E], BF16)
    yts = [p.sb(f"{pfx}yt{i}", [128, E], BF16) for i in range(2)]
    for tt in range(NT):
        yth, t_yth = yts[tt % 2]
        p.dma("qsp", yth[:], y_ap[tt * 128:(tt + 1) * 128, :], reads=[t_y[tt]], writes=[t_yth])
        p.emit("act", lambda yth=yth: nc.scalar.activation(
            gout[:], yth[:], AF.Silu, bias=zz[:], scale=one[:]),
            reads=[t_yth, t_st2 if False else zzt, onet], writes=[t_gout])
        p.dma("qact", g_ap[tt * 128:(tt + 1) * 128, :], gout[:], reads=[t_gout])
    p.pop_scope()


def s_l1_attn(c: C, xnt_ap, wp_ap, wo_ap, s_ap, pfx="a1", dbg=None):
    p, nc = c.p, c.nc
    H, D = 16, 128
    sm = 1.0 / np.sqrt(D)
    zz, zzt = c.const(0.0)
    one, onet = c.const(1.0)
    smc, smct = c.const(sm)
    t_s = [p.tok(f"{pfx}s{i}") for i in range(NT)]

    p.push_scope()
    xnt, t_xnt = p.sb(f"{pfx}xnt", [128, NC, T], BF16)
    for cc in range(NC):
        p.dma("qsp", xnt[:, cc, :], xnt_ap[cc * 128:(cc + 1) * 128, :], writes=[t_xnt])
    oT, t_oT = p.sb(f"{pfx}oT", [128, H, T], BF16)
    p.push_scope()
    wrow = [p.sb(f"{pfx}wr{i}", [128, E], BF16) for i in range(2)]
    wqT, t_wqT = p.sb(f"{pfx}wqT", [128, NC, 128], BF16)
    wkT, t_wkT = p.sb(f"{pfx}wkT", [128, NC, 128], BF16)
    wvT, t_wvT = p.sb(f"{pfx}wvT", [128, NC, 128], BF16)
    qT, t_qT = p.sb(f"{pfx}qT", [128, T], BF16)
    kT, t_kT = p.sb(f"{pfx}kT", [128, T], BF16)
    vtm, t_vtm = p.sb(f"{pfx}vtm", [128, NT, 128], BF16)
    vT, t_vT = p.sb(f"{pfx}vT", [128, T], BF16)
    psls = [(p.sb(f"{pfx}psl{i}", [128, T], BF16)[0],
             [p.tok(f"{pfx}pt{i}_{j}") for j in range(4)]) for i in range(3)]
    lsums = [p.sb(f"{pfx}ls{i}", [128, 12], F32) for i in range(3)]
    diags = [p.sb(f"{pfx}dg{i}", [128, 128], BF16) for i in range(3)]
    pT, t_pT = p.sb(f"{pfx}pT", [128, NT, 512], BF16)

    def load_wT(h, dst, t_dst, row0):
        wr, t_wr = wrow[h % 2]
        p.dma("qpool", wr[:], wp_ap[row0:row0 + 128, :], writes=[t_wr])
        for g4 in range(4):
            bi = g4 % 2
            bh, bt = c.banks[bi]
            for j in range(4):
                cc = g4 * 4 + j
                c.transpose(bi, j * 128, wr[:, cc * 128:(cc + 1) * 128], [t_wr])
            p.emit("act", lambda bh=bh, g4=g4, dst=dst: nc.scalar.copy(
                dst[:, g4 * 4:(g4 + 1) * 4, :].rearrange("p a b -> p (a b)"), bh[:]),
                reads=[bt], writes=[t_dst])

    for h in range(H):
        load_wT(h, wqT, t_wqT, h * 128)
        load_wT(h, wkT, t_wkT, 2048 + h * 128)
        load_wT(h, wvT, t_wvT, 4096 + h * 128)
        for tc in range(4):
            for cc in range(NC):
                c.mm(tc, (0, 512), wqT[:, cc, :], xnt[:, cc, tc * 512:(tc + 1) * 512],
                     [t_wqT, t_xnt], start=(cc == 0), stop=(cc == NC - 1))
            bh, bt = c.banks[tc]
            p.emit("act", lambda bh=bh, tc=tc: nc.scalar.activation(
                qT[:, tc * 512:(tc + 1) * 512], bh[:], AF.Copy, bias=0.0, scale=smc[:]),
                reads=[bt, zzt, smct], writes=[t_qT])
        for tc in range(4):
            for cc in range(NC):
                c.mm(tc, (0, 512), wkT[:, cc, :], xnt[:, cc, tc * 512:(tc + 1) * 512],
                     [t_wkT, t_xnt], start=(cc == 0), stop=(cc == NC - 1))
            bh, bt = c.banks[tc]
            p.emit("act", lambda bh=bh, tc=tc: nc.scalar.copy(
                kT[:, tc * 512:(tc + 1) * 512], bh[:]), reads=[bt], writes=[t_kT])
        for tc in range(4):
            for cc in range(NC):
                c.mm(tc, (0, 512), wvT[:, cc, :], xnt[:, cc, tc * 512:(tc + 1) * 512],
                     [t_wvT, t_xnt], start=(cc == 0), stop=(cc == NC - 1))
            bh, bt = c.banks[tc]
            p.emit("act", lambda bh=bh, tc=tc: nc.scalar.copy(
                vT[:, tc * 512:(tc + 1) * 512], bh[:]), reads=[bt], writes=[t_vT])
        for tt0 in range(0, NT, 4):
            bi = (tt0 // 4) % 2
            bh, bt = c.banks[bi]
            for j in range(4):
                c.transpose(bi, j * 128, vT[:, (tt0 + j) * 128:(tt0 + j + 1) * 128], [t_vT])
            p.emit("act", lambda bh=bh, tt0=tt0: nc.scalar.copy(
                vtm[:, tt0:tt0 + 4, :].rearrange("p a b -> p (a b)"), bh[:, 0:512]),
                reads=[bt], writes=[t_vtm])

        def emit_T(item):
            qt, pslh, ptoks, dgh, t_dgh = item
            qi = qt % 4
            for kt0 in range(0, qt + 1, 4):
                ntl = min(4, qt + 1 - kt0)
                bi = (kt0 // 4) % 2
                bh, bt = c.banks[bi]
                for j in range(ntl):
                    kt = kt0 + j
                    p.emit("pe", lambda bh=bh, j=j, kt=kt, pslh=pslh, dgh=dgh:
                        nc.tensor.matmul(bh[0:128, j * 128:(j + 1) * 128],
                                         pslh[:, kt * 128:(kt + 1) * 128], dgh[:],
                                         start=True, stop=True),
                        reads=[ptoks[kt // 4], t_dgh], writes=[bt])
                if (kt0 // 4) % 2 == 0:
                    p.emit("dve", lambda bh=bh, kt0=kt0, ntl=ntl, qi=qi: nc.vector.tensor_copy(
                        pT[:, kt0:kt0 + ntl, qi * 128:(qi + 1) * 128],
                        bh[:, 0:ntl * 128].rearrange("p (a x) -> p a x", a=ntl)),
                        reads=[bt], writes=[t_pT])
                else:
                    p.emit("act", lambda bh=bh, kt0=kt0, ntl=ntl, qi=qi: nc.scalar.copy(
                        pT[:, kt0:kt0 + ntl, qi * 128:(qi + 1) * 128],
                        bh[:, 0:ntl * 128].rearrange("p (a x) -> p a x", a=ntl)),
                        reads=[bt], writes=[t_pT])

        def emit_O(qg, h):
            qmax = qg * 4 + 3
            bo, bo_t = c.banks[2]
            for kt in range(qmax + 1):
                off = max(0, (kt - qg * 4)) * 128 if kt > qg * 4 else 0
                p.emit("pe", lambda kt=kt, off=off, bo=bo, qmax=qmax: nc.tensor.matmul(
                    bo[0:128, off:512], vtm[:, kt, :], pT[:, kt, off:512],
                    start=(kt == 0), stop=(kt == qmax)),
                    reads=[t_vtm, t_pT], writes=[bo_t])
            p.emit("act", lambda h=h, qg=qg, bo=bo: nc.scalar.copy(
                oT[:, h, qg * 512:(qg + 1) * 512], bo[0:128, 0:512]),
                reads=[bo_t], writes=[t_oT])

        pend_T = []
        for qt in range(NT):
            klen = (qt + 1) * 128
            pslh, ptoks = psls[qt % 3]
            dgh, t_dgh = diags[qt % 3]
            lsh, t_lsh = lsums[qt % 3]
            sc_v, sc_toks = c.scores
            nchunk = (klen + 511) // 512
            mck = qt // 4
            for ck in range(nchunk):
                n = min(512, klen - ck * 512)
                tok = [sc_toks[ck]]
                p.emit("pe", lambda ck=ck, n=n, qt=qt: nc.tensor.matmul(
                    sc_v[:, ck * 512:ck * 512 + n], qT[:, qt * 128:(qt + 1) * 128],
                    kT[:, ck * 512:ck * 512 + n], start=True, stop=True),
                    reads=[t_qT, t_kT], writes=tok)
                if ck == mck:
                    p.emit("dve", lambda qt=qt: nc.vector.tensor_tensor(
                        sc_v[:, qt * 128:(qt + 1) * 128], sc_v[:, qt * 128:(qt + 1) * 128],
                        c.maskc_sb[:], OP.add), reads=tok + [c.t_maskc], writes=tok)
                p.emit("dve", lambda ck=ck, n=n, lsh=lsh: nc.vector.tensor_reduce(
                    lsh[:, 4 + ck:5 + ck], sc_v[:, ck * 512:ck * 512 + n], AX.X, OP.max),
                    reads=tok, writes=[t_lsh])
            p.emit("dve", lambda lsh=lsh, nchunk=nchunk: nc.vector.tensor_reduce(
                lsh[:, 0:1], lsh[:, 4:4 + nchunk], AX.X, OP.max, negate=True),
                reads=[t_lsh], writes=[t_lsh])
            if len(pend_T) >= 2:
                emit_T(pend_T.pop(0))
            for ck in range(nchunk):
                n = min(512, klen - ck * 512)
                tok = [sc_toks[ck]]
                p.emit("act", lambda ck=ck, n=n, pslh=pslh, lsh=lsh: nc.scalar.activation(
                    pslh[:, ck * 512:ck * 512 + n], sc_v[:, ck * 512:ck * 512 + n],
                    AF.Exp, bias=lsh[:, 0:1], scale=one[:], accum_out=lsh[:, 8 + ck:9 + ck]),
                    reads=tok + [t_lsh, onet], writes=[ptoks[ck], t_lsh])
            p.emit("dve", lambda lsh=lsh, nchunk=nchunk: nc.vector.tensor_reduce(
                lsh[:, 1:2], lsh[:, 8:8 + nchunk], AX.X, OP.add), reads=[t_lsh], writes=[t_lsh])
            p.emit("dve", lambda lsh=lsh: nc.vector.reciprocal(lsh[:, 2:3], lsh[:, 1:2]),
                   reads=[t_lsh], writes=[t_lsh])
            p.emit("dve", lambda dgh=dgh, lsh=lsh: nc.vector.tensor_tensor(
                dgh[:], c.ident_sb[:], lsh[:, 2:3].broadcast_to([128, 128]), OP.mult),
                reads=[c.t_ident, t_lsh], writes=[t_dgh])
            pend_T.append((qt, pslh, ptoks, dgh, t_dgh))
            if qt % 4 == 1 and qt >= 5:
                emit_O((qt - 5) // 4, h)
        while pend_T:
            emit_T(pend_T.pop(0))
        emit_O(3, h)
    if dbg is not None:
        p.dma("qsp", dbg["qT"][:, :], qT[:], reads=[t_qT])
        p.dma("qsp", dbg["kT"][:, :], kT[:], reads=[t_kT])
        p.dma("qsp", dbg["v"][:, :], vtm[:].rearrange("p a b -> p (a b)"), reads=[t_vtm])
        p.dma("qsp", dbg["oT"][:, :], oT[:, H - 1, :], reads=[t_oT])
    p.pop_scope()
    wosb = [p.sb(f"{pfx}wo{i}", [128, E], BF16) for i in range(2)]
    woT, t_woT = p.sb(f"{pfx}woT", [128, NC, 512], BF16)
    souts = [p.sb(f"{pfx}so{i}", [128, 512], BF16) for i in range(2)]
    for ec in range(4):
        for ot4 in range(4):
            wr, t_wr = wosb[ot4 % 2]
            p.dma("qpool", wr[:], wo_ap[ec * 512 + ot4 * 128: ec * 512 + (ot4 + 1) * 128, :],
                  writes=[t_wr])
            for g4 in range(4):
                bi = g4 % 2
                bh, bt = c.banks[bi]
                for j in range(4):
                    cc = g4 * 4 + j
                    c.transpose(bi, j * 128, wr[:, cc * 128:(cc + 1) * 128], [t_wr])
                for j in range(4):
                    cc = g4 * 4 + j
                    p.emit("act", lambda bh=bh, j=j, cc=cc, ot4=ot4: nc.scalar.copy(
                        woT[:, cc, ot4 * 128:(ot4 + 1) * 128], bh[:, j * 128:(j + 1) * 128]),
                        reads=[bt], writes=[t_woT])
        for tt in range(NT):
            bi = 4 + tt % 4
            sout, t_sout = souts[tt % 2]
            for hd in range(NC):
                c.mm(bi, (0, 512), oT[:, hd, tt * 128:(tt + 1) * 128], woT[:, hd, :],
                     [t_oT, t_woT], start=(hd == 0), stop=(hd == NC - 1))
            bh, bt = c.banks[bi]
            p.emit("act", lambda bh=bh, sout=sout: nc.scalar.copy(sout[:], bh[:]),
                   reads=[bt], writes=[t_sout])
            p.dma("qact", s_ap[tt * 128:(tt + 1) * 128, ec * 512:(ec + 1) * 512], sout[:],
                  reads=[t_sout], writes=[t_s[tt]])
    p.pop_scope()
    return t_s


def s_l0_attn(c: C, xnt_ap, wqa_ap, qlnw_ap, wqb_ap, wkva_ap, kvlnw_ap, wkvb_ap, wo_ap,
              qn_ap, ot_ap, s_ap, pfx="a0", dbg=None):
    p, nc = c.p, c.nc
    H, DN, DR, DV, R = 16, 128, 64, 128, 512
    sm = 1.0 / np.sqrt(DN + DR)
    zz, zzt = c.const(0.0)
    one, onet = c.const(1.0)
    eb, ebt = c.const(1e-6)
    nh, nht = c.const(-0.5)
    smc, smct = c.const(sm)
    i512, i512t = c.const(1.0 / 512)
    i1536, i1536t = c.const(1.0 / 1536)
    cos_d = nc.dram_tensor("c_cos", [T, 64], F32, kind="ExternalInput").ap()
    sin_d = nc.dram_tensor("c_sin", [T, 64], F32, kind="ExternalInput").ap()
    t_qn = [p.tok(f"{pfx}qn{h}") for h in range(H)]
    t_ot = [p.tok(f"{pfx}ot{h}") for h in range(H)]
    t_s = [p.tok(f"{pfx}s{i}") for i in range(NT)]

    p.push_scope()
    coss, t_cos = p.sb(f"{pfx}cos", [128, NT, 64], F32)
    sins, t_sin = p.sb(f"{pfx}sin", [128, NT, 64], F32)
    for tt in range(NT):
        p.dma("qsp", coss[:, tt, :], cos_d[tt * 128:(tt + 1) * 128, :], writes=[t_cos])
        p.dma("qsp", sins[:, tt, :], sin_d[tt * 128:(tt + 1) * 128, :], writes=[t_sin])
    ckvT, t_ckvT = p.sb(f"{pfx}ckvT", [128, 4, T], BF16)
    kpeT, t_kpeT = p.sb(f"{pfx}kpeT", [128, T], BF16)
    qpeT, t_qpeT = p.sb(f"{pfx}qpeT", [128, 8, T], BF16)

    p.push_scope()
    qanT, t_qanT = p.sb(f"{pfx}qanT", [128, 12, T], BF16)
    p.push_scope()
    xsl = [p.sb(f"{pfx}xsl{i}", [128, NC, 128], BF16) for i in range(2)]
    wkva, t_wkva = p.sb(f"{pfx}wkva", [128, NC, 576], BF16)
    for cc in range(NC):
        p.dma("qpool", wkva[:, cc, :], wkva_ap[cc * 128:(cc + 1) * 128, :], writes=[t_wkva])
    wqa, t_wqa = p.sb(f"{pfx}wqa", [128, NC, 1536], BF16)
    for cc in range(NC):
        p.dma("qpool", wqa[:, cc, :], wqa_ap[cc * 128:(cc + 1) * 128, :], writes=[t_wqa])
    cf32, t_cf32 = p.sb(f"{pfx}cf32", [128, 512], F32)
    qaf, t_qaf = p.sb(f"{pfx}qaf", [128, 1536], F32)
    sqs, t_sqs = p.sb(f"{pfx}sqs", [128, 1536], F32)
    st, t_st = p.sb(f"{pfx}st", [128, 8], F32)
    ckn, t_ckn = p.sb(f"{pfx}ckn", [128, 512], BF16)
    qan, t_qan = p.sb(f"{pfx}qan", [128, 1536], BF16)
    kpe, t_kpe = p.sb(f"{pfx}kpe", [128, 64], F32)
    kper, t_kper = p.sb(f"{pfx}kper", [128, 64], BF16)
    rr1, t_rr1 = p.sb(f"{pfx}rr1", [128, 64], F32)
    rr2, t_rr2 = p.sb(f"{pfx}rr2", [128, 64], F32)
    for tt in range(NT):
        xh, t_xh = xsl[tt % 2]
        src = xnt_ap[:, tt * 128:(tt + 1) * 128].rearrange("(c p) k -> p c k", p=128)
        p.dma("qsp", xh[:], src, writes=[t_xh])
        for cc in range(NC):
            c.mm(0, (0, 512), xh[:, cc, :], wkva[:, cc, 0:512], [t_xh, t_wkva],
                 start=(cc == 0), stop=(cc == NC - 1))
        for cc in range(NC):
            c.mm(1, (0, 64), xh[:, cc, :], wkva[:, cc, 512:576], [t_xh, t_wkva],
                 start=(cc == 0), stop=(cc == NC - 1))
        bh0, bt0 = c.banks[0]
        p.emit("act", lambda bh0=bh0: nc.scalar.copy(cf32[:], bh0[:]), reads=[bt0], writes=[t_cf32])
        p.emit("dve", lambda: nc.vector.tensor_tensor(sqs[:, 0:512], cf32[:], cf32[:], OP.mult),
               reads=[t_cf32], writes=[t_sqs])
        p.emit("dve", lambda: nc.vector.tensor_reduce(st[:, 0:1], sqs[:, 0:512], AX.X, OP.add),
               reads=[t_sqs], writes=[t_st])
        p.emit("act", lambda: nc.scalar.activation(st[:, 1:2], st[:, 0:1], AF.Ln, bias=eb[:], scale=i512[:]),
               reads=[t_st, ebt, i512t], writes=[t_st])
        p.emit("act", lambda: nc.scalar.activation(st[:, 2:3], st[:, 1:2], AF.Exp, bias=zz[:], scale=nh[:]),
               reads=[t_st, nht, zzt], writes=[t_st])
        p.emit("dve", lambda: nc.vector.tensor_scalar(ckn[:], cf32[:], st[:, 2:3], None, OP.mult),
               reads=[t_cf32, t_st], writes=[t_ckn])
        for rc in range(4):
            c.transpose(2, rc * 128, ckn[:, rc * 128:(rc + 1) * 128], [t_ckn])
        bh2, bt2 = c.banks[2]
        for rc in range(4):
            p.emit("act", lambda bh2=bh2, tt=tt, rc=rc: nc.scalar.copy(
                ckvT[:, rc, tt * 128:(tt + 1) * 128], bh2[:, rc * 128:(rc + 1) * 128]),
                reads=[bt2], writes=[t_ckvT])
        bh1, bt1 = c.banks[1]
        p.emit("act", lambda bh1=bh1: nc.scalar.copy(kpe[:], bh1[0:128, 0:64]), reads=[bt1], writes=[t_kpe])
        p.emit("dve", lambda tt=tt: nc.vector.tensor_tensor(
            rr1[:, 0:32], kpe[:, 0:32], coss[:, tt, 0:32], OP.mult), reads=[t_kpe, t_cos], writes=[t_rr1])
        p.emit("dve", lambda tt=tt: nc.vector.tensor_tensor(
            rr2[:, 0:32], kpe[:, 32:64], sins[:, tt, 0:32], OP.mult), reads=[t_kpe, t_sin], writes=[t_rr2])
        p.emit("dve", lambda: nc.vector.tensor_tensor(
            kper[:, 0:32], rr1[:, 0:32], rr2[:, 0:32], OP.subtract), reads=[t_rr1, t_rr2], writes=[t_kper])
        p.emit("dve", lambda tt=tt: nc.vector.tensor_tensor(
            rr1[:, 0:32], kpe[:, 32:64], coss[:, tt, 32:64], OP.mult), reads=[t_kpe, t_cos], writes=[t_rr1])
        p.emit("dve", lambda tt=tt: nc.vector.tensor_tensor(
            rr2[:, 0:32], kpe[:, 0:32], sins[:, tt, 32:64], OP.mult), reads=[t_kpe, t_sin], writes=[t_rr2])
        p.emit("dve", lambda: nc.vector.tensor_tensor(
            kper[:, 32:64], rr1[:, 0:32], rr2[:, 0:32], OP.add), reads=[t_rr1, t_rr2], writes=[t_kper])
        c.transpose(3, 0, kper[:], [t_kper])
        bh3, bt3 = c.banks[3]
        p.emit("act", lambda bh3=bh3, tt=tt: nc.scalar.copy(
            kpeT[0:64, tt * 128:(tt + 1) * 128], bh3[0:64, 0:128]), reads=[bt3], writes=[t_kpeT])
        for oc in range(3):
            for cc in range(NC):
                c.mm(5 + oc, (0, 512), xh[:, cc, :], wqa[:, cc, oc * 512:(oc + 1) * 512],
                     [t_xh, t_wqa], start=(cc == 0), stop=(cc == NC - 1))
            bho, bto = c.banks[5 + oc]
            p.emit("act", lambda bho=bho, oc=oc: nc.scalar.copy(
                qaf[:, oc * 512:(oc + 1) * 512], bho[:]), reads=[bto], writes=[t_qaf])
        p.emit("dve", lambda: nc.vector.tensor_tensor(sqs[:], qaf[:], qaf[:], OP.mult),
               reads=[t_qaf], writes=[t_sqs])
        p.emit("dve", lambda: nc.vector.tensor_reduce(st[:, 4:5], sqs[:], AX.X, OP.add),
               reads=[t_sqs], writes=[t_st])
        p.emit("act", lambda: nc.scalar.activation(st[:, 5:6], st[:, 4:5], AF.Ln, bias=eb[:], scale=i1536[:]),
               reads=[t_st, ebt, i1536t], writes=[t_st])
        p.emit("act", lambda: nc.scalar.activation(st[:, 6:7], st[:, 5:6], AF.Exp, bias=zz[:], scale=nh[:]),
               reads=[t_st, nht, zzt], writes=[t_st])
        p.emit("dve", lambda: nc.vector.tensor_scalar(qan[:], qaf[:], st[:, 6:7], None, OP.mult),
               reads=[t_qaf, t_st], writes=[t_qan])
        for g4 in range(3):
            bh4, bt4 = c.banks[4]
            for j in range(4):
                cc = g4 * 4 + j
                c.transpose(4, j * 128, qan[:, cc * 128:(cc + 1) * 128], [t_qan])
            for j in range(4):
                cc = g4 * 4 + j
                p.emit("act", lambda bh4=bh4, j=j, cc=cc, tt=tt: nc.scalar.copy(
                    qanT[:, cc, tt * 128:(tt + 1) * 128], bh4[:, j * 128:(j + 1) * 128]),
                    reads=[bt4], writes=[t_qanT])
    if dbg is not None:
        p.dma("qsp", dbg["qanT"][:, :], qanT[:].rearrange("p a b -> p (a b)"), reads=[t_qanT])
        p.dma("qsp", dbg["ckvT"][:, :], ckvT[:].rearrange("p a b -> p (a b)"), reads=[t_ckvT])
        p.dma("qsp", dbg["kpeT"][:, :], kpeT[:], reads=[t_kpeT])
    p.pop_scope()

    p.global_barrier()
    p.push_scope()
    lnq, t_lnq = p.sb(f"{pfx}lnq", [128, 12], F32)
    p.dma("qsp", lnq[:], qlnw_ap.rearrange("(cq pq) -> pq cq", pq=128), writes=[t_lnq],
          allow_slow_non_contiguous=True)
    wqbn, t_wqbn = p.sb(f"{pfx}wqbn", [128, 12, 2048], BF16)
    for cq in range(12):
        srcq = wqb_ap[cq * 128:(cq + 1) * 128, :].rearrange("pq (hh dd) -> pq hh dd", hh=16)[:, :, 0:128]
        p.dma("qpool", wqbn[:, cq, :].rearrange("pq (hh dd) -> pq hh dd", hh=16), srcq,
              writes=[t_wqbn])
    for cc in range(12):
        p.emit("dve", lambda cc=cc: nc.vector.tensor_scalar(
            wqbn[:, cc, :], wqbn[:, cc, :], lnq[:, cc:cc + 1], None, OP.mult),
            reads=[t_lnq, t_wqbn], writes=[t_wqbn])
    qnh, t_qnh = p.sb(f"{pfx}qnh", [128, T], BF16)
    if dbg is not None and "wqbn" in dbg:
        p.dma("qsp", dbg["wqbn"][:, :], wqbn[:].rearrange("p a b -> p (a b)"), reads=[t_wqbn])
    for h in range(H):
        for tc in range(T // 512):
            for cc in range(12):
                c.mm(tc, (0, 512), wqbn[:, cc, h * 128:(h + 1) * 128],
                     qanT[:, cc, tc * 512:(tc + 1) * 512], [t_wqbn, t_qanT],
                     start=(cc == 0), stop=(cc == 11))
            bh, bt = c.banks[tc]
            p.emit("act", lambda bh=bh, tc=tc, qnh=qnh: nc.scalar.activation(
                qnh[:, tc * 512:(tc + 1) * 512], bh[:], AF.Copy, bias=0.0, scale=smc[:]),
                reads=[bt, zzt, smct], writes=[t_qnh])
        if dbg is not None and "qnh0" in dbg and h == 0:
            p.dma("qsp", dbg["qnh0"][:, :], qnh[:], reads=[t_qnh])
        p.dma("qact", qn_ap[h * 128:(h + 1) * 128, :], qnh[:], reads=[t_qnh], writes=[t_qn[h]])
    p.pop_scope()
    if dbg is not None and dbg.get("stop_after_b3"):
        p.pop_scope()
        p.pop_scope()
        return t_s

    p.push_scope()
    wqbp, t_wqbp = p.sb(f"{pfx}wqbp", [128, 12, 1024], BF16)
    lnq2, t_lnq2 = p.sb(f"{pfx}lnq2", [128, 12], F32)
    p.dma("qsp", lnq2[:], qlnw_ap.rearrange("(cq pq) -> pq cq", pq=128), writes=[t_lnq2],
          allow_slow_non_contiguous=True)
    for cq in range(12):
        srcq = wqb_ap[cq * 128:(cq + 1) * 128, :].rearrange("pq (hh dd) -> pq hh dd", hh=16)[:, :, 128:192]
        p.dma("qpool", wqbp[:, cq, :].rearrange("pq (hh dd) -> pq hh dd", hh=16), srcq,
              writes=[t_wqbp])
    for cc in range(12):
        p.emit("dve", lambda cc=cc: nc.vector.tensor_scalar(
            wqbp[:, cc, :], wqbp[:, cc, :], lnq2[:, cc:cc + 1], None, OP.mult),
            reads=[t_lnq2, t_wqbp], writes=[t_wqbp])
    qpf, t_qpf = p.sb(f"{pfx}qpf", [128, 1024], BF16)
    qpr, t_qpr = p.sb(f"{pfx}qpr", [128, 1024], BF16)
    qr1, t_qr1 = p.sb(f"{pfx}qr1", [128, 512], F32)
    qr2, t_qr2 = p.sb(f"{pfx}qr2", [128, 512], F32)
    for tt in range(NT):
        for b2 in range(2):
            for cc in range(12):
                c.mm(b2, (0, 512), qanT[:, cc, tt * 128:(tt + 1) * 128],
                     wqbp[:, cc, b2 * 512:(b2 + 1) * 512], [t_qanT, t_wqbp],
                     start=(cc == 0), stop=(cc == 11))
            bh, bt = c.banks[b2]
            p.emit("act", lambda bh=bh, b2=b2: nc.scalar.activation(
                qpf[:, b2 * 512:(b2 + 1) * 512], bh[:], AF.Copy, bias=0.0, scale=smc[:]),
                reads=[bt, zzt, smct], writes=[t_qpf])
        qv = qpf[:].rearrange("p (hh d) -> p hh d", hh=16)
        qo = qpr[:].rearrange("p (hh d) -> p hh d", hh=16)
        cv = coss[:, tt, :].unsqueeze(1).broadcast_to([128, 16, 64])
        sv = sins[:, tt, :].unsqueeze(1).broadcast_to([128, 16, 64])
        r1v = qr1[:].rearrange("p (hh d) -> p hh d", hh=16)
        r2v = qr2[:].rearrange("p (hh d) -> p hh d", hh=16)
        p.emit("dve", lambda r1v=r1v, qv=qv, cv=cv: nc.vector.tensor_tensor(
            r1v[:, :, 0:32], qv[:, :, 0:32], cv[:, :, 0:32], OP.mult), reads=[t_qpf, t_cos], writes=[t_qr1])
        p.emit("dve", lambda r2v=r2v, qv=qv, sv=sv: nc.vector.tensor_tensor(
            r2v[:, :, 0:32], qv[:, :, 32:64], sv[:, :, 0:32], OP.mult), reads=[t_qpf, t_sin], writes=[t_qr2])
        p.emit("dve", lambda qo=qo, r1v=r1v, r2v=r2v: nc.vector.tensor_tensor(
            qo[:, :, 0:32], r1v[:, :, 0:32], r2v[:, :, 0:32], OP.subtract),
            reads=[t_qr1, t_qr2], writes=[t_qpr])
        p.emit("dve", lambda r1v=r1v, qv=qv, cv=cv: nc.vector.tensor_tensor(
            r1v[:, :, 0:32], qv[:, :, 32:64], cv[:, :, 32:64], OP.mult), reads=[t_qpf, t_cos], writes=[t_qr1])
        p.emit("dve", lambda r2v=r2v, qv=qv, sv=sv: nc.vector.tensor_tensor(
            r2v[:, :, 0:32], qv[:, :, 0:32], sv[:, :, 32:64], OP.mult), reads=[t_qpf, t_sin], writes=[t_qr2])
        p.emit("dve", lambda qo=qo, r1v=r1v, r2v=r2v: nc.vector.tensor_tensor(
            qo[:, :, 32:64], r1v[:, :, 0:32], r2v[:, :, 0:32], OP.add),
            reads=[t_qr1, t_qr2], writes=[t_qpr])
        for h in range(H):
            bi = 2 + h % 2
            bh, bt = c.banks[bi]
            c.transpose(bi, 0, qpr[:, h * 64:(h + 1) * 64], [t_qpr])
            b0 = (h % 2) * 64
            p.emit("act", lambda bh=bh, h=h, tt=tt, b0=b0: nc.scalar.copy(
                qpeT[b0:b0 + 64, h // 2, tt * 128:(tt + 1) * 128],
                bh[0:64, 0:128]), reads=[bt], writes=[t_qpeT])
    p.pop_scope()
    p.pop_scope()
    if dbg is not None and dbg.get("stop_after_b2"):
        p.pop_scope()
        return t_s

    p.push_scope()
    lnkv, t_lnkv = p.sb(f"{pfx}lnkv", [128, 4], F32)
    p.dma("qsp", lnkv[:], kvlnw_ap.rearrange("(cq pq) -> pq cq", pq=128), writes=[t_lnkv],
          allow_slow_non_contiguous=True)
    wkvb, t_wkvb = p.sb(f"{pfx}wkvb", [128, 4, 4096], BF16)
    for rc in range(4):
        p.dma("qpool", wkvb[:, rc, :], wkvb_ap[rc * 128:(rc + 1) * 128, :], writes=[t_wkvb])
    for rc in range(4):
        p.emit("dve", lambda rc=rc: nc.vector.tensor_scalar(
            wkvb[:, rc, :], wkvb[:, rc, :], lnkv[:, rc:rc + 1], None, OP.mult),
            reads=[t_lnkv, t_wkvb], writes=[t_wkvb])
    kabsT, t_kabsT = p.sb(f"{pfx}kabsT", [128, T], BF16)
    vT, t_vT = p.sb(f"{pfx}vT", [128, T], BF16)
    vtm, t_vtm = p.sb(f"{pfx}vtm", [128, NT, 128], BF16)
    qnhs = [p.sb(f"{pfx}qnh{i}", [128, T], BF16) for i in range(2)]
    psls = [(p.sb(f"{pfx}psl{i}", [128, T], BF16)[0],
             [p.tok(f"{pfx}pt{i}_{j}") for j in range(4)]) for i in range(3)]
    lsums = [p.sb(f"{pfx}ls{i}", [128, 12], F32) for i in range(3)]
    diags = [p.sb(f"{pfx}dg{i}", [128, 128], BF16) for i in range(3)]
    pT, t_pT = p.sb(f"{pfx}pT", [128, NT, 512], BF16)
    oThs = [p.sb(f"{pfx}oTh{i}", [128, 512], BF16) for i in range(2)]
    qpes, t_qpes = p.sb(f"{pfx}qpes", [128, T], BF16)
    for h in range(H):
        qnhd, t_qnhd = qnhs[h % 2]
        p.dma("qsp", qnhd[:], qn_ap[h * 128:(h + 1) * 128, :], reads=[t_qn[h]], writes=[t_qnhd])
        b0h = (h % 2) * 64
        if dbg is None or not dbg.get("no_qpes"):
            p.dma("qsp", qpes[0:64, :], qpeT[b0h:b0h + 64, h // 2, :], reads=[t_qpeT], writes=[t_qpes])
        for tc in range(T // 512):
            for rc in range(4):
                c.mm(tc, (0, 512), wkvb[:, rc, h * 256:h * 256 + 128],
                     ckvT[:, rc, tc * 512:(tc + 1) * 512], [t_wkvb, t_ckvT],
                     start=(rc == 0), stop=(rc == 3))
            bh, bt = c.banks[tc]
            p.emit("act", lambda bh=bh, tc=tc: nc.scalar.copy(
                kabsT[:, tc * 512:(tc + 1) * 512], bh[:]), reads=[bt], writes=[t_kabsT])
        for tc in range(T // 512):
            for rc in range(4):
                c.mm(tc, (0, 512), wkvb[:, rc, h * 256 + 128:h * 256 + 256],
                     ckvT[:, rc, tc * 512:(tc + 1) * 512], [t_wkvb, t_ckvT],
                     start=(rc == 0), stop=(rc == 3))
            bh, bt = c.banks[tc]
            p.emit("act", lambda bh=bh, tc=tc: nc.scalar.copy(
                vT[:, tc * 512:(tc + 1) * 512], bh[:]), reads=[bt], writes=[t_vT])
        for tt0 in range(0, NT, 4):
            bi = (tt0 // 4) % 2
            bh, bt = c.banks[bi]
            for j in range(4):
                c.transpose(bi, j * 128, vT[:, (tt0 + j) * 128:(tt0 + j + 1) * 128], [t_vT])
            p.emit("act", lambda bh=bh, tt0=tt0: nc.scalar.copy(
                vtm[:, tt0:tt0 + 4, :].rearrange("p a b -> p (a b)"), bh[:, 0:512]),
                reads=[bt], writes=[t_vtm])

        def emit_T(item):
            qt, pslh, ptoks, dgh, t_dgh = item
            qi = qt % 4
            for kt0 in range(0, qt + 1, 4):
                ntl = min(4, qt + 1 - kt0)
                bi = (kt0 // 4) % 2
                bh, bt = c.banks[bi]
                for j in range(ntl):
                    kt = kt0 + j
                    p.emit("pe", lambda bh=bh, j=j, kt=kt, pslh=pslh, dgh=dgh:
                        nc.tensor.matmul(bh[0:128, j * 128:(j + 1) * 128],
                                         pslh[:, kt * 128:(kt + 1) * 128], dgh[:],
                                         start=True, stop=True),
                        reads=[ptoks[kt // 4], t_dgh], writes=[bt])
                if (kt0 // 4) % 2 == 0:
                    p.emit("dve", lambda bh=bh, kt0=kt0, ntl=ntl, qi=qi: nc.vector.tensor_copy(
                        pT[:, kt0:kt0 + ntl, qi * 128:(qi + 1) * 128],
                        bh[:, 0:ntl * 128].rearrange("p (a x) -> p a x", a=ntl)),
                        reads=[bt], writes=[t_pT])
                else:
                    p.emit("act", lambda bh=bh, kt0=kt0, ntl=ntl, qi=qi: nc.scalar.copy(
                        pT[:, kt0:kt0 + ntl, qi * 128:(qi + 1) * 128],
                        bh[:, 0:ntl * 128].rearrange("p (a x) -> p a x", a=ntl)),
                        reads=[bt], writes=[t_pT])

        def emit_O(qg, h):
            qmax = qg * 4 + 3
            bo, bo_t = c.banks[2]
            oTh, t_oTh = oThs[qg % 2]
            for kt in range(qmax + 1):
                off = max(0, kt - qg * 4) * 128
                p.emit("pe", lambda kt=kt, off=off, bo=bo, qmax=qmax: nc.tensor.matmul(
                    bo[0:128, off:512], vtm[:, kt, :], pT[:, kt, off:512],
                    start=(kt == 0), stop=(kt == qmax)), reads=[t_vtm, t_pT], writes=[bo_t])
            p.emit("act", lambda bo=bo, oTh=oTh: nc.scalar.copy(oTh[:], bo[0:128, 0:512]),
                   reads=[bo_t], writes=[t_oTh])
            if dbg is None or not dbg.get("no_ot_write"):
                p.dma("qact", ot_ap[h * 128:(h + 1) * 128, qg * 512:(qg + 1) * 512], oTh[:],
                      reads=[t_oTh], writes=[t_ot[h]])

        pend_T = []
        for qt in range(NT):
            klen = (qt + 1) * 128
            pslh, ptoks = psls[qt % 3]
            dgh, t_dgh = diags[qt % 3]
            lsh, t_lsh = lsums[qt % 3]
            sc_v, sc_toks = c.scores
            nchunk = (klen + 511) // 512
            mck = qt // 4
            for ck in range(nchunk):
                n = min(512, klen - ck * 512)
                tok = [sc_toks[ck]]
                p.emit("pe", lambda ck=ck, n=n, qt=qt, qnhd=qnhd: nc.tensor.matmul(
                    sc_v[:, ck * 512:ck * 512 + n], qnhd[:, qt * 128:(qt + 1) * 128],
                    kabsT[:, ck * 512:ck * 512 + n], start=True, stop=False),
                    reads=[t_qnhd, t_kabsT], writes=tok)
                p.emit("pe", lambda ck=ck, n=n, qt=qt: nc.tensor.matmul(
                    sc_v[:, ck * 512:ck * 512 + n],
                    qpes[0:64, qt * 128:(qt + 1) * 128],
                    kpeT[0:64, ck * 512:ck * 512 + n], start=False, stop=True),
                    reads=[t_qpes, t_kpeT], writes=tok)
                if ck == mck:
                    p.emit("dve", lambda qt=qt: nc.vector.tensor_tensor(
                        sc_v[:, qt * 128:(qt + 1) * 128], sc_v[:, qt * 128:(qt + 1) * 128],
                        c.maskc_sb[:], OP.add), reads=tok + [c.t_maskc], writes=tok)
                p.emit("dve", lambda ck=ck, n=n, lsh=lsh: nc.vector.tensor_reduce(
                    lsh[:, 4 + ck:5 + ck], sc_v[:, ck * 512:ck * 512 + n], AX.X, OP.max),
                    reads=tok, writes=[t_lsh])
            p.emit("dve", lambda lsh=lsh, nchunk=nchunk: nc.vector.tensor_reduce(
                lsh[:, 0:1], lsh[:, 4:4 + nchunk], AX.X, OP.max, negate=True),
                reads=[t_lsh], writes=[t_lsh])
            if len(pend_T) >= 2:
                emit_T(pend_T.pop(0))
            for ck in range(nchunk):
                n = min(512, klen - ck * 512)
                tok = [sc_toks[ck]]
                p.emit("act", lambda ck=ck, n=n, pslh=pslh, lsh=lsh: nc.scalar.activation(
                    pslh[:, ck * 512:ck * 512 + n], sc_v[:, ck * 512:ck * 512 + n],
                    AF.Exp, bias=lsh[:, 0:1], scale=one[:], accum_out=lsh[:, 8 + ck:9 + ck]),
                    reads=tok + [t_lsh, onet], writes=[ptoks[ck], t_lsh])
            p.emit("dve", lambda lsh=lsh, nchunk=nchunk: nc.vector.tensor_reduce(
                lsh[:, 1:2], lsh[:, 8:8 + nchunk], AX.X, OP.add), reads=[t_lsh], writes=[t_lsh])
            p.emit("dve", lambda lsh=lsh: nc.vector.reciprocal(lsh[:, 2:3], lsh[:, 1:2]),
                   reads=[t_lsh], writes=[t_lsh])
            p.emit("dve", lambda dgh=dgh, lsh=lsh: nc.vector.tensor_tensor(
                dgh[:], c.ident_sb[:], lsh[:, 2:3].broadcast_to([128, 128]), OP.mult),
                reads=[c.t_ident, t_lsh], writes=[t_dgh])
            pend_T.append((qt, pslh, ptoks, dgh, t_dgh))
            if qt % 4 == 1 and qt >= 5:
                emit_O((qt - 5) // 4, h)
        while pend_T:
            emit_T(pend_T.pop(0))
        emit_O(3, h)
    p.pop_scope()
    if dbg is not None and dbg.get("stop_after_d"):
        p.pop_scope()
        return t_s

    p.push_scope()
    wos = [p.sb(f"{pfx}wo{i}", [128, NC, 512], BF16) for i in range(1)]
    oslab = [p.sb(f"{pfx}osl{i}", [128, NC, 128], BF16) for i in range(2)]
    sout, t_sout = p.sb(f"{pfx}so", [128, 512], BF16)
    wo_h, t_wo = wos[0]
    for ec in range(4):
        for cc in range(NC):
            p.dma("qpool", wo_h[:, cc, :], wo_ap[cc * 128:(cc + 1) * 128, ec * 512:(ec + 1) * 512],
                  writes=[t_wo])
        for tt in range(NT):
            osl, t_osl = oslab[tt % 2]
            if ec == 0 or True:
                srco = ot_ap[:, tt * 128:(tt + 1) * 128].rearrange("(hh pp) k -> pp hh k", pp=128)
                p.dma("qsp", osl[:], srco, reads=t_ot, writes=[t_osl])
            bi = 4 + tt % 4
            for hd in range(NC):
                c.mm(bi, (0, 512), osl[:, hd, :], wo_h[:, hd, :], [t_osl, t_wo],
                     start=(hd == 0), stop=(hd == NC - 1))
            bh, bt = c.banks[bi]
            p.emit("act", lambda bh=bh: nc.scalar.copy(sout[:], bh[:]), reads=[bt], writes=[t_sout])
            p.dma("qact", s_ap[tt * 128:(tt + 1) * 128, ec * 512:(ec + 1) * 512], sout[:],
                  reads=[t_sout], writes=[t_s[tt]])
    p.pop_scope()
    p.pop_scope()
    return t_s


def s_combine(c: C, x_ap, s_ap, g_ap, xout_ap, pfx, out_f32=True):
    p, nc = c.p, c.nc
    p.push_scope()
    xs = [p.sb(f"{pfx}x{i}", [128, E], F32) for i in range(2)]
    ss = [p.sb(f"{pfx}s{i}", [128, E], BF16) for i in range(2)]
    gs = [p.sb(f"{pfx}g{i}", [128, E], BF16) for i in range(2)]
    sg, t_sg = p.sb(f"{pfx}sg", [128, E], F32)
    xo = [p.sb(f"{pfx}o{i}", [128, E], F32) for i in range(2)]
    for t in range(NT):
        xh, xt = xs[t % 2]
        sh, st_ = ss[t % 2]
        gh, gt = gs[t % 2]
        oh, ot_ = xo[t % 2]
        p.dma("qsp", xh[:], x_ap[t * 128:(t + 1) * 128, :], writes=[xt])
        p.dma("qsp", sh[:], s_ap[t * 128:(t + 1) * 128, :], writes=[st_])
        p.dma("qsp", gh[:], g_ap[t * 128:(t + 1) * 128, :], writes=[gt])
        p.emit("dve", lambda sh=sh, gh=gh: nc.vector.tensor_tensor(sg[:], sh[:], gh[:], OP.mult),
               reads=[st_, gt], writes=[t_sg])
        p.emit("dve", lambda xh=xh, oh=oh: nc.vector.tensor_tensor(oh[:], xh[:], sg[:], OP.add),
               reads=[xt, t_sg], writes=[ot_])
        p.dma("qsp", xout_ap[t * 128:(t + 1) * 128, :], oh[:], reads=[ot_])
    p.pop_scope()



import contextlib as _ctxlib
import numpy as _np
import ml_dtypes as _mld
from concourse.bass_utils import run_bass_kernel_spmd as _run_spmd

_BF = _mld.bfloat16
_T = 2048
_E = 2048
_NCORES = 8
MODE = "single"


def _host_consts():
    DR = 64
    inv = 1.0 / 10000.0 ** (_np.arange(0, DR, 2, dtype=_np.float32) / DR)
    ang = _np.arange(_T, dtype=_np.float32)[:, None] * inv[None, :]
    cos = _np.concatenate([_np.cos(ang), _np.cos(ang)], -1).astype(_np.float32)
    sin = _np.concatenate([_np.sin(ang), _np.sin(ang)], -1).astype(_np.float32)
    return {
        "c_ident": _np.eye(128, dtype=_np.float32),
        "c_mask": _np.where(_np.triu(_np.ones((128, 128), bool), k=1), -1e30, 0.0).astype(_np.float32),
        "c_cos": cos, "c_sin": sin,
    }


def _tof(a):
    a = _np.asarray(a)
    return a.view(_BF) if a.dtype.kind == "V" else a


def _decl(nc, name, shape, dt, kind):
    return nc.dram_tensor(name, list(shape), dt, kind=kind).ap()


class _Prog:

    def __init__(self, name, ins, outs, internals, build_fn):
        self.name = name
        self.ins = ins
        self.outs = outs
        self.internals = internals
        self.build_fn = build_fn

    def build(self):
        nc = make_nc()
        ctx = _ctxlib.ExitStack()
        aps = {}
        with ctx:
            p = P(nc, ctx)
            c = C(p)
            for key, shape, dt in self.ins:
                if key in ("c_cos", "c_sin"):
                    continue
                aps[key] = _decl(nc, key, shape, dt, "ExternalInput")
            for key, shape, dt in self.outs:
                aps[key] = _decl(nc, key, shape, dt, "ExternalOutput")
            for key, shape, dt in self.internals:
                aps[key] = _decl(nc, key, shape, dt, "Internal")
            self.build_fn(c, aps)
            p.finish()
            block = ctx.enter_context(nc.Block())
            p.finalize_into_block(block)
        return nc


def _b_ln0(c, a):
    s_ln(c, a["x"], a["xnt0"], "n0_")


def _b_l0attn(c, a):
    s_l0_attn(c, a["xnt0"], a["L0_attn_w_q_a"], a["L0_attn_q_a_ln_w"], a["L0_attn_w_q_b"],
              a["L0_attn_w_kv_a"], a["L0_attn_kv_a_ln_w"], a["L0_attn_w_kv_b"],
              a["L0_attn_w_o"], a["qn0"], a["ot0"], a["s0"], pfx="a0")


def _b_l0ffn(c, a):
    s_l0_ffn(c, a["xnt0"], a["L0_ffn_gate_up_packed"], a["L0_ffn_gate_up_scales"],
             a["L0_ffn_down_packed"], a["L0_ffn_down_scales"], a["ht0"], a["y0"], a["g0"], pfx="f0")


def _b_comb0(c, a):
    s_combine(c, a["x"], a["s0"], a["g0"], a["x1"], "c0_")
    c.p.global_barrier()
    s_ln(c, a["x1"], a["xnt1"], "n1_")


def _b_l1attn(c, a):
    s_l1_attn(c, a["xnt1"], a["L1_attn_w_project"], a["L1_attn_w_out"], a["s1"], pfx="a1")


def _b_l1ffn(c, a):
    s_l1_ffn(c, a["xnt1"], a["L1_ffn_gate_W_packed"], a["L1_ffn_gate_scales"],
             a["L1_ffn_up_W_packed"], a["L1_ffn_up_scales"], a["L1_ffn_down_W_packed"],
             a["L1_ffn_down_scales"], a["it1"], a["y1"], a["g1"], pfx="f1")


def _b_comb1(c, a):
    s_combine(c, a["x1"], a["s1"], a["g1"], a["out"], "c1_")


_W0A = [("L0_attn_w_q_a", (2048, 1536), F32), ("L0_attn_q_a_ln_w", (1536,), F32),
        ("L0_attn_w_q_b", (1536, 3072), F32), ("L0_attn_w_kv_a", (2048, 576), F32),
        ("L0_attn_kv_a_ln_w", (512,), F32), ("L0_attn_w_kv_b", (512, 4096), F32),
        ("L0_attn_w_o", (2048, 2048), F32)]
_W0F = [("L0_ffn_gate_up_packed", (11264, 256), I32), ("L0_ffn_gate_up_scales", (16, 11264), F16),
        ("L0_ffn_down_packed", (2048, 704), I32), ("L0_ffn_down_scales", (44, 2048), F16)]
_W1A = [("L1_attn_w_project", (6144, 2048), F32), ("L1_attn_w_out", (2048, 2048), F32)]
_W1F = [("L1_ffn_gate_W_packed", (16384, 1024), I32), ("L1_ffn_gate_scales", (16384, 64), F32),
        ("L1_ffn_up_W_packed", (16384, 1024), I32), ("L1_ffn_up_scales", (16384, 64), F32),
        ("L1_ffn_down_W_packed", (2048, 8192), I32), ("L1_ffn_down_scales", (2048, 512), F32)]

_XNT = lambda n: (n, (_E, _T), BF16)
_TME = lambda n, dt=BF16: (n, (_T, _E), dt)

_LAUNCHES = [
    _Prog("ln0", [("x", (_T, _E), F32)], [_XNT("xnt0")], [], _b_ln0),
    _Prog("l0attn", [_XNT("xnt0")] + _W0A + [("c_cos", (_T, 64), F32), ("c_sin", (_T, 64), F32)],
          [_TME("s0")], [("qn0", (2048, _T), BF16), ("ot0", (2048, _T), BF16)], _b_l0attn),
    _Prog("l0ffn", [_XNT("xnt0")] + _W0F, [_TME("g0")],
          [("ht0", (5632, _T), BF16), ("y0", (_T, _E), BF16)], _b_l0ffn),
    _Prog("comb0", [("x", (_T, _E), F32), _TME("s0"), _TME("g0")],
          [("x1", (_T, _E), F32), _XNT("xnt1")], [], _b_comb0),
    _Prog("l1attn", [_XNT("xnt1")] + _W1A, [_TME("s1")], [], _b_l1attn),
    _Prog("l1ffn", [_XNT("xnt1")] + _W1F, [_TME("g1")],
          [("it1", (16384, _T), BF16), ("y1", (_T, _E), BF16)], _b_l1ffn),
    _Prog("comb1", [("x1", (_T, _E), F32), _TME("s1"), _TME("g1")],
          [("out", (_T, _E), F32)], [], _b_comb1),
]


def _build_single():
    nc = make_nc()
    ctx = _ctxlib.ExitStack()
    a = {}
    with ctx:
        p = P(nc, ctx)
        c = C(p)
        a["x"] = _decl(nc, "x", (_T, _E), F32, "ExternalInput")
        for key, shape, dt in _W0A + _W0F + _W1A + _W1F:
            a[key] = _decl(nc, key, shape, dt, "ExternalInput")
        a["out"] = _decl(nc, "out", (_T, _E), F32, "ExternalOutput")
        for key, shape, dt in [_XNT("xnt0"), _XNT("xnt1"), _TME("s0"), _TME("g0"),
                               ("x1", (_T, _E), F32), _TME("s1"), _TME("g1"),
                               ("qn0", (2048, _T), BF16), ("ot0", (2048, _T), BF16),
                               ("ht0", (5632, _T), BF16), ("y0", (_T, _E), BF16),
                               ("it1", (16384, _T), BF16), ("y1", (_T, _E), BF16)]:
            a[key] = _decl(nc, key, shape, dt, "Internal")
        _b_ln0(c, a)
        p.global_barrier()
        _b_l0attn(c, a)
        p.global_barrier()
        _b_l0ffn(c, a)
        p.global_barrier()
        _b_comb0(c, a)
        p.global_barrier()
        _b_l1attn(c, a)
        p.global_barrier()
        _b_l1ffn(c, a)
        p.global_barrier()
        _b_comb1(c, a)
        p.finish()
        block = ctx.enter_context(nc.Block())
        p.finalize_into_block(block)
    return nc


def _kernel_single(inputs):
    consts = _host_consts()
    x_full = _np.ascontiguousarray(inputs["L0_attn_x"].astype(_np.float32))
    nc = _build_single()
    in_maps = []
    for i in range(_NCORES):
        m = {"c_ident": consts["c_ident"], "c_mask": consts["c_mask"],
             "c_cos": consts["c_cos"], "c_sin": consts["c_sin"],
             "x": _np.ascontiguousarray(x_full[i])}
        for key, shape, dt in _W0A + _W0F + _W1A + _W1F:
            m[key] = _np.ascontiguousarray(inputs[key])
        in_maps.append(m)
    res = _run_spmd(nc, in_maps, core_ids=list(range(_NCORES)))
    return _np.stack([_np.asarray(res.results[i]["out"]).astype(_np.float32)
                      for i in range(_NCORES)], axis=0)


def kernel(**inputs):
    if MODE == "single":
        return _kernel_single(inputs)
    consts = _host_consts()
    x_full = _np.ascontiguousarray(inputs["L0_attn_x"].astype(_np.float32))
    percore = {i: {"x": _np.ascontiguousarray(x_full[i])} for i in range(_NCORES)}
    for lp in _LAUNCHES:
        nc = lp.build()
        in_maps = []
        for i in range(_NCORES):
            m = {"c_ident": consts["c_ident"], "c_mask": consts["c_mask"]}
            for key, shape, dt in lp.ins:
                if key in ("c_cos", "c_sin"):
                    m[key] = consts[key]
                elif key in inputs:
                    m[key] = _np.ascontiguousarray(inputs[key])
                else:
                    m[key] = _np.ascontiguousarray(percore[i][key])
            in_maps.append(m)
        res = _run_spmd(nc, in_maps, core_ids=list(range(_NCORES)))
        for i in range(_NCORES):
            for key, shape, dt in lp.outs:
                percore[i][key] = _tof(res.results[i][key])
    out = _np.stack([percore[i]["out"].astype(_np.float32) for i in range(_NCORES)], axis=0)
    return out



# version 09
# speedup vs baseline: 1.2841x; 1.1797x over previous
import sys
sys.path.insert(0, "/opt/trn_rl_repo")
import contextlib
import numpy as np
import concourse.bass as bass
import concourse.mybir as mybir

F32 = mybir.dt.float32
BF16 = mybir.dt.bfloat16
F16 = mybir.dt.float16
I32 = mybir.dt.int32
I16 = mybir.dt.int16
AF = mybir.ActivationFunctionType
AX = mybir.AxisListType
OP = mybir.AluOpType

SECTION_OF = {"pe": "tensor", "dve": "vector", "act": "scalar", "pool": "gpsimd",
              "qsp": "sync", "qact": "scalar", "qpool": "gpsimd"}
for _q in ("qsp", "qact", "qpool"):
    for _i in range(8):
        SECTION_OF[f"{_q}{_i}"] = SECTION_OF[_q]


class Tok:
    __slots__ = ("name", "writer", "readers")

    def __init__(self, name=""):
        self.name = name
        self.writer = None
        self.readers = []


class _Eng:
    __slots__ = ("name", "sem", "seq", "cells", "step", "last_real", "prefix")

    def __init__(self, name, sem, step):
        self.name = name
        self.sem = sem
        self.seq = 0
        self.cells = []
        self.step = step
        self.last_real = -1
        self.prefix = None


class P:
    def __init__(self, nc, ctx):
        self.nc = nc
        self.ctx = ctx
        self._scopes = []
        self.thunks = {s: [] for s in ("tensor", "vector", "scalar", "gpsimd", "sync")}
        self.eng = {}
        self.sem_tok = {}
        self.qslots = {}
        for name in ("pe", "dve", "act", "pool"):
            sem = ctx.enter_context(nc.semaphore(f"s_{name}"))
            self.eng[name] = _Eng(name, sem, 1)
        for q in ("qsp", "qact", "qpool"):
            self.qslots[q] = []
            for i in range(8):
                name = f"{q}{i}"
                sem = ctx.enter_context(nc.semaphore(f"s_{name}"))
                self.eng[name] = _Eng(name, sem, 16)
                self.sem_tok[name] = Tok(f"sem_{name}")
                self.qslots[q].append(name)
        self._qrr = {q: 0 for q in ("qsp", "qact", "qpool")}
        self.waited = {}
        self._sbuf_bytes = 0

    def tok(self, name=""):
        return Tok(name)

    def sb(self, name, shape, dtype, root=False):
        ctx = self.ctx if (root or not self._scopes) else self._scopes[-1]
        h = ctx.enter_context(self.nc.sbuf_tensor(name, shape, dtype))
        nb = int(np.prod(shape[1:])) * mybir.dt.size(dtype)
        self._sbuf_bytes += nb
        return h, Tok(name)

    def ps(self, name, shape):
        h = self.ctx.enter_context(self.nc.psum_tensor(name, shape, F32))
        return h, Tok(name)

    def push_scope(self):
        self._scopes.append(contextlib.ExitStack())

    def pop_scope(self):
        self.global_barrier()
        self._scopes.pop().close()

    def _close(self, e, seq):
        cell = e.cells[seq]
        if cell["inc"] == 0:
            assert not cell.get("noinst"), f"close({e.name},{seq}) targets wait-only op"
            cell["inc"] = e.step
        return seq

    def emit(self, eng_name, fn, reads=(), writes=(), close=False, tag=""):
        e = self.eng[eng_name]
        seq_self = e.seq
        section = SECTION_OF[eng_name]
        need = {}
        for t in reads:
            if t.writer is not None:
                pe_, sq = t.writer
                need[pe_] = max(need.get(pe_, -1), sq)
        for t in writes:
            if t.writer is not None:
                pe_, sq = t.writer
                need[pe_] = max(need.get(pe_, -1), sq)
            for pe_, sq in t.readers:
                need[pe_] = max(need.get(pe_, -1), sq)
        waits = []
        for pe_, sq in need.items():
            key = (section, pe_)
            if self.waited.get(key, -1) >= sq:
                continue
            self.waited[key] = sq
            self._close(self.eng[pe_], sq)
            waits.append((self.eng[pe_], sq))
            st = self.sem_tok.get(pe_)
            if st is not None:
                st.readers.append((eng_name, seq_self))
        e.seq += 1
        cell = {"inc": 0}
        if fn is None:
            cell["noinst"] = True
        e.cells.append(cell)
        if fn is not None:
            e.last_real = seq_self

        def thunk(nc=self.nc, waits=waits, fn=fn, cell=cell, sem=e.sem,
                  eng_obj_name=section):
            eng = getattr(nc, eng_obj_name)
            for pe_obj, sq in waits:
                eng.wait_ge(pe_obj.sem, pe_obj.prefix[sq])
            if fn is not None:
                instr = fn()
                if cell["inc"]:
                    instr.then_inc(sem, cell["inc"])

        self.thunks[section].append(thunk)
        if fn is not None:
            for t in reads:
                t.readers.append((eng_name, seq_self))
            for t in writes:
                t.writer = (eng_name, seq_self)
                t.readers = []
        if close:
            self._close(e, seq_self)
        return seq_self

    def global_barrier(self):
        names = ["pe", "dve", "act", "pool"] + [s for q in self.qslots.values() for s in q]
        vals = {}
        for n in names:
            e = self.eng[n]
            if e.last_real >= 0:
                vals[n] = self._close(e, e.last_real)
        for sec in ("tensor", "vector", "scalar", "gpsimd", "sync"):
            waits = []
            for n, sq in vals.items():
                key = (sec, n)
                if self.waited.get(key, -1) >= sq:
                    continue
                self.waited[key] = sq
                waits.append((self.eng[n], sq))
            if not waits:
                continue

            def thunk(nc=self.nc, waits=tuple(waits), obj=sec):
                eng = getattr(nc, obj)
                for pe_obj, sq in waits:
                    eng.wait_ge(pe_obj.sem, pe_obj.prefix[sq])
            self.thunks[sec].append(thunk)

    def barrier_wait(self, eng_name, toks):
        self.emit(eng_name, None, reads=toks)

    def finish(self):
        for q in ("qsp", "qact", "qpool"):
            obj = SECTION_OF[q]
            for name in self.qslots[q]:
                e = self.eng[name]
                if e.last_real < 0:
                    continue
                self._close(e, e.last_real)

                def thunk(nc=self.nc, e=e, obj=obj):
                    if e.prefix and e.prefix[-1] > 0:
                        getattr(nc, obj).wait_ge(e.sem, e.prefix[-1])
                self.thunks[obj].append(thunk)

    def finalize_into_block(self, block):
        for e in self.eng.values():
            run = 0
            pref = []
            for cell in e.cells:
                run += cell["inc"]
                pref.append(run)
            e.prefix = pref
        for section in ("sync", "gpsimd", "tensor", "vector", "scalar"):
            ths = self.thunks[section]
            if not ths:
                continue
            deco = getattr(block, section)

            def body(engine, ths=ths):
                for t in ths:
                    t()
            deco(body)

    def dma(self, queue, out_ap, in_ap, reads=(), writes=(), **kw):
        slot = self.qslots[queue][self._qrr[queue] % len(self.qslots[queue])]
        self._qrr[queue] += 1
        eng_obj = SECTION_OF[queue]
        return self.emit(slot, lambda: getattr(self.nc, eng_obj).dma_start(out=out_ap, in_=in_ap, **kw),
                         reads=list(reads),
                         writes=list(writes) + [self.sem_tok[slot]], close=True)


def make_nc():
    return bass.Bass("TRN2", target_bir_lowering=False, debug=False)


"""Stage builders for the mk_recurrent_d2 megakernel. All per-core (one batch elem).

Layouts:
 - activations token-major [tok x feat] row-major in HBM
 - xnT in HBM: plain [E, T] bf16 row-major (feature-major)
 - s/g outputs token-major [T, E] bf16
"""
import numpy as np

USE_DMA_TRANSPOSE = False
T = 2048
E = 2048
NT = T // 128
NC = E // 128


class C:

    def __init__(self, p: P):
        self.p = p
        self.nc = p.nc
        self.psum_h, _ = p.ps("psumall", [128, 4096])
        self.banks = [(self.psum_h[:, i * 512:(i + 1) * 512], Tok(f"bank{i}")) for i in range(8)]
        self.scores = (self.psum_h[:, 2048:4096], [self.banks[i][1] for i in range(4, 8)])
        self.ident = self.nc.dram_tensor("c_ident", [128, 128], F32, kind="ExternalInput").ap()
        self.ident_sb, self.t_ident = p.sb("identsb", [128, 128], BF16)
        p.dma("qpool", self.ident_sb[:], self.ident[:, :], writes=[self.t_ident])
        self.ident32_sb, self.t_ident32 = p.sb("ident32sb", [128, 128], F32)
        p.dma("qsp", self.ident32_sb[:], self.ident[:, :], writes=[self.t_ident32])
        self.maskc = self.nc.dram_tensor("c_mask", [128, 128], F32, kind="ExternalInput").ap()
        self.maskc_sb, self.t_maskc = p.sb("maskcsb", [128, 128], F32)
        p.dma("qsp", self.maskc_sb[:], self.maskc[:, :], writes=[self.t_maskc])
        self._consts = {}
        for v in (0.0, 1.0, -0.5, 1e-5, 1e-6, 1.0 / 2048, 1.0 / 512, 1.0 / 1536,
                  1.0 / np.sqrt(192.0), 1.0 / np.sqrt(128.0), -2.0, -6.0, 2.0):
            self.const(v)

    def const(self, val):
        key = float(val)
        if key not in self._consts:
            assert not self.p._scopes, f"const {key} requested inside a scope; pre-register in C.__init__"
            h, t = self.p.sb(f"c{len(self._consts)}", [128, 1], F32, root=True)
            self.p.emit("dve", lambda h=h, v=key: self.nc.vector.memset(h[:], v), writes=[t])
            try:
                self.nc.const_aps.register(mybir.dt.float32, key, h[:])
            except Exception:
                pass
            self._consts[key] = (h, t)
        return self._consts[key]

    def host_consts(self):
        return {"c_ident": np.eye(128, dtype=np.float32),
                "c_mask": np.where(np.triu(np.ones((128, 128), bool), k=1), -1e30, 0.0).astype(np.float32)}

    def transpose(self, bank_i, col_off, src_ap, reads, n_rows=128):
        bh, bt = self.banks[bank_i]
        self.p.emit("pe", lambda: self.nc.tensor.matmul(
            bh[0:src_ap.shape[-1], col_off:col_off + n_rows],
            src_ap, self.ident_sb[0:n_rows, 0:n_rows], start=True, stop=True),
            reads=list(reads) + [self.t_ident], writes=[bt])
        return bh, bt

    def mm(self, bank_i, cols, lhsT_ap, rhs_ap, reads, start, stop, close=False):
        bh, bt = self.banks[bank_i]
        self.p.emit("pe", lambda: self.nc.tensor.matmul(
            bh[0:lhsT_ap.shape[-1], cols[0]:cols[1]], lhsT_ap, rhs_ap, start=start, stop=stop),
            reads=list(reads), writes=[bt], close=close)
        return bh, bt


def s_ln(c: C, x_ap, xnt_ap, pfx, eps=1e-5):
    p, nc = c.p, c.nc
    p.push_scope()
    xs = [p.sb(f"{pfx}x{i}", [128, E], F32) for i in range(2)]
    sq, t_sq = p.sb(f"{pfx}sq", [128, E], F32)
    xns = [p.sb(f"{pfx}xn{i}", [128, E], BF16) for i in range(2)]
    st, t_st = p.sb(f"{pfx}st", [128, 8], F32)
    evs = [p.sb(f"{pfx}ev{i}", [128, 512], BF16) for i in range(4)]
    eb, ebt = c.const(eps)
    one, onet = c.const(1.0)
    nh, nht = c.const(-0.5)
    zz, zzt = c.const(0.0)
    for t in range(NT):
        xh, xt = xs[t % 2]
        xn, t_xn = xns[t % 2]
        p.dma("qsp", xh[:], x_ap[t * 128:(t + 1) * 128, :], writes=[xt])
        p.emit("dve", lambda xh=xh: nc.vector.tensor_reduce(st[:, 0:1], xh[:], AX.X, OP.add),
               reads=[xt], writes=[t_st])
        p.emit("dve", lambda xh=xh: nc.vector.tensor_tensor(sq[:], xh[:], xh[:], OP.mult),
               reads=[xt], writes=[t_sq])
        p.emit("dve", lambda: nc.vector.tensor_reduce(st[:, 1:2], sq[:], AX.X, OP.add),
               reads=[t_sq], writes=[t_st])
        p.emit("dve", lambda: nc.vector.tensor_scalar(st[:, 2:3], st[:, 0:1], 1.0 / E, None, OP.mult),
               reads=[t_st], writes=[t_st])
        p.emit("dve", lambda: nc.vector.tensor_tensor(st[:, 3:4], st[:, 2:3], st[:, 2:3], OP.mult),
               reads=[t_st], writes=[t_st])
        p.emit("dve", lambda: nc.vector.scalar_tensor_tensor(
            st[:, 4:5], st[:, 1:2], 1.0 / E, st[:, 3:4], OP.mult, OP.subtract),
            reads=[t_st], writes=[t_st])
        p.emit("act", lambda: nc.scalar.activation(st[:, 5:6], st[:, 4:5], AF.Ln, bias=eb[:], scale=one[:]),
               reads=[t_st, ebt, onet], writes=[t_st])
        p.emit("act", lambda: nc.scalar.activation(st[:, 6:7], st[:, 5:6], AF.Exp, bias=zz[:], scale=nh[:]),
               reads=[t_st, nht, zzt], writes=[t_st])
        p.emit("dve", lambda xh=xh, xn=xn: nc.vector.tensor_scalar(
            xn[:], xh[:], st[:, 2:3], st[:, 6:7], OP.subtract, OP.mult),
            reads=[xt, t_st], writes=[t_xn])
        for quad in range(4):
            bi = (t % 2) * 4 + quad
            for j in range(4):
                cc = quad * 4 + j
                c.transpose(bi, j * 128, xn[:, cc * 128:(cc + 1) * 128], [t_xn])
            evh, evt = evs[quad]
            bh, bt = c.banks[bi]
            p.emit("act", lambda bh=bh, evh=evh: nc.scalar.copy(evh[:], bh[:]),
                   reads=[bt], writes=[evt])
            dst = xnt_ap[quad * 512:(quad + 1) * 512, t * 128:(t + 1) * 128]
            dst = dst.rearrange("(c f) k -> f c k", c=4)
            p.dma("qact", dst, evh[:].rearrange("f (c k) -> f c k", c=4), reads=[evt])
    p.pop_scope()


def ref_ln(x):
    m = x.mean(-1, keepdims=True)
    v = x.var(-1, keepdims=True)
    return (x - m) / np.sqrt(v + 1e-5)



def decode_fp4_l0(c: C, pk_ap, sct_pair, ds, n_words, reads_pk):
    p, nc = c.p, c.nc
    K = 8 * n_words
    G = n_words // 16
    (ki, t_ki), (sgn, t_sgn), (ssc, t_ssc) = ds["ki"], ds["sgn"], ds["ssc"]
    (va, t_va), (t1h, t_t1), (t2h, t_t2), (t3h, t_t3), (vv, t_vv) = \
        ds["va"], ds["t1"], ds["t2"], ds["t3"], ds["vv"]
    wd, t_wd = ds["wd"]
    sch, t_sch = sct_pair
    for j in range(8):
        sl = slice(j * n_words, (j + 1) * n_words)
        if j == 7:
            p.emit("dve", lambda sl=sl: nc.vector.tensor_scalar(
                ki[:, sl], pk_ap, 0x70000000, None, OP.bitwise_and), reads=reads_pk, writes=[t_ki])
        else:
            p.emit("dve", lambda sl=sl, j=j: nc.vector.tensor_scalar(
                ki[:, sl], pk_ap, 28 - 4 * j, 0x70000000, OP.logical_shift_left, OP.bitwise_and),
                reads=reads_pk, writes=[t_ki])
        if j == 7:
            p.emit("dve", lambda sl=sl: nc.vector.tensor_scalar(
                sgn[:, sl], pk_ap, -0x80000000, None, OP.bitwise_and),
                reads=reads_pk, writes=[t_sgn])
        else:
            p.emit("dve", lambda sl=sl, j=j: nc.vector.tensor_scalar(
                sgn[:, sl], pk_ap, 28 - 4 * j, -0x80000000, OP.logical_shift_left, OP.bitwise_and),
                reads=reads_pk, writes=[t_sgn])
    sb = sch[:, 0:G].bitcast(I32).unsqueeze(1).unsqueeze(3).broadcast_to([128, 8, G, 16])
    p.emit("dve", lambda: nc.vector.tensor_tensor(
        ssc[:, 0:K].bitcast(I32).rearrange("p (j g r) -> p j g r", j=8, g=G),
        sb, sgn[:, 0:K].rearrange("p (j g r) -> p j g r", j=8, g=G), OP.bitwise_or),
        reads=[t_sch, t_sgn], writes=[t_ssc])
    p.emit("dve", lambda: nc.vector.tensor_scalar(va[:, 0:K], ki[:, 0:K], 0.5 / 268435456.0, None, OP.mult),
           reads=[t_ki], writes=[t_va])
    p.emit("dve", lambda: nc.vector.tensor_scalar(t1h[:, 0:K], va[:, 0:K], 2.0, -2.0, OP.mult, OP.add),
           reads=[t_va], writes=[t_t1])
    p.emit("dve", lambda: nc.vector.tensor_scalar(t3h[:, 0:K], t1h[:, 0:K], 2.0, -4.0, OP.mult, OP.add),
           reads=[t_t1], writes=[t_t3])
    p.emit("dve", lambda: nc.vector.tensor_tensor(t2h[:, 0:K], va[:, 0:K], t1h[:, 0:K], OP.max),
           reads=[t_va, t_t1], writes=[t_t2])
    p.emit("dve", lambda: nc.vector.tensor_tensor(vv[:, 0:K], t2h[:, 0:K], t3h[:, 0:K], OP.max),
           reads=[t_t2, t_t3], writes=[t_vv])
    p.emit("pool", lambda: nc.gpsimd.tensor_tensor(
        wd[:, 0:K].rearrange("p (w j) -> p j w", j=8),
        vv[:, 0:K].rearrange("p (j w) -> p j w", j=8),
        ssc[:, 0:K].rearrange("p (j w) -> p j w", j=8), OP.mult),
        reads=[t_vv, t_ssc], writes=[t_wd])
    return wd, t_wd


def decode_bufs(p, pfx, n_words, nset=3):
    K = 8 * n_words
    shared = {}
    for nm, dt in (("ki", I32), ("sgn", I32), ("va", BF16), ("t1", BF16),
                   ("t2", BF16), ("t3", BF16)):
        shared[nm] = p.sb(f"{pfx}{nm}", [128, K], dt)
    sets = []
    for i in range(max(3, nset)):
        d = dict(shared)
        for nm, dt in (("ssc", F32), ("vv", BF16), ("wd", BF16)):
            d[nm] = p.sb(f"{pfx}{nm}{i}", [128, K], dt)
        sets.append(d)
    return sets


def s_l0_ffn(c: C, xnt_ap, pk_ap, sc_ap, dpk_ap, dsc_ap, ht_ap, y_ap, g_ap, pfx="f0"):
    p, nc = c.p, c.nc
    zz, zzt = c.const(0.0)
    one, onet = c.const(1.0)
    eb, ebt = c.const(1e-6)
    nh, nht = c.const(-0.5)
    inve, invet = c.const(1.0 / E)
    t_ht = [p.tok(f"ht{i}") for i in range(44)]
    t_y = [p.tok(f"y{i}") for i in range(NT)]
    p.push_scope()
    gscs = [p.sb(f"{pfx}gsc{i}", [32, 176], F32) for i in range(2)]
    for _gh, _gt in gscs:
        p.emit("dve", lambda _gh=_gh: nc.vector.memset(_gh[:], 0.0), writes=[_gt])
    dec = decode_bufs(p, f"{pfx}d", 256, 3)
    pks = [p.sb(f"{pfx}pk{i}", [128, 256], I32) for i in range(2)]
    scts = [p.sb(f"{pfx}sct{i}", [128, 32], F32) for i in range(2)]
    p.push_scope()
    xnt, t_xnt = p.sb(f"{pfx}xnt", [128, NC, T], BF16)
    for cc in range(NC):
        p.dma("qsp", xnt[:, cc, :], xnt_ap[cc * 128:(cc + 1) * 128, :], writes=[t_xnt])
    wtiles = [p.sb(f"{pfx}wt{i}", [128, 2048], BF16) for i in range(2)]
    gsbs = [p.sb(f"{pfx}gsb{i}", [128, 2048], BF16) for i in range(3)]
    hsbs = [p.sb(f"{pfx}hsb{i}", [128, 2048], BF16) for i in range(3)]
    usbs = [p.sb(f"{pfx}usb{i}", [128, 2048], BF16) for i in range(3)]
    pend = []

    def _flush_hmult(item):
        (j, gsb, t_gsb, usb, t_usb, hsb, t_hsb) = item
        p.emit("dve", lambda gsb=gsb, usb=usb, hsb=hsb: nc.vector.tensor_tensor(
            hsb[:], gsb[:], usb[:], OP.mult), reads=[t_usb, t_gsb], writes=[t_hsb])
        p.dma("qact", ht_ap[j * 128:(j + 1) * 128, :], hsb[:], reads=[t_hsb],
              writes=[t_ht[j]])

    def sc_transpose(src_ap, src_tok, gcount, sct, t_sct):
        for i in range(4):
            p.emit("dve", lambda i=i, src_ap=src_ap, sct=sct: nc.vector.transpose(
                sct[32 * i:32 * (i + 1), 0:32], src_ap[:, 32 * i:32 * (i + 1)]),
                reads=[src_tok], writes=[t_sct])

    def wd_transposes(wd, t_wd, kchunks, store_fn):
        for g4 in range((kchunks + 3) // 4):
            bi = 4 + (g4 % 2)
            bh, bt = c.banks[bi]
            js = list(range(g4 * 4, min(g4 * 4 + 4, kchunks)))
            for idx, kc in enumerate(js):
                c.transpose(bi, idx * 128, wd[:, kc * 128:(kc + 1) * 128], [t_wd])
            store_fn(js, bh, bt)

    for i in range(44):
        for gu in range(2):
            n = i + gu * 44
            it = 2 * i + gu
            pkh, t_pk = pks[it % 2]
            sct, t_sct = scts[it % 2]
            ds = dec[it % 3]
            wtile, t_wtile = wtiles[it % 2]
            p.dma("qsp", pkh[:], pk_ap[n * 128:(n + 1) * 128, :], writes=[t_pk])
            gsch, t_gsch = gscs[it % 2]
            p.dma("qpool", gsch[0:16, 0:128], sc_ap[:, n * 128:(n + 1) * 128], writes=[t_gsch])
            sc_transpose(gsch[0:32, 0:128], t_gsch, 16, sct, t_sct)
            decode_fp4_l0(c, pkh[:], (sct, t_sct), ds, 256, [t_pk])
            wd, t_wd = ds["wd"]

            def store(js, bh, bt, wtile=wtile, t_wtile=t_wtile):
                w0, n_ = js[0], len(js)
                p.emit("act", lambda w0=w0, n_=n_, bh=bh: nc.scalar.copy(
                    wtile[:, w0 * 128:(w0 + n_) * 128], bh[:, 0:n_ * 128]),
                    reads=[bt], writes=[t_wtile])
            wd_transposes(wd, t_wd, 16, store)
            for kc in range(NC):
                for tc in range(4):
                    c.mm(tc, (0, 512), wtile[:, kc * 128:(kc + 1) * 128],
                         xnt[:, kc, tc * 512:(tc + 1) * 512], [t_wtile, t_xnt],
                         start=(kc == 0), stop=(kc == NC - 1))
            gsb, t_gsb = gsbs[i % 3]
            usb, t_usb = usbs[i % 3]
            hsb, t_hsb = hsbs[i % 3]
            if gu == 0:
                for tc in range(4):
                    bh, bt = c.banks[tc]
                    p.emit("act", lambda bh=bh, gsb=gsb, tc=tc: nc.scalar.activation(
                        gsb[:, tc * 512:(tc + 1) * 512], bh[:], AF.Silu, bias=zz[:], scale=one[:]),
                        reads=[bt, zzt, onet], writes=[t_gsb])
            else:
                for tc in range(4):
                    bh, bt = c.banks[tc]
                    p.emit("act", lambda bh=bh, usb=usb, tc=tc: nc.scalar.copy(
                        usb[:, tc * 512:(tc + 1) * 512], bh[:]), reads=[bt], writes=[t_usb])
                pend.append((i, gsb, t_gsb, usb, t_usb, hsb, t_hsb))
                if len(pend) >= 3:
                    _flush_hmult(pend.pop(0))
    while pend:
        _flush_hmult(pend.pop(0))

    p.pop_scope()
    p.push_scope()
    wdt, _twdt0 = p.sb(f"{pfx}wdt", [128, 44, 256], BF16)
    t_wdq = [p.tok(f"{pfx}wdq{i}") for i in range(4)]
    ypsb, t_ypsb = p.sb(f"{pfx}ypsb", [128, 512], BF16)
    hch = [p.sb(f"{pfx}hch{i}", [128, 2048], BF16) for i in range(4)]
    ytev, t_ytev = p.sb(f"{pfx}ytev", [128, 4096], BF16)

    def emit_dec(ps, q):
        for sub2 in range(2):
            hrow = ps * 2 + sub2
            sl4 = q
            pkh, t_pk = pks[sub2 % 2]
            sct, t_sct = scts[sub2 % 2]
            ds = dec[(hrow * 4 + sl4) % 3]
            w0 = sl4 * 176
            pkq = "qsp" if ps == 0 else "qpool"
            p.dma(pkq, pkh[:, 0:176], dpk_ap[hrow * 128:(hrow + 1) * 128, w0:w0 + 176],
                  writes=[t_pk])
            gsch, t_gsch = gscs[sub2 % 2]
            p.dma("qpool", gsch[0:11, 0:128],
                  dsc_ap[sl4 * 11:sl4 * 11 + 11, hrow * 128:(hrow + 1) * 128], writes=[t_gsch])
            sc_transpose(gsch[0:32, 0:128], t_gsch, 11, sct, t_sct)
            decode_fp4_l0(c, pkh[:, 0:176], (sct, t_sct), ds, 176, [t_pk])
            wd, t_wd = ds["wd"]
            if q < 3:
                wdv = wd[:, 0:1408].rearrange("p (kc x) -> p kc x", kc=11)
                dst = wdt[:, sl4 * 11:(sl4 + 1) * 11, sub2 * 128:(sub2 + 1) * 128]
                for i in range(4):
                    for j in range(4):
                        p.emit("dve", lambda i=i, j=j, wdv=wdv, dst=dst: nc.vector.transpose(
                            dst[32 * i:32 * (i + 1), :, 32 * j:32 * (j + 1)],
                            wdv[32 * j:32 * (j + 1), :, 32 * i:32 * (i + 1)]),
                            reads=[t_wd], writes=[t_wdq[q]])
            else:
                def store(js, bh, bt, sl4=sl4, sub2=sub2):
                    for idx, kc in enumerate(js):
                        ic = sl4 * 11 + kc
                        p.emit("act", lambda bh=bh, idx=idx, ic=ic, sub2=sub2: nc.scalar.copy(
                            wdt[:, ic, sub2 * 128:(sub2 + 1) * 128],
                            bh[:, idx * 128:(idx + 1) * 128]),
                            reads=[bt], writes=[t_wdq[3]])
                wd_transposes(wd, t_wd, 11, store)

    for q in range(4):
        emit_dec(0, q)
    for ps in range(8):
        for ic in range(44):
            if ic in (11, 22, 33) and ps + 1 < 8:
                emit_dec(ps + 1, ic // 11 - 1)
            hh, t_hh = hch[ic % 4]
            p.dma("qsp" if ic % 2 == 0 else "qact", hh[:], ht_ap[ic * 128:(ic + 1) * 128, :],
                  reads=[t_ht[ic]], writes=[t_hh])
            for hj in range(2):
                for tc in range(4):
                    c.mm(hj * 4 + tc, (0, 512), wdt[:, ic, hj * 128:(hj + 1) * 128],
                         hh[:, tc * 512:(tc + 1) * 512], [t_wdq[min(ic // 11, 3)], t_hh],
                         start=(ic == 0), stop=(ic == 43))
        for bi in range(8):
            bh, bt = c.banks[bi]
            p.emit("act", lambda bh=bh, bi=bi: nc.scalar.copy(
                ytev[:, bi * 512:(bi + 1) * 512], bh[:]), reads=[bt], writes=[t_ytev])
        for hj in range(2):
            for q in range(4):
                bi = q % 2
                bh, bt = c.banks[bi]
                for idx in range(4):
                    tt = q * 4 + idx
                    c.transpose(bi, idx * 128,
                                ytev[:, hj * 2048 + tt * 128: hj * 2048 + (tt + 1) * 128],
                                [t_ytev])
                tt0 = q * 4
                p.emit("act", lambda bh=bh: nc.scalar.copy(ypsb[:], bh[:]),
                       reads=[bt], writes=[t_ypsb])
                dst = y_ap[tt0 * 128:(tt0 + 4) * 128, ps * 256 + hj * 128: ps * 256 + (hj + 1) * 128]
                dst = dst.rearrange("(a p) h -> p a h", a=4)
                p.dma("qact", dst, ypsb[:].rearrange("p (a h) -> p a h", a=4), reads=[t_ypsb],
                      writes=[t_y[tt0 + _i] for _i in range(4)])
        if ps + 1 < 8:
            emit_dec(ps + 1, 3)
    p.pop_scope()
    p.pop_scope()
    p.push_scope()
    sq2, t_sq2 = p.sb(f"{pfx}sq2", [128, E], F32)
    st2, t_st2 = p.sb(f"{pfx}st2", [128, 4], F32)
    gout, t_gout = p.sb(f"{pfx}gout", [128, E], BF16)
    yts = [p.sb(f"{pfx}yt{i}", [128, E], BF16) for i in range(2)]
    for tt in range(NT):
        yth, t_yth = yts[tt % 2]
        p.dma("qsp", yth[:], y_ap[tt * 128:(tt + 1) * 128, :], reads=[t_y[tt]], writes=[t_yth])
        p.emit("dve", lambda yth=yth: nc.vector.tensor_tensor(
            sq2[:], yth[:], yth[:], OP.mult), reads=[t_yth], writes=[t_sq2])
        p.emit("dve", lambda: nc.vector.tensor_reduce(st2[:, 0:1], sq2[:], AX.X, OP.add),
               reads=[t_sq2], writes=[t_st2])
        p.emit("act", lambda: nc.scalar.activation(st2[:, 1:2], st2[:, 0:1], AF.Ln, bias=eb[:], scale=inve[:]),
               reads=[t_st2, ebt, invet], writes=[t_st2])
        p.emit("act", lambda: nc.scalar.activation(st2[:, 2:3], st2[:, 1:2], AF.Exp, bias=zz[:], scale=nh[:]),
               reads=[t_st2, nht, zzt], writes=[t_st2])
        p.emit("act", lambda yth=yth: nc.scalar.activation(
            gout[:], yth[:], AF.Silu, bias=zz[:], scale=st2[:, 2:3]),
            reads=[t_yth, t_st2, zzt], writes=[t_gout])
        p.dma("qact", g_ap[tt * 128:(tt + 1) * 128, :], gout[:], reads=[t_gout])
    p.pop_scope()



def decode_fp4_l1(c: C, pk_ap, sc_pair, ds, n_words, reads_pk):
    p, nc = c.p, c.nc
    K = 2 * n_words
    G = n_words // 16
    (ki, t_ki), (sgn, t_sgn), (ssc, t_ssc) = ds["ki"], ds["sgn"], ds["ssc"]
    (va, t_va), (t1h, t_t1), (t2h, t_t2), (t3h, t_t3), (vv, t_vv) = \
        ds["va"], ds["t1"], ds["t2"], ds["t3"], ds["vv"]
    wd, t_wd = ds["wd"]
    sch, t_sch = sc_pair
    p.emit("dve", lambda: nc.vector.tensor_scalar(
        ki[:, 0:n_words], pk_ap, 28, 0x70000000, OP.logical_shift_left, OP.bitwise_and),
        reads=reads_pk, writes=[t_ki])
    p.emit("dve", lambda: nc.vector.tensor_scalar(
        ki[:, n_words:K], pk_ap, 24, 0x70000000, OP.logical_shift_left, OP.bitwise_and),
        reads=reads_pk, writes=[t_ki])
    p.emit("dve", lambda: nc.vector.tensor_scalar(
        sgn[:, 0:n_words], pk_ap, 28, -0x80000000, OP.logical_shift_left, OP.bitwise_and),
        reads=reads_pk, writes=[t_sgn])
    p.emit("dve", lambda: nc.vector.tensor_scalar(
        sgn[:, n_words:K], pk_ap, 24, -0x80000000, OP.logical_shift_left, OP.bitwise_and),
        reads=reads_pk, writes=[t_sgn])
    sb = sch[:, 0:G].bitcast(I32).unsqueeze(1).unsqueeze(3).broadcast_to([128, 2, G, 16])
    p.emit("dve", lambda: nc.vector.tensor_tensor(
        ssc[:, 0:K].bitcast(I32).rearrange("p (j g r) -> p j g r", j=2, g=G),
        sb, sgn[:, 0:K].rearrange("p (j g r) -> p j g r", j=2, g=G), OP.bitwise_or),
        reads=[t_sch, t_sgn], writes=[t_ssc])
    p.emit("dve", lambda: nc.vector.tensor_scalar(va[:, 0:K], ki[:, 0:K], 0.5 / 268435456.0, None, OP.mult),
           reads=[t_ki], writes=[t_va])
    p.emit("dve", lambda: nc.vector.tensor_scalar(t1h[:, 0:K], va[:, 0:K], 2.0, -2.0, OP.mult, OP.add),
           reads=[t_va], writes=[t_t1])
    p.emit("dve", lambda: nc.vector.tensor_scalar(t3h[:, 0:K], t1h[:, 0:K], 2.0, -4.0, OP.mult, OP.add),
           reads=[t_t1], writes=[t_t3])
    p.emit("dve", lambda: nc.vector.tensor_tensor(t2h[:, 0:K], va[:, 0:K], t1h[:, 0:K], OP.max),
           reads=[t_va, t_t1], writes=[t_t2])
    p.emit("dve", lambda: nc.vector.tensor_tensor(vv[:, 0:K], t2h[:, 0:K], t3h[:, 0:K], OP.max),
           reads=[t_t2, t_t3], writes=[t_vv])
    p.emit("pool", lambda: nc.gpsimd.tensor_tensor(
        wd[:, 0:K].rearrange("p (w j) -> p j w", j=2),
        vv[:, 0:K].rearrange("p (j w) -> p j w", j=2),
        ssc[:, 0:K].rearrange("p (j w) -> p j w", j=2), OP.mult),
        reads=[t_vv, t_ssc], writes=[t_wd])
    return wd, t_wd


def decode_fp4_l1t(c: C, pk_ap, reads_pk, scpl_pair, ds, pkT_pair, wd_ap=None, wd_tok=None):
    p, nc = c.p, c.nc
    (ki, t_ki), (sgn, t_sgn), (ssc, t_ssc) = ds["ki"], ds["sgn"], ds["ssc"]
    (va, t_va), (t1h, t_t1), (t2h, t_t2), (t3h, t_t3), (vv, t_vv) = \
        ds["va"], ds["t1"], ds["t2"], ds["t3"], ds["vv"]
    scpl, t_scpl = scpl_pair
    if wd_ap is None:
        wdh, wd_tok = ds["wd"]
        wd_ap = wdh[:].rearrange("p (s j n) -> p j s n", s=8, j=2)
    pkT, t_pkT = pkT_pair
    pkv = pk_ap.rearrange("p (s n) -> p s n", s=8)
    pTv = pkT[:].rearrange("p (s n) -> p s n", s=8)
    for i in range(4):
        for j in range(4):
            p.emit("dve", lambda i=i, j=j, pkv=pkv, pTv=pTv: nc.vector.transpose(
                pTv[32 * i:32 * (i + 1), :, 32 * j:32 * (j + 1)],
                pkv[32 * j:32 * (j + 1), :, 32 * i:32 * (i + 1)]),
                reads=list(reads_pk), writes=[t_pkT])
    for jn in range(2):
        p.emit("dve", lambda jn=jn, pkT=pkT: nc.vector.tensor_scalar(
            ki[:, jn * 1024:(jn + 1) * 1024], pkT[:], 28 - 4 * jn, 0x70000000,
            OP.logical_shift_left, OP.bitwise_and), reads=[t_pkT], writes=[t_ki])
        p.emit("dve", lambda jn=jn, pkT=pkT: nc.vector.tensor_scalar(
            sgn[:, jn * 1024:(jn + 1) * 1024], pkT[:], 28 - 4 * jn, -0x80000000,
            OP.logical_shift_left, OP.bitwise_and), reads=[t_pkT], writes=[t_sgn])
    sb = scpl[:].bitcast(I32).unsqueeze(1).broadcast_to([128, 2, 8, 128])
    p.emit("dve", lambda sb=sb: nc.vector.tensor_tensor(
        ssc[:, 0:2048].bitcast(I32).rearrange("p (j s n) -> p j s n", j=2, s=8),
        sb, sgn[:, 0:2048].rearrange("p (j s n) -> p j s n", j=2, s=8), OP.bitwise_or),
        reads=[t_scpl, t_sgn], writes=[t_ssc])
    p.emit("dve", lambda: nc.vector.tensor_scalar(va[:, 0:2048], ki[:, 0:2048],
           0.5 / 268435456.0, None, OP.mult), reads=[t_ki], writes=[t_va])
    p.emit("dve", lambda: nc.vector.tensor_scalar(t1h[:, 0:2048], va[:, 0:2048], 2.0, -2.0,
           OP.mult, OP.add), reads=[t_va], writes=[t_t1])
    p.emit("dve", lambda: nc.vector.tensor_scalar(t3h[:, 0:2048], t1h[:, 0:2048], 2.0, -4.0,
           OP.mult, OP.add), reads=[t_t1], writes=[t_t3])
    p.emit("dve", lambda: nc.vector.tensor_tensor(t2h[:, 0:2048], va[:, 0:2048],
           t1h[:, 0:2048], OP.max), reads=[t_va, t_t1], writes=[t_t2])
    p.emit("dve", lambda: nc.vector.tensor_tensor(vv[:, 0:2048], t2h[:, 0:2048],
           t3h[:, 0:2048], OP.max), reads=[t_t2, t_t3], writes=[t_vv])
    p.emit("pool", lambda wd_ap=wd_ap: nc.gpsimd.tensor_tensor(
        wd_ap, vv[:, 0:2048].rearrange("p (j s n) -> p j s n", j=2, s=8),
        ssc[:, 0:2048].rearrange("p (j s n) -> p j s n", j=2, s=8), OP.mult),
        reads=[t_vv, t_ssc], writes=[wd_tok])
    return wd_tok


def s_l1_ffn(c: C, xnt_ap, gpk_ap, gsc_ap, upk_ap, usc_ap, dpk_ap, dsc_ap,
             it_ap, y_ap, g_ap, pfx="f1", n_tiles=128, phases="all"):
    p, nc = c.p, c.nc
    N1 = 16384
    zz, zzt = c.const(0.0)
    one, onet = c.const(1.0)
    t_it = [p.tok(f"it{i}") for i in range(128)]
    t_y = [p.tok(f"yy{i}") for i in range(NT)]

    p.push_scope()
    xnt, t_xnt = p.sb(f"{pfx}xnt", [128, NC, T], BF16)
    for cc in range(NC):
        p.dma("qsp", xnt[:, cc, :], xnt_ap[cc * 128:(cc + 1) * 128, :], writes=[t_xnt])
    dec = decode_bufs(p, f"{pfx}d", 256, 3)
    pks = [p.sb(f"{pfx}pk{i}", [128, 1024], I32) for i in range(2)]
    wtiles = [p.sb(f"{pfx}wt{i}", [128, 2048], BF16) for i in range(2)]
    scs = [p.sb(f"{pfx}sc{i}", [128, 64], F32) for i in range(2)]
    gsbs = [p.sb(f"{pfx}gsb{i}", [128, 2048], BF16) for i in range(3)]
    hsbs = [p.sb(f"{pfx}hsb{i}", [128, 2048], BF16) for i in range(3)]
    usbs = [p.sb(f"{pfx}usb{i}", [128, 2048], BF16) for i in range(3)]
    pend = []

    def _flush_hmult(item):
        (j, gsb, t_gsb, usb, t_usb, hsb, t_hsb) = item
        p.emit("dve", lambda gsb=gsb, usb=usb, hsb=hsb: nc.vector.tensor_tensor(
            hsb[:], gsb[:], usb[:], OP.mult), reads=[t_usb, t_gsb], writes=[t_hsb])
        p.dma("qact", it_ap[j * 128:(j + 1) * 128, :], hsb[:], reads=[t_hsb],
              writes=[t_it[j]])

    def wd_transposes(wd, t_wd, kchunks, store_fn):
        for g4 in range((kchunks + 3) // 4):
            bi = 4 + (g4 % 2)
            bh, bt = c.banks[bi]
            js = list(range(g4 * 4, min(g4 * 4 + 4, kchunks)))
            for idx, kc in enumerate(js):
                c.transpose(bi, idx * 128, wd[:, kc * 128:(kc + 1) * 128], [t_wd])
            store_fn(js, bh, bt)

    for i in range(n_tiles):
        for gu in range(2):
            it = 2 * i + gu
            pk_src = gpk_ap if gu == 0 else upk_ap
            sc_src = gsc_ap if gu == 0 else usc_ap
            pkh, t_pk = pks[it % 2]
            sch, t_sch = scs[it % 2]
            ds = dec[it % 3]
            wtile, t_wtile = wtiles[it % 2]
            p.dma("qsp", sch[:], sc_src[i * 128:(i + 1) * 128, :], writes=[t_sch])
            p.dma("qsp", pkh[:], pk_src[i * 128:(i + 1) * 128, :], writes=[t_pk])
            decode_fp4_l1(c, pkh[:], (sch, t_sch), ds, 1024, [t_pk])
            wd, t_wd = ds["wd"]

            def store(js, bh, bt, wtile=wtile, t_wtile=t_wtile):
                w0, n_ = js[0], len(js)
                p.emit("act", lambda w0=w0, n_=n_, bh=bh: nc.scalar.copy(
                    wtile[:, w0 * 128:(w0 + n_) * 128], bh[:, 0:n_ * 128]),
                    reads=[bt], writes=[t_wtile])
            wd_transposes(wd, t_wd, 16, store)
            for kc in range(NC):
                for tc in range(4):
                    c.mm(tc, (0, 512), wtile[:, kc * 128:(kc + 1) * 128],
                         xnt[:, kc, tc * 512:(tc + 1) * 512], [t_wtile, t_xnt],
                         start=(kc == 0), stop=(kc == NC - 1))
            gsb, t_gsb = gsbs[i % 3]
            usb, t_usb = usbs[i % 3]
            hsb, t_hsb = hsbs[i % 3]
            if gu == 0:
                for tc in range(4):
                    bh, bt = c.banks[tc]
                    p.emit("act", lambda bh=bh, gsb=gsb, tc=tc: nc.scalar.activation(
                        gsb[:, tc * 512:(tc + 1) * 512], bh[:], AF.Gelu_apprx_tanh,
                        bias=zz[:], scale=one[:]),
                        reads=[bt, zzt, onet], writes=[t_gsb])
            else:
                for tc in range(4):
                    bh, bt = c.banks[tc]
                    p.emit("act", lambda bh=bh, usb=usb, tc=tc: nc.scalar.copy(
                        usb[:, tc * 512:(tc + 1) * 512], bh[:]), reads=[bt], writes=[t_usb])
                pend.append((i, gsb, t_gsb, usb, t_usb, hsb, t_hsb))
                if len(pend) >= 3:
                    _flush_hmult(pend.pop(0))
    while pend:
        _flush_hmult(pend.pop(0))
    p.pop_scope()
    if phases == "gu":
        return

    p.push_scope()
    dec2 = decode_bufs(p, f"{pfx}e", 256, 3)
    pks2 = [p.sb(f"{pfx}qk{i}", [128, 1024], I32) for i in range(2)]
    scs2 = [p.sb(f"{pfx}sd{i}", [128, 512], F32) for i in range(2)]
    wdt, _twdt_unused = p.sb(f"{pfx}wdt", [128, 128, 256], BF16)
    t_wdtq = [p.tok(f"{pfx}wdtq{i}") for i in range(4)]
    hch = [p.sb(f"{pfx}hch{i}", [128, 2048], BF16) for i in range(4)]
    ytev, t_ytev = p.sb(f"{pfx}ytev", [128, 4096], BF16)
    ypsb, t_ypsb = p.sb(f"{pfx}ypsb", [128, 512], BF16)

    def emit_dec(ps, q):
        def dve_T(item):
            wd, t_wd, sl8, sub2 = item
            wdv = wd[:].rearrange("p (kc x) -> p kc x", kc=16)
            dst = wdt[:, sl8 * 16:(sl8 + 1) * 16, sub2 * 128:(sub2 + 1) * 128]
            for i in range(4):
                for j in range(4):
                    p.emit("dve", lambda i=i, j=j, wdv=wdv, dst=dst: nc.vector.transpose(
                        dst[32 * i:32 * (i + 1), :, 32 * j:32 * (j + 1)],
                        wdv[32 * j:32 * (j + 1), :, 32 * i:32 * (i + 1)]),
                        reads=[t_wd], writes=[t_wdtq[q]])

        def pe_T(item):
            wd, t_wd, sl8, sub2 = item

            def store(js, bh, bt, sl8=sl8, sub2=sub2):
                ic0, n_ = sl8 * 16 + js[0], len(js)
                p.emit("act", lambda ic0=ic0, n_=n_, bh=bh, sub2=sub2: nc.scalar.copy(
                    wdt[:, ic0:ic0 + n_, sub2 * 128:(sub2 + 1) * 128],
                    bh[:, 0:n_ * 128].rearrange("p (a x) -> p a x", a=n_)),
                    reads=[bt], writes=[t_wdtq[q]])
            wd_transposes(wd, t_wd, 16, store)

        dpend = []
        flush_at = 2 if q < 3 else 3
        do_T = dve_T if q < 3 else pe_T
        for sub2 in range(2):
            hrow = ps * 2 + sub2
            sch, t_sch = scs2[sub2 % 2]
            pkq = "qsp" if ps == 0 else "qpool"
            if q == 0:
                p.dma(pkq, sch[:], dsc_ap[hrow * 128:(hrow + 1) * 128, :], writes=[t_sch])
            for sl8 in range(q * 2, q * 2 + 2):
                gidx = hrow * 8 + sl8
                pkh, t_pk = pks2[gidx % 2]
                ds = dec2[gidx % 3]
                p.dma(pkq, pkh[:],
                      dpk_ap[hrow * 128:(hrow + 1) * 128, sl8 * 1024:(sl8 + 1) * 1024],
                      writes=[t_pk])
                decode_fp4_l1(c, pkh[:], (sch[:, sl8 * 64:(sl8 + 1) * 64], t_sch),
                              ds, 1024, [t_pk])
                wd, t_wd = ds["wd"]
                dpend.append((wd, t_wd, sl8, sub2))
                if len(dpend) >= flush_at:
                    do_T(dpend.pop(0))
        while dpend:
            do_T(dpend.pop(0))

    for q in range(4):
        emit_dec(0, q)
    for ps in range(8):
        for ic in range(128):
            if ic in (32, 64, 96) and ps + 1 < 8:
                emit_dec(ps + 1, ic // 32 - 1)
            hh, t_hh = hch[ic % 4]
            p.dma("qsp" if ic % 2 == 0 else "qact", hh[:], it_ap[ic * 128:(ic + 1) * 128, :],
                  reads=[t_it[ic]], writes=[t_hh])
            for hj in range(2):
                for tc in range(4):
                    c.mm(hj * 4 + tc, (0, 512), wdt[:, ic, hj * 128:(hj + 1) * 128],
                         hh[:, tc * 512:(tc + 1) * 512], [t_wdtq[ic // 32], t_hh],
                         start=(ic == 0), stop=(ic == 127))
        for bi in range(8):
            bh, bt = c.banks[bi]
            p.emit("act", lambda bh=bh, bi=bi: nc.scalar.copy(
                ytev[:, bi * 512:(bi + 1) * 512], bh[:]), reads=[bt], writes=[t_ytev])
        for hj in range(2):
            for q in range(4):
                bi = q % 2
                bh, bt = c.banks[bi]
                for idx in range(4):
                    tt = q * 4 + idx
                    c.transpose(bi, idx * 128,
                                ytev[:, hj * 2048 + tt * 128: hj * 2048 + (tt + 1) * 128],
                                [t_ytev])
                tt0 = q * 4
                p.emit("act", lambda bh=bh: nc.scalar.copy(ypsb[:], bh[:]),
                       reads=[bt], writes=[t_ypsb])
                dst = y_ap[tt0 * 128:(tt0 + 4) * 128, ps * 256 + hj * 128: ps * 256 + (hj + 1) * 128]
                dst = dst.rearrange("(a p) h -> p a h", a=4)
                p.dma("qact", dst, ypsb[:].rearrange("p (a h) -> p a h", a=4), reads=[t_ypsb],
                      writes=[t_y[tt0 + _i] for _i in range(4)])
        if ps + 1 < 8:
            emit_dec(ps + 1, 3)
    p.pop_scope()
    p.push_scope()
    gout, t_gout = p.sb(f"{pfx}gout", [128, E], BF16)
    yts = [p.sb(f"{pfx}yt{i}", [128, E], BF16) for i in range(2)]
    for tt in range(NT):
        yth, t_yth = yts[tt % 2]
        p.dma("qsp", yth[:], y_ap[tt * 128:(tt + 1) * 128, :], reads=[t_y[tt]], writes=[t_yth])
        p.emit("act", lambda yth=yth: nc.scalar.activation(
            gout[:], yth[:], AF.Silu, bias=zz[:], scale=one[:]),
            reads=[t_yth, t_st2 if False else zzt, onet], writes=[t_gout])
        p.dma("qact", g_ap[tt * 128:(tt + 1) * 128, :], gout[:], reads=[t_gout])
    p.pop_scope()


def s_l1_attn(c: C, xnt_ap, wp_ap, wo_ap, s_ap, pfx="a1", dbg=None):
    p, nc = c.p, c.nc
    H, D = 16, 128
    sm = 1.0 / np.sqrt(D)
    zz, zzt = c.const(0.0)
    one, onet = c.const(1.0)
    smc, smct = c.const(sm)
    t_s = [p.tok(f"{pfx}s{i}") for i in range(NT)]

    p.push_scope()
    xnt, t_xnt = p.sb(f"{pfx}xnt", [128, NC, T], BF16)
    for cc in range(NC):
        p.dma("qsp", xnt[:, cc, :], xnt_ap[cc * 128:(cc + 1) * 128, :], writes=[t_xnt])
    oT, t_oT = p.sb(f"{pfx}oT", [128, H, T], BF16)
    p.push_scope()
    wrow = [p.sb(f"{pfx}wr{i}", [128, E], BF16) for i in range(2)]
    wqT, t_wqT = p.sb(f"{pfx}wqT", [128, NC, 128], BF16)
    wkT, t_wkT = p.sb(f"{pfx}wkT", [128, NC, 128], BF16)
    wvT, t_wvT = p.sb(f"{pfx}wvT", [128, NC, 128], BF16)
    qT, t_qT = p.sb(f"{pfx}qT", [128, T], BF16)
    kT, t_kT = p.sb(f"{pfx}kT", [128, T], BF16)
    vtm, t_vtm = p.sb(f"{pfx}vtm", [128, NT, 128], BF16)
    vT, t_vT = p.sb(f"{pfx}vT", [128, T], BF16)
    psls = [(p.sb(f"{pfx}psl{i}", [128, T], BF16)[0],
             [p.tok(f"{pfx}pt{i}_{j}") for j in range(4)]) for i in range(3)]
    lsums = [p.sb(f"{pfx}ls{i}", [128, 12], F32) for i in range(3)]
    diags = [p.sb(f"{pfx}dg{i}", [128, 128], BF16) for i in range(3)]
    pT, t_pT = p.sb(f"{pfx}pT", [128, NT, 512], BF16)

    def load_wT(h, dst, t_dst, row0):
        wr, t_wr = wrow[h % 2]
        p.dma("qpool", wr[:], wp_ap[row0:row0 + 128, :], writes=[t_wr])
        for g4 in range(4):
            bi = g4 % 2
            bh, bt = c.banks[bi]
            for j in range(4):
                cc = g4 * 4 + j
                c.transpose(bi, j * 128, wr[:, cc * 128:(cc + 1) * 128], [t_wr])
            p.emit("act", lambda bh=bh, g4=g4, dst=dst: nc.scalar.copy(
                dst[:, g4 * 4:(g4 + 1) * 4, :].rearrange("p a b -> p (a b)"), bh[:]),
                reads=[bt], writes=[t_dst])

    for h in range(H):
        load_wT(h, wqT, t_wqT, h * 128)
        load_wT(h, wkT, t_wkT, 2048 + h * 128)
        load_wT(h, wvT, t_wvT, 4096 + h * 128)
        for tc in range(4):
            for cc in range(NC):
                c.mm(tc, (0, 512), wqT[:, cc, :], xnt[:, cc, tc * 512:(tc + 1) * 512],
                     [t_wqT, t_xnt], start=(cc == 0), stop=(cc == NC - 1))
            bh, bt = c.banks[tc]
            p.emit("act", lambda bh=bh, tc=tc: nc.scalar.activation(
                qT[:, tc * 512:(tc + 1) * 512], bh[:], AF.Copy, bias=0.0, scale=smc[:]),
                reads=[bt, zzt, smct], writes=[t_qT])
        for tc in range(4):
            for cc in range(NC):
                c.mm(tc, (0, 512), wkT[:, cc, :], xnt[:, cc, tc * 512:(tc + 1) * 512],
                     [t_wkT, t_xnt], start=(cc == 0), stop=(cc == NC - 1))
            bh, bt = c.banks[tc]
            p.emit("act", lambda bh=bh, tc=tc: nc.scalar.copy(
                kT[:, tc * 512:(tc + 1) * 512], bh[:]), reads=[bt], writes=[t_kT])
        for tc in range(4):
            for cc in range(NC):
                c.mm(tc, (0, 512), wvT[:, cc, :], xnt[:, cc, tc * 512:(tc + 1) * 512],
                     [t_wvT, t_xnt], start=(cc == 0), stop=(cc == NC - 1))
            bh, bt = c.banks[tc]
            p.emit("act", lambda bh=bh, tc=tc: nc.scalar.copy(
                vT[:, tc * 512:(tc + 1) * 512], bh[:]), reads=[bt], writes=[t_vT])
        for tt0 in range(0, NT, 4):
            bi = (tt0 // 4) % 2
            bh, bt = c.banks[bi]
            for j in range(4):
                c.transpose(bi, j * 128, vT[:, (tt0 + j) * 128:(tt0 + j + 1) * 128], [t_vT])
            p.emit("act", lambda bh=bh, tt0=tt0: nc.scalar.copy(
                vtm[:, tt0:tt0 + 4, :].rearrange("p a b -> p (a b)"), bh[:, 0:512]),
                reads=[bt], writes=[t_vtm])

        def emit_T(item):
            qt, pslh, ptoks, dgh, t_dgh = item
            qi = qt % 4
            for kt0 in range(0, qt + 1, 4):
                ntl = min(4, qt + 1 - kt0)
                bi = (kt0 // 4) % 2
                bh, bt = c.banks[bi]
                for j in range(ntl):
                    kt = kt0 + j
                    p.emit("pe", lambda bh=bh, j=j, kt=kt, pslh=pslh, dgh=dgh:
                        nc.tensor.matmul(bh[0:128, j * 128:(j + 1) * 128],
                                         pslh[:, kt * 128:(kt + 1) * 128], dgh[:],
                                         start=True, stop=True),
                        reads=[ptoks[kt // 4], t_dgh], writes=[bt])
                if (kt0 // 4) % 2 == 0:
                    p.emit("dve", lambda bh=bh, kt0=kt0, ntl=ntl, qi=qi: nc.vector.tensor_copy(
                        pT[:, kt0:kt0 + ntl, qi * 128:(qi + 1) * 128],
                        bh[:, 0:ntl * 128].rearrange("p (a x) -> p a x", a=ntl)),
                        reads=[bt], writes=[t_pT])
                else:
                    p.emit("act", lambda bh=bh, kt0=kt0, ntl=ntl, qi=qi: nc.scalar.copy(
                        pT[:, kt0:kt0 + ntl, qi * 128:(qi + 1) * 128],
                        bh[:, 0:ntl * 128].rearrange("p (a x) -> p a x", a=ntl)),
                        reads=[bt], writes=[t_pT])

        def emit_O(qg, h):
            qmax = qg * 4 + 3
            bo, bo_t = c.banks[2]
            for kt in range(qmax + 1):
                off = max(0, (kt - qg * 4)) * 128 if kt > qg * 4 else 0
                p.emit("pe", lambda kt=kt, off=off, bo=bo, qmax=qmax: nc.tensor.matmul(
                    bo[0:128, off:512], vtm[:, kt, :], pT[:, kt, off:512],
                    start=(kt == 0), stop=(kt == qmax)),
                    reads=[t_vtm, t_pT], writes=[bo_t])
            p.emit("act", lambda h=h, qg=qg, bo=bo: nc.scalar.copy(
                oT[:, h, qg * 512:(qg + 1) * 512], bo[0:128, 0:512]),
                reads=[bo_t], writes=[t_oT])

        pend_T = []
        for qt in range(NT):
            klen = (qt + 1) * 128
            pslh, ptoks = psls[qt % 3]
            dgh, t_dgh = diags[qt % 3]
            lsh, t_lsh = lsums[qt % 3]
            sc_v, sc_toks = c.scores
            nchunk = (klen + 511) // 512
            mck = qt // 4
            for ck in range(nchunk):
                n = min(512, klen - ck * 512)
                tok = [sc_toks[ck]]
                p.emit("pe", lambda ck=ck, n=n, qt=qt: nc.tensor.matmul(
                    sc_v[:, ck * 512:ck * 512 + n], qT[:, qt * 128:(qt + 1) * 128],
                    kT[:, ck * 512:ck * 512 + n], start=True, stop=True),
                    reads=[t_qT, t_kT], writes=tok)
                if ck == mck:
                    p.emit("dve", lambda qt=qt: nc.vector.tensor_tensor(
                        sc_v[:, qt * 128:(qt + 1) * 128], sc_v[:, qt * 128:(qt + 1) * 128],
                        c.maskc_sb[:], OP.add), reads=tok + [c.t_maskc], writes=tok)
                p.emit("dve", lambda ck=ck, n=n, lsh=lsh: nc.vector.tensor_reduce(
                    lsh[:, 4 + ck:5 + ck], sc_v[:, ck * 512:ck * 512 + n], AX.X, OP.max),
                    reads=tok, writes=[t_lsh])
            p.emit("dve", lambda lsh=lsh, nchunk=nchunk: nc.vector.tensor_reduce(
                lsh[:, 0:1], lsh[:, 4:4 + nchunk], AX.X, OP.max, negate=True),
                reads=[t_lsh], writes=[t_lsh])
            if len(pend_T) >= 2:
                emit_T(pend_T.pop(0))
            for ck in range(nchunk):
                n = min(512, klen - ck * 512)
                tok = [sc_toks[ck]]
                p.emit("act", lambda ck=ck, n=n, pslh=pslh, lsh=lsh: nc.scalar.activation(
                    pslh[:, ck * 512:ck * 512 + n], sc_v[:, ck * 512:ck * 512 + n],
                    AF.Exp, bias=lsh[:, 0:1], scale=one[:], accum_out=lsh[:, 8 + ck:9 + ck]),
                    reads=tok + [t_lsh, onet], writes=[ptoks[ck], t_lsh])
            p.emit("dve", lambda lsh=lsh, nchunk=nchunk: nc.vector.tensor_reduce(
                lsh[:, 1:2], lsh[:, 8:8 + nchunk], AX.X, OP.add), reads=[t_lsh], writes=[t_lsh])
            p.emit("dve", lambda lsh=lsh: nc.vector.reciprocal(lsh[:, 2:3], lsh[:, 1:2]),
                   reads=[t_lsh], writes=[t_lsh])
            p.emit("dve", lambda dgh=dgh, lsh=lsh: nc.vector.tensor_tensor(
                dgh[:], c.ident_sb[:], lsh[:, 2:3].broadcast_to([128, 128]), OP.mult),
                reads=[c.t_ident, t_lsh], writes=[t_dgh])
            pend_T.append((qt, pslh, ptoks, dgh, t_dgh))
            if qt % 4 == 1 and qt >= 5:
                emit_O((qt - 5) // 4, h)
        while pend_T:
            emit_T(pend_T.pop(0))
        emit_O(3, h)
    if dbg is not None:
        p.dma("qsp", dbg["qT"][:, :], qT[:], reads=[t_qT])
        p.dma("qsp", dbg["kT"][:, :], kT[:], reads=[t_kT])
        p.dma("qsp", dbg["v"][:, :], vtm[:].rearrange("p a b -> p (a b)"), reads=[t_vtm])
        p.dma("qsp", dbg["oT"][:, :], oT[:, H - 1, :], reads=[t_oT])
    p.pop_scope()
    wosb = [p.sb(f"{pfx}wo{i}", [128, E], BF16) for i in range(2)]
    woT, t_woT = p.sb(f"{pfx}woT", [128, NC, 512], BF16)
    souts = [p.sb(f"{pfx}so{i}", [128, 512], BF16) for i in range(2)]
    for ec in range(4):
        for ot4 in range(4):
            wr, t_wr = wosb[ot4 % 2]
            p.dma("qpool", wr[:], wo_ap[ec * 512 + ot4 * 128: ec * 512 + (ot4 + 1) * 128, :],
                  writes=[t_wr])
            for g4 in range(4):
                bi = g4 % 2
                bh, bt = c.banks[bi]
                for j in range(4):
                    cc = g4 * 4 + j
                    c.transpose(bi, j * 128, wr[:, cc * 128:(cc + 1) * 128], [t_wr])
                for j in range(4):
                    cc = g4 * 4 + j
                    p.emit("act", lambda bh=bh, j=j, cc=cc, ot4=ot4: nc.scalar.copy(
                        woT[:, cc, ot4 * 128:(ot4 + 1) * 128], bh[:, j * 128:(j + 1) * 128]),
                        reads=[bt], writes=[t_woT])
        for tt in range(NT):
            bi = 4 + tt % 4
            sout, t_sout = souts[tt % 2]
            for hd in range(NC):
                c.mm(bi, (0, 512), oT[:, hd, tt * 128:(tt + 1) * 128], woT[:, hd, :],
                     [t_oT, t_woT], start=(hd == 0), stop=(hd == NC - 1))
            bh, bt = c.banks[bi]
            p.emit("act", lambda bh=bh, sout=sout: nc.scalar.copy(sout[:], bh[:]),
                   reads=[bt], writes=[t_sout])
            p.dma("qact", s_ap[tt * 128:(tt + 1) * 128, ec * 512:(ec + 1) * 512], sout[:],
                  reads=[t_sout], writes=[t_s[tt]])
    p.pop_scope()
    return t_s


def s_l0_attn(c: C, xnt_ap, wqa_ap, qlnw_ap, wqb_ap, wkva_ap, kvlnw_ap, wkvb_ap, wo_ap,
              qn_ap, ot_ap, s_ap, pfx="a0", dbg=None):
    p, nc = c.p, c.nc
    H, DN, DR, DV, R = 16, 128, 64, 128, 512
    sm = 1.0 / np.sqrt(DN + DR)
    zz, zzt = c.const(0.0)
    one, onet = c.const(1.0)
    eb, ebt = c.const(1e-6)
    nh, nht = c.const(-0.5)
    smc, smct = c.const(sm)
    i512, i512t = c.const(1.0 / 512)
    i1536, i1536t = c.const(1.0 / 1536)
    cos_d = nc.dram_tensor("c_cos", [T, 64], F32, kind="ExternalInput").ap()
    sin_d = nc.dram_tensor("c_sin", [T, 64], F32, kind="ExternalInput").ap()
    t_qn = [p.tok(f"{pfx}qn{h}") for h in range(H)]
    t_ot = [p.tok(f"{pfx}ot{h}") for h in range(H)]
    t_s = [p.tok(f"{pfx}s{i}") for i in range(NT)]

    p.push_scope()
    coss, t_cos = p.sb(f"{pfx}cos", [128, NT, 64], F32)
    sins, t_sin = p.sb(f"{pfx}sin", [128, NT, 64], F32)
    for tt in range(NT):
        p.dma("qsp", coss[:, tt, :], cos_d[tt * 128:(tt + 1) * 128, :], writes=[t_cos])
        p.dma("qsp", sins[:, tt, :], sin_d[tt * 128:(tt + 1) * 128, :], writes=[t_sin])
    ckvT, t_ckvT = p.sb(f"{pfx}ckvT", [128, 4, T], BF16)
    kpeT, t_kpeT = p.sb(f"{pfx}kpeT", [128, T], BF16)
    qpeT, t_qpeT = p.sb(f"{pfx}qpeT", [128, 8, T], BF16)

    p.push_scope()
    qanT, t_qanT = p.sb(f"{pfx}qanT", [128, 12, T], BF16)
    p.push_scope()
    xsl = [p.sb(f"{pfx}xsl{i}", [128, NC, 128], BF16) for i in range(2)]
    wkva, t_wkva = p.sb(f"{pfx}wkva", [128, NC, 576], BF16)
    for cc in range(NC):
        p.dma("qpool", wkva[:, cc, :], wkva_ap[cc * 128:(cc + 1) * 128, :], writes=[t_wkva])
    wqa, t_wqa = p.sb(f"{pfx}wqa", [128, NC, 1536], BF16)
    for cc in range(NC):
        p.dma("qpool", wqa[:, cc, :], wqa_ap[cc * 128:(cc + 1) * 128, :], writes=[t_wqa])
    cf32, t_cf32 = p.sb(f"{pfx}cf32", [128, 512], F32)
    qaf, t_qaf = p.sb(f"{pfx}qaf", [128, 1536], F32)
    sqs, t_sqs = p.sb(f"{pfx}sqs", [128, 1536], F32)
    st, t_st = p.sb(f"{pfx}st", [128, 8], F32)
    ckn, t_ckn = p.sb(f"{pfx}ckn", [128, 512], BF16)
    qan, t_qan = p.sb(f"{pfx}qan", [128, 1536], BF16)
    kpe, t_kpe = p.sb(f"{pfx}kpe", [128, 64], F32)
    kper, t_kper = p.sb(f"{pfx}kper", [128, 64], BF16)
    rr1, t_rr1 = p.sb(f"{pfx}rr1", [128, 64], F32)
    rr2, t_rr2 = p.sb(f"{pfx}rr2", [128, 64], F32)
    for tt in range(NT):
        xh, t_xh = xsl[tt % 2]
        src = xnt_ap[:, tt * 128:(tt + 1) * 128].rearrange("(c p) k -> p c k", p=128)
        p.dma("qsp", xh[:], src, writes=[t_xh])
        for cc in range(NC):
            c.mm(0, (0, 512), xh[:, cc, :], wkva[:, cc, 0:512], [t_xh, t_wkva],
                 start=(cc == 0), stop=(cc == NC - 1))
        for cc in range(NC):
            c.mm(1, (0, 64), xh[:, cc, :], wkva[:, cc, 512:576], [t_xh, t_wkva],
                 start=(cc == 0), stop=(cc == NC - 1))
        bh0, bt0 = c.banks[0]
        p.emit("act", lambda bh0=bh0: nc.scalar.copy(cf32[:], bh0[:]), reads=[bt0], writes=[t_cf32])
        p.emit("dve", lambda: nc.vector.tensor_tensor(sqs[:, 0:512], cf32[:], cf32[:], OP.mult),
               reads=[t_cf32], writes=[t_sqs])
        p.emit("dve", lambda: nc.vector.tensor_reduce(st[:, 0:1], sqs[:, 0:512], AX.X, OP.add),
               reads=[t_sqs], writes=[t_st])
        p.emit("act", lambda: nc.scalar.activation(st[:, 1:2], st[:, 0:1], AF.Ln, bias=eb[:], scale=i512[:]),
               reads=[t_st, ebt, i512t], writes=[t_st])
        p.emit("act", lambda: nc.scalar.activation(st[:, 2:3], st[:, 1:2], AF.Exp, bias=zz[:], scale=nh[:]),
               reads=[t_st, nht, zzt], writes=[t_st])
        p.emit("dve", lambda: nc.vector.tensor_scalar(ckn[:], cf32[:], st[:, 2:3], None, OP.mult),
               reads=[t_cf32, t_st], writes=[t_ckn])
        for rc in range(4):
            c.transpose(2, rc * 128, ckn[:, rc * 128:(rc + 1) * 128], [t_ckn])
        bh2, bt2 = c.banks[2]
        for rc in range(4):
            p.emit("act", lambda bh2=bh2, tt=tt, rc=rc: nc.scalar.copy(
                ckvT[:, rc, tt * 128:(tt + 1) * 128], bh2[:, rc * 128:(rc + 1) * 128]),
                reads=[bt2], writes=[t_ckvT])
        bh1, bt1 = c.banks[1]
        p.emit("act", lambda bh1=bh1: nc.scalar.copy(kpe[:], bh1[0:128, 0:64]), reads=[bt1], writes=[t_kpe])
        p.emit("dve", lambda tt=tt: nc.vector.tensor_tensor(
            rr1[:, 0:32], kpe[:, 0:32], coss[:, tt, 0:32], OP.mult), reads=[t_kpe, t_cos], writes=[t_rr1])
        p.emit("dve", lambda tt=tt: nc.vector.tensor_tensor(
            rr2[:, 0:32], kpe[:, 32:64], sins[:, tt, 0:32], OP.mult), reads=[t_kpe, t_sin], writes=[t_rr2])
        p.emit("dve", lambda: nc.vector.tensor_tensor(
            kper[:, 0:32], rr1[:, 0:32], rr2[:, 0:32], OP.subtract), reads=[t_rr1, t_rr2], writes=[t_kper])
        p.emit("dve", lambda tt=tt: nc.vector.tensor_tensor(
            rr1[:, 0:32], kpe[:, 32:64], coss[:, tt, 32:64], OP.mult), reads=[t_kpe, t_cos], writes=[t_rr1])
        p.emit("dve", lambda tt=tt: nc.vector.tensor_tensor(
            rr2[:, 0:32], kpe[:, 0:32], sins[:, tt, 32:64], OP.mult), reads=[t_kpe, t_sin], writes=[t_rr2])
        p.emit("dve", lambda: nc.vector.tensor_tensor(
            kper[:, 32:64], rr1[:, 0:32], rr2[:, 0:32], OP.add), reads=[t_rr1, t_rr2], writes=[t_kper])
        c.transpose(3, 0, kper[:], [t_kper])
        bh3, bt3 = c.banks[3]
        p.emit("act", lambda bh3=bh3, tt=tt: nc.scalar.copy(
            kpeT[0:64, tt * 128:(tt + 1) * 128], bh3[0:64, 0:128]), reads=[bt3], writes=[t_kpeT])
        for oc in range(3):
            for cc in range(NC):
                c.mm(5 + oc, (0, 512), xh[:, cc, :], wqa[:, cc, oc * 512:(oc + 1) * 512],
                     [t_xh, t_wqa], start=(cc == 0), stop=(cc == NC - 1))
            bho, bto = c.banks[5 + oc]
            p.emit("act", lambda bho=bho, oc=oc: nc.scalar.copy(
                qaf[:, oc * 512:(oc + 1) * 512], bho[:]), reads=[bto], writes=[t_qaf])
        p.emit("dve", lambda: nc.vector.tensor_tensor(sqs[:], qaf[:], qaf[:], OP.mult),
               reads=[t_qaf], writes=[t_sqs])
        p.emit("dve", lambda: nc.vector.tensor_reduce(st[:, 4:5], sqs[:], AX.X, OP.add),
               reads=[t_sqs], writes=[t_st])
        p.emit("act", lambda: nc.scalar.activation(st[:, 5:6], st[:, 4:5], AF.Ln, bias=eb[:], scale=i1536[:]),
               reads=[t_st, ebt, i1536t], writes=[t_st])
        p.emit("act", lambda: nc.scalar.activation(st[:, 6:7], st[:, 5:6], AF.Exp, bias=zz[:], scale=nh[:]),
               reads=[t_st, nht, zzt], writes=[t_st])
        p.emit("dve", lambda: nc.vector.tensor_scalar(qan[:], qaf[:], st[:, 6:7], None, OP.mult),
               reads=[t_qaf, t_st], writes=[t_qan])
        for g4 in range(3):
            bh4, bt4 = c.banks[4]
            for j in range(4):
                cc = g4 * 4 + j
                c.transpose(4, j * 128, qan[:, cc * 128:(cc + 1) * 128], [t_qan])
            for j in range(4):
                cc = g4 * 4 + j
                p.emit("act", lambda bh4=bh4, j=j, cc=cc, tt=tt: nc.scalar.copy(
                    qanT[:, cc, tt * 128:(tt + 1) * 128], bh4[:, j * 128:(j + 1) * 128]),
                    reads=[bt4], writes=[t_qanT])
    if dbg is not None:
        p.dma("qsp", dbg["qanT"][:, :], qanT[:].rearrange("p a b -> p (a b)"), reads=[t_qanT])
        p.dma("qsp", dbg["ckvT"][:, :], ckvT[:].rearrange("p a b -> p (a b)"), reads=[t_ckvT])
        p.dma("qsp", dbg["kpeT"][:, :], kpeT[:], reads=[t_kpeT])
    p.pop_scope()

    p.global_barrier()
    p.push_scope()
    lnq, t_lnq = p.sb(f"{pfx}lnq", [128, 12], F32)
    p.dma("qsp", lnq[:], qlnw_ap.rearrange("(cq pq) -> pq cq", pq=128), writes=[t_lnq],
          allow_slow_non_contiguous=True)
    wqbn, t_wqbn = p.sb(f"{pfx}wqbn", [128, 12, 2048], BF16)
    for cq in range(12):
        srcq = wqb_ap[cq * 128:(cq + 1) * 128, :].rearrange("pq (hh dd) -> pq hh dd", hh=16)[:, :, 0:128]
        p.dma("qpool", wqbn[:, cq, :].rearrange("pq (hh dd) -> pq hh dd", hh=16), srcq,
              writes=[t_wqbn])
    for cc in range(12):
        p.emit("dve", lambda cc=cc: nc.vector.tensor_scalar(
            wqbn[:, cc, :], wqbn[:, cc, :], lnq[:, cc:cc + 1], None, OP.mult),
            reads=[t_lnq, t_wqbn], writes=[t_wqbn])
    qnh, t_qnh = p.sb(f"{pfx}qnh", [128, T], BF16)
    if dbg is not None and "wqbn" in dbg:
        p.dma("qsp", dbg["wqbn"][:, :], wqbn[:].rearrange("p a b -> p (a b)"), reads=[t_wqbn])
    for h in range(H):
        for tc in range(T // 512):
            for cc in range(12):
                c.mm(tc, (0, 512), wqbn[:, cc, h * 128:(h + 1) * 128],
                     qanT[:, cc, tc * 512:(tc + 1) * 512], [t_wqbn, t_qanT],
                     start=(cc == 0), stop=(cc == 11))
            bh, bt = c.banks[tc]
            p.emit("act", lambda bh=bh, tc=tc, qnh=qnh: nc.scalar.activation(
                qnh[:, tc * 512:(tc + 1) * 512], bh[:], AF.Copy, bias=0.0, scale=smc[:]),
                reads=[bt, zzt, smct], writes=[t_qnh])
        if dbg is not None and "qnh0" in dbg and h == 0:
            p.dma("qsp", dbg["qnh0"][:, :], qnh[:], reads=[t_qnh])
        p.dma("qact", qn_ap[h * 128:(h + 1) * 128, :], qnh[:], reads=[t_qnh], writes=[t_qn[h]])
    p.pop_scope()
    if dbg is not None and dbg.get("stop_after_b3"):
        p.pop_scope()
        p.pop_scope()
        return t_s

    p.push_scope()
    wqbp, t_wqbp = p.sb(f"{pfx}wqbp", [128, 12, 1024], BF16)
    lnq2, t_lnq2 = p.sb(f"{pfx}lnq2", [128, 12], F32)
    p.dma("qsp", lnq2[:], qlnw_ap.rearrange("(cq pq) -> pq cq", pq=128), writes=[t_lnq2],
          allow_slow_non_contiguous=True)
    for cq in range(12):
        srcq = wqb_ap[cq * 128:(cq + 1) * 128, :].rearrange("pq (hh dd) -> pq hh dd", hh=16)[:, :, 128:192]
        p.dma("qpool", wqbp[:, cq, :].rearrange("pq (hh dd) -> pq hh dd", hh=16), srcq,
              writes=[t_wqbp])
    for cc in range(12):
        p.emit("dve", lambda cc=cc: nc.vector.tensor_scalar(
            wqbp[:, cc, :], wqbp[:, cc, :], lnq2[:, cc:cc + 1], None, OP.mult),
            reads=[t_lnq2, t_wqbp], writes=[t_wqbp])
    qpf, t_qpf = p.sb(f"{pfx}qpf", [128, 1024], BF16)
    qpr, t_qpr = p.sb(f"{pfx}qpr", [128, 1024], BF16)
    qr1, t_qr1 = p.sb(f"{pfx}qr1", [128, 512], F32)
    qr2, t_qr2 = p.sb(f"{pfx}qr2", [128, 512], F32)
    for tt in range(NT):
        for b2 in range(2):
            for cc in range(12):
                c.mm(b2, (0, 512), qanT[:, cc, tt * 128:(tt + 1) * 128],
                     wqbp[:, cc, b2 * 512:(b2 + 1) * 512], [t_qanT, t_wqbp],
                     start=(cc == 0), stop=(cc == 11))
            bh, bt = c.banks[b2]
            p.emit("act", lambda bh=bh, b2=b2: nc.scalar.activation(
                qpf[:, b2 * 512:(b2 + 1) * 512], bh[:], AF.Copy, bias=0.0, scale=smc[:]),
                reads=[bt, zzt, smct], writes=[t_qpf])
        qv = qpf[:].rearrange("p (hh d) -> p hh d", hh=16)
        qo = qpr[:].rearrange("p (hh d) -> p hh d", hh=16)
        cv = coss[:, tt, :].unsqueeze(1).broadcast_to([128, 16, 64])
        sv = sins[:, tt, :].unsqueeze(1).broadcast_to([128, 16, 64])
        r1v = qr1[:].rearrange("p (hh d) -> p hh d", hh=16)
        r2v = qr2[:].rearrange("p (hh d) -> p hh d", hh=16)
        p.emit("dve", lambda r1v=r1v, qv=qv, cv=cv: nc.vector.tensor_tensor(
            r1v[:, :, 0:32], qv[:, :, 0:32], cv[:, :, 0:32], OP.mult), reads=[t_qpf, t_cos], writes=[t_qr1])
        p.emit("dve", lambda r2v=r2v, qv=qv, sv=sv: nc.vector.tensor_tensor(
            r2v[:, :, 0:32], qv[:, :, 32:64], sv[:, :, 0:32], OP.mult), reads=[t_qpf, t_sin], writes=[t_qr2])
        p.emit("dve", lambda qo=qo, r1v=r1v, r2v=r2v: nc.vector.tensor_tensor(
            qo[:, :, 0:32], r1v[:, :, 0:32], r2v[:, :, 0:32], OP.subtract),
            reads=[t_qr1, t_qr2], writes=[t_qpr])
        p.emit("dve", lambda r1v=r1v, qv=qv, cv=cv: nc.vector.tensor_tensor(
            r1v[:, :, 0:32], qv[:, :, 32:64], cv[:, :, 32:64], OP.mult), reads=[t_qpf, t_cos], writes=[t_qr1])
        p.emit("dve", lambda r2v=r2v, qv=qv, sv=sv: nc.vector.tensor_tensor(
            r2v[:, :, 0:32], qv[:, :, 0:32], sv[:, :, 32:64], OP.mult), reads=[t_qpf, t_sin], writes=[t_qr2])
        p.emit("dve", lambda qo=qo, r1v=r1v, r2v=r2v: nc.vector.tensor_tensor(
            qo[:, :, 32:64], r1v[:, :, 0:32], r2v[:, :, 0:32], OP.add),
            reads=[t_qr1, t_qr2], writes=[t_qpr])
        for h in range(H):
            bi = 2 + h % 2
            bh, bt = c.banks[bi]
            c.transpose(bi, 0, qpr[:, h * 64:(h + 1) * 64], [t_qpr])
            b0 = (h % 2) * 64
            p.emit("act", lambda bh=bh, h=h, tt=tt, b0=b0: nc.scalar.copy(
                qpeT[b0:b0 + 64, h // 2, tt * 128:(tt + 1) * 128],
                bh[0:64, 0:128]), reads=[bt], writes=[t_qpeT])
    p.pop_scope()
    p.pop_scope()
    if dbg is not None and dbg.get("stop_after_b2"):
        p.pop_scope()
        return t_s

    p.push_scope()
    lnkv, t_lnkv = p.sb(f"{pfx}lnkv", [128, 4], F32)
    p.dma("qsp", lnkv[:], kvlnw_ap.rearrange("(cq pq) -> pq cq", pq=128), writes=[t_lnkv],
          allow_slow_non_contiguous=True)
    wkvb, t_wkvb = p.sb(f"{pfx}wkvb", [128, 4, 4096], BF16)
    for rc in range(4):
        p.dma("qpool", wkvb[:, rc, :], wkvb_ap[rc * 128:(rc + 1) * 128, :], writes=[t_wkvb])
    for rc in range(4):
        p.emit("dve", lambda rc=rc: nc.vector.tensor_scalar(
            wkvb[:, rc, :], wkvb[:, rc, :], lnkv[:, rc:rc + 1], None, OP.mult),
            reads=[t_lnkv, t_wkvb], writes=[t_wkvb])
    kabsT, t_kabsT = p.sb(f"{pfx}kabsT", [128, T], BF16)
    vT, t_vT = p.sb(f"{pfx}vT", [128, T], BF16)
    vtm, t_vtm = p.sb(f"{pfx}vtm", [128, NT, 128], BF16)
    qnhs = [p.sb(f"{pfx}qnh{i}", [128, T], BF16) for i in range(2)]
    psls = [(p.sb(f"{pfx}psl{i}", [128, T], BF16)[0],
             [p.tok(f"{pfx}pt{i}_{j}") for j in range(4)]) for i in range(3)]
    lsums = [p.sb(f"{pfx}ls{i}", [128, 12], F32) for i in range(3)]
    diags = [p.sb(f"{pfx}dg{i}", [128, 128], BF16) for i in range(3)]
    pT, t_pT = p.sb(f"{pfx}pT", [128, NT, 512], BF16)
    oThs = [p.sb(f"{pfx}oTh{i}", [128, 512], BF16) for i in range(2)]
    qpes, t_qpes = p.sb(f"{pfx}qpes", [128, T], BF16)
    for h in range(H):
        qnhd, t_qnhd = qnhs[h % 2]
        p.dma("qsp", qnhd[:], qn_ap[h * 128:(h + 1) * 128, :], reads=[t_qn[h]], writes=[t_qnhd])
        b0h = (h % 2) * 64
        if dbg is None or not dbg.get("no_qpes"):
            p.dma("qsp", qpes[0:64, :], qpeT[b0h:b0h + 64, h // 2, :], reads=[t_qpeT], writes=[t_qpes])
        for tc in range(T // 512):
            for rc in range(4):
                c.mm(tc, (0, 512), wkvb[:, rc, h * 256:h * 256 + 128],
                     ckvT[:, rc, tc * 512:(tc + 1) * 512], [t_wkvb, t_ckvT],
                     start=(rc == 0), stop=(rc == 3))
            bh, bt = c.banks[tc]
            p.emit("act", lambda bh=bh, tc=tc: nc.scalar.copy(
                kabsT[:, tc * 512:(tc + 1) * 512], bh[:]), reads=[bt], writes=[t_kabsT])
        for tc in range(T // 512):
            for rc in range(4):
                c.mm(tc, (0, 512), wkvb[:, rc, h * 256 + 128:h * 256 + 256],
                     ckvT[:, rc, tc * 512:(tc + 1) * 512], [t_wkvb, t_ckvT],
                     start=(rc == 0), stop=(rc == 3))
            bh, bt = c.banks[tc]
            p.emit("act", lambda bh=bh, tc=tc: nc.scalar.copy(
                vT[:, tc * 512:(tc + 1) * 512], bh[:]), reads=[bt], writes=[t_vT])
        for tt0 in range(0, NT, 4):
            bi = (tt0 // 4) % 2
            bh, bt = c.banks[bi]
            for j in range(4):
                c.transpose(bi, j * 128, vT[:, (tt0 + j) * 128:(tt0 + j + 1) * 128], [t_vT])
            p.emit("act", lambda bh=bh, tt0=tt0: nc.scalar.copy(
                vtm[:, tt0:tt0 + 4, :].rearrange("p a b -> p (a b)"), bh[:, 0:512]),
                reads=[bt], writes=[t_vtm])

        def emit_T(item):
            qt, pslh, ptoks, dgh, t_dgh = item
            qi = qt % 4
            for kt0 in range(0, qt + 1, 4):
                ntl = min(4, qt + 1 - kt0)
                bi = (kt0 // 4) % 2
                bh, bt = c.banks[bi]
                for j in range(ntl):
                    kt = kt0 + j
                    p.emit("pe", lambda bh=bh, j=j, kt=kt, pslh=pslh, dgh=dgh:
                        nc.tensor.matmul(bh[0:128, j * 128:(j + 1) * 128],
                                         pslh[:, kt * 128:(kt + 1) * 128], dgh[:],
                                         start=True, stop=True),
                        reads=[ptoks[kt // 4], t_dgh], writes=[bt])
                if (kt0 // 4) % 2 == 0:
                    p.emit("dve", lambda bh=bh, kt0=kt0, ntl=ntl, qi=qi: nc.vector.tensor_copy(
                        pT[:, kt0:kt0 + ntl, qi * 128:(qi + 1) * 128],
                        bh[:, 0:ntl * 128].rearrange("p (a x) -> p a x", a=ntl)),
                        reads=[bt], writes=[t_pT])
                else:
                    p.emit("act", lambda bh=bh, kt0=kt0, ntl=ntl, qi=qi: nc.scalar.copy(
                        pT[:, kt0:kt0 + ntl, qi * 128:(qi + 1) * 128],
                        bh[:, 0:ntl * 128].rearrange("p (a x) -> p a x", a=ntl)),
                        reads=[bt], writes=[t_pT])

        def emit_O(qg, h):
            qmax = qg * 4 + 3
            bo, bo_t = c.banks[2]
            oTh, t_oTh = oThs[qg % 2]
            for kt in range(qmax + 1):
                off = max(0, kt - qg * 4) * 128
                p.emit("pe", lambda kt=kt, off=off, bo=bo, qmax=qmax: nc.tensor.matmul(
                    bo[0:128, off:512], vtm[:, kt, :], pT[:, kt, off:512],
                    start=(kt == 0), stop=(kt == qmax)), reads=[t_vtm, t_pT], writes=[bo_t])
            p.emit("act", lambda bo=bo, oTh=oTh: nc.scalar.copy(oTh[:], bo[0:128, 0:512]),
                   reads=[bo_t], writes=[t_oTh])
            if dbg is None or not dbg.get("no_ot_write"):
                p.dma("qact", ot_ap[h * 128:(h + 1) * 128, qg * 512:(qg + 1) * 512], oTh[:],
                      reads=[t_oTh], writes=[t_ot[h]])

        pend_T = []
        for qt in range(NT):
            klen = (qt + 1) * 128
            pslh, ptoks = psls[qt % 3]
            dgh, t_dgh = diags[qt % 3]
            lsh, t_lsh = lsums[qt % 3]
            sc_v, sc_toks = c.scores
            nchunk = (klen + 511) // 512
            mck = qt // 4
            for ck in range(nchunk):
                n = min(512, klen - ck * 512)
                tok = [sc_toks[ck]]
                p.emit("pe", lambda ck=ck, n=n, qt=qt, qnhd=qnhd: nc.tensor.matmul(
                    sc_v[:, ck * 512:ck * 512 + n], qnhd[:, qt * 128:(qt + 1) * 128],
                    kabsT[:, ck * 512:ck * 512 + n], start=True, stop=False),
                    reads=[t_qnhd, t_kabsT], writes=tok)
                p.emit("pe", lambda ck=ck, n=n, qt=qt: nc.tensor.matmul(
                    sc_v[:, ck * 512:ck * 512 + n],
                    qpes[0:64, qt * 128:(qt + 1) * 128],
                    kpeT[0:64, ck * 512:ck * 512 + n], start=False, stop=True),
                    reads=[t_qpes, t_kpeT], writes=tok)
                if ck == mck:
                    p.emit("dve", lambda qt=qt: nc.vector.tensor_tensor(
                        sc_v[:, qt * 128:(qt + 1) * 128], sc_v[:, qt * 128:(qt + 1) * 128],
                        c.maskc_sb[:], OP.add), reads=tok + [c.t_maskc], writes=tok)
                p.emit("dve", lambda ck=ck, n=n, lsh=lsh: nc.vector.tensor_reduce(
                    lsh[:, 4 + ck:5 + ck], sc_v[:, ck * 512:ck * 512 + n], AX.X, OP.max),
                    reads=tok, writes=[t_lsh])
            p.emit("dve", lambda lsh=lsh, nchunk=nchunk: nc.vector.tensor_reduce(
                lsh[:, 0:1], lsh[:, 4:4 + nchunk], AX.X, OP.max, negate=True),
                reads=[t_lsh], writes=[t_lsh])
            if len(pend_T) >= 2:
                emit_T(pend_T.pop(0))
            for ck in range(nchunk):
                n = min(512, klen - ck * 512)
                tok = [sc_toks[ck]]
                p.emit("act", lambda ck=ck, n=n, pslh=pslh, lsh=lsh: nc.scalar.activation(
                    pslh[:, ck * 512:ck * 512 + n], sc_v[:, ck * 512:ck * 512 + n],
                    AF.Exp, bias=lsh[:, 0:1], scale=one[:], accum_out=lsh[:, 8 + ck:9 + ck]),
                    reads=tok + [t_lsh, onet], writes=[ptoks[ck], t_lsh])
            p.emit("dve", lambda lsh=lsh, nchunk=nchunk: nc.vector.tensor_reduce(
                lsh[:, 1:2], lsh[:, 8:8 + nchunk], AX.X, OP.add), reads=[t_lsh], writes=[t_lsh])
            p.emit("dve", lambda lsh=lsh: nc.vector.reciprocal(lsh[:, 2:3], lsh[:, 1:2]),
                   reads=[t_lsh], writes=[t_lsh])
            p.emit("dve", lambda dgh=dgh, lsh=lsh: nc.vector.tensor_tensor(
                dgh[:], c.ident_sb[:], lsh[:, 2:3].broadcast_to([128, 128]), OP.mult),
                reads=[c.t_ident, t_lsh], writes=[t_dgh])
            pend_T.append((qt, pslh, ptoks, dgh, t_dgh))
            if qt % 4 == 1 and qt >= 5:
                emit_O((qt - 5) // 4, h)
        while pend_T:
            emit_T(pend_T.pop(0))
        emit_O(3, h)
    p.pop_scope()
    if dbg is not None and dbg.get("stop_after_d"):
        p.pop_scope()
        return t_s

    p.push_scope()
    wos = [p.sb(f"{pfx}wo{i}", [128, NC, 512], BF16) for i in range(1)]
    oslab = [p.sb(f"{pfx}osl{i}", [128, NC, 128], BF16) for i in range(2)]
    sout, t_sout = p.sb(f"{pfx}so", [128, 512], BF16)
    wo_h, t_wo = wos[0]
    for ec in range(4):
        for cc in range(NC):
            p.dma("qpool", wo_h[:, cc, :], wo_ap[cc * 128:(cc + 1) * 128, ec * 512:(ec + 1) * 512],
                  writes=[t_wo])
        for tt in range(NT):
            osl, t_osl = oslab[tt % 2]
            if ec == 0 or True:
                srco = ot_ap[:, tt * 128:(tt + 1) * 128].rearrange("(hh pp) k -> pp hh k", pp=128)
                p.dma("qsp", osl[:], srco, reads=t_ot, writes=[t_osl])
            bi = 4 + tt % 4
            for hd in range(NC):
                c.mm(bi, (0, 512), osl[:, hd, :], wo_h[:, hd, :], [t_osl, t_wo],
                     start=(hd == 0), stop=(hd == NC - 1))
            bh, bt = c.banks[bi]
            p.emit("act", lambda bh=bh: nc.scalar.copy(sout[:], bh[:]), reads=[bt], writes=[t_sout])
            p.dma("qact", s_ap[tt * 128:(tt + 1) * 128, ec * 512:(ec + 1) * 512], sout[:],
                  reads=[t_sout], writes=[t_s[tt]])
    p.pop_scope()
    p.pop_scope()
    return t_s


def s_combine(c: C, x_ap, s_ap, g_ap, xout_ap, pfx, out_f32=True):
    p, nc = c.p, c.nc
    p.push_scope()
    xs = [p.sb(f"{pfx}x{i}", [128, E], F32) for i in range(2)]
    ss = [p.sb(f"{pfx}s{i}", [128, E], BF16) for i in range(2)]
    gs = [p.sb(f"{pfx}g{i}", [128, E], BF16) for i in range(2)]
    sg, t_sg = p.sb(f"{pfx}sg", [128, E], F32)
    xo = [p.sb(f"{pfx}o{i}", [128, E], F32) for i in range(2)]
    for t in range(NT):
        xh, xt = xs[t % 2]
        sh, st_ = ss[t % 2]
        gh, gt = gs[t % 2]
        oh, ot_ = xo[t % 2]
        p.dma("qsp", xh[:], x_ap[t * 128:(t + 1) * 128, :], writes=[xt])
        p.dma("qsp", sh[:], s_ap[t * 128:(t + 1) * 128, :], writes=[st_])
        p.dma("qsp", gh[:], g_ap[t * 128:(t + 1) * 128, :], writes=[gt])
        p.emit("dve", lambda sh=sh, gh=gh: nc.vector.tensor_tensor(sg[:], sh[:], gh[:], OP.mult),
               reads=[st_, gt], writes=[t_sg])
        p.emit("dve", lambda xh=xh, oh=oh: nc.vector.tensor_tensor(oh[:], xh[:], sg[:], OP.add),
               reads=[xt, t_sg], writes=[ot_])
        p.dma("qsp", xout_ap[t * 128:(t + 1) * 128, :], oh[:], reads=[ot_])
    p.pop_scope()



import contextlib as _ctxlib
import numpy as _np
import ml_dtypes as _mld
from concourse.bass_utils import run_bass_kernel_spmd as _run_spmd

_BF = _mld.bfloat16
_T = 2048
_E = 2048
_NCORES = 8
MODE = "single"


def _host_consts():
    DR = 64
    inv = 1.0 / 10000.0 ** (_np.arange(0, DR, 2, dtype=_np.float32) / DR)
    ang = _np.arange(_T, dtype=_np.float32)[:, None] * inv[None, :]
    cos = _np.concatenate([_np.cos(ang), _np.cos(ang)], -1).astype(_np.float32)
    sin = _np.concatenate([_np.sin(ang), _np.sin(ang)], -1).astype(_np.float32)
    return {
        "c_ident": _np.eye(128, dtype=_np.float32),
        "c_mask": _np.where(_np.triu(_np.ones((128, 128), bool), k=1), -1e30, 0.0).astype(_np.float32),
        "c_cos": cos, "c_sin": sin,
    }


def _tof(a):
    a = _np.asarray(a)
    return a.view(_BF) if a.dtype.kind == "V" else a


def _decl(nc, name, shape, dt, kind):
    return nc.dram_tensor(name, list(shape), dt, kind=kind).ap()


class _Prog:

    def __init__(self, name, ins, outs, internals, build_fn):
        self.name = name
        self.ins = ins
        self.outs = outs
        self.internals = internals
        self.build_fn = build_fn

    def build(self):
        nc = make_nc()
        ctx = _ctxlib.ExitStack()
        aps = {}
        with ctx:
            p = P(nc, ctx)
            c = C(p)
            for key, shape, dt in self.ins:
                if key in ("c_cos", "c_sin"):
                    continue
                aps[key] = _decl(nc, key, shape, dt, "ExternalInput")
            for key, shape, dt in self.outs:
                aps[key] = _decl(nc, key, shape, dt, "ExternalOutput")
            for key, shape, dt in self.internals:
                aps[key] = _decl(nc, key, shape, dt, "Internal")
            self.build_fn(c, aps)
            p.finish()
            block = ctx.enter_context(nc.Block())
            p.finalize_into_block(block)
        return nc


def _b_ln0(c, a):
    s_ln(c, a["x"], a["xnt0"], "n0_")


def _b_l0attn(c, a):
    s_l0_attn(c, a["xnt0"], a["L0_attn_w_q_a"], a["L0_attn_q_a_ln_w"], a["L0_attn_w_q_b"],
              a["L0_attn_w_kv_a"], a["L0_attn_kv_a_ln_w"], a["L0_attn_w_kv_b"],
              a["L0_attn_w_o"], a["qn0"], a["ot0"], a["s0"], pfx="a0")


def _b_l0ffn(c, a):
    s_l0_ffn(c, a["xnt0"], a["L0_ffn_gate_up_packed"], a["L0_ffn_gate_up_scales"],
             a["L0_ffn_down_packed"], a["L0_ffn_down_scales"], a["ht0"], a["y0"], a["g0"], pfx="f0")


def _b_comb0(c, a):
    s_combine(c, a["x"], a["s0"], a["g0"], a["x1"], "c0_")
    c.p.global_barrier()
    s_ln(c, a["x1"], a["xnt1"], "n1_")


def _b_l1attn(c, a):
    s_l1_attn(c, a["xnt1"], a["L1_attn_w_project"], a["L1_attn_w_out"], a["s1"], pfx="a1")


def _b_l1ffn(c, a):
    s_l1_ffn(c, a["xnt1"], a["L1_ffn_gate_W_packed"], a["L1_ffn_gate_scales"],
             a["L1_ffn_up_W_packed"], a["L1_ffn_up_scales"], a["L1_ffn_down_W_packed"],
             a["L1_ffn_down_scales"], a["it1"], a["y1"], a["g1"], pfx="f1")


def _b_comb1(c, a):
    s_combine(c, a["x1"], a["s1"], a["g1"], a["out"], "c1_")


_W0A = [("L0_attn_w_q_a", (2048, 1536), F32), ("L0_attn_q_a_ln_w", (1536,), F32),
        ("L0_attn_w_q_b", (1536, 3072), F32), ("L0_attn_w_kv_a", (2048, 576), F32),
        ("L0_attn_kv_a_ln_w", (512,), F32), ("L0_attn_w_kv_b", (512, 4096), F32),
        ("L0_attn_w_o", (2048, 2048), F32)]
_W0F = [("L0_ffn_gate_up_packed", (11264, 256), I32), ("L0_ffn_gate_up_scales", (16, 11264), F16),
        ("L0_ffn_down_packed", (2048, 704), I32), ("L0_ffn_down_scales", (44, 2048), F16)]
_W1A = [("L1_attn_w_project", (6144, 2048), F32), ("L1_attn_w_out", (2048, 2048), F32)]
_W1F = [("L1_ffn_gate_W_packed", (16384, 1024), I32), ("L1_ffn_gate_scales", (16384, 64), F32),
        ("L1_ffn_up_W_packed", (16384, 1024), I32), ("L1_ffn_up_scales", (16384, 64), F32),
        ("L1_ffn_down_W_packed", (2048, 8192), I32), ("L1_ffn_down_scales", (2048, 512), F32)]

_XNT = lambda n: (n, (_E, _T), BF16)
_TME = lambda n, dt=BF16: (n, (_T, _E), dt)

_LAUNCHES = [
    _Prog("ln0", [("x", (_T, _E), F32)], [_XNT("xnt0")], [], _b_ln0),
    _Prog("l0attn", [_XNT("xnt0")] + _W0A + [("c_cos", (_T, 64), F32), ("c_sin", (_T, 64), F32)],
          [_TME("s0")], [("qn0", (2048, _T), BF16), ("ot0", (2048, _T), BF16)], _b_l0attn),
    _Prog("l0ffn", [_XNT("xnt0")] + _W0F, [_TME("g0")],
          [("ht0", (5632, _T), BF16), ("y0", (_T, _E), BF16)], _b_l0ffn),
    _Prog("comb0", [("x", (_T, _E), F32), _TME("s0"), _TME("g0")],
          [("x1", (_T, _E), F32), _XNT("xnt1")], [], _b_comb0),
    _Prog("l1attn", [_XNT("xnt1")] + _W1A, [_TME("s1")], [], _b_l1attn),
    _Prog("l1ffn", [_XNT("xnt1")] + _W1F, [_TME("g1")],
          [("it1", (16384, _T), BF16), ("y1", (_T, _E), BF16)], _b_l1ffn),
    _Prog("comb1", [("x1", (_T, _E), F32), _TME("s1"), _TME("g1")],
          [("out", (_T, _E), F32)], [], _b_comb1),
]


def _build_single():
    nc = make_nc()
    ctx = _ctxlib.ExitStack()
    a = {}
    with ctx:
        p = P(nc, ctx)
        c = C(p)
        a["x"] = _decl(nc, "x", (_T, _E), F32, "ExternalInput")
        for key, shape, dt in _W0A + _W0F + _W1A + _W1F:
            a[key] = _decl(nc, key, shape, dt, "ExternalInput")
        a["out"] = _decl(nc, "out", (_T, _E), F32, "ExternalOutput")
        for key, shape, dt in [_XNT("xnt0"), _XNT("xnt1"), _TME("s0"), _TME("g0"),
                               ("x1", (_T, _E), F32), _TME("s1"), _TME("g1"),
                               ("qn0", (2048, _T), BF16), ("ot0", (2048, _T), BF16),
                               ("ht0", (5632, _T), BF16), ("y0", (_T, _E), BF16),
                               ("it1", (16384, _T), BF16), ("y1", (_T, _E), BF16)]:
            a[key] = _decl(nc, key, shape, dt, "Internal")
        _b_ln0(c, a)
        p.global_barrier()
        _b_l0attn(c, a)
        p.global_barrier()
        _b_l0ffn(c, a)
        p.global_barrier()
        _b_comb0(c, a)
        p.global_barrier()
        _b_l1attn(c, a)
        p.global_barrier()
        _b_l1ffn(c, a)
        p.global_barrier()
        _b_comb1(c, a)
        p.finish()
        block = ctx.enter_context(nc.Block())
        p.finalize_into_block(block)
    return nc


def _kernel_single(inputs):
    consts = _host_consts()
    x_full = _np.ascontiguousarray(inputs["L0_attn_x"].astype(_np.float32))
    nc = _build_single()
    in_maps = []
    for i in range(_NCORES):
        m = {"c_ident": consts["c_ident"], "c_mask": consts["c_mask"],
             "c_cos": consts["c_cos"], "c_sin": consts["c_sin"],
             "x": _np.ascontiguousarray(x_full[i])}
        for key, shape, dt in _W0A + _W0F + _W1A + _W1F:
            m[key] = _np.ascontiguousarray(inputs[key])
        in_maps.append(m)
    res = _run_spmd(nc, in_maps, core_ids=list(range(_NCORES)))
    return _np.stack([_np.asarray(res.results[i]["out"]).astype(_np.float32)
                      for i in range(_NCORES)], axis=0)


def kernel(**inputs):
    if MODE == "single":
        return _kernel_single(inputs)
    consts = _host_consts()
    x_full = _np.ascontiguousarray(inputs["L0_attn_x"].astype(_np.float32))
    percore = {i: {"x": _np.ascontiguousarray(x_full[i])} for i in range(_NCORES)}
    for lp in _LAUNCHES:
        nc = lp.build()
        in_maps = []
        for i in range(_NCORES):
            m = {"c_ident": consts["c_ident"], "c_mask": consts["c_mask"]}
            for key, shape, dt in lp.ins:
                if key in ("c_cos", "c_sin"):
                    m[key] = consts[key]
                elif key in inputs:
                    m[key] = _np.ascontiguousarray(inputs[key])
                else:
                    m[key] = _np.ascontiguousarray(percore[i][key])
            in_maps.append(m)
        res = _run_spmd(nc, in_maps, core_ids=list(range(_NCORES)))
        for i in range(_NCORES):
            for key, shape, dt in lp.outs:
                percore[i][key] = _tof(res.results[i][key])
    out = _np.stack([percore[i]["out"].astype(_np.float32) for i in range(_NCORES)], axis=0)
    return out



# version 10
# speedup vs baseline: 1.2848x; 1.0005x over previous
import sys
sys.path.insert(0, "/opt/trn_rl_repo")
import contextlib
import numpy as np
import concourse.bass as bass
import concourse.mybir as mybir

F32 = mybir.dt.float32
BF16 = mybir.dt.bfloat16
F16 = mybir.dt.float16
I32 = mybir.dt.int32
I16 = mybir.dt.int16
AF = mybir.ActivationFunctionType
AX = mybir.AxisListType
OP = mybir.AluOpType

SECTION_OF = {"pe": "tensor", "dve": "vector", "act": "scalar", "pool": "gpsimd",
              "qsp": "sync", "qact": "scalar", "qpool": "gpsimd"}
for _q in ("qsp", "qact", "qpool"):
    for _i in range(8):
        SECTION_OF[f"{_q}{_i}"] = SECTION_OF[_q]


class Tok:
    __slots__ = ("name", "writer", "readers")

    def __init__(self, name=""):
        self.name = name
        self.writer = None
        self.readers = []


class _Eng:
    __slots__ = ("name", "sem", "seq", "cells", "step", "last_real", "prefix")

    def __init__(self, name, sem, step):
        self.name = name
        self.sem = sem
        self.seq = 0
        self.cells = []
        self.step = step
        self.last_real = -1
        self.prefix = None


class P:
    def __init__(self, nc, ctx):
        self.nc = nc
        self.ctx = ctx
        self._scopes = []
        self.thunks = {s: [] for s in ("tensor", "vector", "scalar", "gpsimd", "sync")}
        self.eng = {}
        self.sem_tok = {}
        self.qslots = {}
        for name in ("pe", "dve", "act", "pool"):
            sem = ctx.enter_context(nc.semaphore(f"s_{name}"))
            self.eng[name] = _Eng(name, sem, 1)
        for q in ("qsp", "qact", "qpool"):
            self.qslots[q] = []
            for i in range(8):
                name = f"{q}{i}"
                sem = ctx.enter_context(nc.semaphore(f"s_{name}"))
                self.eng[name] = _Eng(name, sem, 16)
                self.sem_tok[name] = Tok(f"sem_{name}")
                self.qslots[q].append(name)
        self._qrr = {q: 0 for q in ("qsp", "qact", "qpool")}
        self.waited = {}
        self._sbuf_bytes = 0

    def tok(self, name=""):
        return Tok(name)

    def sb(self, name, shape, dtype, root=False):
        ctx = self.ctx if (root or not self._scopes) else self._scopes[-1]
        h = ctx.enter_context(self.nc.sbuf_tensor(name, shape, dtype))
        nb = int(np.prod(shape[1:])) * mybir.dt.size(dtype)
        self._sbuf_bytes += nb
        return h, Tok(name)

    def ps(self, name, shape):
        h = self.ctx.enter_context(self.nc.psum_tensor(name, shape, F32))
        return h, Tok(name)

    def push_scope(self):
        self._scopes.append(contextlib.ExitStack())

    def pop_scope(self):
        self.global_barrier()
        self._scopes.pop().close()

    def _close(self, e, seq):
        cell = e.cells[seq]
        if cell["inc"] == 0:
            assert not cell.get("noinst"), f"close({e.name},{seq}) targets wait-only op"
            cell["inc"] = e.step
        return seq

    def emit(self, eng_name, fn, reads=(), writes=(), close=False, tag=""):
        e = self.eng[eng_name]
        seq_self = e.seq
        section = SECTION_OF[eng_name]
        need = {}
        for t in reads:
            if t.writer is not None:
                pe_, sq = t.writer
                need[pe_] = max(need.get(pe_, -1), sq)
        for t in writes:
            if t.writer is not None:
                pe_, sq = t.writer
                need[pe_] = max(need.get(pe_, -1), sq)
            for pe_, sq in t.readers:
                need[pe_] = max(need.get(pe_, -1), sq)
        waits = []
        for pe_, sq in need.items():
            key = (section, pe_)
            if self.waited.get(key, -1) >= sq:
                continue
            self.waited[key] = sq
            self._close(self.eng[pe_], sq)
            waits.append((self.eng[pe_], sq))
            st = self.sem_tok.get(pe_)
            if st is not None:
                st.readers.append((eng_name, seq_self))
        e.seq += 1
        cell = {"inc": 0}
        if fn is None:
            cell["noinst"] = True
        e.cells.append(cell)
        if fn is not None:
            e.last_real = seq_self

        def thunk(nc=self.nc, waits=waits, fn=fn, cell=cell, sem=e.sem,
                  eng_obj_name=section):
            eng = getattr(nc, eng_obj_name)
            for pe_obj, sq in waits:
                eng.wait_ge(pe_obj.sem, pe_obj.prefix[sq])
            if fn is not None:
                instr = fn()
                if cell["inc"]:
                    instr.then_inc(sem, cell["inc"])

        self.thunks[section].append(thunk)
        if fn is not None:
            for t in reads:
                t.readers.append((eng_name, seq_self))
            for t in writes:
                t.writer = (eng_name, seq_self)
                t.readers = []
        if close:
            self._close(e, seq_self)
        return seq_self

    def global_barrier(self):
        names = ["pe", "dve", "act", "pool"] + [s for q in self.qslots.values() for s in q]
        vals = {}
        for n in names:
            e = self.eng[n]
            if e.last_real >= 0:
                vals[n] = self._close(e, e.last_real)
        for sec in ("tensor", "vector", "scalar", "gpsimd", "sync"):
            waits = []
            for n, sq in vals.items():
                key = (sec, n)
                if self.waited.get(key, -1) >= sq:
                    continue
                self.waited[key] = sq
                waits.append((self.eng[n], sq))
            if not waits:
                continue

            def thunk(nc=self.nc, waits=tuple(waits), obj=sec):
                eng = getattr(nc, obj)
                for pe_obj, sq in waits:
                    eng.wait_ge(pe_obj.sem, pe_obj.prefix[sq])
            self.thunks[sec].append(thunk)

    def barrier_wait(self, eng_name, toks):
        self.emit(eng_name, None, reads=toks)

    def finish(self):
        for q in ("qsp", "qact", "qpool"):
            obj = SECTION_OF[q]
            for name in self.qslots[q]:
                e = self.eng[name]
                if e.last_real < 0:
                    continue
                self._close(e, e.last_real)

                def thunk(nc=self.nc, e=e, obj=obj):
                    if e.prefix and e.prefix[-1] > 0:
                        getattr(nc, obj).wait_ge(e.sem, e.prefix[-1])
                self.thunks[obj].append(thunk)

    def finalize_into_block(self, block):
        for e in self.eng.values():
            run = 0
            pref = []
            for cell in e.cells:
                run += cell["inc"]
                pref.append(run)
            e.prefix = pref
        for section in ("sync", "gpsimd", "tensor", "vector", "scalar"):
            ths = self.thunks[section]
            if not ths:
                continue
            deco = getattr(block, section)

            def body(engine, ths=ths):
                for t in ths:
                    t()
            deco(body)

    def dma(self, queue, out_ap, in_ap, reads=(), writes=(), **kw):
        slot = self.qslots[queue][self._qrr[queue] % len(self.qslots[queue])]
        self._qrr[queue] += 1
        eng_obj = SECTION_OF[queue]
        return self.emit(slot, lambda: getattr(self.nc, eng_obj).dma_start(out=out_ap, in_=in_ap, **kw),
                         reads=list(reads),
                         writes=list(writes) + [self.sem_tok[slot]], close=True)


def make_nc():
    return bass.Bass("TRN2", target_bir_lowering=False, debug=False)


"""Stage builders for the mk_recurrent_d2 megakernel. All per-core (one batch elem).

Layouts:
 - activations token-major [tok x feat] row-major in HBM
 - xnT in HBM: plain [E, T] bf16 row-major (feature-major)
 - s/g outputs token-major [T, E] bf16
"""
import numpy as np

USE_DMA_TRANSPOSE = False
T = 2048
E = 2048
NT = T // 128
NC = E // 128


class C:

    def __init__(self, p: P):
        self.p = p
        self.nc = p.nc
        self.psum_h, _ = p.ps("psumall", [128, 4096])
        self.banks = [(self.psum_h[:, i * 512:(i + 1) * 512], Tok(f"bank{i}")) for i in range(8)]
        self.scores = (self.psum_h[:, 2048:4096], [self.banks[i][1] for i in range(4, 8)])
        self.ident = self.nc.dram_tensor("c_ident", [128, 128], F32, kind="ExternalInput").ap()
        self.ident_sb, self.t_ident = p.sb("identsb", [128, 128], BF16)
        p.dma("qpool", self.ident_sb[:], self.ident[:, :], writes=[self.t_ident])
        self.ident32_sb, self.t_ident32 = p.sb("ident32sb", [128, 128], F32)
        p.dma("qsp", self.ident32_sb[:], self.ident[:, :], writes=[self.t_ident32])
        self.maskc = self.nc.dram_tensor("c_mask", [128, 128], F32, kind="ExternalInput").ap()
        self.maskc_sb, self.t_maskc = p.sb("maskcsb", [128, 128], F32)
        p.dma("qsp", self.maskc_sb[:], self.maskc[:, :], writes=[self.t_maskc])
        self._consts = {}
        for v in (0.0, 1.0, -0.5, 1e-5, 1e-6, 1.0 / 2048, 1.0 / 512, 1.0 / 1536,
                  1.0 / np.sqrt(192.0), 1.0 / np.sqrt(128.0), -2.0, -6.0, 2.0):
            self.const(v)

    def const(self, val):
        key = float(val)
        if key not in self._consts:
            assert not self.p._scopes, f"const {key} requested inside a scope; pre-register in C.__init__"
            h, t = self.p.sb(f"c{len(self._consts)}", [128, 1], F32, root=True)
            self.p.emit("dve", lambda h=h, v=key: self.nc.vector.memset(h[:], v), writes=[t])
            try:
                self.nc.const_aps.register(mybir.dt.float32, key, h[:])
            except Exception:
                pass
            self._consts[key] = (h, t)
        return self._consts[key]

    def host_consts(self):
        return {"c_ident": np.eye(128, dtype=np.float32),
                "c_mask": np.where(np.triu(np.ones((128, 128), bool), k=1), -1e30, 0.0).astype(np.float32)}

    def transpose(self, bank_i, col_off, src_ap, reads, n_rows=128):
        bh, bt = self.banks[bank_i]
        self.p.emit("pe", lambda: self.nc.tensor.matmul(
            bh[0:src_ap.shape[-1], col_off:col_off + n_rows],
            src_ap, self.ident_sb[0:n_rows, 0:n_rows], start=True, stop=True),
            reads=list(reads) + [self.t_ident], writes=[bt])
        return bh, bt

    def mm(self, bank_i, cols, lhsT_ap, rhs_ap, reads, start, stop, close=False):
        bh, bt = self.banks[bank_i]
        self.p.emit("pe", lambda: self.nc.tensor.matmul(
            bh[0:lhsT_ap.shape[-1], cols[0]:cols[1]], lhsT_ap, rhs_ap, start=start, stop=stop),
            reads=list(reads), writes=[bt], close=close)
        return bh, bt


def s_ln(c: C, x_ap, xnt_ap, pfx, eps=1e-5):
    p, nc = c.p, c.nc
    p.push_scope()
    xs = [p.sb(f"{pfx}x{i}", [128, E], F32) for i in range(2)]
    sq, t_sq = p.sb(f"{pfx}sq", [128, E], F32)
    xns = [p.sb(f"{pfx}xn{i}", [128, E], BF16) for i in range(2)]
    st, t_st = p.sb(f"{pfx}st", [128, 8], F32)
    evs = [p.sb(f"{pfx}ev{i}", [128, 512], BF16) for i in range(4)]
    eb, ebt = c.const(eps)
    one, onet = c.const(1.0)
    nh, nht = c.const(-0.5)
    zz, zzt = c.const(0.0)
    for t in range(NT):
        xh, xt = xs[t % 2]
        xn, t_xn = xns[t % 2]
        p.dma("qsp", xh[:], x_ap[t * 128:(t + 1) * 128, :], writes=[xt])
        p.emit("dve", lambda xh=xh: nc.vector.tensor_reduce(st[:, 0:1], xh[:], AX.X, OP.add),
               reads=[xt], writes=[t_st])
        p.emit("dve", lambda xh=xh: nc.vector.tensor_tensor(sq[:], xh[:], xh[:], OP.mult),
               reads=[xt], writes=[t_sq])
        p.emit("dve", lambda: nc.vector.tensor_reduce(st[:, 1:2], sq[:], AX.X, OP.add),
               reads=[t_sq], writes=[t_st])
        p.emit("dve", lambda: nc.vector.tensor_scalar(st[:, 2:3], st[:, 0:1], 1.0 / E, None, OP.mult),
               reads=[t_st], writes=[t_st])
        p.emit("dve", lambda: nc.vector.tensor_tensor(st[:, 3:4], st[:, 2:3], st[:, 2:3], OP.mult),
               reads=[t_st], writes=[t_st])
        p.emit("dve", lambda: nc.vector.scalar_tensor_tensor(
            st[:, 4:5], st[:, 1:2], 1.0 / E, st[:, 3:4], OP.mult, OP.subtract),
            reads=[t_st], writes=[t_st])
        p.emit("act", lambda: nc.scalar.activation(st[:, 5:6], st[:, 4:5], AF.Ln, bias=eb[:], scale=one[:]),
               reads=[t_st, ebt, onet], writes=[t_st])
        p.emit("act", lambda: nc.scalar.activation(st[:, 6:7], st[:, 5:6], AF.Exp, bias=zz[:], scale=nh[:]),
               reads=[t_st, nht, zzt], writes=[t_st])
        p.emit("dve", lambda xh=xh, xn=xn: nc.vector.tensor_scalar(
            xn[:], xh[:], st[:, 2:3], st[:, 6:7], OP.subtract, OP.mult),
            reads=[xt, t_st], writes=[t_xn])
        for quad in range(4):
            bi = (t % 2) * 4 + quad
            for j in range(4):
                cc = quad * 4 + j
                c.transpose(bi, j * 128, xn[:, cc * 128:(cc + 1) * 128], [t_xn])
            evh, evt = evs[quad]
            bh, bt = c.banks[bi]
            p.emit("act", lambda bh=bh, evh=evh: nc.scalar.copy(evh[:], bh[:]),
                   reads=[bt], writes=[evt])
            dst = xnt_ap[quad * 512:(quad + 1) * 512, t * 128:(t + 1) * 128]
            dst = dst.rearrange("(c f) k -> f c k", c=4)
            p.dma("qact", dst, evh[:].rearrange("f (c k) -> f c k", c=4), reads=[evt])
    p.pop_scope()


def ref_ln(x):
    m = x.mean(-1, keepdims=True)
    v = x.var(-1, keepdims=True)
    return (x - m) / np.sqrt(v + 1e-5)



def decode_fp4_l0(c: C, pk_ap, sct_pair, ds, n_words, reads_pk):
    p, nc = c.p, c.nc
    K = 8 * n_words
    G = n_words // 16
    (ki, t_ki), (sgn, t_sgn), (ssc, t_ssc) = ds["ki"], ds["sgn"], ds["ssc"]
    (va, t_va), (t1h, t_t1), (t2h, t_t2), (t3h, t_t3), (vv, t_vv) = \
        ds["va"], ds["t1"], ds["t2"], ds["t3"], ds["vv"]
    wd, t_wd = ds["wd"]
    sch, t_sch = sct_pair
    for j in range(8):
        sl = slice(j * n_words, (j + 1) * n_words)
        if j == 7:
            p.emit("dve", lambda sl=sl: nc.vector.tensor_scalar(
                ki[:, sl], pk_ap, 0x70000000, None, OP.bitwise_and), reads=reads_pk, writes=[t_ki])
        else:
            p.emit("dve", lambda sl=sl, j=j: nc.vector.tensor_scalar(
                ki[:, sl], pk_ap, 28 - 4 * j, 0x70000000, OP.logical_shift_left, OP.bitwise_and),
                reads=reads_pk, writes=[t_ki])
        if j == 7:
            p.emit("dve", lambda sl=sl: nc.vector.tensor_scalar(
                sgn[:, sl], pk_ap, -0x80000000, None, OP.bitwise_and),
                reads=reads_pk, writes=[t_sgn])
        else:
            p.emit("dve", lambda sl=sl, j=j: nc.vector.tensor_scalar(
                sgn[:, sl], pk_ap, 28 - 4 * j, -0x80000000, OP.logical_shift_left, OP.bitwise_and),
                reads=reads_pk, writes=[t_sgn])
    sb = sch[:, 0:G].bitcast(I32).unsqueeze(1).unsqueeze(3).broadcast_to([128, 8, G, 16])
    p.emit("dve", lambda: nc.vector.tensor_tensor(
        ssc[:, 0:K].bitcast(I32).rearrange("p (j g r) -> p j g r", j=8, g=G),
        sb, sgn[:, 0:K].rearrange("p (j g r) -> p j g r", j=8, g=G), OP.bitwise_or),
        reads=[t_sch, t_sgn], writes=[t_ssc])
    p.emit("dve", lambda: nc.vector.tensor_scalar(va[:, 0:K], ki[:, 0:K], 0.5 / 268435456.0, None, OP.mult),
           reads=[t_ki], writes=[t_va])
    p.emit("dve", lambda: nc.vector.tensor_scalar(t1h[:, 0:K], va[:, 0:K], 2.0, -2.0, OP.mult, OP.add),
           reads=[t_va], writes=[t_t1])
    p.emit("dve", lambda: nc.vector.tensor_scalar(t3h[:, 0:K], t1h[:, 0:K], 2.0, -4.0, OP.mult, OP.add),
           reads=[t_t1], writes=[t_t3])
    p.emit("dve", lambda: nc.vector.tensor_tensor(t2h[:, 0:K], va[:, 0:K], t1h[:, 0:K], OP.max),
           reads=[t_va, t_t1], writes=[t_t2])
    p.emit("dve", lambda: nc.vector.tensor_tensor(vv[:, 0:K], t2h[:, 0:K], t3h[:, 0:K], OP.max),
           reads=[t_t2, t_t3], writes=[t_vv])
    p.emit("pool", lambda: nc.gpsimd.tensor_tensor(
        wd[:, 0:K].rearrange("p (w j) -> p j w", j=8),
        vv[:, 0:K].rearrange("p (j w) -> p j w", j=8),
        ssc[:, 0:K].rearrange("p (j w) -> p j w", j=8), OP.mult),
        reads=[t_vv, t_ssc], writes=[t_wd])
    return wd, t_wd


def decode_bufs(p, pfx, n_words, nset=3):
    K = 8 * n_words
    shared = {}
    for nm, dt in (("ki", I32), ("sgn", I32), ("va", BF16), ("t1", BF16),
                   ("t2", BF16), ("t3", BF16)):
        shared[nm] = p.sb(f"{pfx}{nm}", [128, K], dt)
    sets = []
    for i in range(max(3, nset)):
        d = dict(shared)
        for nm, dt in (("ssc", F32), ("vv", BF16), ("wd", BF16)):
            d[nm] = p.sb(f"{pfx}{nm}{i}", [128, K], dt)
        sets.append(d)
    return sets


def s_l0_ffn(c: C, xnt_ap, pk_ap, sc_ap, dpk_ap, dsc_ap, ht_ap, y_ap, g_ap, pfx="f0"):
    p, nc = c.p, c.nc
    zz, zzt = c.const(0.0)
    one, onet = c.const(1.0)
    eb, ebt = c.const(1e-6)
    nh, nht = c.const(-0.5)
    inve, invet = c.const(1.0 / E)
    t_ht = [p.tok(f"ht{i}") for i in range(44)]
    t_y = [p.tok(f"y{i}") for i in range(NT)]
    p.push_scope()
    gscs = [p.sb(f"{pfx}gsc{i}", [32, 176], F32) for i in range(2)]
    for _gh, _gt in gscs:
        p.emit("dve", lambda _gh=_gh: nc.vector.memset(_gh[:], 0.0), writes=[_gt])
    dec = decode_bufs(p, f"{pfx}d", 256, 3)
    pks = [p.sb(f"{pfx}pk{i}", [128, 256], I32) for i in range(2)]
    scts = [p.sb(f"{pfx}sct{i}", [128, 32], F32) for i in range(2)]
    p.push_scope()
    xnt, t_xnt = p.sb(f"{pfx}xnt", [128, NC, T], BF16)
    for cc in range(NC):
        p.dma("qsp", xnt[:, cc, :], xnt_ap[cc * 128:(cc + 1) * 128, :], writes=[t_xnt])
    wtiles = [p.sb(f"{pfx}wt{i}", [128, 2048], BF16) for i in range(2)]
    gsbs = [p.sb(f"{pfx}gsb{i}", [128, 2048], BF16) for i in range(3)]
    hsbs = [p.sb(f"{pfx}hsb{i}", [128, 2048], BF16) for i in range(3)]
    usbs = [p.sb(f"{pfx}usb{i}", [128, 2048], BF16) for i in range(3)]
    pend = []

    def _flush_hmult(item):
        (j, gsb, t_gsb, usb, t_usb, hsb, t_hsb) = item
        p.emit("dve", lambda gsb=gsb, usb=usb, hsb=hsb: nc.vector.tensor_tensor(
            hsb[:], gsb[:], usb[:], OP.mult), reads=[t_usb, t_gsb], writes=[t_hsb])
        p.dma("qact", ht_ap[j * 128:(j + 1) * 128, :], hsb[:], reads=[t_hsb],
              writes=[t_ht[j]])

    def sc_transpose(src_ap, src_tok, gcount, sct, t_sct):
        for i in range(4):
            p.emit("dve", lambda i=i, src_ap=src_ap, sct=sct: nc.vector.transpose(
                sct[32 * i:32 * (i + 1), 0:32], src_ap[:, 32 * i:32 * (i + 1)]),
                reads=[src_tok], writes=[t_sct])

    def wd_transposes(wd, t_wd, kchunks, store_fn):
        for g4 in range((kchunks + 3) // 4):
            bi = 4 + (g4 % 2)
            bh, bt = c.banks[bi]
            js = list(range(g4 * 4, min(g4 * 4 + 4, kchunks)))
            for idx, kc in enumerate(js):
                c.transpose(bi, idx * 128, wd[:, kc * 128:(kc + 1) * 128], [t_wd])
            store_fn(js, bh, bt)

    for i in range(44):
        for gu in range(2):
            n = i + gu * 44
            it = 2 * i + gu
            pkh, t_pk = pks[it % 2]
            sct, t_sct = scts[it % 2]
            ds = dec[it % 3]
            wtile, t_wtile = wtiles[it % 2]
            p.dma("qsp", pkh[:], pk_ap[n * 128:(n + 1) * 128, :], writes=[t_pk])
            gsch, t_gsch = gscs[it % 2]
            p.dma("qpool", gsch[0:16, 0:128], sc_ap[:, n * 128:(n + 1) * 128], writes=[t_gsch])
            sc_transpose(gsch[0:32, 0:128], t_gsch, 16, sct, t_sct)
            decode_fp4_l0(c, pkh[:], (sct, t_sct), ds, 256, [t_pk])
            wd, t_wd = ds["wd"]

            def store(js, bh, bt, wtile=wtile, t_wtile=t_wtile):
                w0, n_ = js[0], len(js)
                p.emit("act", lambda w0=w0, n_=n_, bh=bh: nc.scalar.copy(
                    wtile[:, w0 * 128:(w0 + n_) * 128], bh[:, 0:n_ * 128]),
                    reads=[bt], writes=[t_wtile])
            wd_transposes(wd, t_wd, 16, store)
            for kc in range(NC):
                for tc in range(4):
                    c.mm(tc, (0, 512), wtile[:, kc * 128:(kc + 1) * 128],
                         xnt[:, kc, tc * 512:(tc + 1) * 512], [t_wtile, t_xnt],
                         start=(kc == 0), stop=(kc == NC - 1))
            gsb, t_gsb = gsbs[i % 3]
            usb, t_usb = usbs[i % 3]
            hsb, t_hsb = hsbs[i % 3]
            if gu == 0:
                for tc in range(4):
                    bh, bt = c.banks[tc]
                    p.emit("act", lambda bh=bh, gsb=gsb, tc=tc: nc.scalar.activation(
                        gsb[:, tc * 512:(tc + 1) * 512], bh[:], AF.Silu, bias=zz[:], scale=one[:]),
                        reads=[bt, zzt, onet], writes=[t_gsb])
            else:
                for tc in range(4):
                    bh, bt = c.banks[tc]
                    p.emit("act", lambda bh=bh, usb=usb, tc=tc: nc.scalar.copy(
                        usb[:, tc * 512:(tc + 1) * 512], bh[:]), reads=[bt], writes=[t_usb])
                pend.append((i, gsb, t_gsb, usb, t_usb, hsb, t_hsb))
                if len(pend) >= 3:
                    _flush_hmult(pend.pop(0))
    while pend:
        _flush_hmult(pend.pop(0))

    p.pop_scope()
    p.push_scope()
    wdt, _twdt0 = p.sb(f"{pfx}wdt", [128, 44, 256], BF16)
    t_wdq = [p.tok(f"{pfx}wdq{i}") for i in range(4)]
    ypsb, t_ypsb = p.sb(f"{pfx}ypsb", [128, 512], BF16)
    hch = [p.sb(f"{pfx}hch{i}", [128, 2048], BF16) for i in range(4)]
    ytev, t_ytev = p.sb(f"{pfx}ytev", [128, 4096], BF16)

    def emit_dec(ps, q):
        for sub2 in range(2):
            hrow = ps * 2 + sub2
            sl4 = q
            pkh, t_pk = pks[sub2 % 2]
            sct, t_sct = scts[sub2 % 2]
            ds = dec[(hrow * 4 + sl4) % 3]
            w0 = sl4 * 176
            pkq = "qsp" if ps == 0 else "qpool"
            p.dma(pkq, pkh[:, 0:176], dpk_ap[hrow * 128:(hrow + 1) * 128, w0:w0 + 176],
                  writes=[t_pk])
            gsch, t_gsch = gscs[sub2 % 2]
            p.dma("qpool", gsch[0:11, 0:128],
                  dsc_ap[sl4 * 11:sl4 * 11 + 11, hrow * 128:(hrow + 1) * 128], writes=[t_gsch])
            sc_transpose(gsch[0:32, 0:128], t_gsch, 11, sct, t_sct)
            decode_fp4_l0(c, pkh[:, 0:176], (sct, t_sct), ds, 176, [t_pk])
            wd, t_wd = ds["wd"]
            if q < 3:
                wdv = wd[:, 0:1408].rearrange("p (kc x) -> p kc x", kc=11)
                dst = wdt[:, sl4 * 11:(sl4 + 1) * 11, sub2 * 128:(sub2 + 1) * 128]
                for i in range(4):
                    for j in range(4):
                        p.emit("dve", lambda i=i, j=j, wdv=wdv, dst=dst: nc.vector.transpose(
                            dst[32 * i:32 * (i + 1), :, 32 * j:32 * (j + 1)],
                            wdv[32 * j:32 * (j + 1), :, 32 * i:32 * (i + 1)]),
                            reads=[t_wd], writes=[t_wdq[q]])
            else:
                def store(js, bh, bt, sl4=sl4, sub2=sub2):
                    for idx, kc in enumerate(js):
                        ic = sl4 * 11 + kc
                        p.emit("act", lambda bh=bh, idx=idx, ic=ic, sub2=sub2: nc.scalar.copy(
                            wdt[:, ic, sub2 * 128:(sub2 + 1) * 128],
                            bh[:, idx * 128:(idx + 1) * 128]),
                            reads=[bt], writes=[t_wdq[3]])
                wd_transposes(wd, t_wd, 11, store)

    for q in range(4):
        emit_dec(0, q)
    for ps in range(8):
        for ic in range(44):
            if ic in (11, 22, 33) and ps + 1 < 8:
                emit_dec(ps + 1, ic // 11 - 1)
            hh, t_hh = hch[ic % 4]
            p.dma("qsp" if ic % 2 == 0 else "qact", hh[:], ht_ap[ic * 128:(ic + 1) * 128, :],
                  reads=[t_ht[ic]], writes=[t_hh])
            for hj in range(2):
                for tc in range(4):
                    c.mm(hj * 4 + tc, (0, 512), wdt[:, ic, hj * 128:(hj + 1) * 128],
                         hh[:, tc * 512:(tc + 1) * 512], [t_wdq[min(ic // 11, 3)], t_hh],
                         start=(ic == 0), stop=(ic == 43))
        for bi in range(8):
            bh, bt = c.banks[bi]
            p.emit("act", lambda bh=bh, bi=bi: nc.scalar.copy(
                ytev[:, bi * 512:(bi + 1) * 512], bh[:]), reads=[bt], writes=[t_ytev])
        for hj in range(2):
            for q in range(4):
                bi = q % 2
                bh, bt = c.banks[bi]
                for idx in range(4):
                    tt = q * 4 + idx
                    c.transpose(bi, idx * 128,
                                ytev[:, hj * 2048 + tt * 128: hj * 2048 + (tt + 1) * 128],
                                [t_ytev])
                tt0 = q * 4
                p.emit("act", lambda bh=bh: nc.scalar.copy(ypsb[:], bh[:]),
                       reads=[bt], writes=[t_ypsb])
                dst = y_ap[tt0 * 128:(tt0 + 4) * 128, ps * 256 + hj * 128: ps * 256 + (hj + 1) * 128]
                dst = dst.rearrange("(a p) h -> p a h", a=4)
                p.dma("qact", dst, ypsb[:].rearrange("p (a h) -> p a h", a=4), reads=[t_ypsb],
                      writes=[t_y[tt0 + _i] for _i in range(4)])
        if ps + 1 < 8:
            emit_dec(ps + 1, 3)
    p.pop_scope()
    p.pop_scope()
    p.push_scope()
    sq2, t_sq2 = p.sb(f"{pfx}sq2", [128, E], F32)
    st2, t_st2 = p.sb(f"{pfx}st2", [128, 4], F32)
    gouts = [p.sb(f"{pfx}gout{i}", [128, E], BF16) for i in range(2)]
    yts = [p.sb(f"{pfx}yt{i}", [128, E], BF16) for i in range(2)]
    for tt in range(NT):
        gout, t_gout = gouts[tt % 2]
        yth, t_yth = yts[tt % 2]
        p.dma("qsp", yth[:], y_ap[tt * 128:(tt + 1) * 128, :], reads=[t_y[tt]], writes=[t_yth])
        p.emit("dve", lambda yth=yth: nc.vector.tensor_tensor(
            sq2[:], yth[:], yth[:], OP.mult), reads=[t_yth], writes=[t_sq2])
        p.emit("dve", lambda: nc.vector.tensor_reduce(st2[:, 0:1], sq2[:], AX.X, OP.add),
               reads=[t_sq2], writes=[t_st2])
        p.emit("act", lambda: nc.scalar.activation(st2[:, 1:2], st2[:, 0:1], AF.Ln, bias=eb[:], scale=inve[:]),
               reads=[t_st2, ebt, invet], writes=[t_st2])
        p.emit("act", lambda: nc.scalar.activation(st2[:, 2:3], st2[:, 1:2], AF.Exp, bias=zz[:], scale=nh[:]),
               reads=[t_st2, nht, zzt], writes=[t_st2])
        p.emit("act", lambda yth=yth, gout=gout: nc.scalar.activation(
            gout[:], yth[:], AF.Silu, bias=zz[:], scale=st2[:, 2:3]),
            reads=[t_yth, t_st2, zzt], writes=[t_gout])
        p.dma("qact", g_ap[tt * 128:(tt + 1) * 128, :], gout[:], reads=[t_gout])
    p.pop_scope()



def decode_fp4_l1(c: C, pk_ap, sc_pair, ds, n_words, reads_pk):
    p, nc = c.p, c.nc
    K = 2 * n_words
    G = n_words // 16
    (ki, t_ki), (sgn, t_sgn), (ssc, t_ssc) = ds["ki"], ds["sgn"], ds["ssc"]
    (va, t_va), (t1h, t_t1), (t2h, t_t2), (t3h, t_t3), (vv, t_vv) = \
        ds["va"], ds["t1"], ds["t2"], ds["t3"], ds["vv"]
    wd, t_wd = ds["wd"]
    sch, t_sch = sc_pair
    p.emit("dve", lambda: nc.vector.tensor_scalar(
        ki[:, 0:n_words], pk_ap, 28, 0x70000000, OP.logical_shift_left, OP.bitwise_and),
        reads=reads_pk, writes=[t_ki])
    p.emit("dve", lambda: nc.vector.tensor_scalar(
        ki[:, n_words:K], pk_ap, 24, 0x70000000, OP.logical_shift_left, OP.bitwise_and),
        reads=reads_pk, writes=[t_ki])
    p.emit("dve", lambda: nc.vector.tensor_scalar(
        sgn[:, 0:n_words], pk_ap, 28, -0x80000000, OP.logical_shift_left, OP.bitwise_and),
        reads=reads_pk, writes=[t_sgn])
    p.emit("dve", lambda: nc.vector.tensor_scalar(
        sgn[:, n_words:K], pk_ap, 24, -0x80000000, OP.logical_shift_left, OP.bitwise_and),
        reads=reads_pk, writes=[t_sgn])
    sb = sch[:, 0:G].bitcast(I32).unsqueeze(1).unsqueeze(3).broadcast_to([128, 2, G, 16])
    p.emit("dve", lambda: nc.vector.tensor_tensor(
        ssc[:, 0:K].bitcast(I32).rearrange("p (j g r) -> p j g r", j=2, g=G),
        sb, sgn[:, 0:K].rearrange("p (j g r) -> p j g r", j=2, g=G), OP.bitwise_or),
        reads=[t_sch, t_sgn], writes=[t_ssc])
    p.emit("dve", lambda: nc.vector.tensor_scalar(va[:, 0:K], ki[:, 0:K], 0.5 / 268435456.0, None, OP.mult),
           reads=[t_ki], writes=[t_va])
    p.emit("dve", lambda: nc.vector.tensor_scalar(t1h[:, 0:K], va[:, 0:K], 2.0, -2.0, OP.mult, OP.add),
           reads=[t_va], writes=[t_t1])
    p.emit("dve", lambda: nc.vector.tensor_scalar(t3h[:, 0:K], t1h[:, 0:K], 2.0, -4.0, OP.mult, OP.add),
           reads=[t_t1], writes=[t_t3])
    p.emit("dve", lambda: nc.vector.tensor_tensor(t2h[:, 0:K], va[:, 0:K], t1h[:, 0:K], OP.max),
           reads=[t_va, t_t1], writes=[t_t2])
    p.emit("dve", lambda: nc.vector.tensor_tensor(vv[:, 0:K], t2h[:, 0:K], t3h[:, 0:K], OP.max),
           reads=[t_t2, t_t3], writes=[t_vv])
    p.emit("pool", lambda: nc.gpsimd.tensor_tensor(
        wd[:, 0:K].rearrange("p (w j) -> p j w", j=2),
        vv[:, 0:K].rearrange("p (j w) -> p j w", j=2),
        ssc[:, 0:K].rearrange("p (j w) -> p j w", j=2), OP.mult),
        reads=[t_vv, t_ssc], writes=[t_wd])
    return wd, t_wd


def decode_fp4_l1t(c: C, pk_ap, reads_pk, scpl_pair, ds, pkT_pair, wd_ap=None, wd_tok=None):
    p, nc = c.p, c.nc
    (ki, t_ki), (sgn, t_sgn), (ssc, t_ssc) = ds["ki"], ds["sgn"], ds["ssc"]
    (va, t_va), (t1h, t_t1), (t2h, t_t2), (t3h, t_t3), (vv, t_vv) = \
        ds["va"], ds["t1"], ds["t2"], ds["t3"], ds["vv"]
    scpl, t_scpl = scpl_pair
    if wd_ap is None:
        wdh, wd_tok = ds["wd"]
        wd_ap = wdh[:].rearrange("p (s j n) -> p j s n", s=8, j=2)
    pkT, t_pkT = pkT_pair
    pkv = pk_ap.rearrange("p (s n) -> p s n", s=8)
    pTv = pkT[:].rearrange("p (s n) -> p s n", s=8)
    for i in range(4):
        for j in range(4):
            p.emit("dve", lambda i=i, j=j, pkv=pkv, pTv=pTv: nc.vector.transpose(
                pTv[32 * i:32 * (i + 1), :, 32 * j:32 * (j + 1)],
                pkv[32 * j:32 * (j + 1), :, 32 * i:32 * (i + 1)]),
                reads=list(reads_pk), writes=[t_pkT])
    for jn in range(2):
        p.emit("dve", lambda jn=jn, pkT=pkT: nc.vector.tensor_scalar(
            ki[:, jn * 1024:(jn + 1) * 1024], pkT[:], 28 - 4 * jn, 0x70000000,
            OP.logical_shift_left, OP.bitwise_and), reads=[t_pkT], writes=[t_ki])
        p.emit("dve", lambda jn=jn, pkT=pkT: nc.vector.tensor_scalar(
            sgn[:, jn * 1024:(jn + 1) * 1024], pkT[:], 28 - 4 * jn, -0x80000000,
            OP.logical_shift_left, OP.bitwise_and), reads=[t_pkT], writes=[t_sgn])
    sb = scpl[:].bitcast(I32).unsqueeze(1).broadcast_to([128, 2, 8, 128])
    p.emit("dve", lambda sb=sb: nc.vector.tensor_tensor(
        ssc[:, 0:2048].bitcast(I32).rearrange("p (j s n) -> p j s n", j=2, s=8),
        sb, sgn[:, 0:2048].rearrange("p (j s n) -> p j s n", j=2, s=8), OP.bitwise_or),
        reads=[t_scpl, t_sgn], writes=[t_ssc])
    p.emit("dve", lambda: nc.vector.tensor_scalar(va[:, 0:2048], ki[:, 0:2048],
           0.5 / 268435456.0, None, OP.mult), reads=[t_ki], writes=[t_va])
    p.emit("dve", lambda: nc.vector.tensor_scalar(t1h[:, 0:2048], va[:, 0:2048], 2.0, -2.0,
           OP.mult, OP.add), reads=[t_va], writes=[t_t1])
    p.emit("dve", lambda: nc.vector.tensor_scalar(t3h[:, 0:2048], t1h[:, 0:2048], 2.0, -4.0,
           OP.mult, OP.add), reads=[t_t1], writes=[t_t3])
    p.emit("dve", lambda: nc.vector.tensor_tensor(t2h[:, 0:2048], va[:, 0:2048],
           t1h[:, 0:2048], OP.max), reads=[t_va, t_t1], writes=[t_t2])
    p.emit("dve", lambda: nc.vector.tensor_tensor(vv[:, 0:2048], t2h[:, 0:2048],
           t3h[:, 0:2048], OP.max), reads=[t_t2, t_t3], writes=[t_vv])
    p.emit("pool", lambda wd_ap=wd_ap: nc.gpsimd.tensor_tensor(
        wd_ap, vv[:, 0:2048].rearrange("p (j s n) -> p j s n", j=2, s=8),
        ssc[:, 0:2048].rearrange("p (j s n) -> p j s n", j=2, s=8), OP.mult),
        reads=[t_vv, t_ssc], writes=[wd_tok])
    return wd_tok


def s_l1_ffn(c: C, xnt_ap, gpk_ap, gsc_ap, upk_ap, usc_ap, dpk_ap, dsc_ap,
             it_ap, y_ap, g_ap, pfx="f1", n_tiles=128, phases="all"):
    p, nc = c.p, c.nc
    N1 = 16384
    zz, zzt = c.const(0.0)
    one, onet = c.const(1.0)
    t_it = [p.tok(f"it{i}") for i in range(128)]
    t_y = [p.tok(f"yy{i}") for i in range(NT)]

    p.push_scope()
    xnt, t_xnt = p.sb(f"{pfx}xnt", [128, NC, T], BF16)
    for cc in range(NC):
        p.dma("qsp", xnt[:, cc, :], xnt_ap[cc * 128:(cc + 1) * 128, :], writes=[t_xnt])
    dec = decode_bufs(p, f"{pfx}d", 256, 3)
    pks = [p.sb(f"{pfx}pk{i}", [128, 1024], I32) for i in range(3)]
    wtiles = [p.sb(f"{pfx}wt{i}", [128, 2048], BF16) for i in range(3)]
    scs = [p.sb(f"{pfx}sc{i}", [128, 64], F32) for i in range(3)]
    gsbs = [p.sb(f"{pfx}gsb{i}", [128, 2048], BF16) for i in range(3)]
    hsbs = [p.sb(f"{pfx}hsb{i}", [128, 2048], BF16) for i in range(3)]
    usbs = [p.sb(f"{pfx}usb{i}", [128, 2048], BF16) for i in range(3)]
    pend = []

    def _flush_hmult(item):
        (j, gsb, t_gsb, usb, t_usb, hsb, t_hsb) = item
        p.emit("dve", lambda gsb=gsb, usb=usb, hsb=hsb: nc.vector.tensor_tensor(
            hsb[:], gsb[:], usb[:], OP.mult), reads=[t_usb, t_gsb], writes=[t_hsb])
        p.dma("qact", it_ap[j * 128:(j + 1) * 128, :], hsb[:], reads=[t_hsb],
              writes=[t_it[j]])

    def wd_transposes(wd, t_wd, kchunks, store_fn):
        for g4 in range((kchunks + 3) // 4):
            bi = 4 + (g4 % 2)
            bh, bt = c.banks[bi]
            js = list(range(g4 * 4, min(g4 * 4 + 4, kchunks)))
            for idx, kc in enumerate(js):
                c.transpose(bi, idx * 128, wd[:, kc * 128:(kc + 1) * 128], [t_wd])
            store_fn(js, bh, bt)

    for i in range(n_tiles):
        for gu in range(2):
            it = 2 * i + gu
            pk_src = gpk_ap if gu == 0 else upk_ap
            sc_src = gsc_ap if gu == 0 else usc_ap
            pkh, t_pk = pks[it % 3]
            sch, t_sch = scs[it % 3]
            ds = dec[it % 3]
            wtile, t_wtile = wtiles[it % 3]
            p.dma("qsp", sch[:], sc_src[i * 128:(i + 1) * 128, :], writes=[t_sch])
            p.dma("qsp", pkh[:], pk_src[i * 128:(i + 1) * 128, :], writes=[t_pk])
            decode_fp4_l1(c, pkh[:], (sch, t_sch), ds, 1024, [t_pk])
            wd, t_wd = ds["wd"]

            def store(js, bh, bt, wtile=wtile, t_wtile=t_wtile):
                w0, n_ = js[0], len(js)
                p.emit("act", lambda w0=w0, n_=n_, bh=bh: nc.scalar.copy(
                    wtile[:, w0 * 128:(w0 + n_) * 128], bh[:, 0:n_ * 128]),
                    reads=[bt], writes=[t_wtile])
            wd_transposes(wd, t_wd, 16, store)
            for kc in range(NC):
                for tc in range(4):
                    c.mm(tc, (0, 512), wtile[:, kc * 128:(kc + 1) * 128],
                         xnt[:, kc, tc * 512:(tc + 1) * 512], [t_wtile, t_xnt],
                         start=(kc == 0), stop=(kc == NC - 1))
            gsb, t_gsb = gsbs[i % 3]
            usb, t_usb = usbs[i % 3]
            hsb, t_hsb = hsbs[i % 3]
            if gu == 0:
                for tc in range(4):
                    bh, bt = c.banks[tc]
                    p.emit("act", lambda bh=bh, gsb=gsb, tc=tc: nc.scalar.activation(
                        gsb[:, tc * 512:(tc + 1) * 512], bh[:], AF.Gelu_apprx_tanh,
                        bias=zz[:], scale=one[:]),
                        reads=[bt, zzt, onet], writes=[t_gsb])
            else:
                for tc in range(4):
                    bh, bt = c.banks[tc]
                    p.emit("act", lambda bh=bh, usb=usb, tc=tc: nc.scalar.copy(
                        usb[:, tc * 512:(tc + 1) * 512], bh[:]), reads=[bt], writes=[t_usb])
                pend.append((i, gsb, t_gsb, usb, t_usb, hsb, t_hsb))
                if len(pend) >= 3:
                    _flush_hmult(pend.pop(0))
    while pend:
        _flush_hmult(pend.pop(0))
    p.pop_scope()
    if phases == "gu":
        return

    p.push_scope()
    dec2 = decode_bufs(p, f"{pfx}e", 256, 3)
    pks2 = [p.sb(f"{pfx}qk{i}", [128, 1024], I32) for i in range(2)]
    scs2 = [p.sb(f"{pfx}sd{i}", [128, 512], F32) for i in range(2)]
    wdt, _twdt_unused = p.sb(f"{pfx}wdt", [128, 128, 256], BF16)
    t_wdtq = [p.tok(f"{pfx}wdtq{i}") for i in range(4)]
    hch = [p.sb(f"{pfx}hch{i}", [128, 2048], BF16) for i in range(4)]
    ytev, t_ytev = p.sb(f"{pfx}ytev", [128, 4096], BF16)
    ypsb, t_ypsb = p.sb(f"{pfx}ypsb", [128, 512], BF16)

    def emit_dec(ps, q):
        def dve_T(item):
            wd, t_wd, sl8, sub2 = item
            wdv = wd[:].rearrange("p (kc x) -> p kc x", kc=16)
            dst = wdt[:, sl8 * 16:(sl8 + 1) * 16, sub2 * 128:(sub2 + 1) * 128]
            for i in range(4):
                for j in range(4):
                    p.emit("dve", lambda i=i, j=j, wdv=wdv, dst=dst: nc.vector.transpose(
                        dst[32 * i:32 * (i + 1), :, 32 * j:32 * (j + 1)],
                        wdv[32 * j:32 * (j + 1), :, 32 * i:32 * (i + 1)]),
                        reads=[t_wd], writes=[t_wdtq[q]])

        def pe_T(item):
            wd, t_wd, sl8, sub2 = item

            def store(js, bh, bt, sl8=sl8, sub2=sub2):
                ic0, n_ = sl8 * 16 + js[0], len(js)
                p.emit("act", lambda ic0=ic0, n_=n_, bh=bh, sub2=sub2: nc.scalar.copy(
                    wdt[:, ic0:ic0 + n_, sub2 * 128:(sub2 + 1) * 128],
                    bh[:, 0:n_ * 128].rearrange("p (a x) -> p a x", a=n_)),
                    reads=[bt], writes=[t_wdtq[q]])
            wd_transposes(wd, t_wd, 16, store)

        dpend = []
        flush_at = 2 if q < 3 else 3
        do_T = dve_T if q < 3 else pe_T
        for sub2 in range(2):
            hrow = ps * 2 + sub2
            sch, t_sch = scs2[sub2 % 2]
            pkq = "qsp" if ps == 0 else "qpool"
            if q == 0:
                p.dma(pkq, sch[:], dsc_ap[hrow * 128:(hrow + 1) * 128, :], writes=[t_sch])
            for sl8 in range(q * 2, q * 2 + 2):
                gidx = hrow * 8 + sl8
                pkh, t_pk = pks2[gidx % 2]
                ds = dec2[gidx % 3]
                p.dma(pkq, pkh[:],
                      dpk_ap[hrow * 128:(hrow + 1) * 128, sl8 * 1024:(sl8 + 1) * 1024],
                      writes=[t_pk])
                decode_fp4_l1(c, pkh[:], (sch[:, sl8 * 64:(sl8 + 1) * 64], t_sch),
                              ds, 1024, [t_pk])
                wd, t_wd = ds["wd"]
                dpend.append((wd, t_wd, sl8, sub2))
                if len(dpend) >= flush_at:
                    do_T(dpend.pop(0))
        while dpend:
            do_T(dpend.pop(0))

    for q in range(4):
        emit_dec(0, q)
    for ps in range(8):
        for ic in range(128):
            if ic in (32, 64, 96) and ps + 1 < 8:
                emit_dec(ps + 1, ic // 32 - 1)
            hh, t_hh = hch[ic % 4]
            p.dma("qsp" if ic % 2 == 0 else "qact", hh[:], it_ap[ic * 128:(ic + 1) * 128, :],
                  reads=[t_it[ic]], writes=[t_hh])
            for hj in range(2):
                for tc in range(4):
                    c.mm(hj * 4 + tc, (0, 512), wdt[:, ic, hj * 128:(hj + 1) * 128],
                         hh[:, tc * 512:(tc + 1) * 512], [t_wdtq[ic // 32], t_hh],
                         start=(ic == 0), stop=(ic == 127))
        for bi in range(8):
            bh, bt = c.banks[bi]
            p.emit("act", lambda bh=bh, bi=bi: nc.scalar.copy(
                ytev[:, bi * 512:(bi + 1) * 512], bh[:]), reads=[bt], writes=[t_ytev])
        for hj in range(2):
            for q in range(4):
                bi = q % 2
                bh, bt = c.banks[bi]
                for idx in range(4):
                    tt = q * 4 + idx
                    c.transpose(bi, idx * 128,
                                ytev[:, hj * 2048 + tt * 128: hj * 2048 + (tt + 1) * 128],
                                [t_ytev])
                tt0 = q * 4
                p.emit("act", lambda bh=bh: nc.scalar.copy(ypsb[:], bh[:]),
                       reads=[bt], writes=[t_ypsb])
                dst = y_ap[tt0 * 128:(tt0 + 4) * 128, ps * 256 + hj * 128: ps * 256 + (hj + 1) * 128]
                dst = dst.rearrange("(a p) h -> p a h", a=4)
                p.dma("qact", dst, ypsb[:].rearrange("p (a h) -> p a h", a=4), reads=[t_ypsb],
                      writes=[t_y[tt0 + _i] for _i in range(4)])
        if ps + 1 < 8:
            emit_dec(ps + 1, 3)
    p.pop_scope()
    p.push_scope()
    gouts = [p.sb(f"{pfx}gout{i}", [128, E], BF16) for i in range(2)]
    yts = [p.sb(f"{pfx}yt{i}", [128, E], BF16) for i in range(2)]
    for tt in range(NT):
        gout, t_gout = gouts[tt % 2]
        yth, t_yth = yts[tt % 2]
        p.dma("qsp", yth[:], y_ap[tt * 128:(tt + 1) * 128, :], reads=[t_y[tt]], writes=[t_yth])
        p.emit("act", lambda yth=yth, gout=gout: nc.scalar.activation(
            gout[:], yth[:], AF.Silu, bias=zz[:], scale=one[:]),
            reads=[t_yth, zzt, onet], writes=[t_gout])
        p.dma("qact", g_ap[tt * 128:(tt + 1) * 128, :], gout[:], reads=[t_gout])
    p.pop_scope()


def s_l1_attn(c: C, xnt_ap, wp_ap, wo_ap, s_ap, pfx="a1", dbg=None):
    p, nc = c.p, c.nc
    H, D = 16, 128
    sm = 1.0 / np.sqrt(D)
    zz, zzt = c.const(0.0)
    one, onet = c.const(1.0)
    smc, smct = c.const(sm)
    t_s = [p.tok(f"{pfx}s{i}") for i in range(NT)]

    p.push_scope()
    xnt, t_xnt = p.sb(f"{pfx}xnt", [128, NC, T], BF16)
    for cc in range(NC):
        p.dma("qsp", xnt[:, cc, :], xnt_ap[cc * 128:(cc + 1) * 128, :], writes=[t_xnt])
    oT, t_oT = p.sb(f"{pfx}oT", [128, H, T], BF16)
    p.push_scope()
    wrow = [p.sb(f"{pfx}wr{i}", [128, E], BF16) for i in range(2)]
    wqT, t_wqT = p.sb(f"{pfx}wqT", [128, NC, 128], BF16)
    wkT, t_wkT = p.sb(f"{pfx}wkT", [128, NC, 128], BF16)
    wvT, t_wvT = p.sb(f"{pfx}wvT", [128, NC, 128], BF16)
    qT, t_qT = p.sb(f"{pfx}qT", [128, T], BF16)
    kT, t_kT = p.sb(f"{pfx}kT", [128, T], BF16)
    vtm, t_vtm = p.sb(f"{pfx}vtm", [128, NT, 128], BF16)
    vT, t_vT = p.sb(f"{pfx}vT", [128, T], BF16)
    psls = [(p.sb(f"{pfx}psl{i}", [128, T], BF16)[0],
             [p.tok(f"{pfx}pt{i}_{j}") for j in range(4)]) for i in range(3)]
    lsums = [p.sb(f"{pfx}ls{i}", [128, 12], F32) for i in range(3)]
    diags = [p.sb(f"{pfx}dg{i}", [128, 128], BF16) for i in range(3)]
    pT, t_pT = p.sb(f"{pfx}pT", [128, NT, 512], BF16)

    def load_wT(h, dst, t_dst, row0):
        wr, t_wr = wrow[h % 2]
        p.dma("qpool", wr[:], wp_ap[row0:row0 + 128, :], writes=[t_wr])
        for g4 in range(4):
            bi = g4 % 2
            bh, bt = c.banks[bi]
            for j in range(4):
                cc = g4 * 4 + j
                c.transpose(bi, j * 128, wr[:, cc * 128:(cc + 1) * 128], [t_wr])
            p.emit("act", lambda bh=bh, g4=g4, dst=dst: nc.scalar.copy(
                dst[:, g4 * 4:(g4 + 1) * 4, :].rearrange("p a b -> p (a b)"), bh[:]),
                reads=[bt], writes=[t_dst])

    for h in range(H):
        load_wT(h, wqT, t_wqT, h * 128)
        load_wT(h, wkT, t_wkT, 2048 + h * 128)
        load_wT(h, wvT, t_wvT, 4096 + h * 128)
        for tc in range(4):
            for cc in range(NC):
                c.mm(tc, (0, 512), wqT[:, cc, :], xnt[:, cc, tc * 512:(tc + 1) * 512],
                     [t_wqT, t_xnt], start=(cc == 0), stop=(cc == NC - 1))
            bh, bt = c.banks[tc]
            p.emit("act", lambda bh=bh, tc=tc: nc.scalar.activation(
                qT[:, tc * 512:(tc + 1) * 512], bh[:], AF.Copy, bias=0.0, scale=smc[:]),
                reads=[bt, zzt, smct], writes=[t_qT])
        for tc in range(4):
            for cc in range(NC):
                c.mm(tc, (0, 512), wkT[:, cc, :], xnt[:, cc, tc * 512:(tc + 1) * 512],
                     [t_wkT, t_xnt], start=(cc == 0), stop=(cc == NC - 1))
            bh, bt = c.banks[tc]
            p.emit("act", lambda bh=bh, tc=tc: nc.scalar.copy(
                kT[:, tc * 512:(tc + 1) * 512], bh[:]), reads=[bt], writes=[t_kT])
        for tc in range(4):
            for cc in range(NC):
                c.mm(tc, (0, 512), wvT[:, cc, :], xnt[:, cc, tc * 512:(tc + 1) * 512],
                     [t_wvT, t_xnt], start=(cc == 0), stop=(cc == NC - 1))
            bh, bt = c.banks[tc]
            p.emit("act", lambda bh=bh, tc=tc: nc.scalar.copy(
                vT[:, tc * 512:(tc + 1) * 512], bh[:]), reads=[bt], writes=[t_vT])
        for tt0 in range(0, NT, 4):
            bi = (tt0 // 4) % 2
            bh, bt = c.banks[bi]
            for j in range(4):
                c.transpose(bi, j * 128, vT[:, (tt0 + j) * 128:(tt0 + j + 1) * 128], [t_vT])
            p.emit("act", lambda bh=bh, tt0=tt0: nc.scalar.copy(
                vtm[:, tt0:tt0 + 4, :].rearrange("p a b -> p (a b)"), bh[:, 0:512]),
                reads=[bt], writes=[t_vtm])

        def emit_T(item):
            qt, pslh, ptoks, dgh, t_dgh = item
            qi = qt % 4
            for kt0 in range(0, qt + 1, 4):
                ntl = min(4, qt + 1 - kt0)
                bi = (kt0 // 4) % 2
                bh, bt = c.banks[bi]
                for j in range(ntl):
                    kt = kt0 + j
                    p.emit("pe", lambda bh=bh, j=j, kt=kt, pslh=pslh, dgh=dgh:
                        nc.tensor.matmul(bh[0:128, j * 128:(j + 1) * 128],
                                         pslh[:, kt * 128:(kt + 1) * 128], dgh[:],
                                         start=True, stop=True),
                        reads=[ptoks[kt // 4], t_dgh], writes=[bt])
                if (kt0 // 4) % 2 == 0:
                    p.emit("dve", lambda bh=bh, kt0=kt0, ntl=ntl, qi=qi: nc.vector.tensor_copy(
                        pT[:, kt0:kt0 + ntl, qi * 128:(qi + 1) * 128],
                        bh[:, 0:ntl * 128].rearrange("p (a x) -> p a x", a=ntl)),
                        reads=[bt], writes=[t_pT])
                else:
                    p.emit("act", lambda bh=bh, kt0=kt0, ntl=ntl, qi=qi: nc.scalar.copy(
                        pT[:, kt0:kt0 + ntl, qi * 128:(qi + 1) * 128],
                        bh[:, 0:ntl * 128].rearrange("p (a x) -> p a x", a=ntl)),
                        reads=[bt], writes=[t_pT])

        def emit_O(qg, h):
            qmax = qg * 4 + 3
            bo, bo_t = c.banks[2]
            for kt in range(qmax + 1):
                off = max(0, (kt - qg * 4)) * 128 if kt > qg * 4 else 0
                p.emit("pe", lambda kt=kt, off=off, bo=bo, qmax=qmax: nc.tensor.matmul(
                    bo[0:128, off:512], vtm[:, kt, :], pT[:, kt, off:512],
                    start=(kt == 0), stop=(kt == qmax)),
                    reads=[t_vtm, t_pT], writes=[bo_t])
            p.emit("act", lambda h=h, qg=qg, bo=bo: nc.scalar.copy(
                oT[:, h, qg * 512:(qg + 1) * 512], bo[0:128, 0:512]),
                reads=[bo_t], writes=[t_oT])

        pend_T = []
        for qt in range(NT):
            klen = (qt + 1) * 128
            pslh, ptoks = psls[qt % 3]
            dgh, t_dgh = diags[qt % 3]
            lsh, t_lsh = lsums[qt % 3]
            sc_v, sc_toks = c.scores
            nchunk = (klen + 511) // 512
            mck = qt // 4
            for ck in range(nchunk):
                n = min(512, klen - ck * 512)
                tok = [sc_toks[ck]]
                p.emit("pe", lambda ck=ck, n=n, qt=qt: nc.tensor.matmul(
                    sc_v[:, ck * 512:ck * 512 + n], qT[:, qt * 128:(qt + 1) * 128],
                    kT[:, ck * 512:ck * 512 + n], start=True, stop=True),
                    reads=[t_qT, t_kT], writes=tok)
                if ck == mck:
                    p.emit("dve", lambda qt=qt: nc.vector.tensor_tensor(
                        sc_v[:, qt * 128:(qt + 1) * 128], sc_v[:, qt * 128:(qt + 1) * 128],
                        c.maskc_sb[:], OP.add), reads=tok + [c.t_maskc], writes=tok)
                p.emit("dve", lambda ck=ck, n=n, lsh=lsh: nc.vector.tensor_reduce(
                    lsh[:, 4 + ck:5 + ck], sc_v[:, ck * 512:ck * 512 + n], AX.X, OP.max),
                    reads=tok, writes=[t_lsh])
            p.emit("dve", lambda lsh=lsh, nchunk=nchunk: nc.vector.tensor_reduce(
                lsh[:, 0:1], lsh[:, 4:4 + nchunk], AX.X, OP.max, negate=True),
                reads=[t_lsh], writes=[t_lsh])
            if len(pend_T) >= 2:
                emit_T(pend_T.pop(0))
            for ck in range(nchunk):
                n = min(512, klen - ck * 512)
                tok = [sc_toks[ck]]
                p.emit("act", lambda ck=ck, n=n, pslh=pslh, lsh=lsh: nc.scalar.activation(
                    pslh[:, ck * 512:ck * 512 + n], sc_v[:, ck * 512:ck * 512 + n],
                    AF.Exp, bias=lsh[:, 0:1], scale=one[:], accum_out=lsh[:, 8 + ck:9 + ck]),
                    reads=tok + [t_lsh, onet], writes=[ptoks[ck], t_lsh])
            p.emit("dve", lambda lsh=lsh, nchunk=nchunk: nc.vector.tensor_reduce(
                lsh[:, 1:2], lsh[:, 8:8 + nchunk], AX.X, OP.add), reads=[t_lsh], writes=[t_lsh])
            p.emit("dve", lambda lsh=lsh: nc.vector.reciprocal(lsh[:, 2:3], lsh[:, 1:2]),
                   reads=[t_lsh], writes=[t_lsh])
            p.emit("dve", lambda dgh=dgh, lsh=lsh: nc.vector.tensor_tensor(
                dgh[:], c.ident_sb[:], lsh[:, 2:3].broadcast_to([128, 128]), OP.mult),
                reads=[c.t_ident, t_lsh], writes=[t_dgh])
            pend_T.append((qt, pslh, ptoks, dgh, t_dgh))
            if qt % 4 == 1 and qt >= 5:
                emit_O((qt - 5) // 4, h)
        while pend_T:
            emit_T(pend_T.pop(0))
        emit_O(3, h)
    if dbg is not None:
        p.dma("qsp", dbg["qT"][:, :], qT[:], reads=[t_qT])
        p.dma("qsp", dbg["kT"][:, :], kT[:], reads=[t_kT])
        p.dma("qsp", dbg["v"][:, :], vtm[:].rearrange("p a b -> p (a b)"), reads=[t_vtm])
        p.dma("qsp", dbg["oT"][:, :], oT[:, H - 1, :], reads=[t_oT])
    p.pop_scope()
    wosb = [p.sb(f"{pfx}wo{i}", [128, E], BF16) for i in range(2)]
    woT, t_woT = p.sb(f"{pfx}woT", [128, NC, 512], BF16)
    souts = [p.sb(f"{pfx}so{i}", [128, 512], BF16) for i in range(2)]
    for ec in range(4):
        for ot4 in range(4):
            wr, t_wr = wosb[ot4 % 2]
            p.dma("qpool", wr[:], wo_ap[ec * 512 + ot4 * 128: ec * 512 + (ot4 + 1) * 128, :],
                  writes=[t_wr])
            for g4 in range(4):
                bi = g4 % 2
                bh, bt = c.banks[bi]
                for j in range(4):
                    cc = g4 * 4 + j
                    c.transpose(bi, j * 128, wr[:, cc * 128:(cc + 1) * 128], [t_wr])
                for j in range(4):
                    cc = g4 * 4 + j
                    p.emit("act", lambda bh=bh, j=j, cc=cc, ot4=ot4: nc.scalar.copy(
                        woT[:, cc, ot4 * 128:(ot4 + 1) * 128], bh[:, j * 128:(j + 1) * 128]),
                        reads=[bt], writes=[t_woT])
        for tt in range(NT):
            bi = 4 + tt % 4
            sout, t_sout = souts[tt % 2]
            for hd in range(NC):
                c.mm(bi, (0, 512), oT[:, hd, tt * 128:(tt + 1) * 128], woT[:, hd, :],
                     [t_oT, t_woT], start=(hd == 0), stop=(hd == NC - 1))
            bh, bt = c.banks[bi]
            p.emit("act", lambda bh=bh, sout=sout: nc.scalar.copy(sout[:], bh[:]),
                   reads=[bt], writes=[t_sout])
            p.dma("qact", s_ap[tt * 128:(tt + 1) * 128, ec * 512:(ec + 1) * 512], sout[:],
                  reads=[t_sout], writes=[t_s[tt]])
    p.pop_scope()
    return t_s


def s_l0_attn(c: C, xnt_ap, wqa_ap, qlnw_ap, wqb_ap, wkva_ap, kvlnw_ap, wkvb_ap, wo_ap,
              qn_ap, ot_ap, s_ap, pfx="a0", dbg=None):
    p, nc = c.p, c.nc
    H, DN, DR, DV, R = 16, 128, 64, 128, 512
    sm = 1.0 / np.sqrt(DN + DR)
    zz, zzt = c.const(0.0)
    one, onet = c.const(1.0)
    eb, ebt = c.const(1e-6)
    nh, nht = c.const(-0.5)
    smc, smct = c.const(sm)
    i512, i512t = c.const(1.0 / 512)
    i1536, i1536t = c.const(1.0 / 1536)
    cos_d = nc.dram_tensor("c_cos", [T, 64], F32, kind="ExternalInput").ap()
    sin_d = nc.dram_tensor("c_sin", [T, 64], F32, kind="ExternalInput").ap()
    t_qn = [p.tok(f"{pfx}qn{h}") for h in range(H)]
    t_ot = [p.tok(f"{pfx}ot{h}") for h in range(H)]
    t_s = [p.tok(f"{pfx}s{i}") for i in range(NT)]

    p.push_scope()
    coss, t_cos = p.sb(f"{pfx}cos", [128, NT, 64], F32)
    sins, t_sin = p.sb(f"{pfx}sin", [128, NT, 64], F32)
    for tt in range(NT):
        p.dma("qsp", coss[:, tt, :], cos_d[tt * 128:(tt + 1) * 128, :], writes=[t_cos])
        p.dma("qsp", sins[:, tt, :], sin_d[tt * 128:(tt + 1) * 128, :], writes=[t_sin])
    ckvT, t_ckvT = p.sb(f"{pfx}ckvT", [128, 4, T], BF16)
    kpeT, t_kpeT = p.sb(f"{pfx}kpeT", [128, T], BF16)
    qpeT, t_qpeT = p.sb(f"{pfx}qpeT", [128, 8, T], BF16)

    p.push_scope()
    qanT, t_qanT = p.sb(f"{pfx}qanT", [128, 12, T], BF16)
    p.push_scope()
    xsl = [p.sb(f"{pfx}xsl{i}", [128, NC, 128], BF16) for i in range(2)]
    wkva, t_wkva = p.sb(f"{pfx}wkva", [128, NC, 576], BF16)
    for cc in range(NC):
        p.dma("qpool", wkva[:, cc, :], wkva_ap[cc * 128:(cc + 1) * 128, :], writes=[t_wkva])
    wqa, t_wqa = p.sb(f"{pfx}wqa", [128, NC, 1536], BF16)
    for cc in range(NC):
        p.dma("qpool", wqa[:, cc, :], wqa_ap[cc * 128:(cc + 1) * 128, :], writes=[t_wqa])
    cf32, t_cf32 = p.sb(f"{pfx}cf32", [128, 512], F32)
    qaf, t_qaf = p.sb(f"{pfx}qaf", [128, 1536], F32)
    sqs, t_sqs = p.sb(f"{pfx}sqs", [128, 1536], F32)
    st, t_st = p.sb(f"{pfx}st", [128, 8], F32)
    ckn, t_ckn = p.sb(f"{pfx}ckn", [128, 512], BF16)
    qan, t_qan = p.sb(f"{pfx}qan", [128, 1536], BF16)
    kpe, t_kpe = p.sb(f"{pfx}kpe", [128, 64], F32)
    kper, t_kper = p.sb(f"{pfx}kper", [128, 64], BF16)
    rr1, t_rr1 = p.sb(f"{pfx}rr1", [128, 64], F32)
    rr2, t_rr2 = p.sb(f"{pfx}rr2", [128, 64], F32)
    for tt in range(NT):
        xh, t_xh = xsl[tt % 2]
        src = xnt_ap[:, tt * 128:(tt + 1) * 128].rearrange("(c p) k -> p c k", p=128)
        p.dma("qsp", xh[:], src, writes=[t_xh])
        for cc in range(NC):
            c.mm(0, (0, 512), xh[:, cc, :], wkva[:, cc, 0:512], [t_xh, t_wkva],
                 start=(cc == 0), stop=(cc == NC - 1))
        for cc in range(NC):
            c.mm(1, (0, 64), xh[:, cc, :], wkva[:, cc, 512:576], [t_xh, t_wkva],
                 start=(cc == 0), stop=(cc == NC - 1))
        bh0, bt0 = c.banks[0]
        p.emit("act", lambda bh0=bh0: nc.scalar.copy(cf32[:], bh0[:]), reads=[bt0], writes=[t_cf32])
        p.emit("dve", lambda: nc.vector.tensor_tensor(sqs[:, 0:512], cf32[:], cf32[:], OP.mult),
               reads=[t_cf32], writes=[t_sqs])
        p.emit("dve", lambda: nc.vector.tensor_reduce(st[:, 0:1], sqs[:, 0:512], AX.X, OP.add),
               reads=[t_sqs], writes=[t_st])
        p.emit("act", lambda: nc.scalar.activation(st[:, 1:2], st[:, 0:1], AF.Ln, bias=eb[:], scale=i512[:]),
               reads=[t_st, ebt, i512t], writes=[t_st])
        p.emit("act", lambda: nc.scalar.activation(st[:, 2:3], st[:, 1:2], AF.Exp, bias=zz[:], scale=nh[:]),
               reads=[t_st, nht, zzt], writes=[t_st])
        p.emit("dve", lambda: nc.vector.tensor_scalar(ckn[:], cf32[:], st[:, 2:3], None, OP.mult),
               reads=[t_cf32, t_st], writes=[t_ckn])
        for rc in range(4):
            c.transpose(2, rc * 128, ckn[:, rc * 128:(rc + 1) * 128], [t_ckn])
        bh2, bt2 = c.banks[2]
        for rc in range(4):
            p.emit("act", lambda bh2=bh2, tt=tt, rc=rc: nc.scalar.copy(
                ckvT[:, rc, tt * 128:(tt + 1) * 128], bh2[:, rc * 128:(rc + 1) * 128]),
                reads=[bt2], writes=[t_ckvT])
        bh1, bt1 = c.banks[1]
        p.emit("act", lambda bh1=bh1: nc.scalar.copy(kpe[:], bh1[0:128, 0:64]), reads=[bt1], writes=[t_kpe])
        p.emit("dve", lambda tt=tt: nc.vector.tensor_tensor(
            rr1[:, 0:32], kpe[:, 0:32], coss[:, tt, 0:32], OP.mult), reads=[t_kpe, t_cos], writes=[t_rr1])
        p.emit("dve", lambda tt=tt: nc.vector.tensor_tensor(
            rr2[:, 0:32], kpe[:, 32:64], sins[:, tt, 0:32], OP.mult), reads=[t_kpe, t_sin], writes=[t_rr2])
        p.emit("dve", lambda: nc.vector.tensor_tensor(
            kper[:, 0:32], rr1[:, 0:32], rr2[:, 0:32], OP.subtract), reads=[t_rr1, t_rr2], writes=[t_kper])
        p.emit("dve", lambda tt=tt: nc.vector.tensor_tensor(
            rr1[:, 0:32], kpe[:, 32:64], coss[:, tt, 32:64], OP.mult), reads=[t_kpe, t_cos], writes=[t_rr1])
        p.emit("dve", lambda tt=tt: nc.vector.tensor_tensor(
            rr2[:, 0:32], kpe[:, 0:32], sins[:, tt, 32:64], OP.mult), reads=[t_kpe, t_sin], writes=[t_rr2])
        p.emit("dve", lambda: nc.vector.tensor_tensor(
            kper[:, 32:64], rr1[:, 0:32], rr2[:, 0:32], OP.add), reads=[t_rr1, t_rr2], writes=[t_kper])
        c.transpose(3, 0, kper[:], [t_kper])
        bh3, bt3 = c.banks[3]
        p.emit("act", lambda bh3=bh3, tt=tt: nc.scalar.copy(
            kpeT[0:64, tt * 128:(tt + 1) * 128], bh3[0:64, 0:128]), reads=[bt3], writes=[t_kpeT])
        for oc in range(3):
            for cc in range(NC):
                c.mm(5 + oc, (0, 512), xh[:, cc, :], wqa[:, cc, oc * 512:(oc + 1) * 512],
                     [t_xh, t_wqa], start=(cc == 0), stop=(cc == NC - 1))
            bho, bto = c.banks[5 + oc]
            p.emit("act", lambda bho=bho, oc=oc: nc.scalar.copy(
                qaf[:, oc * 512:(oc + 1) * 512], bho[:]), reads=[bto], writes=[t_qaf])
        p.emit("dve", lambda: nc.vector.tensor_tensor(sqs[:], qaf[:], qaf[:], OP.mult),
               reads=[t_qaf], writes=[t_sqs])
        p.emit("dve", lambda: nc.vector.tensor_reduce(st[:, 4:5], sqs[:], AX.X, OP.add),
               reads=[t_sqs], writes=[t_st])
        p.emit("act", lambda: nc.scalar.activation(st[:, 5:6], st[:, 4:5], AF.Ln, bias=eb[:], scale=i1536[:]),
               reads=[t_st, ebt, i1536t], writes=[t_st])
        p.emit("act", lambda: nc.scalar.activation(st[:, 6:7], st[:, 5:6], AF.Exp, bias=zz[:], scale=nh[:]),
               reads=[t_st, nht, zzt], writes=[t_st])
        p.emit("dve", lambda: nc.vector.tensor_scalar(qan[:], qaf[:], st[:, 6:7], None, OP.mult),
               reads=[t_qaf, t_st], writes=[t_qan])
        for g4 in range(3):
            bh4, bt4 = c.banks[4]
            for j in range(4):
                cc = g4 * 4 + j
                c.transpose(4, j * 128, qan[:, cc * 128:(cc + 1) * 128], [t_qan])
            for j in range(4):
                cc = g4 * 4 + j
                p.emit("act", lambda bh4=bh4, j=j, cc=cc, tt=tt: nc.scalar.copy(
                    qanT[:, cc, tt * 128:(tt + 1) * 128], bh4[:, j * 128:(j + 1) * 128]),
                    reads=[bt4], writes=[t_qanT])
    if dbg is not None:
        p.dma("qsp", dbg["qanT"][:, :], qanT[:].rearrange("p a b -> p (a b)"), reads=[t_qanT])
        p.dma("qsp", dbg["ckvT"][:, :], ckvT[:].rearrange("p a b -> p (a b)"), reads=[t_ckvT])
        p.dma("qsp", dbg["kpeT"][:, :], kpeT[:], reads=[t_kpeT])
    p.pop_scope()

    p.global_barrier()
    p.push_scope()
    lnq, t_lnq = p.sb(f"{pfx}lnq", [128, 12], F32)
    p.dma("qsp", lnq[:], qlnw_ap.rearrange("(cq pq) -> pq cq", pq=128), writes=[t_lnq],
          allow_slow_non_contiguous=True)
    wqbn, t_wqbn = p.sb(f"{pfx}wqbn", [128, 12, 2048], BF16)
    for cq in range(12):
        srcq = wqb_ap[cq * 128:(cq + 1) * 128, :].rearrange("pq (hh dd) -> pq hh dd", hh=16)[:, :, 0:128]
        p.dma("qpool", wqbn[:, cq, :].rearrange("pq (hh dd) -> pq hh dd", hh=16), srcq,
              writes=[t_wqbn])
    for cc in range(12):
        p.emit("dve", lambda cc=cc: nc.vector.tensor_scalar(
            wqbn[:, cc, :], wqbn[:, cc, :], lnq[:, cc:cc + 1], None, OP.mult),
            reads=[t_lnq, t_wqbn], writes=[t_wqbn])
    qnh, t_qnh = p.sb(f"{pfx}qnh", [128, T], BF16)
    if dbg is not None and "wqbn" in dbg:
        p.dma("qsp", dbg["wqbn"][:, :], wqbn[:].rearrange("p a b -> p (a b)"), reads=[t_wqbn])
    for h in range(H):
        for tc in range(T // 512):
            for cc in range(12):
                c.mm(tc, (0, 512), wqbn[:, cc, h * 128:(h + 1) * 128],
                     qanT[:, cc, tc * 512:(tc + 1) * 512], [t_wqbn, t_qanT],
                     start=(cc == 0), stop=(cc == 11))
            bh, bt = c.banks[tc]
            p.emit("act", lambda bh=bh, tc=tc, qnh=qnh: nc.scalar.activation(
                qnh[:, tc * 512:(tc + 1) * 512], bh[:], AF.Copy, bias=0.0, scale=smc[:]),
                reads=[bt, zzt, smct], writes=[t_qnh])
        if dbg is not None and "qnh0" in dbg and h == 0:
            p.dma("qsp", dbg["qnh0"][:, :], qnh[:], reads=[t_qnh])
        p.dma("qact", qn_ap[h * 128:(h + 1) * 128, :], qnh[:], reads=[t_qnh], writes=[t_qn[h]])
    p.pop_scope()
    if dbg is not None and dbg.get("stop_after_b3"):
        p.pop_scope()
        p.pop_scope()
        return t_s

    p.push_scope()
    wqbp, t_wqbp = p.sb(f"{pfx}wqbp", [128, 12, 1024], BF16)
    lnq2, t_lnq2 = p.sb(f"{pfx}lnq2", [128, 12], F32)
    p.dma("qsp", lnq2[:], qlnw_ap.rearrange("(cq pq) -> pq cq", pq=128), writes=[t_lnq2],
          allow_slow_non_contiguous=True)
    for cq in range(12):
        srcq = wqb_ap[cq * 128:(cq + 1) * 128, :].rearrange("pq (hh dd) -> pq hh dd", hh=16)[:, :, 128:192]
        p.dma("qpool", wqbp[:, cq, :].rearrange("pq (hh dd) -> pq hh dd", hh=16), srcq,
              writes=[t_wqbp])
    for cc in range(12):
        p.emit("dve", lambda cc=cc: nc.vector.tensor_scalar(
            wqbp[:, cc, :], wqbp[:, cc, :], lnq2[:, cc:cc + 1], None, OP.mult),
            reads=[t_lnq2, t_wqbp], writes=[t_wqbp])
    qpf, t_qpf = p.sb(f"{pfx}qpf", [128, 1024], BF16)
    qpr, t_qpr = p.sb(f"{pfx}qpr", [128, 1024], BF16)
    qr1, t_qr1 = p.sb(f"{pfx}qr1", [128, 512], F32)
    qr2, t_qr2 = p.sb(f"{pfx}qr2", [128, 512], F32)
    for tt in range(NT):
        for b2 in range(2):
            for cc in range(12):
                c.mm(b2, (0, 512), qanT[:, cc, tt * 128:(tt + 1) * 128],
                     wqbp[:, cc, b2 * 512:(b2 + 1) * 512], [t_qanT, t_wqbp],
                     start=(cc == 0), stop=(cc == 11))
            bh, bt = c.banks[b2]
            p.emit("act", lambda bh=bh, b2=b2: nc.scalar.activation(
                qpf[:, b2 * 512:(b2 + 1) * 512], bh[:], AF.Copy, bias=0.0, scale=smc[:]),
                reads=[bt, zzt, smct], writes=[t_qpf])
        qv = qpf[:].rearrange("p (hh d) -> p hh d", hh=16)
        qo = qpr[:].rearrange("p (hh d) -> p hh d", hh=16)
        cv = coss[:, tt, :].unsqueeze(1).broadcast_to([128, 16, 64])
        sv = sins[:, tt, :].unsqueeze(1).broadcast_to([128, 16, 64])
        r1v = qr1[:].rearrange("p (hh d) -> p hh d", hh=16)
        r2v = qr2[:].rearrange("p (hh d) -> p hh d", hh=16)
        p.emit("dve", lambda r1v=r1v, qv=qv, cv=cv: nc.vector.tensor_tensor(
            r1v[:, :, 0:32], qv[:, :, 0:32], cv[:, :, 0:32], OP.mult), reads=[t_qpf, t_cos], writes=[t_qr1])
        p.emit("dve", lambda r2v=r2v, qv=qv, sv=sv: nc.vector.tensor_tensor(
            r2v[:, :, 0:32], qv[:, :, 32:64], sv[:, :, 0:32], OP.mult), reads=[t_qpf, t_sin], writes=[t_qr2])
        p.emit("dve", lambda qo=qo, r1v=r1v, r2v=r2v: nc.vector.tensor_tensor(
            qo[:, :, 0:32], r1v[:, :, 0:32], r2v[:, :, 0:32], OP.subtract),
            reads=[t_qr1, t_qr2], writes=[t_qpr])
        p.emit("dve", lambda r1v=r1v, qv=qv, cv=cv: nc.vector.tensor_tensor(
            r1v[:, :, 0:32], qv[:, :, 32:64], cv[:, :, 32:64], OP.mult), reads=[t_qpf, t_cos], writes=[t_qr1])
        p.emit("dve", lambda r2v=r2v, qv=qv, sv=sv: nc.vector.tensor_tensor(
            r2v[:, :, 0:32], qv[:, :, 0:32], sv[:, :, 32:64], OP.mult), reads=[t_qpf, t_sin], writes=[t_qr2])
        p.emit("dve", lambda qo=qo, r1v=r1v, r2v=r2v: nc.vector.tensor_tensor(
            qo[:, :, 32:64], r1v[:, :, 0:32], r2v[:, :, 0:32], OP.add),
            reads=[t_qr1, t_qr2], writes=[t_qpr])
        for h in range(H):
            bi = 2 + h % 2
            bh, bt = c.banks[bi]
            c.transpose(bi, 0, qpr[:, h * 64:(h + 1) * 64], [t_qpr])
            b0 = (h % 2) * 64
            p.emit("act", lambda bh=bh, h=h, tt=tt, b0=b0: nc.scalar.copy(
                qpeT[b0:b0 + 64, h // 2, tt * 128:(tt + 1) * 128],
                bh[0:64, 0:128]), reads=[bt], writes=[t_qpeT])
    p.pop_scope()
    p.pop_scope()
    if dbg is not None and dbg.get("stop_after_b2"):
        p.pop_scope()
        return t_s

    p.push_scope()
    lnkv, t_lnkv = p.sb(f"{pfx}lnkv", [128, 4], F32)
    p.dma("qsp", lnkv[:], kvlnw_ap.rearrange("(cq pq) -> pq cq", pq=128), writes=[t_lnkv],
          allow_slow_non_contiguous=True)
    wkvb, t_wkvb = p.sb(f"{pfx}wkvb", [128, 4, 4096], BF16)
    for rc in range(4):
        p.dma("qpool", wkvb[:, rc, :], wkvb_ap[rc * 128:(rc + 1) * 128, :], writes=[t_wkvb])
    for rc in range(4):
        p.emit("dve", lambda rc=rc: nc.vector.tensor_scalar(
            wkvb[:, rc, :], wkvb[:, rc, :], lnkv[:, rc:rc + 1], None, OP.mult),
            reads=[t_lnkv, t_wkvb], writes=[t_wkvb])
    kabsT, t_kabsT = p.sb(f"{pfx}kabsT", [128, T], BF16)
    vT, t_vT = p.sb(f"{pfx}vT", [128, T], BF16)
    vtm, t_vtm = p.sb(f"{pfx}vtm", [128, NT, 128], BF16)
    qnhs = [p.sb(f"{pfx}qnh{i}", [128, T], BF16) for i in range(2)]
    psls = [(p.sb(f"{pfx}psl{i}", [128, T], BF16)[0],
             [p.tok(f"{pfx}pt{i}_{j}") for j in range(4)]) for i in range(3)]
    lsums = [p.sb(f"{pfx}ls{i}", [128, 12], F32) for i in range(3)]
    diags = [p.sb(f"{pfx}dg{i}", [128, 128], BF16) for i in range(3)]
    pT, t_pT = p.sb(f"{pfx}pT", [128, NT, 512], BF16)
    oThs = [p.sb(f"{pfx}oTh{i}", [128, 512], BF16) for i in range(2)]
    qpes, t_qpes = p.sb(f"{pfx}qpes", [128, T], BF16)
    for h in range(H):
        qnhd, t_qnhd = qnhs[h % 2]
        p.dma("qsp", qnhd[:], qn_ap[h * 128:(h + 1) * 128, :], reads=[t_qn[h]], writes=[t_qnhd])
        b0h = (h % 2) * 64
        if dbg is None or not dbg.get("no_qpes"):
            p.dma("qsp", qpes[0:64, :], qpeT[b0h:b0h + 64, h // 2, :], reads=[t_qpeT], writes=[t_qpes])
        for tc in range(T // 512):
            for rc in range(4):
                c.mm(tc, (0, 512), wkvb[:, rc, h * 256:h * 256 + 128],
                     ckvT[:, rc, tc * 512:(tc + 1) * 512], [t_wkvb, t_ckvT],
                     start=(rc == 0), stop=(rc == 3))
            bh, bt = c.banks[tc]
            p.emit("act", lambda bh=bh, tc=tc: nc.scalar.copy(
                kabsT[:, tc * 512:(tc + 1) * 512], bh[:]), reads=[bt], writes=[t_kabsT])
        for tc in range(T // 512):
            for rc in range(4):
                c.mm(tc, (0, 512), wkvb[:, rc, h * 256 + 128:h * 256 + 256],
                     ckvT[:, rc, tc * 512:(tc + 1) * 512], [t_wkvb, t_ckvT],
                     start=(rc == 0), stop=(rc == 3))
            bh, bt = c.banks[tc]
            p.emit("act", lambda bh=bh, tc=tc: nc.scalar.copy(
                vT[:, tc * 512:(tc + 1) * 512], bh[:]), reads=[bt], writes=[t_vT])
        for tt0 in range(0, NT, 4):
            bi = (tt0 // 4) % 2
            bh, bt = c.banks[bi]
            for j in range(4):
                c.transpose(bi, j * 128, vT[:, (tt0 + j) * 128:(tt0 + j + 1) * 128], [t_vT])
            p.emit("act", lambda bh=bh, tt0=tt0: nc.scalar.copy(
                vtm[:, tt0:tt0 + 4, :].rearrange("p a b -> p (a b)"), bh[:, 0:512]),
                reads=[bt], writes=[t_vtm])

        def emit_T(item):
            qt, pslh, ptoks, dgh, t_dgh = item
            qi = qt % 4
            for kt0 in range(0, qt + 1, 4):
                ntl = min(4, qt + 1 - kt0)
                bi = (kt0 // 4) % 2
                bh, bt = c.banks[bi]
                for j in range(ntl):
                    kt = kt0 + j
                    p.emit("pe", lambda bh=bh, j=j, kt=kt, pslh=pslh, dgh=dgh:
                        nc.tensor.matmul(bh[0:128, j * 128:(j + 1) * 128],
                                         pslh[:, kt * 128:(kt + 1) * 128], dgh[:],
                                         start=True, stop=True),
                        reads=[ptoks[kt // 4], t_dgh], writes=[bt])
                if (kt0 // 4) % 2 == 0:
                    p.emit("dve", lambda bh=bh, kt0=kt0, ntl=ntl, qi=qi: nc.vector.tensor_copy(
                        pT[:, kt0:kt0 + ntl, qi * 128:(qi + 1) * 128],
                        bh[:, 0:ntl * 128].rearrange("p (a x) -> p a x", a=ntl)),
                        reads=[bt], writes=[t_pT])
                else:
                    p.emit("act", lambda bh=bh, kt0=kt0, ntl=ntl, qi=qi: nc.scalar.copy(
                        pT[:, kt0:kt0 + ntl, qi * 128:(qi + 1) * 128],
                        bh[:, 0:ntl * 128].rearrange("p (a x) -> p a x", a=ntl)),
                        reads=[bt], writes=[t_pT])

        def emit_O(qg, h):
            qmax = qg * 4 + 3
            bo, bo_t = c.banks[2]
            oTh, t_oTh = oThs[qg % 2]
            for kt in range(qmax + 1):
                off = max(0, kt - qg * 4) * 128
                p.emit("pe", lambda kt=kt, off=off, bo=bo, qmax=qmax: nc.tensor.matmul(
                    bo[0:128, off:512], vtm[:, kt, :], pT[:, kt, off:512],
                    start=(kt == 0), stop=(kt == qmax)), reads=[t_vtm, t_pT], writes=[bo_t])
            p.emit("act", lambda bo=bo, oTh=oTh: nc.scalar.copy(oTh[:], bo[0:128, 0:512]),
                   reads=[bo_t], writes=[t_oTh])
            if dbg is None or not dbg.get("no_ot_write"):
                p.dma("qact", ot_ap[h * 128:(h + 1) * 128, qg * 512:(qg + 1) * 512], oTh[:],
                      reads=[t_oTh], writes=[t_ot[h]])

        pend_T = []
        for qt in range(NT):
            klen = (qt + 1) * 128
            pslh, ptoks = psls[qt % 3]
            dgh, t_dgh = diags[qt % 3]
            lsh, t_lsh = lsums[qt % 3]
            sc_v, sc_toks = c.scores
            nchunk = (klen + 511) // 512
            mck = qt // 4
            for ck in range(nchunk):
                n = min(512, klen - ck * 512)
                tok = [sc_toks[ck]]
                p.emit("pe", lambda ck=ck, n=n, qt=qt, qnhd=qnhd: nc.tensor.matmul(
                    sc_v[:, ck * 512:ck * 512 + n], qnhd[:, qt * 128:(qt + 1) * 128],
                    kabsT[:, ck * 512:ck * 512 + n], start=True, stop=False),
                    reads=[t_qnhd, t_kabsT], writes=tok)
                p.emit("pe", lambda ck=ck, n=n, qt=qt: nc.tensor.matmul(
                    sc_v[:, ck * 512:ck * 512 + n],
                    qpes[0:64, qt * 128:(qt + 1) * 128],
                    kpeT[0:64, ck * 512:ck * 512 + n], start=False, stop=True),
                    reads=[t_qpes, t_kpeT], writes=tok)
                if ck == mck:
                    p.emit("dve", lambda qt=qt: nc.vector.tensor_tensor(
                        sc_v[:, qt * 128:(qt + 1) * 128], sc_v[:, qt * 128:(qt + 1) * 128],
                        c.maskc_sb[:], OP.add), reads=tok + [c.t_maskc], writes=tok)
                p.emit("dve", lambda ck=ck, n=n, lsh=lsh: nc.vector.tensor_reduce(
                    lsh[:, 4 + ck:5 + ck], sc_v[:, ck * 512:ck * 512 + n], AX.X, OP.max),
                    reads=tok, writes=[t_lsh])
            p.emit("dve", lambda lsh=lsh, nchunk=nchunk: nc.vector.tensor_reduce(
                lsh[:, 0:1], lsh[:, 4:4 + nchunk], AX.X, OP.max, negate=True),
                reads=[t_lsh], writes=[t_lsh])
            if len(pend_T) >= 2:
                emit_T(pend_T.pop(0))
            for ck in range(nchunk):
                n = min(512, klen - ck * 512)
                tok = [sc_toks[ck]]
                p.emit("act", lambda ck=ck, n=n, pslh=pslh, lsh=lsh: nc.scalar.activation(
                    pslh[:, ck * 512:ck * 512 + n], sc_v[:, ck * 512:ck * 512 + n],
                    AF.Exp, bias=lsh[:, 0:1], scale=one[:], accum_out=lsh[:, 8 + ck:9 + ck]),
                    reads=tok + [t_lsh, onet], writes=[ptoks[ck], t_lsh])
            p.emit("dve", lambda lsh=lsh, nchunk=nchunk: nc.vector.tensor_reduce(
                lsh[:, 1:2], lsh[:, 8:8 + nchunk], AX.X, OP.add), reads=[t_lsh], writes=[t_lsh])
            p.emit("dve", lambda lsh=lsh: nc.vector.reciprocal(lsh[:, 2:3], lsh[:, 1:2]),
                   reads=[t_lsh], writes=[t_lsh])
            p.emit("dve", lambda dgh=dgh, lsh=lsh: nc.vector.tensor_tensor(
                dgh[:], c.ident_sb[:], lsh[:, 2:3].broadcast_to([128, 128]), OP.mult),
                reads=[c.t_ident, t_lsh], writes=[t_dgh])
            pend_T.append((qt, pslh, ptoks, dgh, t_dgh))
            if qt % 4 == 1 and qt >= 5:
                emit_O((qt - 5) // 4, h)
        while pend_T:
            emit_T(pend_T.pop(0))
        emit_O(3, h)
    p.pop_scope()
    if dbg is not None and dbg.get("stop_after_d"):
        p.pop_scope()
        return t_s

    p.push_scope()
    wos = [p.sb(f"{pfx}wo{i}", [128, NC, 512], BF16) for i in range(1)]
    oslab = [p.sb(f"{pfx}osl{i}", [128, NC, 128], BF16) for i in range(2)]
    sout, t_sout = p.sb(f"{pfx}so", [128, 512], BF16)
    wo_h, t_wo = wos[0]
    for ec in range(4):
        for cc in range(NC):
            p.dma("qpool", wo_h[:, cc, :], wo_ap[cc * 128:(cc + 1) * 128, ec * 512:(ec + 1) * 512],
                  writes=[t_wo])
        for tt in range(NT):
            osl, t_osl = oslab[tt % 2]
            if ec == 0 or True:
                srco = ot_ap[:, tt * 128:(tt + 1) * 128].rearrange("(hh pp) k -> pp hh k", pp=128)
                p.dma("qsp", osl[:], srco, reads=t_ot, writes=[t_osl])
            bi = 4 + tt % 4
            for hd in range(NC):
                c.mm(bi, (0, 512), osl[:, hd, :], wo_h[:, hd, :], [t_osl, t_wo],
                     start=(hd == 0), stop=(hd == NC - 1))
            bh, bt = c.banks[bi]
            p.emit("act", lambda bh=bh: nc.scalar.copy(sout[:], bh[:]), reads=[bt], writes=[t_sout])
            p.dma("qact", s_ap[tt * 128:(tt + 1) * 128, ec * 512:(ec + 1) * 512], sout[:],
                  reads=[t_sout], writes=[t_s[tt]])
    p.pop_scope()
    p.pop_scope()
    return t_s


def s_combine(c: C, x_ap, s_ap, g_ap, xout_ap, pfx, out_f32=True):
    p, nc = c.p, c.nc
    p.push_scope()
    xs = [p.sb(f"{pfx}x{i}", [128, E], F32) for i in range(2)]
    ss = [p.sb(f"{pfx}s{i}", [128, E], BF16) for i in range(2)]
    gs = [p.sb(f"{pfx}g{i}", [128, E], BF16) for i in range(2)]
    sg, t_sg = p.sb(f"{pfx}sg", [128, E], F32)
    xo = [p.sb(f"{pfx}o{i}", [128, E], F32) for i in range(2)]
    for t in range(NT):
        xh, xt = xs[t % 2]
        sh, st_ = ss[t % 2]
        gh, gt = gs[t % 2]
        oh, ot_ = xo[t % 2]
        p.dma("qsp", xh[:], x_ap[t * 128:(t + 1) * 128, :], writes=[xt])
        p.dma("qsp", sh[:], s_ap[t * 128:(t + 1) * 128, :], writes=[st_])
        p.dma("qsp", gh[:], g_ap[t * 128:(t + 1) * 128, :], writes=[gt])
        p.emit("dve", lambda sh=sh, gh=gh: nc.vector.tensor_tensor(sg[:], sh[:], gh[:], OP.mult),
               reads=[st_, gt], writes=[t_sg])
        p.emit("dve", lambda xh=xh, oh=oh: nc.vector.tensor_tensor(oh[:], xh[:], sg[:], OP.add),
               reads=[xt, t_sg], writes=[ot_])
        p.dma("qsp", xout_ap[t * 128:(t + 1) * 128, :], oh[:], reads=[ot_])
    p.pop_scope()



import contextlib as _ctxlib
import numpy as _np
import ml_dtypes as _mld
from concourse.bass_utils import run_bass_kernel_spmd as _run_spmd

_BF = _mld.bfloat16
_T = 2048
_E = 2048
_NCORES = 8
MODE = "single"


def _host_consts():
    DR = 64
    inv = 1.0 / 10000.0 ** (_np.arange(0, DR, 2, dtype=_np.float32) / DR)
    ang = _np.arange(_T, dtype=_np.float32)[:, None] * inv[None, :]
    cos = _np.concatenate([_np.cos(ang), _np.cos(ang)], -1).astype(_np.float32)
    sin = _np.concatenate([_np.sin(ang), _np.sin(ang)], -1).astype(_np.float32)
    return {
        "c_ident": _np.eye(128, dtype=_np.float32),
        "c_mask": _np.where(_np.triu(_np.ones((128, 128), bool), k=1), -1e30, 0.0).astype(_np.float32),
        "c_cos": cos, "c_sin": sin,
    }


def _tof(a):
    a = _np.asarray(a)
    return a.view(_BF) if a.dtype.kind == "V" else a


def _decl(nc, name, shape, dt, kind):
    return nc.dram_tensor(name, list(shape), dt, kind=kind).ap()


class _Prog:

    def __init__(self, name, ins, outs, internals, build_fn):
        self.name = name
        self.ins = ins
        self.outs = outs
        self.internals = internals
        self.build_fn = build_fn

    def build(self):
        nc = make_nc()
        ctx = _ctxlib.ExitStack()
        aps = {}
        with ctx:
            p = P(nc, ctx)
            c = C(p)
            for key, shape, dt in self.ins:
                if key in ("c_cos", "c_sin"):
                    continue
                aps[key] = _decl(nc, key, shape, dt, "ExternalInput")
            for key, shape, dt in self.outs:
                aps[key] = _decl(nc, key, shape, dt, "ExternalOutput")
            for key, shape, dt in self.internals:
                aps[key] = _decl(nc, key, shape, dt, "Internal")
            self.build_fn(c, aps)
            p.finish()
            block = ctx.enter_context(nc.Block())
            p.finalize_into_block(block)
        return nc


def _b_ln0(c, a):
    s_ln(c, a["x"], a["xnt0"], "n0_")


def _b_l0attn(c, a):
    s_l0_attn(c, a["xnt0"], a["L0_attn_w_q_a"], a["L0_attn_q_a_ln_w"], a["L0_attn_w_q_b"],
              a["L0_attn_w_kv_a"], a["L0_attn_kv_a_ln_w"], a["L0_attn_w_kv_b"],
              a["L0_attn_w_o"], a["qn0"], a["ot0"], a["s0"], pfx="a0")


def _b_l0ffn(c, a):
    s_l0_ffn(c, a["xnt0"], a["L0_ffn_gate_up_packed"], a["L0_ffn_gate_up_scales"],
             a["L0_ffn_down_packed"], a["L0_ffn_down_scales"], a["ht0"], a["y0"], a["g0"], pfx="f0")


def _b_comb0(c, a):
    s_combine(c, a["x"], a["s0"], a["g0"], a["x1"], "c0_")
    c.p.global_barrier()
    s_ln(c, a["x1"], a["xnt1"], "n1_")


def _b_l1attn(c, a):
    s_l1_attn(c, a["xnt1"], a["L1_attn_w_project"], a["L1_attn_w_out"], a["s1"], pfx="a1")


def _b_l1ffn(c, a):
    s_l1_ffn(c, a["xnt1"], a["L1_ffn_gate_W_packed"], a["L1_ffn_gate_scales"],
             a["L1_ffn_up_W_packed"], a["L1_ffn_up_scales"], a["L1_ffn_down_W_packed"],
             a["L1_ffn_down_scales"], a["it1"], a["y1"], a["g1"], pfx="f1")


def _b_comb1(c, a):
    s_combine(c, a["x1"], a["s1"], a["g1"], a["out"], "c1_")


_W0A = [("L0_attn_w_q_a", (2048, 1536), F32), ("L0_attn_q_a_ln_w", (1536,), F32),
        ("L0_attn_w_q_b", (1536, 3072), F32), ("L0_attn_w_kv_a", (2048, 576), F32),
        ("L0_attn_kv_a_ln_w", (512,), F32), ("L0_attn_w_kv_b", (512, 4096), F32),
        ("L0_attn_w_o", (2048, 2048), F32)]
_W0F = [("L0_ffn_gate_up_packed", (11264, 256), I32), ("L0_ffn_gate_up_scales", (16, 11264), F16),
        ("L0_ffn_down_packed", (2048, 704), I32), ("L0_ffn_down_scales", (44, 2048), F16)]
_W1A = [("L1_attn_w_project", (6144, 2048), F32), ("L1_attn_w_out", (2048, 2048), F32)]
_W1F = [("L1_ffn_gate_W_packed", (16384, 1024), I32), ("L1_ffn_gate_scales", (16384, 64), F32),
        ("L1_ffn_up_W_packed", (16384, 1024), I32), ("L1_ffn_up_scales", (16384, 64), F32),
        ("L1_ffn_down_W_packed", (2048, 8192), I32), ("L1_ffn_down_scales", (2048, 512), F32)]

_XNT = lambda n: (n, (_E, _T), BF16)
_TME = lambda n, dt=BF16: (n, (_T, _E), dt)

_LAUNCHES = [
    _Prog("ln0", [("x", (_T, _E), F32)], [_XNT("xnt0")], [], _b_ln0),
    _Prog("l0attn", [_XNT("xnt0")] + _W0A + [("c_cos", (_T, 64), F32), ("c_sin", (_T, 64), F32)],
          [_TME("s0")], [("qn0", (2048, _T), BF16), ("ot0", (2048, _T), BF16)], _b_l0attn),
    _Prog("l0ffn", [_XNT("xnt0")] + _W0F, [_TME("g0")],
          [("ht0", (5632, _T), BF16), ("y0", (_T, _E), BF16)], _b_l0ffn),
    _Prog("comb0", [("x", (_T, _E), F32), _TME("s0"), _TME("g0")],
          [("x1", (_T, _E), F32), _XNT("xnt1")], [], _b_comb0),
    _Prog("l1attn", [_XNT("xnt1")] + _W1A, [_TME("s1")], [], _b_l1attn),
    _Prog("l1ffn", [_XNT("xnt1")] + _W1F, [_TME("g1")],
          [("it1", (16384, _T), BF16), ("y1", (_T, _E), BF16)], _b_l1ffn),
    _Prog("comb1", [("x1", (_T, _E), F32), _TME("s1"), _TME("g1")],
          [("out", (_T, _E), F32)], [], _b_comb1),
]


def _build_single():
    nc = make_nc()
    ctx = _ctxlib.ExitStack()
    a = {}
    with ctx:
        p = P(nc, ctx)
        c = C(p)
        a["x"] = _decl(nc, "x", (_T, _E), F32, "ExternalInput")
        for key, shape, dt in _W0A + _W0F + _W1A + _W1F:
            a[key] = _decl(nc, key, shape, dt, "ExternalInput")
        a["out"] = _decl(nc, "out", (_T, _E), F32, "ExternalOutput")
        for key, shape, dt in [_XNT("xnt0"), _XNT("xnt1"), _TME("s0"), _TME("g0"),
                               ("x1", (_T, _E), F32), _TME("s1"), _TME("g1"),
                               ("qn0", (2048, _T), BF16), ("ot0", (2048, _T), BF16),
                               ("ht0", (5632, _T), BF16), ("y0", (_T, _E), BF16),
                               ("it1", (16384, _T), BF16), ("y1", (_T, _E), BF16)]:
            a[key] = _decl(nc, key, shape, dt, "Internal")
        _b_ln0(c, a)
        p.global_barrier()
        _b_l0attn(c, a)
        p.global_barrier()
        _b_l0ffn(c, a)
        p.global_barrier()
        _b_comb0(c, a)
        p.global_barrier()
        _b_l1attn(c, a)
        p.global_barrier()
        _b_l1ffn(c, a)
        p.global_barrier()
        _b_comb1(c, a)
        p.finish()
        block = ctx.enter_context(nc.Block())
        p.finalize_into_block(block)
    return nc


def _kernel_single(inputs):
    consts = _host_consts()
    x_full = _np.ascontiguousarray(inputs["L0_attn_x"].astype(_np.float32))
    nc = _build_single()
    in_maps = []
    for i in range(_NCORES):
        m = {"c_ident": consts["c_ident"], "c_mask": consts["c_mask"],
             "c_cos": consts["c_cos"], "c_sin": consts["c_sin"],
             "x": _np.ascontiguousarray(x_full[i])}
        for key, shape, dt in _W0A + _W0F + _W1A + _W1F:
            m[key] = _np.ascontiguousarray(inputs[key])
        in_maps.append(m)
    res = _run_spmd(nc, in_maps, core_ids=list(range(_NCORES)))
    return _np.stack([_np.asarray(res.results[i]["out"]).astype(_np.float32)
                      for i in range(_NCORES)], axis=0)


def kernel(**inputs):
    if MODE == "single":
        return _kernel_single(inputs)
    consts = _host_consts()
    x_full = _np.ascontiguousarray(inputs["L0_attn_x"].astype(_np.float32))
    percore = {i: {"x": _np.ascontiguousarray(x_full[i])} for i in range(_NCORES)}
    for lp in _LAUNCHES:
        nc = lp.build()
        in_maps = []
        for i in range(_NCORES):
            m = {"c_ident": consts["c_ident"], "c_mask": consts["c_mask"]}
            for key, shape, dt in lp.ins:
                if key in ("c_cos", "c_sin"):
                    m[key] = consts[key]
                elif key in inputs:
                    m[key] = _np.ascontiguousarray(inputs[key])
                else:
                    m[key] = _np.ascontiguousarray(percore[i][key])
            in_maps.append(m)
        res = _run_spmd(nc, in_maps, core_ids=list(range(_NCORES)))
        for i in range(_NCORES):
            for key, shape, dt in lp.outs:
                percore[i][key] = _tof(res.results[i][key])
    out = _np.stack([percore[i]["out"].astype(_np.float32) for i in range(_NCORES)], axis=0)
    return out

